# version 04
# baseline (speedup 1.0000x reference)
import numpy as np
from contextlib import ExitStack
import concourse.bass as bass
import concourse.mybir as mybir
from concourse.bass_core_rust import IndirectOffsetOnAxis
from concourse.bass_utils import run_bass_kernel_spmd

F32 = mybir.dt.float32
BF16 = mybir.dt.bfloat16
F16 = mybir.dt.float16
I8 = mybir.dt.int8
I32 = mybir.dt.int32
U32 = mybir.dt.uint32
AF = mybir.ActivationFunctionType
ALU = mybir.AluOpType
AX = mybir.AxisListType

S = 2048
D = 1024
DC = 8
NT = 16
QB = 512
NQB = S // QB
ATT_SCALE = 1.0 / 32.0


class Ctr:
    __slots__ = ("h", "v", "name")

    def __init__(self, h, name):
        self.h = h
        self.v = 0
        self.name = name

    def tok(self):
        return (self, self.v)


class DmaPool:

    def __init__(self, subs):
        self.subs = subs
        self.i = 0
        self.last = None

    def nxt(self):
        c = self.subs[self.i]
        self.i = (self.i + 1) % len(self.subs)
        return c

    def tok(self):
        return self.last

    def all(self):
        return [(c, c.v) for c in self.subs if c.v > 0]


class Prog:
    def __init__(self):
        self.nc = bass.Bass("TRN2", target_bir_lowering=False)
        self.es = ExitStack()

    def sb(self, name, shape, dt=F32):
        return self.es.enter_context(self.nc.sbuf_tensor(name, shape, dt))

    def psum(self, name, shape, dt=F32):
        return self.es.enter_context(self.nc.psum_tensor(name, shape, dt))

    def sem(self, name):
        return Ctr(self.es.enter_context(self.nc.semaphore(name)), name)

    def din(self, name, shape, dt=F32):
        return self.nc.dram_tensor(name, shape, dt, kind="ExternalInput").ap()

    def dout(self, name, shape, dt=F32):
        return self.nc.dram_tensor(name, shape, dt, kind="ExternalOutput").ap()

    def dtmp(self, name, shape, dt=F32):
        return self.nc.dram_tensor(name, shape, dt, kind="Internal").ap()

    def fin(self):
        self.es.close()
        return self.nc


ENGS = ("t", "v", "s", "g", "y")


class Plan:

    def __init__(self, P, own=None):
        self.P = P
        self.q = {e: [] for e in ENGS}
        self.last_wait = {e: {} for e in ENGS}
        self.own = own or {}

    def step(self, eng, emit, waits=(), bump=None, n=1):
        own = self.own.get(eng)
        waits = list(waits)
        if isinstance(bump, DmaPool):
            pool = bump
            bump = pool.nxt()
            waits.append(bump.tok())
            pool.last = (bump, bump.v + n)
        if own is not None:
            waits.append(own.tok())
            if bump is None:
                bump = own
                n = 1
        wl = []
        flat = []

        def _fl(ws):
            for w in ws:
                if isinstance(w, list):
                    _fl(w)
                else:
                    flat.append(w)
        _fl(waits)
        for w in flat:
            if w is None:
                continue
            c, v = w if isinstance(w, tuple) else (w, w.v)
            assert v <= c.v, f"wait on future value of {c.name}"
            if v <= 0:
                continue
            lw = self.last_wait[eng].get(c, 0)
            if v <= lw:
                continue
            self.last_wait[eng][c] = v
            wl.append((c, v))
        if bump is not None:
            bump.v += n
        self.q[eng].append((wl, emit, bump, n))

    def emit_all(self, block):
        def runner(lst):
            def run(e):
                for wl, emit, bmp, n in lst:
                    for c, v in wl:
                        e.wait_ge(c.h, v)
                    ins = emit(e)
                    if bmp is not None:
                        assert ins is not None
                        ins.then_inc(bmp.h, n)
            return run
        for key, meth in (("y", block.sync), ("g", block.gpsimd),
                          ("t", block.tensor), ("v", block.vector),
                          ("s", block.scalar)):
            if self.q[key]:
                meth(runner(self.q[key]))


class NS:
    def __init__(self, **kw):
        self.__dict__.update(kw)



def alloc_common(P):
    B = NS()
    B.c_pe = P.sem("c_pe")
    B.c_v = P.sem("c_v")
    B.c_a = P.sem("c_a")
    B.c_g = P.sem("c_g")
    B.c_ld = DmaPool([P.sem(f"c_ld{i}") for i in range(6)])
    B.c_wg = DmaPool([P.sem(f"c_wg{i}") for i in range(6)])
    B.c_st = DmaPool([P.sem(f"c_st{i}") for i in range(4)])
    B.c_cc = P.sem("c_cc")
    B.iotat = P.sb("iotat", [128, 128], I32)
    B.ident_f = P.sb("ident_f", [128, 128], F32)
    B.ones128 = P.sb("ones128", [128, 128], BF16)
    B.ones_c = P.sb("ones_c", [128, 1], BF16)
    B.ones_r = P.sb("ones_r", [1, 128], BF16)
    B.ones_rf = P.sb("ones_rf", [1, 128], F32)
    B.AV = [P.psum(f"av{i}", [128, 512], F32) for i in range(4)]
    B.SC = [P.psum(f"sc{i}", [128, 512], F32) for i in range(2)]
    B.AUX = P.psum("aux", [128, 512], F32)
    B.NRM = P.psum("nrm", [128, 512], F32)
    B.bank_tok = {}
    return B


def emit_setup(pl, B):
    pl.step("g", lambda g: (g.iota(B.iotat[:], [[1, 128]], base=0,
                            channel_multiplier=-1), g.engine_nop())[1])
    pl.step("g", lambda g: g.tensor_scalar(B.ident_f[:], B.iotat[:], 0, None, ALU.is_equal))
    pl.step("g", lambda g: g.memset(B.ones128[:], 1.0), bump=B.c_g)
    pl.step("v", lambda v: v.memset(B.ones_c[:], 1.0))
    pl.step("v", lambda v: v.memset(B.ones_r[:], 1.0))
    pl.step("v", lambda v: v.memset(B.ones_rf[:], 1.0), bump=B.c_v)
    B.tok_setup_g = B.c_g.tok()
    B.tok_setup_v = B.c_v.tok()


def mk_plan(P, B):
    return Plan(P, own={"v": B.c_v, "s": B.c_a, "g": B.c_g})


def bank_wait(B, ps):
    t = B.bank_tok.get(id(ps))
    return [t] if t is not None else []


def bank_set(B, ps, tok):
    B.bank_tok[id(ps)] = tok



def alloc_l0_attn(P, B):
    B.XN = P.sb("XN", [128, DC * S], BF16)
    B.QF = P.sb("QF", [128, DC * S], BF16)
    B.KF = P.sb("KF", [128, DC * S], BF16)
    B.VT = P.sb("VT", [128, NT * D], BF16)
    B.STASH = P.sb("STASH", [128, 8 * 512], F16)
    B.ATT = B.STASH.bitcast(BF16)
    B.EXPB = P.sb("EXPB", [128, NT * 512], BF16)
    B.WARE = [P.sb(f"ware{i}", [128, DC * 256], BF16) for i in range(2)]
    B.RZ = P.sb("RZN", [128, 512], F32)
    B.RN = B.RZ
    B.MXR = P.sb("MXR", [1, 8], F32)
    B.NCR = P.sb("NCR", [1, 8], F32)
    B.CBS = P.sb("CBS", [128, 8], F32)
    B.BQ = P.sb("BQ", [128, DC], F32)
    B.BK = P.sb("BK", [128, DC], F32)
    B.BO = P.sb("BO", [128, DC], F32)
    B.BVR = P.sb("BVR", [1, D], BF16)
    B.BVB = P.sb("BVB", [128, D], BF16)
    B.A0T = [P.sb(f"a0t{i}", [128, 512], F32) for i in range(2)]


def load_xn(pl, B, xn_dram):
    pl.step("y", lambda y: y.dma_start(
        out=B.XN[:].rearrange("p (c t) -> p c t", c=DC),
        in_=xn_dram.rearrange("(c p) t -> p c t", p=128)), bump=B.c_ld, n=16)
    return B.c_ld.tok()


def emit_l0_attn(pl, B, wq_t, wk_t, wv_t, wo_t, bq2, bk2, bv_row, bo2,
                 a0_dram, tok_xn, guard=None):
    for buf, src in ((B.BQ, bq2), (B.BK, bk2), (B.BO, bo2)):
        pl.step("y", lambda y, buf=buf, src=src: y.dma_start(out=buf[:], in_=src),
                bump=B.c_ld, n=16)
    tok_bias_l = B.c_ld.all()
    pl.step("g", lambda g: g.dma_start(out=B.BVR[:], in_=bv_row),
            bump=B.c_wg, n=16)
    tok_bvr = B.c_wg.tok()
    pl.step("v", lambda v: v.tensor_scalar(B.BQ[:], B.BQ[:], ATT_SCALE, None, ALU.mult),
            waits=tok_bias_l, bump=B.c_v)
    tok_bq = B.c_v.tok()
    for h in range(2):
        sl = slice(h * 512, (h + 1) * 512)
        pl.step("t", lambda t, sl=sl: t.matmul(
            B.AUX[:], B.ones_r[:], B.BVR[0:1, sl], start=True, stop=True),
            waits=[tok_bvr, B.tok_setup_v] + bank_wait(B, B.AUX), bump=B.c_pe)
        pl.step("s", lambda s, sl=sl: s.copy(B.BVB[:, sl], B.AUX[:]),
                waits=[B.c_pe.tok()], bump=B.c_a)
        bank_set(B, B.AUX, B.c_a.tok())
    tok_bvb = B.c_a.tok()

    def load_wq(wdram, h, buf, waits=()):
        def go(g):
            return g.dma_start(
                out=buf[:].rearrange("p (c i) -> p c i", c=DC),
                in_=wdram[:, h * 256:(h + 1) * 256].rearrange("(c p) i -> p c i", p=128))
        pl.step("g", go, waits=list(waits), bump=B.c_wg, n=16)
        return B.c_wg.tok()

    rr = [0]

    def next_bank():
        b = B.AV[rr[0]]
        rr[0] = (rr[0] + 1) % 4
        return b

    seqw = [("q", wq_t), ("k", wk_t), ("v", wv_t), ("o", wo_t)]
    order = [(nm, h) for nm, _ in seqw for h in range(4)]
    dram_of = dict(seqw)
    wtok = {order[0]: load_wq(wq_t, 0, B.WARE[0]),
            order[1]: load_wq(wq_t, 1, B.WARE[1])}

    def prefetch_after(idx):
        if idx + 2 < len(order):
            nm, nh = order[idx + 2]
            wtok[(nm, nh)] = load_wq(dram_of[nm], nh, B.WARE[idx % 2],
                                     waits=[B.c_pe.tok()])

    oi = 0
    for name, bias, dstb in (("q", B.BQ, B.QF), ("k", B.BK, B.KF)):
        for h in range(4):
            wt = wtok[(name, h)]
            for ic in range(2 * h, 2 * h + 2):
                icl = (ic % 2) * 128
                for qb in range(NQB):
                    bank = next_bank()

                    def pe(t, oi=oi, icl=icl, qb=qb, bank=bank):
                        last = None
                        for dk in range(DC):
                            last = t.matmul(
                                bank[:],
                                B.WARE[oi % 2][:, dk * 256 + icl: dk * 256 + icl + 128],
                                B.XN[:, dk * S + qb * QB: dk * S + (qb + 1) * QB],
                                start=(dk == 0), stop=(dk == DC - 1))
                        return last
                    pl.step("t", pe, waits=[wt, list(tok_xn)] + bank_wait(B, bank),
                            bump=B.c_pe)
                    ptok = B.c_pe.tok()
                    scl = ATT_SCALE if name == "q" else 1.0

                    def ev(s, bank=bank, ic=ic, qb=qb, bias=bias, dstb=dstb,
                           scl=scl):
                        return s.activation(
                            dstb[:, ic * S + qb * QB: ic * S + (qb + 1) * QB],
                            bank[:], AF.Identity, bias=bias[:, ic:ic + 1],
                            scale=scl)
                    pl.step("s", ev, waits=[ptok, tok_bq, guard] + tok_bias_l,
                            bump=B.c_a)
                    bank_set(B, bank, B.c_a.tok())
            prefetch_after(oi)
            oi += 1
    for h in range(4):
        wt = wtok[("v", h)]
        wb = oi % 2
        for tc in range(NT):
            bank = next_bank()

            def pe(t, wb=wb, tc=tc, bank=bank):
                last = None
                for dk in range(DC):
                    last = t.matmul(
                        bank[0:128, 0:256],
                        B.XN[:, dk * S + tc * 128: dk * S + tc * 128 + 128],
                        B.WARE[wb][:, dk * 256:(dk + 1) * 256],
                        start=(dk == 0), stop=(dk == DC - 1))
                return last
            pl.step("t", pe, waits=[wt, list(tok_xn)] + bank_wait(B, bank), bump=B.c_pe)
            ptok = B.c_pe.tok()

            def ev(v, h=h, tc=tc, bank=bank):
                return v.tensor_tensor(
                    B.VT[:, tc * D + h * 256: tc * D + (h + 1) * 256],
                    bank[0:128, 0:256], B.BVB[:, h * 256:(h + 1) * 256], ALU.add)
            pl.step("v", ev, waits=[ptok, tok_bvb], bump=B.c_v)
            bank_set(B, bank, B.c_v.tok())
        prefetch_after(oi)
        oi += 1
    tok_qf = B.c_a.tok()
    tok_vt = B.c_v.tok()

    expb_guard = tok_vt
    att_guard = None
    slot_tok = [None] * 8
    bc_tok = [None] * 8
    for qb in range(NQB):
        q0 = qb * QB
        whtok = [None] * NT
        for t_ in range(NT):
            tt = t_ % 8
            bank = B.SC[t_ % 2]

            def pe(t, t_=t_, q0=q0, bank=bank):
                last = None
                for dk in range(DC):
                    last = t.matmul(
                        bank[:],
                        B.KF[:, dk * S + t_ * 128: dk * S + t_ * 128 + 128],
                        B.QF[:, dk * S + q0: dk * S + q0 + QB],
                        start=(dk == 0), stop=(dk == DC - 1))
                return last
            pl.step("t", pe, waits=[tok_qf] + bank_wait(B, bank), bump=B.c_pe)
            ptok = B.c_pe.tok()
            st_sl = slice(tt * 512, (tt + 1) * 512)
            pl.step("s", lambda s, bank=bank, st_sl=st_sl:
                    s.copy(B.STASH[:, st_sl], bank[:]),
                    waits=[ptok, slot_tok[tt]], bump=B.c_a)
            bank_set(B, bank, B.c_a.tok())
            tok_stash = B.c_a.tok()
            pl.step("g", lambda g, tt=tt, st_sl=st_sl: g.reduce_max(
                B.MXR[0:1, tt:tt + 1], B.STASH[:, st_sl], axis=AX.XYZWC),
                waits=[tok_stash])
            pl.step("g", lambda g, tt=tt: g.tensor_scalar(
                B.NCR[0:1, tt:tt + 1], B.MXR[0:1, tt:tt + 1], -1.0, None,
                ALU.mult), waits=[bc_tok[tt]], bump=B.c_g)
            slot_tok[tt] = B.c_g.tok()
            pl.step("t", lambda t, tt=tt: t.matmul(
                B.NRM[:, 0:1], B.ones_rf[:], B.NCR[0:1, tt:tt + 1],
                start=True, stop=True),
                waits=[slot_tok[tt]] + bank_wait(B, B.NRM), bump=B.c_pe)
            tok_cbmm = B.c_pe.tok()
            bc_tok[tt] = tok_cbmm
            pl.step("s", lambda s, tt=tt: s.copy(B.CBS[:, tt:tt + 1],
                                                 B.NRM[:, 0:1]),
                    waits=[tok_cbmm], bump=B.c_a)
            bank_set(B, B.NRM, B.c_a.tok())
            e_sl = slice(t_ * 512, (t_ + 1) * 512)
            pl.step("s", lambda s, tt=tt, e_sl=e_sl, st_sl=st_sl: s.activation(
                B.EXPB[:, e_sl], B.STASH[:, st_sl],
                AF.Exp, bias=B.CBS[:, tt:tt + 1], scale=1.0),
                waits=[expb_guard], bump=B.c_a)
            tok_exp = B.c_a.tok()
            pl.step("t", lambda t, e_sl=e_sl: t.matmul(
                B.AUX[:], B.ones128[:], B.EXPB[:, e_sl], start=True, stop=True),
                waits=[tok_exp] + bank_wait(B, B.AUX), bump=B.c_pe)
            pl.step("v", lambda v: v.reciprocal(B.RZ[:], B.AUX[:]),
                    waits=[B.c_pe.tok()], bump=B.c_v)
            bank_set(B, B.AUX, B.c_v.tok())
            pl.step("v", lambda v, e_sl=e_sl: v.tensor_tensor(
                B.EXPB[:, e_sl], B.EXPB[:, e_sl], B.RZ[:], ALU.mult),
                waits=[B.c_v.tok()], bump=B.c_v)
            whtok[t_] = B.c_v.tok()
        tok_rn = None
        for p in range(2):
            for t_ in range(NT):
                def pe(t, p=p, t_=t_):
                    last = None
                    for md in range(4):
                        dch = p * 4 + md
                        last = t.matmul(
                            B.AV[md][:],
                            B.VT[:, t_ * D + dch * 128: t_ * D + dch * 128 + 128],
                            B.EXPB[:, t_ * 512:(t_ + 1) * 512],
                            start=(t_ == 0), stop=(t_ == NT - 1))
                    if p == 0:
                        last = t.matmul(B.NRM[:], B.ones128[:],
                                        B.EXPB[:, t_ * 512:(t_ + 1) * 512],
                                        start=(t_ == 0), stop=(t_ == NT - 1))
                    return last
                w = [whtok[t_], tok_vt]
                if t_ == 0:
                    for md in range(4):
                        w += bank_wait(B, B.AV[md])
                    if p == 0:
                        w += bank_wait(B, B.NRM)
                pl.step("t", pe, waits=w, bump=B.c_pe)
            tok_av = B.c_pe.tok()
            if p == 0:
                pl.step("v", lambda v: v.tensor_scalar(B.RN[:], B.NRM[:],
                                                       1e-6, None, ALU.add),
                        waits=[tok_av], bump=B.c_v)
                pl.step("v", lambda v: v.reciprocal(B.RN[:], B.RN[:]),
                        waits=[B.c_v.tok()], bump=B.c_v)
                bank_set(B, B.NRM, B.c_v.tok())
                tok_rn = B.c_v.tok()
            for md in range(4):
                dch = p * 4 + md
                pl.step("v", lambda v, md=md, dch=dch: v.tensor_tensor(
                    B.ATT[:, dch * 512:(dch + 1) * 512], B.AV[md][:], B.RN[:],
                    ALU.mult),
                    waits=[tok_av, tok_rn, att_guard], bump=B.c_v)
                bank_set(B, B.AV[md], B.c_v.tok())
        tok_att = B.c_v.tok()
        expb_guard = B.c_pe.tok()
        for qh in range(4):
            wt = wtok[("o", qh)]
            wb = qh % 2
            for ic in range(2 * qh, 2 * qh + 2):
                bank = B.SC[ic % 2]
                icl = (ic % 2) * 128

                def pe(t, wb=wb, icl=icl, bank=bank):
                    last = None
                    for dk in range(DC):
                        last = t.matmul(
                            bank[:],
                            B.WARE[wb][:, dk * 256 + icl: dk * 256 + icl + 128],
                            B.ATT[:, dk * 512:(dk + 1) * 512],
                            start=(dk == 0), stop=(dk == DC - 1))
                    return last
                pl.step("t", pe, waits=[wt, tok_att] + bank_wait(B, bank),
                        bump=B.c_pe)
                ptok = B.c_pe.tok()
                stg = B.A0T[ic % 2]
                pl.step("s", lambda s, bank=bank, ic=ic, stg=stg: s.activation(
                    stg[:], bank[:], AF.Identity, bias=B.BO[:, ic:ic + 1], scale=1.0),
                    waits=[ptok] + bank_wait(B, stg), bump=B.c_a)
                bank_set(B, bank, B.c_a.tok())
                tok_stg = B.c_a.tok()
                pl.step("y", lambda y, ic=ic, q0=q0, stg=stg: y.dma_start(
                    out=a0_dram[ic * 128:(ic + 1) * 128, q0:q0 + QB], in_=stg[:]),
                    waits=[tok_stg], bump=B.c_st, n=16)
                bank_set(B, stg, B.c_st.tok())
            nq = (qh + 2) % 4
            if not (qb == NQB - 1 and qh >= 2):
                wtok[("o", nq)] = load_wq(wo_t, nq, B.WARE[wb],
                                          waits=[B.c_pe.tok()])
        att_guard = B.c_pe.tok()
    return B.c_st.all()



def alloc_ln(P, B):
    B.X2B = [P.sb(f"x2b{i}", [128, 512], BF16) for i in range(2)]
    B.ROWA = P.sb("rowa", [1, 512], F32)
    B.ROWB = P.sb("rowb", [1, 512], F32)
    B.RSTB = P.sb("rstb", [128, 512], F32)
    B.MRB = P.sb("mrb", [128, 512], F32)
    B.EPSB = P.sb("epsb", [1, 1], F32)
    B.eps_init = False


def emit_ln(pl, B, x_dram, wait=None):
    B.ln_xn_guard = ()
    if not B.eps_init:
        B.eps_init = True
        pl.step("v", lambda v: v.memset(B.EPSB[:], 1e-5), bump=B.c_v)
        B.tok_eps = B.c_v.tok()
    wl = wait if isinstance(wait, list) else [wait]
    pl.step("g", lambda g: g.dma_start(
        out=B.XN[:].rearrange("p (c t) -> p c t", c=DC),
        in_=x_dram.rearrange("(c p) t -> p c t", p=128)),
        waits=wl, bump=B.c_wg, n=16)
    tok_x = B.c_wg.tok()
    x2tok = [None, None]
    for qb in range(NQB):
        q0 = qb * QB
        for c in range(DC):
            xb_sl = slice(c * S + q0, c * S + q0 + QB)
            pl.step("s", lambda s, xb_sl=xb_sl, c=c: s.activation(
                B.X2B[c % 2][:], B.XN[:, xb_sl], AF.Square, scale=1.0),
                waits=[tok_x, x2tok[c % 2]], bump=B.c_a)
            tok_sq = B.c_a.tok()

            def pe(t, xb_sl=xb_sl, c=c):
                t.matmul(B.AUX[0:1, :], B.ones_c[:], B.XN[:, xb_sl],
                         start=(c == 0), stop=(c == DC - 1))
                return t.matmul(B.NRM[0:1, :], B.ones_c[:], B.X2B[c % 2][:],
                                start=(c == 0), stop=(c == DC - 1))
            w = [tok_sq, B.tok_setup_v]
            if c == 0:
                w += bank_wait(B, B.AUX) + bank_wait(B, B.NRM)
            pl.step("t", pe, waits=w, bump=B.c_pe)
            x2tok[c % 2] = B.c_pe.tok()
        tok_sums = B.c_pe.tok()
        pl.step("v", lambda v: v.tensor_scalar(B.ROWA[:], B.AUX[0:1, :],
                                               1.0 / D, None, ALU.mult),
                waits=[tok_sums] + bank_wait(B, B.ROWA), bump=B.c_v)
        bank_set(B, B.AUX, B.c_v.tok())
        pl.step("v", lambda v: v.tensor_scalar(B.ROWB[:], B.NRM[0:1, :],
                                               1.0 / D, None, ALU.mult),
                waits=[tok_sums] + bank_wait(B, B.ROWB), bump=B.c_v)
        bank_set(B, B.NRM, B.c_v.tok())
        pl.step("s", lambda s: s.activation(B.AUX[0:1, :], B.ROWA[:],
                                            AF.Square, scale=1.0),
                waits=[B.c_v.tok()] + bank_wait(B, B.AUX), bump=B.c_a)
        pl.step("v", lambda v: v.tensor_tensor(B.ROWB[:], B.ROWB[:],
                                               B.AUX[0:1, :], ALU.subtract),
                waits=[B.c_a.tok()], bump=B.c_v)
        bank_set(B, B.AUX, B.c_v.tok())
        tok_var = B.c_v.tok()
        pl.step("s", lambda s: s.activation(B.ROWB[:], B.ROWB[:], AF.Sqrt,
                                            bias=B.EPSB[0:1, 0:1], scale=1.0),
                waits=[tok_var, B.tok_eps], bump=B.c_a)
        pl.step("v", lambda v: v.reciprocal(B.ROWB[:], B.ROWB[:]),
                waits=[B.c_a.tok()], bump=B.c_v)
        pl.step("v", lambda v: v.tensor_tensor(B.ROWA[:], B.ROWA[:], B.ROWB[:],
                                               ALU.mult), bump=B.c_v)
        tok_rows = B.c_v.tok()
        for row, dstb in ((B.ROWB, B.RSTB), (B.ROWA, B.MRB)):
            pl.step("t", lambda t, row=row: t.matmul(
                B.AUX[:], B.ones_rf[:], row[0:1, :], start=True, stop=True),
                waits=[tok_rows] + bank_wait(B, B.AUX), bump=B.c_pe)
            pl.step("s", lambda s, dstb=dstb: s.copy(dstb[:], B.AUX[:]),
                    waits=[B.c_pe.tok()] + list(B.ln_xn_guard)
                    + bank_wait(B, dstb), bump=B.c_a)
            bank_set(B, B.AUX, B.c_a.tok())
            bank_set(B, dstb, B.c_a.tok())
        tok_b = B.c_a.tok()
        for c in range(DC):
            eng = "v" if c % 2 == 0 else "g"
            xb_sl = slice(c * S + q0, c * S + q0 + QB)
            pl.step(eng, lambda e, xb_sl=xb_sl: e.tensor_tensor(
                B.XN[:, xb_sl], B.XN[:, xb_sl], B.RSTB[:], ALU.mult),
                waits=[tok_b, tok_sums])
            pl.step(eng, lambda e, xb_sl=xb_sl: e.tensor_tensor(
                B.XN[:, xb_sl], B.XN[:, xb_sl], B.MRB[:], ALU.subtract),
                bump=B.c_v if eng == "v" else B.c_g)
        B.ln_xn_guard = (B.c_v.tok(), B.c_g.tok())
    return (B.c_v.tok(), B.c_g.tok())



def alloc_compose(P, B):
    B.LG = P.sb("LG", [1, 256], F32)
    B.LGW = P.sb("LGW", [1, 256], F32)
    B.MV = P.sb("MV", [1, 32], F32)
    B.IV = P.sb("IV", [1, 32], U32)
    B.IVF = P.sb("IVF", [1, 32], F32)
    B.EW = P.sb("EW", [1, 32], F32)
    B.SWr = P.sb("SWr", [1, 2], F32)
    B.W32 = P.sb("W32", [1, 32], F32)
    B.SELB = P.sb("SELB", [128, 32], F32)
    B.W32B = P.sb("W32B", [128, 32], F32)
    B.PEVB = P.sb("PEVB", [128, 32], F32)
    B.EOT = P.sb("EOT", [128, 32], F32)
    B.UB = P.sb("UB", [128, 32], F32)
    B.U2 = P.sb("U2", [128, 16], F32)
    B.MS = P.sb("MS", [128, 16], F32)
    B.IDXAf = P.sb("IDXAf", [128, 16], F32)
    B.IDXA = P.sb("IDXA", [128, 16], U32)
    B.IDX8 = P.sb("IDX8", [128, 16], U32)
    B.QM64 = P.sb("QM64", [128, 1], F32)
    B.QI = P.sb("QI", [128, 1], I32)
    B.COREB = P.sb("COREB", [128, 1], F32)
    B.GA = [P.sb(f"ga{i}", [128, 1024], BF16) for i in range(2)]
    B.GAS = [P.sb(f"gas{i}", [128, 1024], BF16) for i in range(2)]
    B.GB = [P.sb(f"gb{i}", [128, 1024], BF16) for i in range(2)]
    B.STG = [P.sb(f"stg{i}", [128, 512], BF16) for i in range(2)]


def emit_compose_setup(pl, B, coreid):
    pl.step("g", lambda g: (g.iota(B.QI[:], [[0, 1]], base=0,
                            channel_multiplier=1), g.engine_nop())[1])
    pl.step("g", lambda g: (g.iota(
        B.iotat[:, 0:32].rearrange("p (a b) -> p a b", b=2),
        [[0, 16], [1, 2]], base=0, channel_multiplier=0), g.engine_nop())[1])
    tok_io = B.c_g.tok()
    pl.step("v", lambda v: v.tensor_scalar(B.QM64[:], B.QI[:], 64, None,
                                           ALU.is_ge), waits=[tok_io])
    pl.step("v", lambda v: v.scalar_tensor_tensor(B.QM64[:], B.QM64[:], -64.0,
                                                  B.QI[:], ALU.mult, ALU.add))
    pl.step("v", lambda v: v.tensor_scalar(B.EOT[:], B.iotat[:, 0:32], 0, None,
                                           ALU.is_equal), bump=B.c_v)
    pl.step("v", lambda v: v.tensor_copy(B.PEVB[0:64, :], B.EOT[0:64, :]))
    pl.step("v", lambda v: v.tensor_scalar(B.PEVB[64:128, :], B.EOT[64:128, :],
                                           1.0, None, ALU.subtract))
    pl.step("v", lambda v: v.tensor_scalar(B.PEVB[64:128, :], B.PEVB[64:128, :],
                                           -1.0, None, ALU.mult), bump=B.c_v)
    pl.step("g", lambda g: g.dma_start(out=B.SWr[0:1, 0:1], in_=coreid),
            bump=B.c_wg, n=16)
    pl.step("t", lambda t: t.matmul(B.AUX[:, 0:1], B.ones_rf[:], B.SWr[0:1, 0:1],
                                    start=True, stop=True),
            waits=[B.c_wg.tok(), B.tok_setup_v] + bank_wait(B, B.AUX),
            bump=B.c_pe)
    pl.step("s", lambda s: s.copy(B.COREB[:], B.AUX[:, 0:1]),
            waits=[B.c_pe.tok()], bump=B.c_a)
    bank_set(B, B.AUX, B.c_a.tok())
    B.tok_cmp_setup = (B.c_v.tok(), B.c_g.tok(), B.c_a.tok(), B.c_pe.tok())


def emit_topk_idx(pl, B, lg_row, guard=None):
    pl.step("y", lambda y: y.dma_start(out=B.LG[:], in_=lg_row), bump=B.c_ld, n=16)
    tok_lg = B.c_ld.tok()
    pl.step("v", lambda v: v.tensor_copy(B.LGW[:], B.LG[:]),
            waits=[tok_lg, B.tok_cmp_setup[0]] + (guard or []), bump=B.c_v)
    for r in range(4):
        sl = slice(r * 8, (r + 1) * 8)
        pl.step("v", lambda v, sl=sl: v.max(out=B.MV[:, sl], in_=B.LGW[:]),
                waits=[B.c_v.tok()], bump=B.c_v)
        pl.step("v", lambda v, sl=sl: v.max_index(out=B.IV[:, sl],
                                                  in_max=B.MV[:, sl],
                                                  in_values=B.LGW[:]),
                waits=[B.c_v.tok()], bump=B.c_v)
        pl.step("v", lambda v, sl=sl: v.match_replace(
            out=B.LGW[:], in_to_replace=B.MV[:, sl], in_values=B.LGW[:],
            imm_value=-1e30), waits=[B.c_v.tok()], bump=B.c_v)
    pl.step("v", lambda v: v.tensor_scalar(B.SWr[0:1, 0:1], B.MV[0:1, 0:1],
                                           -1.0, None, ALU.mult),
            waits=[B.c_v.tok(), B.tok_cmp_setup[3]], bump=B.c_v)
    pl.step("s", lambda s: s.activation(B.EW[:], B.MV[:], AF.Exp,
                                        bias=B.SWr[0:1, 0:1], scale=1.0),
            waits=[B.c_v.tok()], bump=B.c_a)
    pl.step("v", lambda v: v.reduce_sum(B.SWr[0:1, 1:2], B.EW[:], axis=AX.X),
            waits=[B.c_a.tok()], bump=B.c_v)
    pl.step("v", lambda v: v.reciprocal(B.SWr[0:1, 1:2], B.SWr[0:1, 1:2]),
            waits=[B.c_v.tok()], bump=B.c_v)
    pl.step("v", lambda v: v.tensor_scalar(B.W32[:], B.EW[:], B.SWr[0:1, 1:2],
                                           None, ALU.mult),
            waits=[B.c_v.tok()], bump=B.c_v)
    pl.step("v", lambda v: v.tensor_copy(B.IVF[:], B.IV[:]), bump=B.c_v)
    tok_rows = B.c_v.tok()
    for row, dstb in ((B.IVF, B.SELB), (B.W32, B.W32B)):
        pl.step("t", lambda t, row=row: t.matmul(
            B.AUX[:, 0:32], B.ones_rf[:], row[0:1, :], start=True, stop=True),
            waits=[tok_rows] + bank_wait(B, B.AUX), bump=B.c_pe)
        pl.step("s", lambda s, dstb=dstb: s.copy(dstb[:], B.AUX[:, 0:32]),
                waits=[B.c_pe.tok()], bump=B.c_a)
        bank_set(B, B.AUX, B.c_a.tok())
    tok_bc = B.c_a.tok()
    pl.step("v", lambda v: v.tensor_tensor(B.UB[:], B.SELB[:], B.PEVB[:],
                                           ALU.mult), waits=[tok_bc], bump=B.c_v)
    pl.step("v", lambda v: v.tensor_tensor(B.U2[:], B.UB[:, 0::2], B.UB[:, 1::2],
                                           ALU.add), bump=B.c_v)
    pl.step("v", lambda v: v.scalar_tensor_tensor(
        B.IDXAf[:], B.U2[:], 64.0, B.QM64[:].to_broadcast([128, 16]),
        ALU.mult, ALU.add), waits=[B.tok_cmp_setup[1]], bump=B.c_v)
    pl.step("v", lambda v: v.tensor_copy(B.IDXA[:], B.IDXAf[:]), bump=B.c_v)
    pl.step("v", lambda v: v.scalar_tensor_tensor(
        B.IDXAf[:], B.IDXAf[:], 8.0, B.COREB[:].to_broadcast([128, 16]),
        ALU.mult, ALU.add), waits=[B.tok_cmp_setup[2]], bump=B.c_v)
    pl.step("v", lambda v: v.tensor_copy(B.IDX8[:], B.IDXAf[:]), bump=B.c_v)
    pl.step("v", lambda v: v.tensor_tensor(B.UB[:], B.W32B[:], B.PEVB[:],
                                           ALU.mult), bump=B.c_v)
    pl.step("v", lambda v: v.tensor_tensor(B.MS[:], B.UB[:, 0::2], B.UB[:, 1::2],
                                           ALU.add), bump=B.c_v)
    return B.c_v.tok()


def emit_compose_one(pl, B, tok_idx, a_rows, a_width, b_rows, b_width,
                     a_idx, b_idx, n_m, n_n, out_emit):
    banks = B.AV + B.SC + [B.AUX, B.NRM]
    ga_tok = [None, None]
    gas_used = [None, None]
    for j in range(16):
        bi = j % 2

        def ld_a(g, j=j, bi=bi, a_rows=a_rows, a_width=a_width):
            return g.indirect_dma_start(
                out=B.GA[bi][:, 0:a_width], out_offset=None,
                in_=a_rows, in_offset=IndirectOffsetOnAxis(ap=a_idx(j), axis=0))
        pl.step("g", ld_a, waits=[tok_idx, ga_tok[bi]], bump=B.c_wg, n=16)
        tok_ga = B.c_wg.tok()

        def ld_b(g, j=j, bi=bi, b_rows=b_rows, b_width=b_width):
            return g.indirect_dma_start(
                out=B.GB[bi][:, 0:b_width], out_offset=None,
                in_=b_rows, in_offset=IndirectOffsetOnAxis(ap=b_idx(j), axis=0))
        pl.step("g", ld_b, waits=[], bump=B.c_wg, n=16)
        tok_gb = B.c_wg.tok()
        pl.step("v", lambda v, bi=bi, j=j, a_width=a_width: v.tensor_scalar(
            B.GAS[bi][:, 0:a_width], B.GA[bi][:, 0:a_width],
            B.MS[:, j:j + 1], None, ALU.mult),
            waits=[tok_ga, gas_used[bi]], bump=B.c_v)
        tok_gas = B.c_v.tok()

        def pe(t, j=j, bi=bi, n_m=n_m, n_n=n_n, b_width=b_width):
            last = None
            for m in range(n_m):
                for n in range(n_n):
                    last = t.matmul(
                        banks[m * n_n + n][:],
                        B.GAS[bi][:, m * 128:(m + 1) * 128],
                        B.GB[bi][:, n * 512:(n + 1) * 512],
                        start=(j == 0), stop=(j == 15))
            return last
        w = [tok_gas, tok_gb]
        if j == 0:
            for k in range(n_m * n_n):
                w += bank_wait(B, banks[k])
        pl.step("t", pe, waits=w, bump=B.c_pe)
        ga_tok[bi] = B.c_pe.tok()
        gas_used[bi] = B.c_pe.tok()
    tok_mm = B.c_pe.tok()
    for m in range(n_m):
        for n in range(n_n):
            bank = banks[m * n_n + n]
            out_emit(m, n, bank, tok_mm)


def emit_compose(pl, B, lg1_row, lg2_row, a1t, b1, a2t, b2,
                 w1sl_dram, w2sl_dram):
    a1_rows = a1t.rearrange("p r d -> (p r) d")
    b1_rows = b1.rearrange("p r (a b) -> (p r a) b", b=512)
    a2_rows = a2t.rearrange("p r (a b) -> (p r a) b", b=512)
    b2_rows = b2.rearrange("p r d -> (p r) d")

    def store(dram, m, n, bank, ptok):
        stg = B.STG[(m * 4 + n) % 2]
        pl.step("s", lambda s, stg=stg, bank=bank: s.copy(stg[:], bank[:]),
                waits=[ptok] + bank_wait(B, stg), bump=B.c_a)
        bank_set(B, bank, B.c_a.tok())
        tok_stg = B.c_a.tok()
        pl.step("y", lambda y, dram=dram, m=m, n=n, stg=stg: y.dma_start(
            out=dram[m * 128:(m + 1) * 128, n * 512:(n + 1) * 512], in_=stg[:]),
            waits=[tok_stg], bump=B.c_st, n=16)
        bank_set(B, stg, B.c_st.tok())

    tok_idx = emit_topk_idx(pl, B, lg1_row)
    emit_compose_one(pl, B, tok_idx, a1_rows, 1024, b1_rows, 512,
                     lambda j: B.IDXA[:, j:j + 1], lambda j: B.IDX8[:, j:j + 1],
                     8, 1, lambda m, n, bank, ptok: store(w1sl_dram, m, n, bank, ptok))
    tok_idx = emit_topk_idx(pl, B, lg2_row, guard=B.c_wg.all())
    emit_compose_one(pl, B, tok_idx, a2_rows, 512, b2_rows, 1024,
                     lambda j: B.IDX8[:, j:j + 1], lambda j: B.IDXA[:, j:j + 1],
                     4, 2, lambda m, n, bank, ptok: store(w2sl_dram, m, n, bank, ptok))
    return B.c_st.all()



def alloc_ffn0(P, B):
    B.H = P.sb("H", [128, 32 * 512], BF16)
    B.WST = [P.sb(f"wst{i}", [128, 8 * 1024], BF16) for i in range(2)]
    B.GU = P.sb("GU", [128, 512], F32)
    B.GZ = P.sb("GZ", [128, 512], F32)
    B.GS = P.sb("GS", [128, 512], F32)
    B.XRT = [P.sb(f"xrt{i}", [128, 512], F32) for i in range(2)]
    B.ART = [P.sb(f"art{i}", [128, 512], F32) for i in range(2)]
    B.XPT = [P.sb(f"xpt{i}", [128, 512], F32) for i in range(2)]


GELU_A = float(np.sqrt(0.044715))
GELU_K1 = 2.0 * 0.7978845608028654


def emit_ffn0_apply(pl, B, w1g, w2g, x_dram, a0_dram, x1_dram, tok_xn,
                    no_gelu=False, guard=None, wsrc_tok=None):
    h_guard = guard
    B.gu_guard = None
    B.gs_guard = None
    wst_guard = guard
    for tb in range(NQB):
        q0 = tb * QB
        wtok = [None, None]
        wused = [wst_guard, wst_guard]
        for fs in range(8):
            bi = fs % 2

            def ldw(y, fs=fs, bi=bi):
                return y.dma_start(
                    out=B.WST[bi][:, 0:8 * 512].rearrange("p (c i) -> p c i", c=DC),
                    in_=w1g[fs].rearrange("(c p) i -> p c i", p=128))
            pl.step("y", ldw, waits=[wused[bi], wsrc_tok], bump=B.c_ld, n=16)
            wtok[bi] = B.c_ld.tok()
            for fm in range(4):
                bank = B.AV[(fs * 4 + fm) % 4]

                def pe(t, bi=bi, fm=fm, q0=q0, bank=bank):
                    last = None
                    for dk in range(DC):
                        last = t.matmul(
                            bank[:],
                            B.WST[bi][:, dk * 512 + fm * 128: dk * 512 + fm * 128 + 128],
                            B.XN[:, dk * S + q0: dk * S + q0 + QB],
                            start=(dk == 0), stop=(dk == DC - 1))
                    return last
                pl.step("t", pe, waits=[wtok[bi], tok_xn[0], tok_xn[1]]
                        + bank_wait(B, bank), bump=B.c_pe)
                ptok = B.c_pe.tok()
                f = fs * 4 + fm
                h_sl = slice(f * 512, (f + 1) * 512)
                if no_gelu:
                    pl.step("s", lambda s, bank=bank, h_sl=h_sl: s.copy(
                        B.H[:, h_sl], bank[:]), waits=[ptok, h_guard])
                    bank_set(B, bank, B.c_a.tok())
                    continue
                pl.step("s", lambda s, bank=bank: s.activation(
                    B.GU[:], bank[:], AF.Square, scale=GELU_A),
                    waits=[ptok, B.gu_guard], bump=B.c_a)
                pl.step("v", lambda v, bank=bank: v.scalar_tensor_tensor(
                    B.GU[:], B.GU[:], 1.0, bank[:], ALU.add, ALU.mult),
                    waits=[B.c_a.tok()], bump=B.c_v)
                pl.step("s", lambda s: s.activation(
                    B.GS[:], B.GU[:], AF.Sigmoid, scale=GELU_K1),
                    waits=[B.c_v.tok(), B.gs_guard], bump=B.c_a)
                B.gu_guard = B.c_a.tok()
                pl.step("v", lambda v, bank=bank, h_sl=h_sl: v.tensor_tensor(
                    B.H[:, h_sl], B.GS[:], bank[:], ALU.mult),
                    waits=[B.c_a.tok(), h_guard], bump=B.c_v)
                B.gs_guard = B.c_v.tok()
                bank_set(B, bank, B.c_v.tok())
            wused[bi] = B.c_pe.tok()
        tok_h = B.c_v.tok()
        tok_h2 = B.c_a.tok()
        wst_guard = B.c_pe.tok()
        banks = B.AV + B.SC + [B.AUX, B.NRM]
        wtok = [None, None]
        wused = [wst_guard, wst_guard]
        for fs in range(8):
            bi = fs % 2

            def ldw2(y, fs=fs, bi=bi):
                return y.dma_start(
                    out=B.WST[bi][:, 0:4 * 1024].rearrange("p (c d) -> p c d", c=4),
                    in_=w2g[fs * 512:(fs + 1) * 512, :].rearrange(
                        "(c p) d -> p c d", p=128))
            pl.step("y", ldw2, waits=[wused[bi], wsrc_tok], bump=B.c_ld, n=16)
            wtok[bi] = B.c_ld.tok()

            def pe2(t, fs=fs, bi=bi):
                last = None
                for fl in range(4):
                    f = fs * 4 + fl
                    for m in range(8):
                        last = t.matmul(
                            banks[m][:],
                            B.WST[bi][:, fl * 1024 + m * 128: fl * 1024 + m * 128 + 128],
                            B.H[:, f * 512:(f + 1) * 512],
                            start=(f == 0), stop=(f == 31))
                return last
            w = [wtok[bi], tok_h, tok_h2]
            if fs == 0:
                for k in range(8):
                    w += bank_wait(B, banks[k])
            pl.step("t", pe2, waits=w, bump=B.c_pe)
            wused[bi] = B.c_pe.tok()
        tok_out = B.c_pe.tok()
        h_guard = tok_out
        wst_guard = tok_out
        ld_tok = [None, None]
        for m in range(8):
            bi = m % 2

            pl.step("y", lambda y, m=m, q0=q0, bi=bi: y.dma_start(
                out=B.XRT[bi][:], in_=x_dram[m * 128:(m + 1) * 128, q0:q0 + QB]),
                waits=[ld_tok[bi]] + bank_wait(B, B.XRT[bi]), bump=B.c_ld, n=16)
            tok_xr = B.c_ld.tok()
            pl.step("y", lambda y, m=m, q0=q0, bi=bi: y.dma_start(
                out=B.ART[bi][:], in_=a0_dram[m * 128:(m + 1) * 128, q0:q0 + QB]),
                waits=[ld_tok[bi]] + bank_wait(B, B.ART[bi]), bump=B.c_ld, n=16)
            tok_res = B.c_ld.tok()
            pl.step("v", lambda v, m=m, bi=bi: v.scalar_tensor_tensor(
                B.XRT[bi][:], banks[m][:], 1.0, B.XRT[bi][:], ALU.mult, ALU.add),
                waits=[tok_out, tok_xr], bump=B.c_v)
            bank_set(B, banks[m], B.c_v.tok())
            pl.step("v", lambda v, m=m, bi=bi: v.tensor_tensor(
                B.XRT[bi][:], B.XRT[bi][:], B.ART[bi][:], ALU.add),
                waits=[tok_res], bump=B.c_v)
            tok_xp = B.c_v.tok()
            pl.step("y", lambda y, m=m, q0=q0, bi=bi: y.dma_start(
                out=x1_dram[m * 128:(m + 1) * 128, q0:q0 + QB], in_=B.XRT[bi][:]),
                waits=[tok_xp], bump=B.c_st, n=16)
            ld_tok[bi] = B.c_st.tok()
            bank_set(B, B.XRT[bi], B.c_st.tok())
            bank_set(B, B.ART[bi], B.c_st.tok())
    return B.c_st.all()



def alloc_l1_attn(P, B):
    if not hasattr(B, "QF"):
        B.QF = P.sb("QF", [128, DC * S], BF16)
        B.KF = P.sb("KF", [128, DC * S], BF16)
        B.VT = P.sb("VT", [128, NT * D], BF16)
        B.WARE = [P.sb(f"ware{i}", [128, DC * 256], BF16) for i in range(2)]
        B.A0T = [P.sb(f"a0t{i}", [128, 512], F32) for i in range(2)]
    B.IP = B.QF
    B.RG = B.KF
    B.IG = B.VT
    B.BIN = P.sb("BIN", [128, DC], F32)
    B.BR = P.sb("BR", [128, DC], F32)
    B.BI = P.sb("BI", [128, DC], F32)
    B.BOUT = P.sb("BOUT", [128, DC], F32)
    B.ASG = P.sb("ASG", [128, DC], F32)


def emit_l1_attn(pl, B, w_in, w_r, w_i, w_out, b_in2, b_r2, b_i2, b_out2,
                 log_a2, a1_dram, tok_xn, guard=None, evac_guard=None):
    for buf, src in ((B.BIN, b_in2), (B.BR, b_r2), (B.BI, b_i2),
                     (B.BOUT, b_out2), (B.ASG, log_a2)):
        pl.step("y", lambda y, buf=buf, src=src: y.dma_start(out=buf[:], in_=src),
                bump=B.c_ld, n=16)
    tok_bias_l = B.c_ld.all()
    pl.step("s", lambda s: s.activation(B.ASG[:], B.ASG[:], AF.Sigmoid),
            waits=tok_bias_l)
    tok_asg = B.c_a.tok()

    def load_wq(wdram, h, buf, waits=()):
        def go(g):
            return g.dma_start(
                out=buf[:].rearrange("p (c i) -> p c i", c=DC),
                in_=wdram[:, h * 256:(h + 1) * 256].rearrange("(c p) i -> p c i", p=128))
        pl.step("g", go, waits=list(waits), bump=B.c_wg, n=16)
        return B.c_wg.tok()

    rr = [0]
    seqw = [("in", w_in), ("r", w_r), ("i", w_i), ("o", w_out)]
    order = [(nm, h) for nm, _ in seqw for h in range(4)]
    dram_of = dict(seqw)
    wtok = {order[0]: load_wq(w_in, 0, B.WARE[0], waits=[guard]),
            order[1]: load_wq(w_in, 1, B.WARE[1], waits=[guard])}

    def prefetch_after(idx):
        if idx + 2 < len(order):
            nm, nh = order[idx + 2]
            wtok[(nm, nh)] = load_wq(dram_of[nm], nh, B.WARE[idx % 2],
                                     waits=[B.c_pe.tok()])

    projs = (("in", B.BIN, B.IP, AF.Identity), ("r", B.BR, B.RG, AF.Sigmoid),
             ("i", B.BI, B.IG, AF.Sigmoid))
    oi = 0
    for name, bias, dstb, fn in projs:
        for h in range(4):
            wt = wtok[(name, h)]
            for ic in range(2 * h, 2 * h + 2):
                icl = (ic % 2) * 128
                for qb in range(NQB):
                    bank = B.AV[rr[0]]
                    rr[0] = (rr[0] + 1) % 4

                    def pe(t, oi=oi, icl=icl, qb=qb, bank=bank):
                        last = None
                        for dk in range(DC):
                            last = t.matmul(
                                bank[:],
                                B.WARE[oi % 2][:, dk * 256 + icl: dk * 256 + icl + 128],
                                B.XN[:, dk * S + qb * QB: dk * S + (qb + 1) * QB],
                                start=(dk == 0), stop=(dk == DC - 1))
                        return last
                    pl.step("t", pe, waits=[wt, tok_xn[0], tok_xn[1]]
                            + bank_wait(B, bank), bump=B.c_pe)
                    ptok = B.c_pe.tok()

                    def ev(s, bank=bank, ic=ic, qb=qb, bias=bias, dstb=dstb, fn=fn):
                        return s.activation(
                            dstb[:, ic * S + qb * QB: ic * S + (qb + 1) * QB],
                            bank[:], fn, bias=bias[:, ic:ic + 1], scale=1.0)
                    pl.step("s", ev, waits=[ptok, evac_guard] + tok_bias_l)
                    bank_set(B, bank, B.c_a.tok())
            prefetch_after(oi)
            oi += 1
    tok_proj = B.c_a.tok()
    for c in range(DC):
        eng = "v" if c % 2 == 0 else "g"
        for qb in range(NQB):
            sl = slice(c * S + qb * QB, c * S + (qb + 1) * QB)
            pl.step(eng, lambda e, sl=sl, c=c: e.tensor_scalar(
                B.RG[:, sl], B.RG[:, sl], B.ASG[:, c:c + 1], None, ALU.mult),
                waits=[tok_proj, tok_asg])
            pl.step(eng, lambda e, sl=sl: e.tensor_tensor(
                B.IP[:, sl], B.IG[:, sl], B.IP[:, sl], ALU.mult))
            pl.step(eng, lambda e, sl=sl: e.tensor_scalar(
                B.IG[:, sl], B.RG[:, sl], -1.0, 1.0, ALU.mult, ALU.add))
            pl.step(eng, lambda e, sl=sl: e.tensor_tensor(
                B.IP[:, sl], B.IP[:, sl], B.IG[:, sl], ALU.mult))
    tok_bt = (B.c_v.tok(), B.c_g.tok())
    for c in range(DC):
        sl = slice(c * S, (c + 1) * S)
        pl.step("v", lambda v, sl=sl: v.tensor_tensor_scan(
            B.IG[:, sl], B.RG[:, sl], B.IP[:, sl], 0.0, ALU.mult, ALU.add),
            waits=[tok_bt[0], tok_bt[1]])
    tok_h = B.c_v.tok()
    for h in range(4):
        wt = wtok[("o", h)]
        for ic in range(2 * h, 2 * h + 2):
            icl = (ic % 2) * 128
            for qb in range(NQB):
                bank = B.SC[(ic * NQB + qb) % 2]

                def pe(t, oi=oi, icl=icl, qb=qb, bank=bank):
                    last = None
                    for dk in range(DC):
                        last = t.matmul(
                            bank[:],
                            B.WARE[oi % 2][:, dk * 256 + icl: dk * 256 + icl + 128],
                            B.IG[:, dk * S + qb * QB: dk * S + (qb + 1) * QB],
                            start=(dk == 0), stop=(dk == DC - 1))
                    return last
                pl.step("t", pe, waits=[wt, tok_h] + bank_wait(B, bank),
                        bump=B.c_pe)
                ptok = B.c_pe.tok()
                stg = B.A0T[(ic * NQB + qb) % 2]
                pl.step("s", lambda s, bank=bank, ic=ic, stg=stg: s.activation(
                    stg[:], bank[:], AF.Identity, bias=B.BOUT[:, ic:ic + 1],
                    scale=1.0), waits=[ptok] + bank_wait(B, stg))
                bank_set(B, bank, B.c_a.tok())
                tok_stg = B.c_a.tok()
                pl.step("y", lambda y, ic=ic, qb=qb, stg=stg: y.dma_start(
                    out=a1_dram[ic * 128:(ic + 1) * 128,
                                qb * QB:(qb + 1) * QB], in_=stg[:]),
                    waits=[tok_stg], bump=B.c_st, n=16)
                bank_set(B, stg, B.c_st.tok())
        prefetch_after(oi)
        oi += 1
    return B.c_st.all()



def alloc_l1_quant(P, B):
    B.QWB = P.sb("QWB", [128, 3 * 1024], F32)
    B.QDB = P.sb("QDB", [128, 2816], F32)
    B.SCL = P.sb("SCL", [128, 4], F32)
    B.RCL = P.sb("RCL", [128, 4], F32)
    B.QI8 = P.sb("QI8", [128, 2816], I8)
    B.QBF = P.sb("QBF", [128, 3 * 1024], BF16)
    B.TQB = P.sb("TQB", [128, DC * 352], BF16)
    B.TQD = P.sb("TQD", [128, 22 * 128], BF16)


def emit_l1_quant(pl, B, wg_sl, wu_sl, wd_sl, wgt_out, wut_out, wdt_out,
                  sg_out, su_out, sd_out, guard=None):
    rows = [128, 128, 96]
    tr_guard = None
    scl_guard = []
    qwb_guard = guard
    qbf_guard = []
    for w_sl, wt_out, s_out in ((wg_sl, wgt_out, sg_out),
                                (wu_sl, wut_out, su_out)):
        ldt = []
        for c in range(3):
            r0 = c * 128
            nr = rows[c]
            pl.step("g", lambda g, c=c, r0=r0, nr=nr, w_sl=w_sl: g.dma_start(
                out=B.QWB[0:nr, c * 1024:(c + 1) * 1024],
                in_=w_sl[r0:r0 + nr, :]), waits=[qwb_guard], bump=B.c_wg, n=16)
            ldt.append(B.c_wg.tok())
        for c in range(3):
            nr = rows[c]
            csl = slice(c * 1024, (c + 1) * 1024)
            pl.step("v", lambda v, c=c, nr=nr, csl=csl: v.reduce_max(
                B.SCL[0:nr, c:c + 1], B.QWB[0:nr, csl], axis=AX.X,
                apply_absolute_value=True), waits=[ldt[c]] + scl_guard + qbf_guard)
            pl.step("v", lambda v, c=c, nr=nr: v.tensor_scalar(
                B.SCL[0:nr, c:c + 1], B.SCL[0:nr, c:c + 1], 1.0 / 127.0,
                None, ALU.mult))
            pl.step("v", lambda v, c=c, nr=nr: v.reciprocal(
                B.RCL[0:nr, c:c + 1], B.SCL[0:nr, c:c + 1]))
            pl.step("v", lambda v, c=c, nr=nr, csl=csl: v.tensor_scalar(
                B.QI8[0:nr, 0:1024], B.QWB[0:nr, csl], B.RCL[0:nr, c:c + 1],
                None, ALU.mult))
            pl.step("v", lambda v, c=c, nr=nr, csl=csl: v.tensor_copy(
                B.QBF[0:nr, csl], B.QI8[0:nr, 0:1024]), bump=B.c_v)
            tok_q = B.c_v.tok()
            for dc in range(DC):
                pl.step("y", lambda y, c=c, nr=nr, dc=dc: y.dma_start_transpose(
                    B.TQB[:, dc * 352 + c * 128: dc * 352 + c * 128 + nr],
                    B.QBF[0:nr, c * 1024 + dc * 128: c * 1024 + (dc + 1) * 128]),
                    waits=[tok_q, tr_guard], bump=B.c_ld, n=16)
        tok_tr_l = B.c_ld.all()
        pl.step("y", lambda y, wt_out=wt_out: y.dma_start(
            out=wt_out.rearrange("(c p) r -> p c r", p=128),
            in_=B.TQB[:].rearrange("p (c r) -> p c r", c=DC)),
            waits=tok_tr_l, bump=B.c_st, n=16)
        tr_guard = B.c_st.tok()
        scl_guard = []
        for c in range(3):
            nr = rows[c]
            pl.step("y", lambda y, c=c, nr=nr, s_out=s_out: y.dma_start(
                out=s_out[0:1, c * 128:c * 128 + nr].rearrange("a b -> b a"),
                in_=B.SCL[0:nr, c:c + 1]), bump=B.c_st, n=16)
            scl_guard.append(B.c_st.tok())
        qwb_guard = B.c_v.tok()
        qbf_guard = B.c_ld.all()
    pl.step("g", lambda g: g.dma_start(out=B.QDB[:], in_=wd_sl),
            waits=[qwb_guard], bump=B.c_wg, n=16)
    tok_ld = B.c_wg.tok()
    pl.step("v", lambda v: v.reduce_max(B.SCL[:, 3:4], B.QDB[:], axis=AX.X,
                                        apply_absolute_value=True),
            waits=[tok_ld] + scl_guard)
    pl.step("v", lambda v: v.tensor_scalar(B.SCL[:, 3:4], B.SCL[:, 3:4],
                                           1.0 / 127.0, None, ALU.mult))
    pl.step("v", lambda v: v.reciprocal(B.RCL[:, 3:4], B.SCL[:, 3:4]))
    pl.step("v", lambda v: v.tensor_scalar(B.QI8[:], B.QDB[:], B.RCL[:, 3:4],
                                           None, ALU.mult))
    pl.step("v", lambda v: v.tensor_copy(B.QBF[:, 0:2816], B.QI8[:]),
            waits=qbf_guard, bump=B.c_v)
    tok_q = B.c_v.tok()
    for ic in range(22):
        pl.step("y", lambda y, ic=ic: y.dma_start_transpose(
            B.TQD[:, ic * 128:(ic + 1) * 128],
            B.QBF[:, ic * 128:(ic + 1) * 128]),
            waits=[tok_q, tr_guard], bump=B.c_ld, n=16)
    tok_tr_l = B.c_ld.all()
    pl.step("y", lambda y: y.dma_start(
        out=wdt_out.rearrange("(c p) r -> p c r", p=128),
        in_=B.TQD[:].rearrange("p (c r) -> p c r", c=22)),
        waits=tok_tr_l, bump=B.c_st, n=16)
    pl.step("y", lambda y: y.dma_start(
        out=sd_out.rearrange("a b -> b a"), in_=B.SCL[:, 3:4]),
        bump=B.c_st, n=16)
    return B.c_st.all()



def bcast_scalar(pl, B, src11, dst_col, tok):
    pl.step("t", lambda t: t.matmul(B.AUX[:, 0:1], B.ones_rf[:], src11,
                                    start=True, stop=True),
            waits=[tok, B.tok_setup_v] + bank_wait(B, B.AUX), bump=B.c_pe)
    pl.step("s", lambda s: s.copy(dst_col, B.AUX[:, 0:1]),
            waits=[B.c_pe.tok()])
    bank_set(B, B.AUX, B.c_a.tok())
    return B.c_a.tok()


def emit_absmax_xn(pl, B, tok_xn):
    for c in range(DC):
        pl.step("v", lambda v, c=c: v.reduce_max(
            B.AMC[:, c:c + 1], B.XN[:, c * S:(c + 1) * S], axis=AX.X,
            apply_absolute_value=True), waits=[tok_xn[0], tok_xn[1]])
    pl.step("v", lambda v: v.reduce_max(B.AMC[:, DC:DC + 1], B.AMC[:, 0:DC],
                                        axis=AX.X), bump=B.c_v)
    tok = B.c_v.tok()
    pl.step("g", lambda g: g.tensor_reduce(
        B.AMX[0:1, 0:1], B.AMC[:, DC:DC + 1], axis=AX.XYZWC, op=ALU.max),
        waits=[tok], bump=B.c_g)
    return B.c_g.tok()



def alloc_l1_ffn(P, B, phase):
    if phase == "a":
        B.XQB = B.KF
        B.XQI = P.sb("XQI", [128, S], I8)
        B.H1 = P.sb("H1", [128, 22 * 512], BF16)
        B.SIL = P.sb("SIL", [128, 512], F32)
        B.SGB = P.sb("SGB", [128, 22], F32)
        B.SUB = P.sb("SUB", [128, 22], F32)
        B.WQS = [P.sb(f"wqs{i}", [128, DC * 128], BF16) for i in range(4)]
        B.MXH = P.sb("MXH", [1, 92], F32)
        B.AMC = P.sb("AMC", [128, DC + 1], F32)
        B.AMX = P.sb("AMX", [1, 16], F32)
        B.CLP = P.sb("CLP", [1, 4], F32)
        B.RSXB = P.sb("RSXB", [128, 1], F32)
        B.SXB = P.sb("SXB", [128, 1], F32)
    else:
        B.H1 = P.sb("H1", [128, 22 * 512], BF16)
        B.H1I = P.sb("H1I", [128, 512], I8)
        B.WQS = [P.sb(f"wqs{i}", [128, DC * 128], BF16) for i in range(2)]
        B.SDB = P.sb("SDB", [128, DC], F32)
        B.AMX = P.sb("AMX", [1, 16], F32)
        B.CLP = P.sb("CLP", [1, 4], F32)
        B.RSHB = P.sb("RSHB", [128, 1], F32)
        B.SHB = P.sb("SHB", [128, 1], F32)
        B.XRT = [P.sb(f"xrt{i}", [128, 512], F32) for i in range(2)]
        B.ART = [P.sb(f"art{i}", [128, 512], F32) for i in range(2)]
        B.XPT = [P.sb(f"xpt{i}", [128, 512], F32) for i in range(2)]


def emit_sx(pl, B, mx8_dram, clip_dram, rsxb_col, shb_col=None, gmax11=False,
            ld_guard=None):
    pl.step("y", lambda y: y.dma_start(out=B.CLP[0:1, 0:1], in_=clip_dram),
            bump=B.c_ld, n=16)
    if gmax11:
        pl.step("y", lambda y: y.dma_start(out=B.AMX[0:1, 8:9], in_=mx8_dram),
                waits=[ld_guard], bump=B.c_ld, n=16)
    else:
        pl.step("y", lambda y: y.dma_start(out=B.AMX[0:1, 0:8], in_=mx8_dram),
                bump=B.c_ld, n=16)
    tok_ld_l = B.c_ld.all()
    pl.step("s", lambda s: s.activation(B.CLP[0:1, 1:2], B.CLP[0:1, 0:1],
                                        AF.Sigmoid), waits=tok_ld_l)
    tok_sig = B.c_a.tok()
    if not gmax11:
        pl.step("v", lambda v: v.reduce_max(B.AMX[0:1, 8:9], B.AMX[0:1, 0:8],
                                            axis=AX.X), waits=tok_ld_l)
    pl.step("v", lambda v: v.tensor_scalar(
        B.AMX[0:1, 9:10], B.AMX[0:1, 8:9], B.CLP[0:1, 1:2], 1.0 / 127.0,
        ALU.mult, ALU.mult), waits=[tok_sig] + tok_ld_l)
    pl.step("v", lambda v: v.reciprocal(B.AMX[0:1, 10:11], B.AMX[0:1, 9:10]),
            bump=B.c_v)
    tok = B.c_v.tok()
    t1 = bcast_scalar(pl, B, B.AMX[0:1, 10:11], rsxb_col, tok)
    if shb_col is not None:
        t1 = bcast_scalar(pl, B, B.AMX[0:1, 9:10], shb_col, tok)
    return t1


def emit_l1_ffn_a(pl, B, mx8_dram, clip_x_dram, wgt_dram, wut_dram,
                  sg_dram, su_dram, h1_dram, hmax_out, tok_xn, guard=None,
                  gmax11=False, wg_pieces=None):
    B.ffna_guard = guard
    tok_sx = emit_sx(pl, B, mx8_dram, clip_x_dram, B.RSXB[:], B.SXB[:],
                     gmax11=gmax11, ld_guard=B.sx_ld_guard)
    for c in range(DC):
        sl = slice(c * S, (c + 1) * S)
        pl.step("v", lambda v, sl=sl: v.tensor_scalar(
            B.XQI[:], B.XN[:, sl], B.RSXB[:], None, ALU.mult),
            waits=[tok_sx, tok_xn[0], tok_xn[1]])
        pl.step("v", lambda v, sl=sl: v.tensor_copy(B.XQB[:, sl], B.XQI[:]))
    tok_xq = B.c_v.tok()
    def pieces(i0, ln):
        out = []
        while ln > 0:
            r = i0 // 352
            lo = i0 - r * 352
            n = min(352 - lo, ln)
            out.append((r, lo, n))
            i0 += n
            ln -= n
        return out

    for ic in range(22):
        for dst, sdram in ((B.SGB, sg_dram), (B.SUB, su_dram)):
            if wg_pieces:
                off = 0
                for r, lo, n in pieces(ic * 128, 128):
                    pl.step("y", lambda y, ic=ic, dst=dst, sdram=sdram, r=r,
                            lo=lo, n=n, off=off: y.dma_start(
                        out=dst[off:off + n, ic:ic + 1],
                        in_=sdram[r:r + 1, lo:lo + n].rearrange("a b -> b a")),
                        bump=B.c_ld, n=16)
                    off += n
            else:
                pl.step("y", lambda y, ic=ic, dst=dst, sdram=sdram: y.dma_start(
                    out=dst[:, ic:ic + 1],
                    in_=sdram[0:1, ic * 128:(ic + 1) * 128].rearrange("a b -> b a")),
                    bump=B.c_ld, n=16)
    tok_s_l = B.c_ld.all()
    pl.step("v", lambda v: v.tensor_scalar(B.SGB[:], B.SGB[:], B.SXB[:],
                                           None, ALU.mult), waits=tok_s_l + [tok_sx])
    pl.step("v", lambda v: v.tensor_scalar(B.SUB[:], B.SUB[:], B.SXB[:],
                                           None, ALU.mult), bump=B.c_v)
    tok_sc = B.c_v.tok()
    wtok = [None] * 4
    wused = [B.ffna_guard] * 4
    hst_tok = B.ffna_guard
    B.sil_guard = None
    for tb in range(NQB):
        q0 = tb * QB
        for ic in range(22):
            bg = ic % 2
            bu = 2 + ic % 2
            for which, wdram, bi in (("g", wgt_dram, bg), ("u", wut_dram, bu)):
                if wg_pieces:
                    off = 0
                    first = True
                    for r, lo, n in pieces(ic * 128, 128):
                        def ldw(g, wdram=wdram, r=r, lo=lo, n=n, off=off, bi=bi):
                            return g.dma_start(
                                out=B.WQS[bi][:].rearrange(
                                    "p (c i) -> p c i", c=DC)[:, :, off:off + n],
                                in_=wdram[r].rearrange(
                                    "(c p) i -> p c i", p=128)[:, :, lo:lo + n])
                        pl.step("g", ldw, waits=[wused[bi]] if first else [],
                                bump=B.c_wg, n=16)
                        first = False
                        off += n
                else:
                    def ldw(g, wdram=wdram, ic=ic, bi=bi):
                        return g.dma_start(
                            out=B.WQS[bi][:].rearrange("p (c i) -> p c i", c=DC),
                            in_=wdram[:, ic * 128:(ic + 1) * 128].rearrange(
                                "(c p) i -> p c i", p=128))
                    pl.step("g", ldw, waits=[wused[bi]], bump=B.c_wg, n=16)
                wtok[bi] = B.c_wg.tok()
            bank_g = B.AV[ic % 2]
            bank_u = B.AV[2 + ic % 2]

            def peg(t, ic=ic, q0=q0, bg=bg, bank_g=bank_g):
                last = None
                for dk in range(DC):
                    last = t.matmul(
                        bank_g[:], B.WQS[bg][:, dk * 128:(dk + 1) * 128],
                        B.XQB[:, dk * S + q0: dk * S + q0 + QB],
                        start=(dk == 0), stop=(dk == DC - 1))
                return last
            pl.step("t", peg, waits=[wtok[bg], tok_xq] + bank_wait(B, bank_g),
                    bump=B.c_pe)
            wused[bg] = B.c_pe.tok()

            def peu(t, ic=ic, q0=q0, bu=bu, bank_u=bank_u):
                last = None
                for dk in range(DC):
                    last = t.matmul(
                        bank_u[:], B.WQS[bu][:, dk * 128:(dk + 1) * 128],
                        B.XQB[:, dk * S + q0: dk * S + q0 + QB],
                        start=(dk == 0), stop=(dk == DC - 1))
                return last
            pl.step("t", peu, waits=[wtok[bu]] + bank_wait(B, bank_u),
                    bump=B.c_pe)
            wused[bu] = B.c_pe.tok()
            tok_mm = B.c_pe.tok()
            pl.step("s", lambda s, ic=ic, bank_g=bank_g: s.activation(
                B.SIL[:], bank_g[:], AF.Silu, scale=B.SGB[:, ic:ic + 1]),
                waits=[tok_mm, tok_sc, B.sil_guard])
            bank_set(B, bank_g, B.c_a.tok())
            pl.step("v", lambda v, ic=ic, bank_u=bank_u: v.scalar_tensor_tensor(
                B.H1[:, ic * 512:(ic + 1) * 512], B.SIL[:], B.SUB[:, ic:ic + 1],
                bank_u[:], ALU.mult, ALU.mult),
                waits=[B.c_a.tok(), hst_tok])
            B.sil_guard = B.c_v.tok()
            bank_set(B, bank_u, B.c_v.tok())
            pl.step("g", lambda g, tb=tb, ic=ic: g.reduce_max(
                B.MXH[0:1, tb * 22 + ic: tb * 22 + ic + 1],
                B.H1[:, ic * 512:(ic + 1) * 512], axis=AX.XYZWC,
                apply_absolute_value=True), waits=[B.c_v.tok()])
        tok_hready = (B.c_v.tok(), B.c_g.tok())
        pl.step("y", lambda y, q0=q0: y.dma_start(
            out=h1_dram.rearrange("(c p) t -> p c t", p=128)[:, :, q0:q0 + QB],
            in_=B.H1[:].rearrange("p (c t) -> p c t", c=22)),
            waits=[tok_hready[0], tok_hready[1]], bump=B.c_st, n=16)
        hst_tok = B.c_st.tok()
    pl.step("g", lambda g: g.tensor_reduce(
        B.MXH[0:1, 88:89], B.MXH[0:1, 0:88], axis=AX.XYZWC, op=ALU.max),
        bump=B.c_g)
    tok_hm = B.c_g.tok()
    pl.step("y", lambda y: y.dma_start(out=hmax_out, in_=B.MXH[0:1, 88:89]),
            waits=[tok_hm], bump=B.c_st, n=16)
    return B.c_st.all()


def emit_l1_ffn_b(pl, B, hmax8_dram, clip_d_dram, wdt_dram, sd_dram,
                  h1_dram, x1_dram, a1_dram, out_dram, guard=None,
                  gmax11=False, wd_pieces=False):
    B.ffnb_guard = guard
    tok_sh = emit_sx(pl, B, hmax8_dram, clip_d_dram, B.RSHB[:], B.SHB[:],
                     gmax11=gmax11, ld_guard=B.sx_ld_guard)
    for m in range(DC):
        if wd_pieces:
            pl.step("y", lambda y, m=m: y.dma_start(
                out=B.SDB[:, m:m + 1],
                in_=sd_dram[m:m + 1, :].rearrange("a b -> b a")),
                bump=B.c_ld, n=16)
        else:
            pl.step("y", lambda y, m=m: y.dma_start(
                out=B.SDB[:, m:m + 1],
                in_=sd_dram[0:1, m * 128:(m + 1) * 128].rearrange("a b -> b a")),
                bump=B.c_ld, n=16)
    tok_s_l = B.c_ld.all()
    pl.step("v", lambda v: v.tensor_scalar(B.SDB[:], B.SDB[:], B.SHB[:],
                                           None, ALU.mult),
            waits=tok_s_l + [tok_sh], bump=B.c_v)
    tok_sd = B.c_v.tok()
    banks = B.AV + B.SC + [B.AUX, B.NRM]
    wused = [B.ffnb_guard, B.ffnb_guard]
    h1_guard = B.ffnb_guard
    for tb in range(NQB):
        q0 = tb * QB
        pl.step("y", lambda y, q0=q0: y.dma_start(
            out=B.H1[:].rearrange("p (c t) -> p c t", c=22),
            in_=h1_dram.rearrange("(c p) t -> p c t", p=128)[:, :, q0:q0 + QB]),
            waits=[h1_guard], bump=B.c_ld, n=16)
        tok_h1 = B.c_ld.tok()
        for ic in range(22):
            sl = slice(ic * 512, (ic + 1) * 512)
            pl.step("v", lambda v, sl=sl: v.tensor_scalar(
                B.H1I[:], B.H1[:, sl], B.RSHB[:], None, ALU.mult),
                waits=[tok_h1, tok_sh])
            pl.step("v", lambda v, sl=sl: v.tensor_copy(B.H1[:, sl], B.H1I[:]))
        tok_hq = B.c_v.tok()
        wtok = [None, None]
        for ic in range(22):
            bi = ic % 2
            if wd_pieces:
                first = True
                for r in range(DC):
                    def ldw(g, ic=ic, bi=bi, r=r):
                        return g.dma_start(
                            out=B.WQS[bi][:].rearrange(
                                "p (c d) -> p c d", c=DC)[:, r:r + 1, :],
                            in_=wdt_dram[r:r + 1, ic * 128:(ic + 1) * 128, :]
                            .rearrange("a p d -> p a d"))
                    pl.step("g", ldw, waits=[wused[bi]] if first else [],
                            bump=B.c_wg, n=16)
                    first = False
            else:
                def ldw(g, ic=ic, bi=bi):
                    return g.dma_start(
                        out=B.WQS[bi][:].rearrange("p (c d) -> p c d", c=DC),
                        in_=wdt_dram[ic * 128:(ic + 1) * 128, :].rearrange(
                            "p (c d) -> p c d", c=DC))
                pl.step("g", ldw, waits=[wused[bi]], bump=B.c_wg, n=16)
            wtok[bi] = B.c_wg.tok()

            def pe(t, ic=ic, bi=bi):
                last = None
                for m in range(DC):
                    last = t.matmul(
                        banks[m][:], B.WQS[bi][:, m * 128:(m + 1) * 128],
                        B.H1[:, ic * 512:(ic + 1) * 512],
                        start=(ic == 0), stop=(ic == 21))
                return last
            w = [wtok[bi], tok_hq]
            if ic == 0:
                for k in range(DC):
                    w += bank_wait(B, banks[k])
            pl.step("t", pe, waits=w, bump=B.c_pe)
            wused[bi] = B.c_pe.tok()
        tok_out = B.c_pe.tok()
        h1_guard = tok_out
        ld_tok = [None, None]
        for m in range(DC):
            bi = m % 2
            pl.step("y", lambda y, m=m, q0=q0, bi=bi: y.dma_start(
                out=B.XRT[bi][:], in_=x1_dram[m * 128:(m + 1) * 128, q0:q0 + QB]),
                waits=[ld_tok[bi]] + bank_wait(B, B.XRT[bi]), bump=B.c_ld, n=16)
            tok_xr = B.c_ld.tok()
            pl.step("y", lambda y, m=m, q0=q0, bi=bi: y.dma_start(
                out=B.ART[bi][:], in_=a1_dram[m * 128:(m + 1) * 128, q0:q0 + QB]),
                waits=[ld_tok[bi]] + bank_wait(B, B.ART[bi]), bump=B.c_ld, n=16)
            tok_res = B.c_ld.tok()
            pl.step("v", lambda v, m=m, bi=bi: v.scalar_tensor_tensor(
                B.XRT[bi][:], banks[m][:], B.SDB[:, m:m + 1], B.XRT[bi][:],
                ALU.mult, ALU.add), waits=[tok_out, tok_sd, tok_xr], bump=B.c_v)
            bank_set(B, banks[m], B.c_v.tok())
            pl.step("v", lambda v, m=m, bi=bi: v.tensor_tensor(
                B.XRT[bi][:], B.XRT[bi][:], B.ART[bi][:], ALU.add),
                waits=[tok_res], bump=B.c_v)
            tok_xp = B.c_v.tok()
            pl.step("y", lambda y, m=m, q0=q0, bi=bi: y.dma_start(
                out=out_dram[m * 128:(m + 1) * 128, q0:q0 + QB],
                in_=B.XRT[bi][:]), waits=[tok_xp], bump=B.c_st, n=16)
            ld_tok[bi] = B.c_st.tok()
            bank_set(B, B.XRT[bi], B.c_st.tok())
            bank_set(B, B.ART[bi], B.c_st.tok())
    return B.c_st.all()


def finish(P, pl, st_tok):
    wl = st_tok if isinstance(st_tok, list) else [st_tok]
    pl.step("y", lambda y: None, waits=wl)
    with P.nc.Block() as block:
        pl.emit_all(block)
    return P.fin()


def build_l0_attn(dbg=False):
    P = Prog()
    xn = P.din("xn", [D, S], BF16)
    wq_t = P.din("wq_t", [D, D], F32)
    wk_t = P.din("wk_t", [D, D], F32)
    wv_t = P.din("wv_t", [D, D], F32)
    wo_t = P.din("wo_t", [D, D], F32)
    bq2 = P.din("bq2", [128, DC], F32)
    bk2 = P.din("bk2", [128, DC], F32)
    bv_row = P.din("bv_row", [1, D], F32)
    bo2 = P.din("bo2", [128, DC], F32)
    a0 = P.dout("a0", [D, S], F32)

    B = alloc_common(P)
    alloc_l0_attn(P, B)
    pl = mk_plan(P, B)
    emit_setup(pl, B)
    t_ = load_xn(pl, B, xn)
    st_tok = emit_l0_attn(pl, B, wq_t, wk_t, wv_t, wo_t, bq2, bk2, bv_row, bo2,
                          a0, (t_, t_))
    if dbg:
        dumps = [("QF", B.QF, BF16), ("KF", B.KF, BF16), ("VT", B.VT, BF16),
                 ("EXPB", B.EXPB, BF16), ("CB", B.CBS, F32), ("MXR", B.MXR, F32),
                 ("ATT", B.ATT, BF16), ("BVB", B.BVB, BF16)]
        allw = [B.c_pe.tok(), B.c_v.tok(), B.c_a.tok(), B.c_g.tok()]
        for nm, buf, dt in dumps:
            od = P.dout("dbg_" + nm, list(buf.shape), dt)
            pl.step("y", lambda y, od=od, buf=buf: y.dma_start(out=od, in_=buf[:]),
                    waits=allw, bump=B.c_st, n=16)
        st_tok = B.c_st.all()
    return finish(P, pl, st_tok)


def build_l0_pre(dbg=False, no_ln=False):
    P = Prog()
    x = P.din("x", [D, S], F32)
    lg1 = P.din("lg1", [1, 256], F32)
    lg2 = P.din("lg2", [1, 256], F32)
    a1t = P.din("a1t", [256, 64, 1024], F32)
    b1 = P.din("b1", [256, 64, 4096], F32)
    a2t = P.din("a2t", [256, 64, 4096], F32)
    b2 = P.din("b2", [256, 64, 1024], F32)
    coreid = P.din("coreid", [1, 1], F32)
    xn0 = P.dout("xn0", [D, S], BF16)
    w1sl = P.dout("w1sl", [D, 512], BF16)
    w2sl = P.dout("w2sl", [512, D], BF16)

    B = alloc_common(P)
    B.XN = P.sb("XN", [128, DC * S], BF16)
    alloc_ln(P, B)
    alloc_compose(P, B)
    pl = mk_plan(P, B)
    emit_setup(pl, B)
    if not no_ln:
        tok_xn = emit_ln(pl, B, x)
        pl.step("y", lambda y: y.dma_start(
            out=xn0.rearrange("(c p) t -> p c t", p=128),
            in_=B.XN[:].rearrange("p (c t) -> p c t", c=DC)),
            waits=[tok_xn[0], tok_xn[1]], bump=B.c_st, n=16)
    else:
        pl.step("y", lambda y: y.dma_start(
            out=xn0.rearrange("(c p) t -> p c t", p=128),
            in_=B.XN[:].rearrange("p (c t) -> p c t", c=DC)),
            bump=B.c_st, n=16)
    emit_compose_setup(pl, B, coreid)
    st = emit_compose(pl, B, lg1, lg2, a1t, b1, a2t, b2, w1sl, w2sl)
    if dbg:
        allw0 = [B.c_pe.tok(), B.c_v.tok(), B.c_a.tok(), B.c_g.tok()] \
            + B.c_wg.all()
        pl.step("v", lambda v: v.tensor_copy(B.U2[:], B.IDXA[:]), waits=allw0)
        pl.step("v", lambda v: v.tensor_copy(B.IDXAf[:], B.IDX8[:]),
                bump=B.c_v)
        dumps = [("IDXA", B.U2, F32), ("IDX8", B.IDXAf, F32),
                 ("MS", B.MS, F32), ("W32", B.W32, F32), ("IVF", B.IVF, F32),
                 ("GA0", B.GA[0], BF16), ("GB0", B.GB[0], BF16),
                 ("GAS0", B.GAS[0], BF16), ("QM64", B.QM64, F32),
                 ("PEVB", B.PEVB, F32), ("SELB", B.SELB, F32)]
        allw = [B.c_pe.tok(), B.c_v.tok(), B.c_a.tok(), B.c_g.tok()]
        for nm, buf, dt in dumps:
            od = P.dout("dbg_" + nm, list(buf.shape), dt)
            pl.step("y", lambda y, od=od, buf=buf: y.dma_start(out=od, in_=buf[:]),
                    waits=allw, bump=B.c_st, n=16)
    return finish(P, pl, B.c_st.all())


def build_l0_ffn(dbg=False):
    P = Prog()
    xn = P.din("xn", [D, S], BF16)
    w1g = P.din("w1g", [8, D, 512], BF16)
    w2g = P.din("w2g", [4096, D], BF16)
    x = P.din("x", [D, S], F32)
    a0 = P.din("a0", [D, S], F32)
    x1 = P.dout("x1", [D, S], F32)
    xn1 = P.dout("xn1", [D, S], BF16)
    mx = P.dout("mx", [1, 1], F32)

    B = alloc_common(P)
    B.XN = P.sb("XN", [128, DC * S], BF16)
    B.AMC = P.sb("AMC", [128, DC + 1], F32)
    B.AMX = P.sb("AMX", [1, 16], F32)
    alloc_ffn0(P, B)
    alloc_ln(P, B)
    pl = mk_plan(P, B)
    emit_setup(pl, B)
    t = load_xn(pl, B, xn)
    st = emit_ffn0_apply(pl, B, w1g, w2g, x, a0, x1, (t, t),
                         no_gelu=(dbg == "nogelu"))
    tok_xn1 = emit_ln(pl, B, x1, wait=st)
    pl.step("y", lambda y: y.dma_start(
        out=xn1.rearrange("(c p) t -> p c t", p=128),
        in_=B.XN[:].rearrange("p (c t) -> p c t", c=DC)),
        waits=[tok_xn1[0], tok_xn1[1]], bump=B.c_st, n=16)
    tok_mx = emit_absmax_xn(pl, B, tok_xn1)
    pl.step("y", lambda y: y.dma_start(out=mx, in_=B.AMX[0:1, 0:1]),
            waits=[tok_mx], bump=B.c_st, n=16)
    st = B.c_st.all()
    if dbg:
        allw = [B.c_pe.tok(), B.c_v.tok(), B.c_a.tok(), B.c_g.tok(), st]
        od = P.dout("dbg_H", [128, 32 * 512], BF16)
        pl.step("y", lambda y, od=od: y.dma_start(out=od, in_=B.H[:]),
                waits=allw, bump=B.c_st, n=16)
        st = B.c_st.all()
    return finish(P, pl, st)


def build_l1_attn():
    P = Prog()
    xn = P.din("xn", [D, S], BF16)
    w_in = P.din("w_in", [D, D], F32)
    w_r = P.din("w_r", [D, D], F32)
    w_i = P.din("w_i", [D, D], F32)
    w_out = P.din("w_out", [D, D], F32)
    b_in2 = P.din("b_in2", [128, DC], F32)
    b_r2 = P.din("b_r2", [128, DC], F32)
    b_i2 = P.din("b_i2", [128, DC], F32)
    b_out2 = P.din("b_out2", [128, DC], F32)
    log_a2 = P.din("log_a2", [128, DC], F32)
    a1 = P.dout("a1", [D, S], F32)
    B = alloc_common(P)
    B.XN = P.sb("XN", [128, DC * S], BF16)
    alloc_l1_attn(P, B)
    pl = mk_plan(P, B)
    emit_setup(pl, B)
    t = load_xn(pl, B, xn)
    st = emit_l1_attn(pl, B, w_in, w_r, w_i, w_out, b_in2, b_r2, b_i2, b_out2,
                      log_a2, a1, (t, t))
    return finish(P, pl, st)


def build_l1_quant():
    P = Prog()
    wg_sl = P.din("wg_sl", [352, 1024], F32)
    wu_sl = P.din("wu_sl", [352, 1024], F32)
    wd_sl = P.din("wd_sl", [128, 2816], F32)
    wgt_sl = P.dout("wgt_sl", [D, 352], BF16)
    wut_sl = P.dout("wut_sl", [D, 352], BF16)
    wdt_sl = P.dout("wdt_sl", [2816, 128], BF16)
    sg_sl = P.dout("sg_sl", [1, 352], F32)
    su_sl = P.dout("su_sl", [1, 352], F32)
    sd_sl = P.dout("sd_sl", [1, 128], F32)
    B = alloc_common(P)
    alloc_l1_quant(P, B)
    pl = mk_plan(P, B)
    emit_setup(pl, B)
    st = emit_l1_quant(pl, B, wg_sl, wu_sl, wd_sl, wgt_sl, wut_sl, wdt_sl,
                       sg_sl, su_sl, sd_sl)
    return finish(P, pl, st)


def build_l1_ffn_a():
    P = Prog()
    xn = P.din("xn", [D, S], BF16)
    mx8 = P.din("mx8", [1, 8], F32)
    clip_x = P.din("clip_x", [1, 1], F32)
    wgt = P.din("wgt", [D, 2816], BF16)
    wut = P.din("wut", [D, 2816], BF16)
    sg = P.din("sg", [1, 2816], F32)
    su = P.din("su", [1, 2816], F32)
    h1 = P.dout("h1", [2816, S], BF16)
    hmax = P.dout("hmax", [1, 1], F32)
    B = alloc_common(P)
    B.XN = P.sb("XN", [128, DC * S], BF16)
    B.QF = P.sb("QF", [128, DC * S], BF16)
    B.KF = P.sb("KF", [128, DC * S], BF16)
    alloc_l1_ffn(P, B, "a")
    B.sx_ld_guard = None
    pl = mk_plan(P, B)
    emit_setup(pl, B)
    t = load_xn(pl, B, xn)
    st = emit_l1_ffn_a(pl, B, mx8, clip_x, wgt, wut, sg, su, h1, hmax, (t, t))
    return finish(P, pl, st)


def build_l1_ffn_b():
    P = Prog()
    hmax8 = P.din("hmax8", [1, 8], F32)
    clip_d = P.din("clip_d", [1, 1], F32)
    wdt = P.din("wdt", [2816, D], BF16)
    sd = P.din("sd", [1, D], F32)
    h1 = P.din("h1", [2816, S], BF16)
    x1 = P.din("x1", [D, S], F32)
    a1 = P.din("a1", [D, S], F32)
    out = P.dout("out", [D, S], F32)
    B = alloc_common(P)
    alloc_l1_ffn(P, B, "b")
    B.sx_ld_guard = None
    pl = mk_plan(P, B)
    emit_setup(pl, B)
    st = emit_l1_ffn_b(pl, B, hmax8, clip_d, wdt, sd, h1, x1, a1, out)
    return finish(P, pl, st)



def alloc_fused(P):
    B = alloc_common(P)
    B.XN = P.sb("XN", [128, DC * S], BF16)
    B.QF = P.sb("QF", [128, DC * S], BF16)
    B.KF = P.sb("KF", [128, DC * S], BF16)
    B.VT = P.sb("VT", [128, NT * D], BF16)
    B.STASH = P.sb("STASH", [128, 8 * 512], F16)
    B.ATT = B.STASH.bitcast(BF16)
    B.EXPB = P.sb("EXPB", [128, NT * 512], BF16)
    B.WARE = [P.sb(f"ware{i}", [128, DC * 256], BF16) for i in range(2)]
    B.RZ = P.sb("RZN", [128, 512], F32)
    B.RN = B.RZ
    B.FS = P.sb("FS", [128, 5 * 512], F32)
    FSs = [B.FS[:, i * 512:(i + 1) * 512] for i in range(5)]
    B.A0T = [FSs[0], FSs[1]]
    B.RSTB = FSs[2]
    B.MRB = FSs[3]
    B.GU = B.RZ[:]
    B.GS = FSs[4]
    B.XRT = [FSs[0], FSs[1]]
    B.ART = [FSs[2], FSs[3]]
    B.SIL = FSs[4]
    B.X2B = [B.EXPB[:, 0:512], B.EXPB[:, 512:1024]]
    B.ROWA = B.FS[0:1, 2048:2560]
    B.ROWB = B.RZ[0:1, 0:512]
    B.EPSB = P.sb("epsb", [1, 1], F32)
    B.eps_init = False
    B.MXR = P.sb("MXR", [1, 8], F32)
    B.NCR = P.sb("NCR", [1, 8], F32)
    B.CBS = P.sb("CBS", [128, 8], F32)
    B.BQ = P.sb("BQ", [128, DC], F32)
    B.BK = P.sb("BK", [128, DC], F32)
    B.BO = P.sb("BO", [128, DC], F32)
    B.BVR = B.EXPB[0:1, 2048:3072]
    B.BVB = B.EXPB[:, 1024:2048]
    B.LG = B.FS[0:1, 0:256]
    B.LGW = B.FS[32:33, 0:256]
    B.MV = P.sb("MV", [1, 32], F32)
    B.IV = P.sb("IV", [1, 32], U32)
    B.IVF = P.sb("IVF", [1, 32], F32)
    B.EW = P.sb("EW", [1, 32], F32)
    B.SWr = P.sb("SWr", [1, 2], F32)
    B.W32 = P.sb("W32", [1, 32], F32)
    B.SELB = P.sb("SELB", [128, 32], F32)
    B.W32B = P.sb("W32B", [128, 32], F32)
    B.PEVB = P.sb("PEVB", [128, 32], F32)
    B.EOT = P.sb("EOT", [128, 32], F32)
    B.UB = P.sb("UB", [128, 32], F32)
    B.U2 = P.sb("U2", [128, 16], F32)
    B.MS = P.sb("MS", [128, 16], F32)
    B.IDXAf = P.sb("IDXAf", [128, 16], F32)
    B.IDXA = P.sb("IDXA", [128, 16], U32)
    B.IDX8 = P.sb("IDX8", [128, 16], U32)
    B.QM64 = P.sb("QM64", [128, 1], F32)
    B.QI = P.sb("QI", [128, 1], I32)
    B.COREB = P.sb("COREB", [128, 1], F32)
    B.GA = [B.QF[:, 0:1024], B.QF[:, 1024:2048]]
    B.GAS = [B.QF[:, 2048:3072], B.QF[:, 3072:4096]]
    B.GB = [B.QF[:, 4096:5120], B.QF[:, 5120:6144]]
    B.STG = [B.QF[:, 6144:6656], B.QF[:, 6656:7168]]
    B.H = B.QF
    B.WST = [B.KF[:, 0:8192], B.KF[:, 8192:16384]]
    B.QWB = B.VT[:, 0:3072]
    B.QDB = B.VT[:, 3072:5888]
    B.QBF = B.VT[:, 5888:8960]
    B.TQB = B.VT[:, 8960:11776]
    B.TQD = B.VT[:, 11776:14592]
    B.IBUF = P.sb("IBUF", [128, 2816], I8)
    B.QI8 = B.IBUF[:, 0:2816]
    B.SCL = P.sb("SCL", [128, 4], F32)
    B.RCL = P.sb("RCL", [128, 4], F32)
    B.IP = B.QF
    B.RG = B.KF
    B.IG = B.VT
    B.BIN = P.sb("BIN", [128, DC], F32)
    B.BR = P.sb("BR", [128, DC], F32)
    B.BI = P.sb("BI", [128, DC], F32)
    B.BOUT = P.sb("BOUT", [128, DC], F32)
    B.ASG = P.sb("ASG", [128, DC], F32)
    B.XQB = B.KF
    B.XQI = B.IBUF[:, 0:2048]
    B.H1 = B.VT[:, 0:22 * 512]
    B.H1I = B.IBUF[:, 0:512]
    B.SGB = P.sb("SGB", [128, 22], F32)
    B.SUB = P.sb("SUB", [128, 22], F32)
    B.SDB = P.sb("SDB", [128, DC], F32)
    B.WQS = [B.QF[:, 0:1024], B.QF[:, 1024:2048],
             B.QF[:, 2048:3072], B.QF[:, 3072:4096]]
    B.MXH = P.sb("MXH", [1, 92], F32)
    B.AMC = P.sb("AMC", [128, DC + 1], F32)
    B.AMX = P.sb("AMX", [1, 16], F32)
    B.CLP = P.sb("CLP", [1, 4], F32)
    B.RSXB = P.sb("RSXB", [128, 1], F32)
    B.SXB = P.sb("SXB", [128, 1], F32)
    B.RSHB = P.sb("RSHB", [128, 1], F32)
    B.SHB = P.sb("SHB", [128, 1], F32)
    return B


RG8 = [list(range(8))]


def build_fused():
    P = Prog()
    x = P.din("x", [D, S], F32)
    wq_t = P.din("wq_t", [D, D], F32)
    wk_t = P.din("wk_t", [D, D], F32)
    wv_t = P.din("wv_t", [D, D], F32)
    wo_t = P.din("wo_t", [D, D], F32)
    bq2 = P.din("bq2", [128, DC], F32)
    bk2 = P.din("bk2", [128, DC], F32)
    bv_row = P.din("bv_row", [1, D], F32)
    bo2 = P.din("bo2", [128, DC], F32)
    lg1 = P.din("lg1", [1, 256], F32)
    lg2 = P.din("lg2", [1, 256], F32)
    a1t = P.din("a1t", [256, 64, 1024], F32)
    b1 = P.din("b1", [256, 64, 4096], F32)
    a2t = P.din("a2t", [256, 64, 4096], F32)
    b2 = P.din("b2", [256, 64, 1024], F32)
    coreid = P.din("coreid", [1, 1], F32)
    w_in = P.din("w_in", [D, D], F32)
    w_r = P.din("w_r", [D, D], F32)
    w_i = P.din("w_i", [D, D], F32)
    w_out = P.din("w_out", [D, D], F32)
    b_in2 = P.din("b_in2", [128, DC], F32)
    b_r2 = P.din("b_r2", [128, DC], F32)
    b_i2 = P.din("b_i2", [128, DC], F32)
    b_out2 = P.din("b_out2", [128, DC], F32)
    log_a2 = P.din("log_a2", [128, DC], F32)
    wg_sl = P.din("wg_sl", [352, 1024], F32)
    wu_sl = P.din("wu_sl", [352, 1024], F32)
    wd_sl = P.din("wd_sl", [128, 2816], F32)
    clip_x = P.din("clip_x", [1, 1], F32)
    clip_d = P.din("clip_d", [1, 1], F32)
    out = P.dout("out", [D, S], F32)
    a0 = P.dtmp("a0", [D, S], F32)
    a1 = P.dtmp("a1", [D, S], F32)
    x1 = P.dtmp("x1", [D, S], F32)
    h1 = P.dtmp("h1", [2816, S], BF16)
    w1in = P.dtmp("w1in", [D, 512], BF16)
    w1out = P.dtmp("w1out", [8 * D, 512], BF16)
    w2in = P.dtmp("w2in", [512, D], BF16)
    w2out = P.dtmp("w2out", [4096, D], BF16)
    wgt_in = P.dtmp("wgt_in", [D, 352], BF16)
    wgt_out = P.dtmp("wgt_out", [8 * D, 352], BF16)
    wut_in = P.dtmp("wut_in", [D, 352], BF16)
    wut_out = P.dtmp("wut_out", [8 * D, 352], BF16)
    wdt_in = P.dtmp("wdt_in", [2816, 128], BF16)
    wdt_out = P.dtmp("wdt_out", [8 * 2816, 128], BF16)
    sg_in = P.dtmp("sg_in", [1, 352], F32)
    sg_out = P.dtmp("sg_out", [8, 352], F32)
    su_in = P.dtmp("su_in", [1, 352], F32)
    su_out = P.dtmp("su_out", [8, 352], F32)
    sd_in = P.dtmp("sd_in", [1, 128], F32)
    sd_out = P.dtmp("sd_out", [8, 128], F32)
    mx_in = P.dtmp("mx_in", [1, 16], F32)
    mx_out = P.dtmp("mx_out", [1, 16], F32)
    hm_in = P.dtmp("hm_in", [1, 16], F32)
    hm_out = P.dtmp("hm_out", [1, 16], F32)

    B = alloc_fused(P)
    B.sx_ld_guard = None
    pl = mk_plan(P, B)
    emit_setup(pl, B)
    emit_compose_setup(pl, B, coreid)

    tok_xn0 = emit_ln(pl, B, x)
    st_comp = emit_compose(pl, B, lg1, lg2, a1t, b1, a2t, b2, w1in, w2in)
    for cin, cout in ((w1in, w1out), (w2in, w2out)):
        pl.step("g", lambda g, cin=cin, cout=cout: g.collective_compute(
            "AllGather", ALU.bypass, replica_groups=RG8,
            ins=[cin.opt()], outs=[cout.opt()]),
            waits=[st_comp], bump=B.c_cc)
    tok_ag1 = B.c_cc.tok()
    bank_set(B, B.A0T[0], B.c_v.tok())
    st_attn = emit_l0_attn(pl, B, wq_t, wk_t, wv_t, wo_t, bq2, bk2, bv_row,
                           bo2, a0, tok_xn0, guard=[st_comp])
    attn_done = B.c_pe.tok()
    st_ffn0 = emit_ffn0_apply(
        pl, B, w1out.rearrange("(r d) i -> r d i", r=8), w2out, x, a0, x1,
        tok_xn0, guard=[attn_done], wsrc_tok=tok_ag1)
    q_done = emit_l1_quant(pl, B, wg_sl, wu_sl, wd_sl, wgt_in, wut_in, wdt_in,
                           sg_in, su_in, sd_in, guard=attn_done)
    for cin, cout in ((wgt_in, wgt_out), (wut_in, wut_out), (wdt_in, wdt_out),
                      (sg_in, sg_out), (su_in, su_out), (sd_in, sd_out)):
        pl.step("g", lambda g, cin=cin, cout=cout: g.collective_compute(
            "AllGather", ALU.bypass, replica_groups=RG8,
            ins=[cin.opt()], outs=[cout.opt()]),
            waits=[q_done], bump=B.c_cc)
    tok_ag2 = B.c_cc.tok()

    tok_xn1 = emit_ln(pl, B, x1, wait=st_ffn0)
    tok_mx = emit_absmax_xn(pl, B, tok_xn1)
    pl.step("y", lambda y: y.dma_start(out=mx_in[0:1, 0:1], in_=B.AMX[0:1, 0:1]),
            waits=[tok_mx], bump=B.c_st, n=16)
    tok_mxst = B.c_st.tok()
    pl.step("g", lambda g: g.collective_compute(
        "AllReduce", ALU.max, replica_groups=RG8,
        ins=[mx_in.opt()], outs=[mx_out.opt()]),
        waits=[tok_mxst], bump=B.c_cc)
    tok_ar3 = B.c_cc.tok()
    st_a1 = emit_l1_attn(pl, B, w_in, w_r, w_i, w_out, b_in2, b_r2, b_i2,
                         b_out2, log_a2, a1, tok_xn1, guard=attn_done,
                         evac_guard=q_done)
    l1_done_v = B.c_v.tok()
    B.sx_ld_guard = tok_ar3
    st_h1 = emit_l1_ffn_a(
        pl, B, mx_out[0:1, 0:1], clip_x,
        wgt_out.rearrange("(r d) i -> r d i", r=8),
        wut_out.rearrange("(r d) i -> r d i", r=8),
        sg_out, su_out, h1, hm_in[0:1, 0:1], tok_xn1,
        guard=[l1_done_v, tok_ag2, q_done], gmax11=True, wg_pieces=True)
    pl.step("g", lambda g: g.collective_compute(
        "AllReduce", ALU.max, replica_groups=RG8,
        ins=[hm_in.opt()], outs=[hm_out.opt()]),
        waits=[st_h1], bump=B.c_cc)
    tok_ar4 = B.c_cc.tok()
    ffnb_guard = [B.c_pe.tok(), B.c_v.tok(), B.c_g.tok()] + B.c_st.all()
    B.sx_ld_guard = tok_ar4
    st_out = emit_l1_ffn_b(
        pl, B, hm_out[0:1, 0:1], clip_d,
        wdt_out.rearrange("(r i) d -> r i d", r=8), sd_out,
        h1, x1, a1, out, guard=ffnb_guard, gmax11=True, wd_pieces=True)
    return finish(P, pl, st_out)


def lay_bias(b):
    return np.ascontiguousarray(b.reshape(DC, 128).T)


CORES = list(range(8))


def host_prep(g):
    x = np.asarray(g["L0_attn_batch"])
    shared = {
        "wq_t": np.ascontiguousarray(np.asarray(g["L0_attn_Wq"]).T),
        "wk_t": np.ascontiguousarray(np.asarray(g["L0_attn_Wk"]).T),
        "wv_t": np.ascontiguousarray(np.asarray(g["L0_attn_Wv"]).T),
        "wo_t": np.ascontiguousarray(np.asarray(g["L0_attn_Wo"]).T),
        "bq2": lay_bias(np.asarray(g["L0_attn_bq"])),
        "bk2": lay_bias(np.asarray(g["L0_attn_bk"])),
        "bo2": lay_bias(np.asarray(g["L0_attn_bo"])),
        "bv_row": np.ascontiguousarray(np.asarray(g["L0_attn_bv"]).reshape(1, D)),
        "lg1": np.ascontiguousarray(np.asarray(g["L0_ffn_fc1_logits"]).reshape(1, 256)),
        "lg2": np.ascontiguousarray(np.asarray(g["L0_ffn_fc2_logits"]).reshape(1, 256)),
        "a1t": np.ascontiguousarray(np.asarray(g["L0_ffn_fc1_A"]).transpose(0, 2, 1)),
        "b1": np.ascontiguousarray(g["L0_ffn_fc1_B"]),
        "a2t": np.ascontiguousarray(np.asarray(g["L0_ffn_fc2_A"]).transpose(0, 2, 1)),
        "b2": np.ascontiguousarray(g["L0_ffn_fc2_B"]),
        "w_in": np.ascontiguousarray(g["L1_attn_W_in"]),
        "w_r": np.ascontiguousarray(g["L1_attn_W_r"]),
        "w_i": np.ascontiguousarray(g["L1_attn_W_i"]),
        "w_out": np.ascontiguousarray(g["L1_attn_W_out"]),
        "b_in2": lay_bias(np.asarray(g["L1_attn_b_in"])),
        "b_r2": lay_bias(np.asarray(g["L1_attn_b_r"])),
        "b_i2": lay_bias(np.asarray(g["L1_attn_b_i"])),
        "b_out2": lay_bias(np.asarray(g["L1_attn_b_out"])),
        "log_a2": lay_bias(np.asarray(g["L1_attn_log_a"])),
        "clip_x": np.ascontiguousarray(np.asarray(g["L1_ffn_clip_x"]).reshape(1, 1)),
        "clip_d": np.ascontiguousarray(np.asarray(g["L1_ffn_clip_down"]).reshape(1, 1)),
    }
    wg = np.asarray(g["L1_ffn_w_gate"])
    wu = np.asarray(g["L1_ffn_w_up"])
    wd = np.asarray(g["L1_ffn_w_down"])
    in_maps = []
    for c in range(8):
        m = dict(shared)
        m["x"] = np.ascontiguousarray(x[c].T)
        m["coreid"] = np.array([[c]], dtype=np.float32)
        m["wg_sl"] = np.ascontiguousarray(wg[c * 352:(c + 1) * 352])
        m["wu_sl"] = np.ascontiguousarray(wu[c * 352:(c + 1) * 352])
        m["wd_sl"] = np.ascontiguousarray(wd[c * 128:(c + 1) * 128])
        in_maps.append(m)
    return in_maps


def kernel(**inputs):
    in_maps = host_prep(inputs)
    nc = build_fused()
    res = run_bass_kernel_spmd(nc, in_maps, core_ids=CORES)
    out = np.stack([np.asarray(res.results[c]["out"]).T for c in range(8)])
    return np.ascontiguousarray(out.astype(np.float32))


def kernel_unfused(**inputs):
    g = inputs
    x = np.asarray(g["L0_attn_batch"])
    xT = [np.ascontiguousarray(x[c].T) for c in range(8)]

    nc = build_l0_pre()
    shared = {
        "lg1": np.ascontiguousarray(g["L0_ffn_fc1_logits"].reshape(1, 256)),
        "lg2": np.ascontiguousarray(g["L0_ffn_fc2_logits"].reshape(1, 256)),
        "a1t": np.ascontiguousarray(np.asarray(g["L0_ffn_fc1_A"]).transpose(0, 2, 1)),
        "b1": np.ascontiguousarray(g["L0_ffn_fc1_B"]),
        "a2t": np.ascontiguousarray(np.asarray(g["L0_ffn_fc2_A"]).transpose(0, 2, 1)),
        "b2": np.ascontiguousarray(g["L0_ffn_fc2_B"]),
    }
    in_maps = []
    for c in range(8):
        m = dict(shared)
        m["x"] = xT[c]
        m["coreid"] = np.array([[c]], dtype=np.float32)
        in_maps.append(m)
    r1 = run_bass_kernel_spmd(nc, in_maps, core_ids=CORES)
    xn0 = [np.asarray(r1.results[c]["xn0"]) for c in range(8)]
    w1g = np.ascontiguousarray(np.stack(
        [np.asarray(r1.results[c]["w1sl"]) for c in range(8)]))
    w2g = np.ascontiguousarray(np.concatenate(
        [np.asarray(r1.results[c]["w2sl"]) for c in range(8)], axis=0))

    nc = build_l0_attn()
    shared = {
        "wq_t": np.ascontiguousarray(np.asarray(g["L0_attn_Wq"]).T),
        "wk_t": np.ascontiguousarray(np.asarray(g["L0_attn_Wk"]).T),
        "wv_t": np.ascontiguousarray(np.asarray(g["L0_attn_Wv"]).T),
        "wo_t": np.ascontiguousarray(np.asarray(g["L0_attn_Wo"]).T),
        "bq2": lay_bias(np.asarray(g["L0_attn_bq"])),
        "bk2": lay_bias(np.asarray(g["L0_attn_bk"])),
        "bo2": lay_bias(np.asarray(g["L0_attn_bo"])),
        "bv_row": np.ascontiguousarray(np.asarray(g["L0_attn_bv"]).reshape(1, D)),
    }
    in_maps = [dict(shared, xn=xn0[c]) for c in range(8)]
    r2 = run_bass_kernel_spmd(nc, in_maps, core_ids=CORES)
    a0 = [np.asarray(r2.results[c]["a0"]) for c in range(8)]

    nc = build_l0_ffn()
    in_maps = [{"xn": xn0[c], "w1g": w1g, "w2g": w2g, "x": xT[c], "a0": a0[c]}
               for c in range(8)]
    r3 = run_bass_kernel_spmd(nc, in_maps, core_ids=CORES)
    x1 = [np.asarray(r3.results[c]["x1"]) for c in range(8)]
    xn1 = [np.asarray(r3.results[c]["xn1"]) for c in range(8)]
    mx8 = np.ascontiguousarray(np.concatenate(
        [np.asarray(r3.results[c]["mx"]) for c in range(8)], axis=1))

    nc = build_l1_quant()
    wg = np.asarray(g["L1_ffn_w_gate"])
    wu = np.asarray(g["L1_ffn_w_up"])
    wd = np.asarray(g["L1_ffn_w_down"])
    in_maps = [{
        "wg_sl": np.ascontiguousarray(wg[c * 352:(c + 1) * 352]),
        "wu_sl": np.ascontiguousarray(wu[c * 352:(c + 1) * 352]),
        "wd_sl": np.ascontiguousarray(wd[c * 128:(c + 1) * 128]),
    } for c in range(8)]
    r4 = run_bass_kernel_spmd(nc, in_maps, core_ids=CORES)
    wgt = np.ascontiguousarray(np.concatenate(
        [np.asarray(r4.results[c]["wgt_sl"]) for c in range(8)], axis=1))
    wut = np.ascontiguousarray(np.concatenate(
        [np.asarray(r4.results[c]["wut_sl"]) for c in range(8)], axis=1))
    wdt = np.ascontiguousarray(np.concatenate(
        [np.asarray(r4.results[c]["wdt_sl"]) for c in range(8)], axis=1))
    sg = np.ascontiguousarray(np.concatenate(
        [np.asarray(r4.results[c]["sg_sl"]) for c in range(8)], axis=1))
    su = np.ascontiguousarray(np.concatenate(
        [np.asarray(r4.results[c]["su_sl"]) for c in range(8)], axis=1))
    sd = np.ascontiguousarray(np.concatenate(
        [np.asarray(r4.results[c]["sd_sl"]) for c in range(8)], axis=1))

    nc = build_l1_attn()
    shared = {
        "w_in": np.ascontiguousarray(g["L1_attn_W_in"]),
        "w_r": np.ascontiguousarray(g["L1_attn_W_r"]),
        "w_i": np.ascontiguousarray(g["L1_attn_W_i"]),
        "w_out": np.ascontiguousarray(g["L1_attn_W_out"]),
        "b_in2": lay_bias(np.asarray(g["L1_attn_b_in"])),
        "b_r2": lay_bias(np.asarray(g["L1_attn_b_r"])),
        "b_i2": lay_bias(np.asarray(g["L1_attn_b_i"])),
        "b_out2": lay_bias(np.asarray(g["L1_attn_b_out"])),
        "log_a2": lay_bias(np.asarray(g["L1_attn_log_a"])),
    }
    in_maps = [dict(shared, xn=xn1[c]) for c in range(8)]
    r5 = run_bass_kernel_spmd(nc, in_maps, core_ids=CORES)
    a1 = [np.asarray(r5.results[c]["a1"]) for c in range(8)]

    nc = build_l1_ffn_a()
    clip_x = np.ascontiguousarray(np.asarray(g["L1_ffn_clip_x"]).reshape(1, 1))
    clip_d = np.ascontiguousarray(np.asarray(g["L1_ffn_clip_down"]).reshape(1, 1))
    shared = {"mx8": mx8, "clip_x": clip_x, "wgt": wgt, "wut": wut,
              "sg": sg, "su": su}
    in_maps = [dict(shared, xn=xn1[c]) for c in range(8)]
    r6 = run_bass_kernel_spmd(nc, in_maps, core_ids=CORES)
    h1 = [np.asarray(r6.results[c]["h1"]) for c in range(8)]
    hm8 = np.ascontiguousarray(np.concatenate(
        [np.asarray(r6.results[c]["hmax"]) for c in range(8)], axis=1))

    nc = build_l1_ffn_b()
    shared = {"hmax8": hm8, "clip_d": clip_d, "wdt": wdt, "sd": sd}
    in_maps = [dict(shared, h1=h1[c], x1=x1[c], a1=a1[c]) for c in range(8)]
    r7 = run_bass_kernel_spmd(nc, in_maps, core_ids=CORES)
    out = np.stack([np.asarray(r7.results[c]["out"]).T for c in range(8)])
    return np.ascontiguousarray(out.astype(np.float32))


# version 05
# speedup vs baseline: 1.3580x; 1.3580x over previous
import numpy as np
from contextlib import ExitStack
import concourse.bass as bass
import concourse.mybir as mybir
from concourse.bass_core_rust import IndirectOffsetOnAxis
from concourse.bass_utils import run_bass_kernel_spmd

F32 = mybir.dt.float32
BF16 = mybir.dt.bfloat16
F16 = mybir.dt.float16
I8 = mybir.dt.int8
I32 = mybir.dt.int32
U32 = mybir.dt.uint32
AF = mybir.ActivationFunctionType
ALU = mybir.AluOpType
AX = mybir.AxisListType

S = 2048
D = 1024
DC = 8
NT = 16
QB = 512
NQB = S // QB
ATT_SCALE = 1.0 / 32.0


class Ctr:
    __slots__ = ("h", "v", "name")

    def __init__(self, h, name):
        self.h = h
        self.v = 0
        self.name = name

    def tok(self):
        return (self, self.v)


class DmaPool:

    def __init__(self, subs):
        self.subs = subs
        self.i = 0
        self.last = None

    def nxt(self):
        c = self.subs[self.i]
        self.i = (self.i + 1) % len(self.subs)
        return c

    def tok(self):
        return self.last

    def all(self):
        return [(c, c.v) for c in self.subs if c.v > 0]


class Prog:
    def __init__(self):
        self.nc = bass.Bass("TRN2", target_bir_lowering=False)
        self.es = ExitStack()

    def sb(self, name, shape, dt=F32):
        return self.es.enter_context(self.nc.sbuf_tensor(name, shape, dt))

    def psum(self, name, shape, dt=F32):
        return self.es.enter_context(self.nc.psum_tensor(name, shape, dt))

    def sem(self, name):
        return Ctr(self.es.enter_context(self.nc.semaphore(name)), name)

    def din(self, name, shape, dt=F32):
        return self.nc.dram_tensor(name, shape, dt, kind="ExternalInput").ap()

    def dout(self, name, shape, dt=F32):
        return self.nc.dram_tensor(name, shape, dt, kind="ExternalOutput").ap()

    def dtmp(self, name, shape, dt=F32):
        return self.nc.dram_tensor(name, shape, dt, kind="Internal").ap()

    def fin(self):
        self.es.close()
        return self.nc


ENGS = ("t", "v", "s", "g", "y")


class Plan:

    def __init__(self, P, own=None):
        self.P = P
        self.q = {e: [] for e in ENGS}
        self.last_wait = {e: {} for e in ENGS}
        self.own = own or {}

    def step(self, eng, emit, waits=(), bump=None, n=1):
        own = self.own.get(eng)
        waits = list(waits)
        if isinstance(bump, DmaPool):
            pool = bump
            bump = pool.nxt()
            waits.append(bump.tok())
            pool.last = (bump, bump.v + n)
        if own is not None:
            waits.append(own.tok())
            if bump is None:
                bump = own
                n = 1
        wl = []
        flat = []

        def _fl(ws):
            for w in ws:
                if isinstance(w, list):
                    _fl(w)
                else:
                    flat.append(w)
        _fl(waits)
        for w in flat:
            if w is None:
                continue
            c, v = w if isinstance(w, tuple) else (w, w.v)
            assert v <= c.v, f"wait on future value of {c.name}"
            if v <= 0:
                continue
            lw = self.last_wait[eng].get(c, 0)
            if v <= lw:
                continue
            self.last_wait[eng][c] = v
            wl.append((c, v))
        if bump is not None:
            bump.v += n
        self.q[eng].append((wl, emit, bump, n))

    def emit_all(self, block):
        def runner(lst):
            def run(e):
                for wl, emit, bmp, n in lst:
                    for c, v in wl:
                        e.wait_ge(c.h, v)
                    ins = emit(e)
                    if bmp is not None:
                        assert ins is not None
                        ins.then_inc(bmp.h, n)
            return run
        for key, meth in (("y", block.sync), ("g", block.gpsimd),
                          ("t", block.tensor), ("v", block.vector),
                          ("s", block.scalar)):
            if self.q[key]:
                meth(runner(self.q[key]))


class NS:
    def __init__(self, **kw):
        self.__dict__.update(kw)



def alloc_common(P):
    B = NS()
    B.c_pe = P.sem("c_pe")
    B.c_v = P.sem("c_v")
    B.c_a = P.sem("c_a")
    B.c_g = P.sem("c_g")
    B.c_ld = DmaPool([P.sem(f"c_ld{i}") for i in range(6)])
    B.c_wg = DmaPool([P.sem(f"c_wg{i}") for i in range(6)])
    B.c_st = DmaPool([P.sem(f"c_st{i}") for i in range(4)])
    B.c_sd = DmaPool([P.sem(f"c_sd{i}") for i in range(4)])
    B.c_cc = P.sem("c_cc")
    B.iotat = P.sb("iotat", [128, 128], I32)
    B.ident_f = P.sb("ident_f", [128, 128], F32)
    B.ones128 = P.sb("ones128", [128, 128], BF16)
    B.ones_c = P.sb("ones_c", [128, 1], BF16)
    B.ones_r = P.sb("ones_r", [1, 128], BF16)
    B.ones_rf = P.sb("ones_rf", [1, 128], F32)
    B.AV = [P.psum(f"av{i}", [128, 512], F32) for i in range(4)]
    B.SC = [P.psum(f"sc{i}", [128, 512], F32) for i in range(2)]
    B.AUX = P.psum("aux", [128, 512], F32)
    B.NRM = P.psum("nrm", [128, 512], F32)
    B.bank_tok = {}
    return B


def emit_setup(pl, B):
    pl.step("g", lambda g: (g.iota(B.iotat[:], [[1, 128]], base=0,
                            channel_multiplier=-1), g.engine_nop())[1])
    pl.step("g", lambda g: g.tensor_scalar(B.ident_f[:], B.iotat[:], 0, None, ALU.is_equal))
    pl.step("g", lambda g: g.memset(B.ones128[:], 1.0), bump=B.c_g)
    pl.step("v", lambda v: v.memset(B.ones_c[:], 1.0))
    pl.step("v", lambda v: v.memset(B.ones_r[:], 1.0))
    pl.step("v", lambda v: v.memset(B.ones_rf[:], 1.0), bump=B.c_v)
    B.tok_setup_g = B.c_g.tok()
    B.tok_setup_v = B.c_v.tok()


def mk_plan(P, B):
    return Plan(P, own={"v": B.c_v, "s": B.c_a, "g": B.c_g})


def bank_wait(B, ps):
    t = B.bank_tok.get(id(ps))
    return [t] if t is not None else []


def bank_set(B, ps, tok):
    B.bank_tok[id(ps)] = tok



def alloc_l0_attn(P, B):
    B.XN = P.sb("XN", [128, DC * S], BF16)
    B.QF = P.sb("QF", [128, DC * S], BF16)
    B.KF = P.sb("KF", [128, DC * S], BF16)
    B.VT = P.sb("VT", [128, NT * D], BF16)
    B.STASH = P.sb("STASH", [128, 8 * 512], F16)
    B.ATT = B.STASH.bitcast(BF16)
    B.EXPB = P.sb("EXPB", [128, NT * 512], BF16)
    B.WARE = [P.sb(f"ware{i}", [128, DC * 256], BF16) for i in range(2)]
    B.RZ = P.sb("RZN", [128, 512], F32)
    B.RN = B.RZ
    B.MXR = P.sb("MXR", [1, 8], F32)
    B.NCR = P.sb("NCR", [1, 8], F32)
    B.CBS = P.sb("CBS", [128, 8], F32)
    B.BQ = P.sb("BQ", [128, DC], F32)
    B.BK = P.sb("BK", [128, DC], F32)
    B.BO = P.sb("BO", [128, DC], F32)
    B.BVR = P.sb("BVR", [1, D], BF16)
    B.BVB = P.sb("BVB", [128, D], BF16)
    B.A0T = [P.sb(f"a0t{i}", [128, 512], F32) for i in range(2)]


def load_xn(pl, B, xn_dram):
    pl.step("y", lambda y: y.dma_start(
        out=B.XN[:].rearrange("p (c t) -> p c t", c=DC),
        in_=xn_dram.rearrange("(c p) t -> p c t", p=128)), bump=B.c_ld, n=16)
    return B.c_ld.tok()


def emit_l0_attn(pl, B, wq_t, wk_t, wv_t, wo_t, bq2, bk2, bv_row, bo2,
                 a0_dram, tok_xn, guard=None):
    for buf, src in ((B.BQ, bq2), (B.BK, bk2), (B.BO, bo2)):
        pl.step("y", lambda y, buf=buf, src=src: y.dma_start(out=buf[:], in_=src),
                bump=B.c_ld, n=16)
    tok_bias_l = B.c_ld.all()
    pl.step("g", lambda g: g.dma_start(out=B.BVR[:], in_=bv_row),
            bump=B.c_wg, n=16)
    tok_bvr = B.c_wg.tok()
    pl.step("v", lambda v: v.tensor_scalar(B.BQ[:], B.BQ[:], ATT_SCALE, None, ALU.mult),
            waits=tok_bias_l, bump=B.c_v)
    tok_bq = B.c_v.tok()
    for h in range(2):
        sl = slice(h * 512, (h + 1) * 512)
        pl.step("t", lambda t, sl=sl: t.matmul(
            B.AUX[:], B.ones_r[:], B.BVR[0:1, sl], start=True, stop=True),
            waits=[tok_bvr, B.tok_setup_v] + bank_wait(B, B.AUX), bump=B.c_pe)
        pl.step("s", lambda s, sl=sl: s.copy(B.BVB[:, sl], B.AUX[:]),
                waits=[B.c_pe.tok()], bump=B.c_a)
        bank_set(B, B.AUX, B.c_a.tok())
    tok_bvb = B.c_a.tok()

    def load_wq(wdram, h, buf, waits=()):
        def go(g):
            return g.dma_start(
                out=buf[:].rearrange("p (c i) -> p c i", c=DC),
                in_=wdram[:, h * 256:(h + 1) * 256].rearrange("(c p) i -> p c i", p=128))
        pl.step("g", go, waits=list(waits), bump=B.c_wg, n=16)
        return B.c_wg.tok()

    rr = [0]

    def next_bank():
        b = B.AV[rr[0]]
        rr[0] = (rr[0] + 1) % 4
        return b

    seqw = [("q", wq_t), ("k", wk_t), ("v", wv_t), ("o", wo_t)]
    order = [(nm, h) for nm, _ in seqw for h in range(4)]
    dram_of = dict(seqw)
    wtok = {order[0]: load_wq(wq_t, 0, B.WARE[0]),
            order[1]: load_wq(wq_t, 1, B.WARE[1])}

    def prefetch_after(idx):
        if idx + 2 < len(order):
            nm, nh = order[idx + 2]
            wtok[(nm, nh)] = load_wq(dram_of[nm], nh, B.WARE[idx % 2],
                                     waits=[B.c_pe.tok()])

    oi = 0
    for name, bias, dstb in (("q", B.BQ, B.QF), ("k", B.BK, B.KF)):
        for h in range(4):
            wt = wtok[(name, h)]
            for ic in range(2 * h, 2 * h + 2):
                icl = (ic % 2) * 128
                for qb in range(NQB):
                    bank = next_bank()

                    def pe(t, oi=oi, icl=icl, qb=qb, bank=bank):
                        last = None
                        for dk in range(DC):
                            last = t.matmul(
                                bank[:],
                                B.WARE[oi % 2][:, dk * 256 + icl: dk * 256 + icl + 128],
                                B.XN[:, dk * S + qb * QB: dk * S + (qb + 1) * QB],
                                start=(dk == 0), stop=(dk == DC - 1))
                        return last
                    pl.step("t", pe, waits=[wt, list(tok_xn)] + bank_wait(B, bank),
                            bump=B.c_pe)
                    ptok = B.c_pe.tok()
                    scl = ATT_SCALE if name == "q" else 1.0

                    def ev(s, bank=bank, ic=ic, qb=qb, bias=bias, dstb=dstb,
                           scl=scl):
                        return s.activation(
                            dstb[:, ic * S + qb * QB: ic * S + (qb + 1) * QB],
                            bank[:], AF.Identity, bias=bias[:, ic:ic + 1],
                            scale=scl)
                    pl.step("s", ev, waits=[ptok, tok_bq, guard] + tok_bias_l,
                            bump=B.c_a)
                    bank_set(B, bank, B.c_a.tok())
            prefetch_after(oi)
            oi += 1
    for h in range(4):
        wt = wtok[("v", h)]
        wb = oi % 2
        for tc in range(NT):
            bank = next_bank()

            def pe(t, wb=wb, tc=tc, bank=bank):
                last = None
                for dk in range(DC):
                    last = t.matmul(
                        bank[0:128, 0:256],
                        B.XN[:, dk * S + tc * 128: dk * S + tc * 128 + 128],
                        B.WARE[wb][:, dk * 256:(dk + 1) * 256],
                        start=(dk == 0), stop=(dk == DC - 1))
                return last
            pl.step("t", pe, waits=[wt, list(tok_xn)] + bank_wait(B, bank), bump=B.c_pe)
            ptok = B.c_pe.tok()

            def ev(v, h=h, tc=tc, bank=bank):
                return v.tensor_tensor(
                    B.VT[:, tc * D + h * 256: tc * D + (h + 1) * 256],
                    bank[0:128, 0:256], B.BVB[:, h * 256:(h + 1) * 256], ALU.add)
            pl.step("v", ev, waits=[ptok, tok_bvb], bump=B.c_v)
            bank_set(B, bank, B.c_v.tok())
        prefetch_after(oi)
        oi += 1
    tok_qf = B.c_a.tok()
    tok_vt = B.c_v.tok()

    expb_guard = tok_vt
    att_guard = None
    slot_tok = [None] * 8
    bc_tok = [None] * 8
    for qb in range(NQB):
        q0 = qb * QB
        whtok = [None] * NT
        for t_ in range(NT):
            tt = t_ % 8
            bank = B.SC[t_ % 2]

            def pe(t, t_=t_, q0=q0, bank=bank):
                last = None
                for dk in range(DC):
                    last = t.matmul(
                        bank[:],
                        B.KF[:, dk * S + t_ * 128: dk * S + t_ * 128 + 128],
                        B.QF[:, dk * S + q0: dk * S + q0 + QB],
                        start=(dk == 0), stop=(dk == DC - 1))
                return last
            pl.step("t", pe, waits=[tok_qf] + bank_wait(B, bank), bump=B.c_pe)
            ptok = B.c_pe.tok()
            st_sl = slice(tt * 512, (tt + 1) * 512)
            pl.step("s", lambda s, bank=bank, st_sl=st_sl:
                    s.copy(B.STASH[:, st_sl], bank[:]),
                    waits=[ptok, slot_tok[tt]], bump=B.c_a)
            bank_set(B, bank, B.c_a.tok())
            tok_stash = B.c_a.tok()
            pl.step("g", lambda g, tt=tt, st_sl=st_sl: g.reduce_max(
                B.MXR[0:1, tt:tt + 1], B.STASH[:, st_sl], axis=AX.XYZWC),
                waits=[tok_stash])
            pl.step("g", lambda g, tt=tt: g.tensor_scalar(
                B.NCR[0:1, tt:tt + 1], B.MXR[0:1, tt:tt + 1], -1.0, None,
                ALU.mult), waits=[bc_tok[tt]], bump=B.c_g)
            slot_tok[tt] = B.c_g.tok()
            pl.step("t", lambda t, tt=tt: t.matmul(
                B.NRM[:, 0:1], B.ones_rf[:], B.NCR[0:1, tt:tt + 1],
                start=True, stop=True),
                waits=[slot_tok[tt]] + bank_wait(B, B.NRM), bump=B.c_pe)
            tok_cbmm = B.c_pe.tok()
            bc_tok[tt] = tok_cbmm
            pl.step("s", lambda s, tt=tt: s.copy(B.CBS[:, tt:tt + 1],
                                                 B.NRM[:, 0:1]),
                    waits=[tok_cbmm], bump=B.c_a)
            bank_set(B, B.NRM, B.c_a.tok())
            e_sl = slice(t_ * 512, (t_ + 1) * 512)
            pl.step("s", lambda s, tt=tt, e_sl=e_sl, st_sl=st_sl: s.activation(
                B.EXPB[:, e_sl], B.STASH[:, st_sl],
                AF.Exp, bias=B.CBS[:, tt:tt + 1], scale=1.0),
                waits=[expb_guard], bump=B.c_a)
            tok_exp = B.c_a.tok()
            pl.step("t", lambda t, e_sl=e_sl: t.matmul(
                B.AUX[:], B.ones128[:], B.EXPB[:, e_sl], start=True, stop=True),
                waits=[tok_exp] + bank_wait(B, B.AUX), bump=B.c_pe)
            pl.step("v", lambda v: v.reciprocal(B.RZ[:], B.AUX[:]),
                    waits=[B.c_pe.tok()], bump=B.c_v)
            bank_set(B, B.AUX, B.c_v.tok())
            pl.step("v", lambda v, e_sl=e_sl: v.tensor_tensor(
                B.EXPB[:, e_sl], B.EXPB[:, e_sl], B.RZ[:], ALU.mult),
                waits=[B.c_v.tok()], bump=B.c_v)
            whtok[t_] = B.c_v.tok()
        tok_rn = None
        for p in range(2):
            for t_ in range(NT):
                def pe(t, p=p, t_=t_):
                    last = None
                    for md in range(4):
                        dch = p * 4 + md
                        last = t.matmul(
                            B.AV[md][:],
                            B.VT[:, t_ * D + dch * 128: t_ * D + dch * 128 + 128],
                            B.EXPB[:, t_ * 512:(t_ + 1) * 512],
                            start=(t_ == 0), stop=(t_ == NT - 1))
                    if p == 0:
                        last = t.matmul(B.NRM[:], B.ones128[:],
                                        B.EXPB[:, t_ * 512:(t_ + 1) * 512],
                                        start=(t_ == 0), stop=(t_ == NT - 1))
                    return last
                w = [whtok[t_], tok_vt]
                if t_ == 0:
                    for md in range(4):
                        w += bank_wait(B, B.AV[md])
                    if p == 0:
                        w += bank_wait(B, B.NRM)
                pl.step("t", pe, waits=w, bump=B.c_pe)
            tok_av = B.c_pe.tok()
            if p == 0:
                pl.step("v", lambda v: v.tensor_scalar(B.RN[:], B.NRM[:],
                                                       1e-6, None, ALU.add),
                        waits=[tok_av], bump=B.c_v)
                pl.step("v", lambda v: v.reciprocal(B.RN[:], B.RN[:]),
                        waits=[B.c_v.tok()], bump=B.c_v)
                bank_set(B, B.NRM, B.c_v.tok())
                tok_rn = B.c_v.tok()
            for md in range(4):
                dch = p * 4 + md
                pl.step("v", lambda v, md=md, dch=dch: v.tensor_tensor(
                    B.ATT[:, dch * 512:(dch + 1) * 512], B.AV[md][:], B.RN[:],
                    ALU.mult),
                    waits=[tok_av, tok_rn, att_guard], bump=B.c_v)
                bank_set(B, B.AV[md], B.c_v.tok())
        tok_att = B.c_v.tok()
        expb_guard = B.c_pe.tok()
        for qh in range(4):
            wt = wtok[("o", qh)]
            wb = qh % 2
            for ic in range(2 * qh, 2 * qh + 2):
                bank = B.SC[ic % 2]
                icl = (ic % 2) * 128

                def pe(t, wb=wb, icl=icl, bank=bank):
                    last = None
                    for dk in range(DC):
                        last = t.matmul(
                            bank[:],
                            B.WARE[wb][:, dk * 256 + icl: dk * 256 + icl + 128],
                            B.ATT[:, dk * 512:(dk + 1) * 512],
                            start=(dk == 0), stop=(dk == DC - 1))
                    return last
                pl.step("t", pe, waits=[wt, tok_att] + bank_wait(B, bank),
                        bump=B.c_pe)
                ptok = B.c_pe.tok()
                stg = B.A0T[ic % 2]
                pl.step("s", lambda s, bank=bank, ic=ic, stg=stg: s.activation(
                    stg[:], bank[:], AF.Identity, bias=B.BO[:, ic:ic + 1], scale=1.0),
                    waits=[ptok] + bank_wait(B, stg), bump=B.c_a)
                bank_set(B, bank, B.c_a.tok())
                tok_stg = B.c_a.tok()
                pl.step("y", lambda y, ic=ic, q0=q0, stg=stg: y.dma_start(
                    out=a0_dram[ic * 128:(ic + 1) * 128, q0:q0 + QB], in_=stg[:]),
                    waits=[tok_stg], bump=B.c_st, n=16)
                bank_set(B, stg, B.c_st.tok())
            nq = (qh + 2) % 4
            if not (qb == NQB - 1 and qh >= 2):
                wtok[("o", nq)] = load_wq(wo_t, nq, B.WARE[wb],
                                          waits=[B.c_pe.tok()])
        att_guard = B.c_pe.tok()
    return B.c_st.all()



def alloc_ln(P, B):
    B.X2B = [P.sb(f"x2b{i}", [128, 512], BF16) for i in range(2)]
    B.ROWA = P.sb("rowa", [1, 512], F32)
    B.ROWB = P.sb("rowb", [1, 512], F32)
    B.RSTB = P.sb("rstb", [128, 512], F32)
    B.MRB = P.sb("mrb", [128, 512], F32)
    B.EPSB = P.sb("epsb", [1, 1], F32)
    B.eps_init = False


def emit_ln(pl, B, x_dram, wait=None):
    B.ln_xn_guard = ()
    if not B.eps_init:
        B.eps_init = True
        pl.step("v", lambda v: v.memset(B.EPSB[:], 1e-5), bump=B.c_v)
        B.tok_eps = B.c_v.tok()
    wl = wait if isinstance(wait, list) else [wait]
    pl.step("g", lambda g: g.dma_start(
        out=B.XN[:].rearrange("p (c t) -> p c t", c=DC),
        in_=x_dram.rearrange("(c p) t -> p c t", p=128)),
        waits=wl, bump=B.c_wg, n=16)
    tok_x = B.c_wg.tok()
    x2tok = [None, None]
    for qb in range(NQB):
        q0 = qb * QB
        for c in range(DC):
            xb_sl = slice(c * S + q0, c * S + q0 + QB)
            pl.step("s", lambda s, xb_sl=xb_sl, c=c: s.activation(
                B.X2B[c % 2][:], B.XN[:, xb_sl], AF.Square, scale=1.0),
                waits=[tok_x, x2tok[c % 2]], bump=B.c_a)
            tok_sq = B.c_a.tok()

            def pe(t, xb_sl=xb_sl, c=c):
                t.matmul(B.AUX[0:1, :], B.ones_c[:], B.XN[:, xb_sl],
                         start=(c == 0), stop=(c == DC - 1))
                return t.matmul(B.NRM[0:1, :], B.ones_c[:], B.X2B[c % 2][:],
                                start=(c == 0), stop=(c == DC - 1))
            w = [tok_sq, B.tok_setup_v]
            if c == 0:
                w += bank_wait(B, B.AUX) + bank_wait(B, B.NRM)
            pl.step("t", pe, waits=w, bump=B.c_pe)
            x2tok[c % 2] = B.c_pe.tok()
        tok_sums = B.c_pe.tok()
        pl.step("v", lambda v: v.tensor_scalar(B.ROWA[:], B.AUX[0:1, :],
                                               1.0 / D, None, ALU.mult),
                waits=[tok_sums] + bank_wait(B, B.ROWA), bump=B.c_v)
        bank_set(B, B.AUX, B.c_v.tok())
        pl.step("v", lambda v: v.tensor_scalar(B.ROWB[:], B.NRM[0:1, :],
                                               1.0 / D, None, ALU.mult),
                waits=[tok_sums] + bank_wait(B, B.ROWB), bump=B.c_v)
        bank_set(B, B.NRM, B.c_v.tok())
        pl.step("s", lambda s: s.activation(B.AUX[0:1, :], B.ROWA[:],
                                            AF.Square, scale=1.0),
                waits=[B.c_v.tok()] + bank_wait(B, B.AUX), bump=B.c_a)
        pl.step("v", lambda v: v.tensor_tensor(B.ROWB[:], B.ROWB[:],
                                               B.AUX[0:1, :], ALU.subtract),
                waits=[B.c_a.tok()], bump=B.c_v)
        bank_set(B, B.AUX, B.c_v.tok())
        tok_var = B.c_v.tok()
        pl.step("s", lambda s: s.activation(B.ROWB[:], B.ROWB[:], AF.Sqrt,
                                            bias=B.EPSB[0:1, 0:1], scale=1.0),
                waits=[tok_var, B.tok_eps], bump=B.c_a)
        pl.step("v", lambda v: v.reciprocal(B.ROWB[:], B.ROWB[:]),
                waits=[B.c_a.tok()], bump=B.c_v)
        pl.step("v", lambda v: v.tensor_tensor(B.ROWA[:], B.ROWA[:], B.ROWB[:],
                                               ALU.mult), bump=B.c_v)
        tok_rows = B.c_v.tok()
        for row, dstb in ((B.ROWB, B.RSTB), (B.ROWA, B.MRB)):
            pl.step("t", lambda t, row=row: t.matmul(
                B.AUX[:], B.ones_rf[:], row[0:1, :], start=True, stop=True),
                waits=[tok_rows] + bank_wait(B, B.AUX), bump=B.c_pe)
            pl.step("s", lambda s, dstb=dstb: s.copy(dstb[:], B.AUX[:]),
                    waits=[B.c_pe.tok()] + list(B.ln_xn_guard)
                    + bank_wait(B, dstb), bump=B.c_a)
            bank_set(B, B.AUX, B.c_a.tok())
            bank_set(B, dstb, B.c_a.tok())
        tok_b = B.c_a.tok()
        for c in range(DC):
            eng = "v" if c % 2 == 0 else "g"
            xb_sl = slice(c * S + q0, c * S + q0 + QB)
            pl.step(eng, lambda e, xb_sl=xb_sl: e.tensor_tensor(
                B.XN[:, xb_sl], B.XN[:, xb_sl], B.RSTB[:], ALU.mult),
                waits=[tok_b, tok_sums])
            pl.step(eng, lambda e, xb_sl=xb_sl: e.tensor_tensor(
                B.XN[:, xb_sl], B.XN[:, xb_sl], B.MRB[:], ALU.subtract),
                bump=B.c_v if eng == "v" else B.c_g)
        B.ln_xn_guard = (B.c_v.tok(), B.c_g.tok())
    return (B.c_v.tok(), B.c_g.tok())



def alloc_compose(P, B):
    B.LG = P.sb("LG", [1, 256], F32)
    B.LGW = P.sb("LGW", [1, 256], F32)
    B.MV = P.sb("MV", [1, 32], F32)
    B.IV = P.sb("IV", [1, 32], U32)
    B.IVF = P.sb("IVF", [1, 32], F32)
    B.EW = P.sb("EW", [1, 32], F32)
    B.SWr = P.sb("SWr", [1, 2], F32)
    B.W32 = P.sb("W32", [1, 32], F32)
    B.SELB = P.sb("SELB", [128, 32], F32)
    B.W32B = P.sb("W32B", [128, 32], F32)
    B.PEVB = P.sb("PEVB", [128, 32], F32)
    B.EOT = P.sb("EOT", [128, 32], F32)
    B.UB = P.sb("UB", [128, 32], F32)
    B.U2 = P.sb("U2", [128, 16], F32)
    B.MS = P.sb("MS", [128, 16], F32)
    B.IDXAf = P.sb("IDXAf", [128, 16], F32)
    B.IDXA = P.sb("IDXA", [128, 16], U32)
    B.IDX8 = P.sb("IDX8", [128, 16], U32)
    B.QM64 = P.sb("QM64", [128, 1], F32)
    B.QI = P.sb("QI", [128, 1], I32)
    B.COREB = P.sb("COREB", [128, 1], F32)
    B.GA = [P.sb(f"ga{i}", [128, 1024], BF16) for i in range(2)]
    B.GAS = [P.sb(f"gas{i}", [128, 1024], BF16) for i in range(2)]
    B.GB = [P.sb(f"gb{i}", [128, 1024], BF16) for i in range(2)]
    B.STG = [P.sb(f"stg{i}", [128, 512], BF16) for i in range(2)]


def emit_compose_setup(pl, B, coreid):
    pl.step("g", lambda g: (g.iota(B.QI[:], [[0, 1]], base=0,
                            channel_multiplier=1), g.engine_nop())[1])
    pl.step("g", lambda g: (g.iota(
        B.iotat[:, 0:32].rearrange("p (a b) -> p a b", b=2),
        [[0, 16], [1, 2]], base=0, channel_multiplier=0), g.engine_nop())[1])
    tok_io = B.c_g.tok()
    pl.step("v", lambda v: v.tensor_scalar(B.QM64[:], B.QI[:], 64, None,
                                           ALU.is_ge), waits=[tok_io])
    pl.step("v", lambda v: v.scalar_tensor_tensor(B.QM64[:], B.QM64[:], -64.0,
                                                  B.QI[:], ALU.mult, ALU.add))
    pl.step("v", lambda v: v.tensor_scalar(B.EOT[:], B.iotat[:, 0:32], 0, None,
                                           ALU.is_equal), bump=B.c_v)
    pl.step("v", lambda v: v.tensor_copy(B.PEVB[0:64, :], B.EOT[0:64, :]))
    pl.step("v", lambda v: v.tensor_scalar(B.PEVB[64:128, :], B.EOT[64:128, :],
                                           1.0, None, ALU.subtract))
    pl.step("v", lambda v: v.tensor_scalar(B.PEVB[64:128, :], B.PEVB[64:128, :],
                                           -1.0, None, ALU.mult), bump=B.c_v)
    pl.step("g", lambda g: g.dma_start(out=B.SWr[0:1, 0:1], in_=coreid),
            bump=B.c_wg, n=16)
    pl.step("t", lambda t: t.matmul(B.AUX[:, 0:1], B.ones_rf[:], B.SWr[0:1, 0:1],
                                    start=True, stop=True),
            waits=[B.c_wg.tok(), B.tok_setup_v] + bank_wait(B, B.AUX),
            bump=B.c_pe)
    pl.step("s", lambda s: s.copy(B.COREB[:], B.AUX[:, 0:1]),
            waits=[B.c_pe.tok()], bump=B.c_a)
    bank_set(B, B.AUX, B.c_a.tok())
    B.tok_cmp_setup = (B.c_v.tok(), B.c_g.tok(), B.c_a.tok(), B.c_pe.tok())


def emit_topk_idx(pl, B, lg_row, guard=None):
    pl.step("y", lambda y: y.dma_start(out=B.LG[:], in_=lg_row), bump=B.c_ld, n=16)
    tok_lg = B.c_ld.tok()
    pl.step("v", lambda v: v.tensor_copy(B.LGW[:], B.LG[:]),
            waits=[tok_lg, B.tok_cmp_setup[0]] + (guard or []), bump=B.c_v)
    for r in range(4):
        sl = slice(r * 8, (r + 1) * 8)
        pl.step("v", lambda v, sl=sl: v.max(out=B.MV[:, sl], in_=B.LGW[:]),
                waits=[B.c_v.tok()], bump=B.c_v)
        pl.step("v", lambda v, sl=sl: v.max_index(out=B.IV[:, sl],
                                                  in_max=B.MV[:, sl],
                                                  in_values=B.LGW[:]),
                waits=[B.c_v.tok()], bump=B.c_v)
        pl.step("v", lambda v, sl=sl: v.match_replace(
            out=B.LGW[:], in_to_replace=B.MV[:, sl], in_values=B.LGW[:],
            imm_value=-1e30), waits=[B.c_v.tok()], bump=B.c_v)
    pl.step("v", lambda v: v.tensor_scalar(B.SWr[0:1, 0:1], B.MV[0:1, 0:1],
                                           -1.0, None, ALU.mult),
            waits=[B.c_v.tok(), B.tok_cmp_setup[3]], bump=B.c_v)
    pl.step("s", lambda s: s.activation(B.EW[:], B.MV[:], AF.Exp,
                                        bias=B.SWr[0:1, 0:1], scale=1.0),
            waits=[B.c_v.tok()], bump=B.c_a)
    pl.step("v", lambda v: v.reduce_sum(B.SWr[0:1, 1:2], B.EW[:], axis=AX.X),
            waits=[B.c_a.tok()], bump=B.c_v)
    pl.step("v", lambda v: v.reciprocal(B.SWr[0:1, 1:2], B.SWr[0:1, 1:2]),
            waits=[B.c_v.tok()], bump=B.c_v)
    pl.step("v", lambda v: v.tensor_scalar(B.W32[:], B.EW[:], B.SWr[0:1, 1:2],
                                           None, ALU.mult),
            waits=[B.c_v.tok()], bump=B.c_v)
    pl.step("v", lambda v: v.tensor_copy(B.IVF[:], B.IV[:]), bump=B.c_v)
    tok_rows = B.c_v.tok()
    for row, dstb in ((B.IVF, B.SELB), (B.W32, B.W32B)):
        pl.step("t", lambda t, row=row: t.matmul(
            B.AUX[:, 0:32], B.ones_rf[:], row[0:1, :], start=True, stop=True),
            waits=[tok_rows] + bank_wait(B, B.AUX), bump=B.c_pe)
        pl.step("s", lambda s, dstb=dstb: s.copy(dstb[:], B.AUX[:, 0:32]),
                waits=[B.c_pe.tok()], bump=B.c_a)
        bank_set(B, B.AUX, B.c_a.tok())
    tok_bc = B.c_a.tok()
    pl.step("v", lambda v: v.tensor_tensor(B.UB[:], B.SELB[:], B.PEVB[:],
                                           ALU.mult), waits=[tok_bc], bump=B.c_v)
    pl.step("v", lambda v: v.tensor_tensor(B.U2[:], B.UB[:, 0::2], B.UB[:, 1::2],
                                           ALU.add), bump=B.c_v)
    pl.step("v", lambda v: v.scalar_tensor_tensor(
        B.IDXAf[:], B.U2[:], 64.0, B.QM64[:].to_broadcast([128, 16]),
        ALU.mult, ALU.add), waits=[B.tok_cmp_setup[1]], bump=B.c_v)
    pl.step("v", lambda v: v.tensor_copy(B.IDXA[:], B.IDXAf[:]), bump=B.c_v)
    pl.step("v", lambda v: v.scalar_tensor_tensor(
        B.IDXAf[:], B.IDXAf[:], 8.0, B.COREB[:].to_broadcast([128, 16]),
        ALU.mult, ALU.add), waits=[B.tok_cmp_setup[2]], bump=B.c_v)
    pl.step("v", lambda v: v.tensor_copy(B.IDX8[:], B.IDXAf[:]), bump=B.c_v)
    pl.step("v", lambda v: v.tensor_tensor(B.UB[:], B.W32B[:], B.PEVB[:],
                                           ALU.mult), bump=B.c_v)
    pl.step("v", lambda v: v.tensor_tensor(B.MS[:], B.UB[:, 0::2], B.UB[:, 1::2],
                                           ALU.add), bump=B.c_v)
    return B.c_v.tok()


def emit_compose_one(pl, B, tok_idx, a_rows, a_width, b_rows, b_width,
                     a_idx, b_idx, n_m, n_n, out_emit):
    banks = B.AV + B.SC + [B.AUX, B.NRM]
    ga_tok = [None, None]
    gas_used = [None, None]
    for j in range(16):
        bi = j % 2

        def ld_a(g, j=j, bi=bi, a_rows=a_rows, a_width=a_width):
            return g.indirect_dma_start(
                out=B.GA[bi][:, 0:a_width], out_offset=None,
                in_=a_rows, in_offset=IndirectOffsetOnAxis(ap=a_idx(j), axis=0))
        pl.step("g", ld_a, waits=[tok_idx, ga_tok[bi]], bump=B.c_wg, n=16)
        tok_ga = B.c_wg.tok()

        def ld_b(g, j=j, bi=bi, b_rows=b_rows, b_width=b_width):
            return g.indirect_dma_start(
                out=B.GB[bi][:, 0:b_width], out_offset=None,
                in_=b_rows, in_offset=IndirectOffsetOnAxis(ap=b_idx(j), axis=0))
        pl.step("g", ld_b, waits=[], bump=B.c_wg, n=16)
        tok_gb = B.c_wg.tok()
        pl.step("v", lambda v, bi=bi, j=j, a_width=a_width: v.tensor_scalar(
            B.GAS[bi][:, 0:a_width], B.GA[bi][:, 0:a_width],
            B.MS[:, j:j + 1], None, ALU.mult),
            waits=[tok_ga, gas_used[bi]], bump=B.c_v)
        tok_gas = B.c_v.tok()

        def pe(t, j=j, bi=bi, n_m=n_m, n_n=n_n, b_width=b_width):
            last = None
            for m in range(n_m):
                for n in range(n_n):
                    last = t.matmul(
                        banks[m * n_n + n][:],
                        B.GAS[bi][:, m * 128:(m + 1) * 128],
                        B.GB[bi][:, n * 512:(n + 1) * 512],
                        start=(j == 0), stop=(j == 15))
            return last
        w = [tok_gas, tok_gb]
        if j == 0:
            for k in range(n_m * n_n):
                w += bank_wait(B, banks[k])
        pl.step("t", pe, waits=w, bump=B.c_pe)
        ga_tok[bi] = B.c_pe.tok()
        gas_used[bi] = B.c_pe.tok()
    tok_mm = B.c_pe.tok()
    for m in range(n_m):
        for n in range(n_n):
            bank = banks[m * n_n + n]
            out_emit(m, n, bank, tok_mm)


def emit_compose(pl, B, lg1_row, lg2_row, a1t, b1, a2t, b2,
                 w1sl_dram, w2sl_dram):
    a1_rows = a1t.rearrange("p r d -> (p r) d")
    b1_rows = b1.rearrange("p r (a b) -> (p r a) b", b=512)
    a2_rows = a2t.rearrange("p r (a b) -> (p r a) b", b=512)
    b2_rows = b2.rearrange("p r d -> (p r) d")

    def store(dram, m, n, bank, ptok):
        stg = B.STG[(m * 4 + n) % 2]
        pl.step("s", lambda s, stg=stg, bank=bank: s.copy(stg[:], bank[:]),
                waits=[ptok] + bank_wait(B, stg), bump=B.c_a)
        bank_set(B, bank, B.c_a.tok())
        tok_stg = B.c_a.tok()
        pl.step("y", lambda y, dram=dram, m=m, n=n, stg=stg: y.dma_start(
            out=dram[m * 128:(m + 1) * 128, n * 512:(n + 1) * 512], in_=stg[:]),
            waits=[tok_stg], bump=B.c_st, n=16)
        bank_set(B, stg, B.c_st.tok())

    tok_idx = emit_topk_idx(pl, B, lg1_row)
    emit_compose_one(pl, B, tok_idx, a1_rows, 1024, b1_rows, 512,
                     lambda j: B.IDXA[:, j:j + 1], lambda j: B.IDX8[:, j:j + 1],
                     8, 1, lambda m, n, bank, ptok: store(w1sl_dram, m, n, bank, ptok))
    tok_idx = emit_topk_idx(pl, B, lg2_row, guard=B.c_wg.all())
    emit_compose_one(pl, B, tok_idx, a2_rows, 512, b2_rows, 1024,
                     lambda j: B.IDX8[:, j:j + 1], lambda j: B.IDXA[:, j:j + 1],
                     4, 2, lambda m, n, bank, ptok: store(w2sl_dram, m, n, bank, ptok))
    return B.c_st.all()



def alloc_ffn0(P, B):
    B.H = P.sb("H", [128, 32 * 512], BF16)
    B.WST = [P.sb(f"wst{i}", [128, 8 * 1024], BF16) for i in range(2)]
    B.GU = P.sb("GU", [128, 512], F32)
    B.GZ = P.sb("GZ", [128, 512], F32)
    B.GS = P.sb("GS", [128, 512], F32)
    B.XRT = [P.sb(f"xrt{i}", [128, 512], F32) for i in range(2)]
    B.ART = [P.sb(f"art{i}", [128, 512], F32) for i in range(2)]
    B.XPT = [P.sb(f"xpt{i}", [128, 512], F32) for i in range(2)]


GELU_A = float(np.sqrt(0.044715))
GELU_K1 = 2.0 * 0.7978845608028654


def emit_ffn0_apply(pl, B, w1g, w2g, x_dram, a0_dram, x1_dram, tok_xn,
                    no_gelu=False, guard=None, wsrc_tok=None):
    h_guard = guard
    B.gu_guard = None
    B.gs_guard = None
    wst_guard = guard
    for tb in range(NQB):
        q0 = tb * QB
        wtok = [None, None]
        wused = [wst_guard, wst_guard]
        for fs in range(8):
            bi = fs % 2

            def ldw(y, fs=fs, bi=bi):
                return y.dma_start(
                    out=B.WST[bi][:, 0:8 * 512].rearrange("p (c i) -> p c i", c=DC),
                    in_=w1g[fs].rearrange("(c p) i -> p c i", p=128))
            pl.step("y", ldw, waits=[wused[bi], wsrc_tok], bump=B.c_ld, n=16)
            wtok[bi] = B.c_ld.tok()
            for fm in range(4):
                bank = B.AV[(fs * 4 + fm) % 4]

                def pe(t, bi=bi, fm=fm, q0=q0, bank=bank):
                    last = None
                    for dk in range(DC):
                        last = t.matmul(
                            bank[:],
                            B.WST[bi][:, dk * 512 + fm * 128: dk * 512 + fm * 128 + 128],
                            B.XN[:, dk * S + q0: dk * S + q0 + QB],
                            start=(dk == 0), stop=(dk == DC - 1))
                    return last
                pl.step("t", pe, waits=[wtok[bi], tok_xn[0], tok_xn[1]]
                        + bank_wait(B, bank), bump=B.c_pe)
                ptok = B.c_pe.tok()
                f = fs * 4 + fm
                h_sl = slice(f * 512, (f + 1) * 512)
                if no_gelu:
                    pl.step("s", lambda s, bank=bank, h_sl=h_sl: s.copy(
                        B.H[:, h_sl], bank[:]), waits=[ptok, h_guard])
                    bank_set(B, bank, B.c_a.tok())
                    continue
                pl.step("s", lambda s, bank=bank: s.activation(
                    B.GU[:], bank[:], AF.Square, scale=GELU_A),
                    waits=[ptok, B.gu_guard], bump=B.c_a)
                pl.step("v", lambda v, bank=bank: v.scalar_tensor_tensor(
                    B.GU[:], B.GU[:], 1.0, bank[:], ALU.add, ALU.mult),
                    waits=[B.c_a.tok()], bump=B.c_v)
                pl.step("s", lambda s: s.activation(
                    B.GS[:], B.GU[:], AF.Sigmoid, scale=GELU_K1),
                    waits=[B.c_v.tok(), B.gs_guard], bump=B.c_a)
                B.gu_guard = B.c_a.tok()
                pl.step("v", lambda v, bank=bank, h_sl=h_sl: v.tensor_tensor(
                    B.H[:, h_sl], B.GS[:], bank[:], ALU.mult),
                    waits=[B.c_a.tok(), h_guard], bump=B.c_v)
                B.gs_guard = B.c_v.tok()
                bank_set(B, bank, B.c_v.tok())
            wused[bi] = B.c_pe.tok()
        tok_h = B.c_v.tok()
        tok_h2 = B.c_a.tok()
        wst_guard = B.c_pe.tok()
        banks = B.AV + B.SC + [B.AUX, B.NRM]
        wtok = [None, None]
        wused = [wst_guard, wst_guard]
        for fs in range(8):
            bi = fs % 2

            def ldw2(y, fs=fs, bi=bi):
                return y.dma_start(
                    out=B.WST[bi][:, 0:4 * 1024].rearrange("p (c d) -> p c d", c=4),
                    in_=w2g[fs * 512:(fs + 1) * 512, :].rearrange(
                        "(c p) d -> p c d", p=128))
            pl.step("y", ldw2, waits=[wused[bi], wsrc_tok], bump=B.c_ld, n=16)
            wtok[bi] = B.c_ld.tok()

            def pe2(t, fs=fs, bi=bi):
                last = None
                for fl in range(4):
                    f = fs * 4 + fl
                    for m in range(8):
                        last = t.matmul(
                            banks[m][:],
                            B.WST[bi][:, fl * 1024 + m * 128: fl * 1024 + m * 128 + 128],
                            B.H[:, f * 512:(f + 1) * 512],
                            start=(f == 0), stop=(f == 31))
                return last
            w = [wtok[bi], tok_h, tok_h2]
            if fs == 0:
                for k in range(8):
                    w += bank_wait(B, banks[k])
            pl.step("t", pe2, waits=w, bump=B.c_pe)
            wused[bi] = B.c_pe.tok()
        tok_out = B.c_pe.tok()
        h_guard = tok_out
        wst_guard = tok_out
        ld_tok = [None, None]
        for m in range(8):
            bi = m % 2

            pl.step("y", lambda y, m=m, q0=q0, bi=bi: y.dma_start(
                out=B.XRT[bi][:], in_=x_dram[m * 128:(m + 1) * 128, q0:q0 + QB]),
                waits=[ld_tok[bi]] + bank_wait(B, B.XRT[bi]), bump=B.c_ld, n=16)
            tok_xr = B.c_ld.tok()
            pl.step("y", lambda y, m=m, q0=q0, bi=bi: y.dma_start(
                out=B.ART[bi][:], in_=a0_dram[m * 128:(m + 1) * 128, q0:q0 + QB]),
                waits=[ld_tok[bi]] + bank_wait(B, B.ART[bi]), bump=B.c_ld, n=16)
            tok_res = B.c_ld.tok()
            pl.step("v", lambda v, m=m, bi=bi: v.scalar_tensor_tensor(
                B.XRT[bi][:], banks[m][:], 1.0, B.XRT[bi][:], ALU.mult, ALU.add),
                waits=[tok_out, tok_xr], bump=B.c_v)
            bank_set(B, banks[m], B.c_v.tok())
            pl.step("v", lambda v, m=m, bi=bi: v.tensor_tensor(
                B.XRT[bi][:], B.XRT[bi][:], B.ART[bi][:], ALU.add),
                waits=[tok_res], bump=B.c_v)
            tok_xp = B.c_v.tok()
            pl.step("y", lambda y, m=m, q0=q0, bi=bi: y.dma_start(
                out=x1_dram[m * 128:(m + 1) * 128, q0:q0 + QB], in_=B.XRT[bi][:]),
                waits=[tok_xp], bump=B.c_st, n=16)
            ld_tok[bi] = B.c_st.tok()
            bank_set(B, B.XRT[bi], B.c_st.tok())
            bank_set(B, B.ART[bi], B.c_st.tok())
    return B.c_st.all()



def alloc_l1_attn(P, B):
    if not hasattr(B, "QF"):
        B.QF = P.sb("QF", [128, DC * S], BF16)
        B.KF = P.sb("KF", [128, DC * S], BF16)
        B.VT = P.sb("VT", [128, NT * D], BF16)
        B.WARE = [P.sb(f"ware{i}", [128, DC * 256], BF16) for i in range(2)]
        B.A0T = [P.sb(f"a0t{i}", [128, 512], F32) for i in range(2)]
    B.IP = B.QF
    B.RG = B.KF
    B.IG = B.VT
    B.BIN = P.sb("BIN", [128, DC], F32)
    B.BR = P.sb("BR", [128, DC], F32)
    B.BI = P.sb("BI", [128, DC], F32)
    B.BOUT = P.sb("BOUT", [128, DC], F32)
    B.ASG = P.sb("ASG", [128, DC], F32)


def emit_l1_attn(pl, B, w_in, w_r, w_i, w_out, b_in2, b_r2, b_i2, b_out2,
                 log_a2, a1_dram, tok_xn, guard=None, evac_guard=None):
    for buf, src in ((B.BIN, b_in2), (B.BR, b_r2), (B.BI, b_i2),
                     (B.BOUT, b_out2), (B.ASG, log_a2)):
        pl.step("y", lambda y, buf=buf, src=src: y.dma_start(out=buf[:], in_=src),
                bump=B.c_ld, n=16)
    tok_bias_l = B.c_ld.all()
    pl.step("s", lambda s: s.activation(B.ASG[:], B.ASG[:], AF.Sigmoid),
            waits=tok_bias_l)
    tok_asg = B.c_a.tok()

    def load_wq(wdram, h, buf, waits=()):
        def go(g):
            return g.dma_start(
                out=buf[:].rearrange("p (c i) -> p c i", c=DC),
                in_=wdram[:, h * 256:(h + 1) * 256].rearrange("(c p) i -> p c i", p=128))
        pl.step("g", go, waits=list(waits), bump=B.c_wg, n=16)
        return B.c_wg.tok()

    rr = [0]
    seqw = [("in", w_in), ("r", w_r), ("i", w_i), ("o", w_out)]
    order = [(nm, h) for nm, _ in seqw for h in range(4)]
    dram_of = dict(seqw)
    wtok = {order[0]: load_wq(w_in, 0, B.WARE[0], waits=[guard]),
            order[1]: load_wq(w_in, 1, B.WARE[1], waits=[guard])}

    def prefetch_after(idx):
        if idx + 2 < len(order):
            nm, nh = order[idx + 2]
            wtok[(nm, nh)] = load_wq(dram_of[nm], nh, B.WARE[idx % 2],
                                     waits=[B.c_pe.tok()])

    projs = (("in", B.BIN, B.IP, AF.Identity), ("r", B.BR, B.RG, AF.Sigmoid),
             ("i", B.BI, B.IG, AF.Sigmoid))
    oi = 0
    for name, bias, dstb, fn in projs:
        for h in range(4):
            wt = wtok[(name, h)]
            for ic in range(2 * h, 2 * h + 2):
                icl = (ic % 2) * 128
                for qb in range(NQB):
                    bank = B.AV[rr[0]]
                    rr[0] = (rr[0] + 1) % 4

                    def pe(t, oi=oi, icl=icl, qb=qb, bank=bank):
                        last = None
                        for dk in range(DC):
                            last = t.matmul(
                                bank[:],
                                B.WARE[oi % 2][:, dk * 256 + icl: dk * 256 + icl + 128],
                                B.XN[:, dk * S + qb * QB: dk * S + (qb + 1) * QB],
                                start=(dk == 0), stop=(dk == DC - 1))
                        return last
                    pl.step("t", pe, waits=[wt, tok_xn[0], tok_xn[1]]
                            + bank_wait(B, bank), bump=B.c_pe)
                    ptok = B.c_pe.tok()

                    def ev(s, bank=bank, ic=ic, qb=qb, bias=bias, dstb=dstb, fn=fn):
                        return s.activation(
                            dstb[:, ic * S + qb * QB: ic * S + (qb + 1) * QB],
                            bank[:], fn, bias=bias[:, ic:ic + 1], scale=1.0)
                    pl.step("s", ev, waits=[ptok, evac_guard] + tok_bias_l)
                    bank_set(B, bank, B.c_a.tok())
            prefetch_after(oi)
            oi += 1
    tok_proj = B.c_a.tok()
    for c in range(DC):
        eng = "v" if c % 2 == 0 else "g"
        for qb in range(NQB):
            sl = slice(c * S + qb * QB, c * S + (qb + 1) * QB)
            pl.step(eng, lambda e, sl=sl, c=c: e.tensor_scalar(
                B.RG[:, sl], B.RG[:, sl], B.ASG[:, c:c + 1], None, ALU.mult),
                waits=[tok_proj, tok_asg])
            pl.step(eng, lambda e, sl=sl: e.tensor_tensor(
                B.IP[:, sl], B.IG[:, sl], B.IP[:, sl], ALU.mult))
            pl.step(eng, lambda e, sl=sl: e.tensor_scalar(
                B.IG[:, sl], B.RG[:, sl], -1.0, 1.0, ALU.mult, ALU.add))
            pl.step(eng, lambda e, sl=sl: e.tensor_tensor(
                B.IP[:, sl], B.IP[:, sl], B.IG[:, sl], ALU.mult))
    tok_bt = (B.c_v.tok(), B.c_g.tok())
    for c in range(DC):
        sl = slice(c * S, (c + 1) * S)
        pl.step("v", lambda v, sl=sl: v.tensor_tensor_scan(
            B.IG[:, sl], B.RG[:, sl], B.IP[:, sl], 0.0, ALU.mult, ALU.add),
            waits=[tok_bt[0], tok_bt[1]])
    tok_h = B.c_v.tok()
    for h in range(4):
        wt = wtok[("o", h)]
        for ic in range(2 * h, 2 * h + 2):
            icl = (ic % 2) * 128
            for qb in range(NQB):
                bank = B.SC[(ic * NQB + qb) % 2]

                def pe(t, oi=oi, icl=icl, qb=qb, bank=bank):
                    last = None
                    for dk in range(DC):
                        last = t.matmul(
                            bank[:],
                            B.WARE[oi % 2][:, dk * 256 + icl: dk * 256 + icl + 128],
                            B.IG[:, dk * S + qb * QB: dk * S + (qb + 1) * QB],
                            start=(dk == 0), stop=(dk == DC - 1))
                    return last
                pl.step("t", pe, waits=[wt, tok_h] + bank_wait(B, bank),
                        bump=B.c_pe)
                ptok = B.c_pe.tok()
                stg = B.A0T[(ic * NQB + qb) % 2]
                pl.step("s", lambda s, bank=bank, ic=ic, stg=stg: s.activation(
                    stg[:], bank[:], AF.Identity, bias=B.BOUT[:, ic:ic + 1],
                    scale=1.0), waits=[ptok] + bank_wait(B, stg))
                bank_set(B, bank, B.c_a.tok())
                tok_stg = B.c_a.tok()
                pl.step("y", lambda y, ic=ic, qb=qb, stg=stg: y.dma_start(
                    out=a1_dram[ic * 128:(ic + 1) * 128,
                                qb * QB:(qb + 1) * QB], in_=stg[:]),
                    waits=[tok_stg], bump=B.c_st, n=16)
                bank_set(B, stg, B.c_st.tok())
        prefetch_after(oi)
        oi += 1
    return B.c_st.all()



def alloc_l1_quant(P, B):
    B.QWB = P.sb("QWB", [128, 3 * 1024], F32)
    B.QDB = P.sb("QDB", [128, 2816], F32)
    B.SCL = P.sb("SCL", [128, 4], F32)
    B.RCL = P.sb("RCL", [128, 4], F32)
    B.QI8 = P.sb("QI8", [128, 2816], I8)
    B.QBF = P.sb("QBF", [128, 3 * 1024], BF16)
    B.TQB = P.sb("TQB", [128, DC * 352], BF16)
    B.TQD = P.sb("TQD", [128, 22 * 128], BF16)


def emit_l1_quant(pl, B, wg_sl, wu_sl, wd_sl, wgt_out, wut_out, wdt_out,
                  sg_out, su_out, sd_out, guard=None):
    rows = [128, 128, 96]
    tr_guard = None
    scl_guard = []
    qwb_guard = guard
    qbf_guard = []
    for w_sl, wt_out, s_out in ((wg_sl, wgt_out, sg_out),
                                (wu_sl, wut_out, su_out)):
        ldt = []
        for c in range(3):
            r0 = c * 128
            nr = rows[c]
            pl.step("g", lambda g, c=c, r0=r0, nr=nr, w_sl=w_sl: g.dma_start(
                out=B.QWB[0:nr, c * 1024:(c + 1) * 1024],
                in_=w_sl[r0:r0 + nr, :]), waits=[qwb_guard], bump=B.c_wg, n=16)
            ldt.append(B.c_wg.tok())
        for c in range(3):
            nr = rows[c]
            csl = slice(c * 1024, (c + 1) * 1024)
            pl.step("v", lambda v, c=c, nr=nr, csl=csl: v.reduce_max(
                B.SCL[0:nr, c:c + 1], B.QWB[0:nr, csl], axis=AX.X,
                apply_absolute_value=True), waits=[ldt[c]] + scl_guard + qbf_guard)
            pl.step("v", lambda v, c=c, nr=nr: v.tensor_scalar(
                B.SCL[0:nr, c:c + 1], B.SCL[0:nr, c:c + 1], 1.0 / 127.0,
                None, ALU.mult))
            pl.step("v", lambda v, c=c, nr=nr: v.reciprocal(
                B.RCL[0:nr, c:c + 1], B.SCL[0:nr, c:c + 1]))
            pl.step("v", lambda v, c=c, nr=nr, csl=csl: v.tensor_scalar(
                B.QI8[0:nr, 0:1024], B.QWB[0:nr, csl], B.RCL[0:nr, c:c + 1],
                None, ALU.mult))
            pl.step("v", lambda v, c=c, nr=nr, csl=csl: v.tensor_copy(
                B.QBF[0:nr, csl], B.QI8[0:nr, 0:1024]), bump=B.c_v)
            tok_q = B.c_v.tok()
            for dc in range(DC):
                pl.step("s", lambda y, c=c, nr=nr, dc=dc: y.dma_start_transpose(
                    B.TQB[:, dc * 352 + c * 128: dc * 352 + c * 128 + nr],
                    B.QBF[0:nr, c * 1024 + dc * 128: c * 1024 + (dc + 1) * 128]),
                    waits=[tok_q, tr_guard], bump=B.c_sd, n=16)
        tok_tr_l = B.c_sd.all()
        pl.step("s", lambda y, wt_out=wt_out: y.dma_start(
            out=wt_out.rearrange("(c p) r -> p c r", p=128),
            in_=B.TQB[:].rearrange("p (c r) -> p c r", c=DC)),
            waits=tok_tr_l, bump=B.c_sd, n=16)
        tr_guard = B.c_sd.tok()
        scl_guard = []
        for c in range(3):
            nr = rows[c]
            pl.step("s", lambda y, c=c, nr=nr, s_out=s_out: y.dma_start(
                out=s_out[0:1, c * 128:c * 128 + nr].rearrange("a b -> b a"),
                in_=B.SCL[0:nr, c:c + 1]), bump=B.c_sd, n=16)
            scl_guard.append(B.c_sd.tok())
        qwb_guard = B.c_v.tok()
        qbf_guard = B.c_sd.all()
    pl.step("g", lambda g: g.dma_start(out=B.QDB[:], in_=wd_sl),
            waits=[qwb_guard], bump=B.c_wg, n=16)
    tok_ld = B.c_wg.tok()
    pl.step("v", lambda v: v.reduce_max(B.SCL[:, 3:4], B.QDB[:], axis=AX.X,
                                        apply_absolute_value=True),
            waits=[tok_ld] + scl_guard)
    pl.step("v", lambda v: v.tensor_scalar(B.SCL[:, 3:4], B.SCL[:, 3:4],
                                           1.0 / 127.0, None, ALU.mult))
    pl.step("v", lambda v: v.reciprocal(B.RCL[:, 3:4], B.SCL[:, 3:4]))
    pl.step("v", lambda v: v.tensor_scalar(B.QI8[:], B.QDB[:], B.RCL[:, 3:4],
                                           None, ALU.mult))
    pl.step("v", lambda v: v.tensor_copy(B.QBF[:, 0:2816], B.QI8[:]),
            waits=qbf_guard, bump=B.c_v)
    tok_q = B.c_v.tok()
    for ic in range(22):
        pl.step("s", lambda y, ic=ic: y.dma_start_transpose(
            B.TQD[:, ic * 128:(ic + 1) * 128],
            B.QBF[:, ic * 128:(ic + 1) * 128]),
            waits=[tok_q, tr_guard], bump=B.c_sd, n=16)
    tok_tr_l = B.c_sd.all()
    pl.step("s", lambda y: y.dma_start(
        out=wdt_out.rearrange("(c p) r -> p c r", p=128),
        in_=B.TQD[:].rearrange("p (c r) -> p c r", c=22)),
        waits=tok_tr_l, bump=B.c_sd, n=16)
    pl.step("s", lambda y: y.dma_start(
        out=sd_out.rearrange("a b -> b a"), in_=B.SCL[:, 3:4]),
        bump=B.c_sd, n=16)
    return B.c_sd.all()



def bcast_scalar(pl, B, src11, dst_col, tok):
    pl.step("t", lambda t: t.matmul(B.AUX[:, 0:1], B.ones_rf[:], src11,
                                    start=True, stop=True),
            waits=[tok, B.tok_setup_v] + bank_wait(B, B.AUX), bump=B.c_pe)
    pl.step("s", lambda s: s.copy(dst_col, B.AUX[:, 0:1]),
            waits=[B.c_pe.tok()])
    bank_set(B, B.AUX, B.c_a.tok())
    return B.c_a.tok()


def emit_absmax_xn(pl, B, tok_xn):
    for c in range(DC):
        pl.step("v", lambda v, c=c: v.reduce_max(
            B.AMC[:, c:c + 1], B.XN[:, c * S:(c + 1) * S], axis=AX.X,
            apply_absolute_value=True), waits=[tok_xn[0], tok_xn[1]])
    pl.step("v", lambda v: v.reduce_max(B.AMC[:, DC:DC + 1], B.AMC[:, 0:DC],
                                        axis=AX.X), bump=B.c_v)
    tok = B.c_v.tok()
    pl.step("g", lambda g: g.tensor_reduce(
        B.AMX[0:1, 0:1], B.AMC[:, DC:DC + 1], axis=AX.XYZWC, op=ALU.max),
        waits=[tok], bump=B.c_g)
    return B.c_g.tok()



def alloc_l1_ffn(P, B, phase):
    if phase == "a":
        B.XQB = B.KF
        B.XQI = P.sb("XQI", [128, S], I8)
        B.H1 = P.sb("H1", [128, 22 * 512], BF16)
        B.SIL = P.sb("SIL", [128, 512], F32)
        B.SGB = P.sb("SGB", [128, 22], F32)
        B.SUB = P.sb("SUB", [128, 22], F32)
        B.WQS = [P.sb(f"wqs{i}", [128, DC * 128], BF16) for i in range(4)]
        B.MXH = P.sb("MXH", [1, 92], F32)
        B.AMC = P.sb("AMC", [128, DC + 1], F32)
        B.AMX = P.sb("AMX", [1, 16], F32)
        B.CLP = P.sb("CLP", [1, 4], F32)
        B.RSXB = P.sb("RSXB", [128, 1], F32)
        B.SXB = P.sb("SXB", [128, 1], F32)
    else:
        B.H1 = P.sb("H1", [128, 22 * 512], BF16)
        B.H1I = P.sb("H1I", [128, 512], I8)
        B.WQS = [P.sb(f"wqs{i}", [128, DC * 128], BF16) for i in range(2)]
        B.SDB = P.sb("SDB", [128, DC], F32)
        B.AMX = P.sb("AMX", [1, 16], F32)
        B.CLP = P.sb("CLP", [1, 4], F32)
        B.RSHB = P.sb("RSHB", [128, 1], F32)
        B.SHB = P.sb("SHB", [128, 1], F32)
        B.XRT = [P.sb(f"xrt{i}", [128, 512], F32) for i in range(2)]
        B.ART = [P.sb(f"art{i}", [128, 512], F32) for i in range(2)]
        B.XPT = [P.sb(f"xpt{i}", [128, 512], F32) for i in range(2)]


def emit_sx(pl, B, mx8_dram, clip_dram, rsxb_col, shb_col=None, gmax11=False,
            ld_guard=None):
    pl.step("y", lambda y: y.dma_start(out=B.CLP[0:1, 0:1], in_=clip_dram),
            bump=B.c_ld, n=16)
    if gmax11:
        pl.step("y", lambda y: y.dma_start(out=B.AMX[0:1, 8:9], in_=mx8_dram),
                waits=[ld_guard], bump=B.c_ld, n=16)
    else:
        pl.step("y", lambda y: y.dma_start(out=B.AMX[0:1, 0:8], in_=mx8_dram),
                bump=B.c_ld, n=16)
    tok_ld_l = B.c_ld.all()
    pl.step("s", lambda s: s.activation(B.CLP[0:1, 1:2], B.CLP[0:1, 0:1],
                                        AF.Sigmoid), waits=tok_ld_l)
    tok_sig = B.c_a.tok()
    if not gmax11:
        pl.step("v", lambda v: v.reduce_max(B.AMX[0:1, 8:9], B.AMX[0:1, 0:8],
                                            axis=AX.X), waits=tok_ld_l)
    pl.step("v", lambda v: v.tensor_scalar(
        B.AMX[0:1, 9:10], B.AMX[0:1, 8:9], B.CLP[0:1, 1:2], 1.0 / 127.0,
        ALU.mult, ALU.mult), waits=[tok_sig] + tok_ld_l)
    pl.step("v", lambda v: v.reciprocal(B.AMX[0:1, 10:11], B.AMX[0:1, 9:10]),
            bump=B.c_v)
    tok = B.c_v.tok()
    t1 = bcast_scalar(pl, B, B.AMX[0:1, 10:11], rsxb_col, tok)
    if shb_col is not None:
        t1 = bcast_scalar(pl, B, B.AMX[0:1, 9:10], shb_col, tok)
    return t1


def emit_l1_ffn_a(pl, B, mx8_dram, clip_x_dram, wgt_dram, wut_dram,
                  sg_dram, su_dram, h1_dram, hmax_out, tok_xn, guard=None,
                  gmax11=False, wg_pieces=None):
    B.ffna_guard = guard
    tok_sx = emit_sx(pl, B, mx8_dram, clip_x_dram, B.RSXB[:], B.SXB[:],
                     gmax11=gmax11, ld_guard=B.sx_ld_guard)
    for c in range(DC):
        sl = slice(c * S, (c + 1) * S)
        pl.step("v", lambda v, sl=sl: v.tensor_scalar(
            B.XQI[:], B.XN[:, sl], B.RSXB[:], None, ALU.mult),
            waits=[tok_sx, tok_xn[0], tok_xn[1]])
        pl.step("v", lambda v, sl=sl: v.tensor_copy(B.XQB[:, sl], B.XQI[:]))
    tok_xq = B.c_v.tok()
    def pieces(i0, ln):
        out = []
        while ln > 0:
            r = i0 // 352
            lo = i0 - r * 352
            n = min(352 - lo, ln)
            out.append((r, lo, n))
            i0 += n
            ln -= n
        return out

    for ic in range(22):
        for dst, sdram in ((B.SGB, sg_dram), (B.SUB, su_dram)):
            if wg_pieces:
                off = 0
                for r, lo, n in pieces(ic * 128, 128):
                    pl.step("y", lambda y, ic=ic, dst=dst, sdram=sdram, r=r,
                            lo=lo, n=n, off=off: y.dma_start(
                        out=dst[off:off + n, ic:ic + 1],
                        in_=sdram[r:r + 1, lo:lo + n].rearrange("a b -> b a")),
                        bump=B.c_ld, n=16)
                    off += n
            else:
                pl.step("y", lambda y, ic=ic, dst=dst, sdram=sdram: y.dma_start(
                    out=dst[:, ic:ic + 1],
                    in_=sdram[0:1, ic * 128:(ic + 1) * 128].rearrange("a b -> b a")),
                    bump=B.c_ld, n=16)
    tok_s_l = B.c_ld.all()
    pl.step("v", lambda v: v.tensor_scalar(B.SGB[:], B.SGB[:], B.SXB[:],
                                           None, ALU.mult), waits=tok_s_l + [tok_sx])
    pl.step("v", lambda v: v.tensor_scalar(B.SUB[:], B.SUB[:], B.SXB[:],
                                           None, ALU.mult), bump=B.c_v)
    tok_sc = B.c_v.tok()
    wtok = [None] * 4
    wused = [B.ffna_guard] * 4
    hst_tok = B.ffna_guard
    B.sil_guard = None
    for tb in range(NQB):
        q0 = tb * QB
        for ic in range(22):
            bg = ic % 2
            bu = 2 + ic % 2
            for which, wdram, bi in (("g", wgt_dram, bg), ("u", wut_dram, bu)):
                if wg_pieces:
                    off = 0
                    first = True
                    for r, lo, n in pieces(ic * 128, 128):
                        def ldw(g, wdram=wdram, r=r, lo=lo, n=n, off=off, bi=bi):
                            return g.dma_start(
                                out=B.WQS[bi][:].rearrange(
                                    "p (c i) -> p c i", c=DC)[:, :, off:off + n],
                                in_=wdram[r].rearrange(
                                    "(c p) i -> p c i", p=128)[:, :, lo:lo + n])
                        pl.step("g", ldw, waits=[wused[bi]] if first else [],
                                bump=B.c_wg, n=16)
                        first = False
                        off += n
                else:
                    def ldw(g, wdram=wdram, ic=ic, bi=bi):
                        return g.dma_start(
                            out=B.WQS[bi][:].rearrange("p (c i) -> p c i", c=DC),
                            in_=wdram[:, ic * 128:(ic + 1) * 128].rearrange(
                                "(c p) i -> p c i", p=128))
                    pl.step("g", ldw, waits=[wused[bi]], bump=B.c_wg, n=16)
                wtok[bi] = B.c_wg.tok()
            bank_g = B.AV[ic % 2]
            bank_u = B.AV[2 + ic % 2]

            def peg(t, ic=ic, q0=q0, bg=bg, bank_g=bank_g):
                last = None
                for dk in range(DC):
                    last = t.matmul(
                        bank_g[:], B.WQS[bg][:, dk * 128:(dk + 1) * 128],
                        B.XQB[:, dk * S + q0: dk * S + q0 + QB],
                        start=(dk == 0), stop=(dk == DC - 1))
                return last
            pl.step("t", peg, waits=[wtok[bg], tok_xq] + bank_wait(B, bank_g),
                    bump=B.c_pe)
            wused[bg] = B.c_pe.tok()

            def peu(t, ic=ic, q0=q0, bu=bu, bank_u=bank_u):
                last = None
                for dk in range(DC):
                    last = t.matmul(
                        bank_u[:], B.WQS[bu][:, dk * 128:(dk + 1) * 128],
                        B.XQB[:, dk * S + q0: dk * S + q0 + QB],
                        start=(dk == 0), stop=(dk == DC - 1))
                return last
            pl.step("t", peu, waits=[wtok[bu]] + bank_wait(B, bank_u),
                    bump=B.c_pe)
            wused[bu] = B.c_pe.tok()
            tok_mm = B.c_pe.tok()
            pl.step("s", lambda s, ic=ic, bank_g=bank_g: s.activation(
                B.SIL[:], bank_g[:], AF.Silu, scale=B.SGB[:, ic:ic + 1]),
                waits=[tok_mm, tok_sc, B.sil_guard])
            bank_set(B, bank_g, B.c_a.tok())
            pl.step("v", lambda v, ic=ic, bank_u=bank_u: v.scalar_tensor_tensor(
                B.H1[:, ic * 512:(ic + 1) * 512], B.SIL[:], B.SUB[:, ic:ic + 1],
                bank_u[:], ALU.mult, ALU.mult),
                waits=[B.c_a.tok(), hst_tok])
            B.sil_guard = B.c_v.tok()
            bank_set(B, bank_u, B.c_v.tok())
            pl.step("g", lambda g, tb=tb, ic=ic: g.reduce_max(
                B.MXH[0:1, tb * 22 + ic: tb * 22 + ic + 1],
                B.H1[:, ic * 512:(ic + 1) * 512], axis=AX.XYZWC,
                apply_absolute_value=True), waits=[B.c_v.tok()])
        tok_hready = (B.c_v.tok(), B.c_g.tok())
        pl.step("y", lambda y, q0=q0: y.dma_start(
            out=h1_dram.rearrange("(c p) t -> p c t", p=128)[:, :, q0:q0 + QB],
            in_=B.H1[:].rearrange("p (c t) -> p c t", c=22)),
            waits=[tok_hready[0], tok_hready[1]], bump=B.c_st, n=16)
        hst_tok = B.c_st.tok()
    pl.step("g", lambda g: g.tensor_reduce(
        B.MXH[0:1, 88:89], B.MXH[0:1, 0:88], axis=AX.XYZWC, op=ALU.max),
        bump=B.c_g)
    tok_hm = B.c_g.tok()
    pl.step("y", lambda y: y.dma_start(out=hmax_out, in_=B.MXH[0:1, 88:89]),
            waits=[tok_hm], bump=B.c_st, n=16)
    return B.c_st.all()


def emit_l1_ffn_b(pl, B, hmax8_dram, clip_d_dram, wdt_dram, sd_dram,
                  h1_dram, x1_dram, a1_dram, out_dram, guard=None,
                  gmax11=False, wd_pieces=False):
    B.ffnb_guard = guard
    tok_sh = emit_sx(pl, B, hmax8_dram, clip_d_dram, B.RSHB[:], B.SHB[:],
                     gmax11=gmax11, ld_guard=B.sx_ld_guard)
    for m in range(DC):
        if wd_pieces:
            pl.step("y", lambda y, m=m: y.dma_start(
                out=B.SDB[:, m:m + 1],
                in_=sd_dram[m:m + 1, :].rearrange("a b -> b a")),
                bump=B.c_ld, n=16)
        else:
            pl.step("y", lambda y, m=m: y.dma_start(
                out=B.SDB[:, m:m + 1],
                in_=sd_dram[0:1, m * 128:(m + 1) * 128].rearrange("a b -> b a")),
                bump=B.c_ld, n=16)
    tok_s_l = B.c_ld.all()
    pl.step("v", lambda v: v.tensor_scalar(B.SDB[:], B.SDB[:], B.SHB[:],
                                           None, ALU.mult),
            waits=tok_s_l + [tok_sh], bump=B.c_v)
    tok_sd = B.c_v.tok()
    banks = B.AV + B.SC + [B.AUX, B.NRM]
    wused = [B.ffnb_guard, B.ffnb_guard]
    h1_guard = B.ffnb_guard
    for tb in range(NQB):
        q0 = tb * QB
        pl.step("y", lambda y, q0=q0: y.dma_start(
            out=B.H1[:].rearrange("p (c t) -> p c t", c=22),
            in_=h1_dram.rearrange("(c p) t -> p c t", p=128)[:, :, q0:q0 + QB]),
            waits=[h1_guard], bump=B.c_ld, n=16)
        tok_h1 = B.c_ld.tok()
        for ic in range(22):
            sl = slice(ic * 512, (ic + 1) * 512)
            pl.step("v", lambda v, sl=sl: v.tensor_scalar(
                B.H1I[:], B.H1[:, sl], B.RSHB[:], None, ALU.mult),
                waits=[tok_h1, tok_sh])
            pl.step("v", lambda v, sl=sl: v.tensor_copy(B.H1[:, sl], B.H1I[:]))
        tok_hq = B.c_v.tok()
        wtok = [None, None]
        for ic in range(22):
            bi = ic % 2
            if wd_pieces:
                first = True
                for r in range(DC):
                    def ldw(g, ic=ic, bi=bi, r=r):
                        return g.dma_start(
                            out=B.WQS[bi][:].rearrange(
                                "p (c d) -> p c d", c=DC)[:, r:r + 1, :],
                            in_=wdt_dram[r:r + 1, ic * 128:(ic + 1) * 128, :]
                            .rearrange("a p d -> p a d"))
                    pl.step("g", ldw, waits=[wused[bi]] if first else [],
                            bump=B.c_wg, n=16)
                    first = False
            else:
                def ldw(g, ic=ic, bi=bi):
                    return g.dma_start(
                        out=B.WQS[bi][:].rearrange("p (c d) -> p c d", c=DC),
                        in_=wdt_dram[ic * 128:(ic + 1) * 128, :].rearrange(
                            "p (c d) -> p c d", c=DC))
                pl.step("g", ldw, waits=[wused[bi]], bump=B.c_wg, n=16)
            wtok[bi] = B.c_wg.tok()

            def pe(t, ic=ic, bi=bi):
                last = None
                for m in range(DC):
                    last = t.matmul(
                        banks[m][:], B.WQS[bi][:, m * 128:(m + 1) * 128],
                        B.H1[:, ic * 512:(ic + 1) * 512],
                        start=(ic == 0), stop=(ic == 21))
                return last
            w = [wtok[bi], tok_hq]
            if ic == 0:
                for k in range(DC):
                    w += bank_wait(B, banks[k])
            pl.step("t", pe, waits=w, bump=B.c_pe)
            wused[bi] = B.c_pe.tok()
        tok_out = B.c_pe.tok()
        h1_guard = tok_out
        ld_tok = [None, None]
        for m in range(DC):
            bi = m % 2
            pl.step("y", lambda y, m=m, q0=q0, bi=bi: y.dma_start(
                out=B.XRT[bi][:], in_=x1_dram[m * 128:(m + 1) * 128, q0:q0 + QB]),
                waits=[ld_tok[bi]] + bank_wait(B, B.XRT[bi]), bump=B.c_ld, n=16)
            tok_xr = B.c_ld.tok()
            pl.step("y", lambda y, m=m, q0=q0, bi=bi: y.dma_start(
                out=B.ART[bi][:], in_=a1_dram[m * 128:(m + 1) * 128, q0:q0 + QB]),
                waits=[ld_tok[bi]] + bank_wait(B, B.ART[bi]), bump=B.c_ld, n=16)
            tok_res = B.c_ld.tok()
            pl.step("v", lambda v, m=m, bi=bi: v.scalar_tensor_tensor(
                B.XRT[bi][:], banks[m][:], B.SDB[:, m:m + 1], B.XRT[bi][:],
                ALU.mult, ALU.add), waits=[tok_out, tok_sd, tok_xr], bump=B.c_v)
            bank_set(B, banks[m], B.c_v.tok())
            pl.step("v", lambda v, m=m, bi=bi: v.tensor_tensor(
                B.XRT[bi][:], B.XRT[bi][:], B.ART[bi][:], ALU.add),
                waits=[tok_res], bump=B.c_v)
            tok_xp = B.c_v.tok()
            pl.step("y", lambda y, m=m, q0=q0, bi=bi: y.dma_start(
                out=out_dram[m * 128:(m + 1) * 128, q0:q0 + QB],
                in_=B.XRT[bi][:]), waits=[tok_xp], bump=B.c_st, n=16)
            ld_tok[bi] = B.c_st.tok()
            bank_set(B, B.XRT[bi], B.c_st.tok())
            bank_set(B, B.ART[bi], B.c_st.tok())
    return B.c_st.all()


def ag_pieces(i0, ln):
    out = []
    while ln > 0:
        r = i0 // 352
        lo = i0 - r * 352
        n = min(352 - lo, ln)
        out.append((r, lo, n))
        i0 += n
        ln -= n
    return out


def emit_ffn1a_res(pl, B, mx8_dram, clip_x_dram, wgt_dram, wut_dram,
                   sg_dram, su_dram, hmax_out, tok_xn, guard=None):
    B.ffna_guard = guard
    tok_sx = emit_sx(pl, B, mx8_dram, clip_x_dram, B.RSXB[:], B.SXB[:],
                     gmax11=True, ld_guard=B.sx_ld_guard)
    for c in range(DC):
        sl = slice(c * S, (c + 1) * S)
        pl.step("v", lambda v, sl=sl: v.tensor_scalar(
            B.XQI[:], B.XN[:, sl], B.RSXB[:], None, ALU.mult),
            waits=[tok_sx, tok_xn[0], tok_xn[1]])
        pl.step("v", lambda v, sl=sl: v.tensor_copy(B.XQB[:, sl], B.XQI[:]))
    tok_xq = B.c_v.tok()
    for ic in range(22):
        for dst, sdram in ((B.SGB, sg_dram), (B.SUB, su_dram)):
            off = 0
            for r, lo, n in ag_pieces(ic * 128, 128):
                pl.step("y", lambda y, ic=ic, dst=dst, sdram=sdram, r=r,
                        lo=lo, n=n, off=off: y.dma_start(
                    out=dst[off:off + n, ic:ic + 1],
                    in_=sdram[r:r + 1, lo:lo + n].rearrange("a b -> b a")),
                    bump=B.c_ld, n=16)
                off += n
    tok_s_l = B.c_ld.all()
    pl.step("v", lambda v: v.tensor_scalar(B.SGB[:], B.SGB[:], B.SXB[:],
                                           None, ALU.mult), waits=tok_s_l + [tok_sx])
    pl.step("v", lambda v: v.tensor_scalar(B.SUB[:], B.SUB[:], B.SXB[:],
                                           None, ALU.mult), bump=B.c_v)
    tok_sc = B.c_v.tok()
    wtok = [None] * 4
    wused = [B.ffna_guard] * 4

    def load_pair(ic):
        for wdram, bi in ((wgt_dram, ic % 2), (wut_dram, 2 + ic % 2)):
            off = 0
            first = True
            for r, lo, n in ag_pieces(ic * 128, 128):
                def ldw(g, wdram=wdram, r=r, lo=lo, n=n, off=off, bi=bi):
                    return g.dma_start(
                        out=B.WQS[bi][:].rearrange(
                            "p (c i) -> p c i", c=DC)[:, :, off:off + n],
                        in_=wdram[r].rearrange(
                            "(c p) i -> p c i", p=128)[:, :, lo:lo + n])
                pl.step("g", ldw, waits=[wused[bi]] if first else [],
                        bump=B.c_wg, n=16)
                first = False
                off += n
            wtok[bi] = B.c_wg.tok()

    load_pair(0)
    sil_bufs = [B.SIL, B.SIL2]
    sil_guard = [None, None]
    for ic in range(22):
        if ic + 1 < 22:
            load_pair(ic + 1)
        bg, bu = ic % 2, 2 + ic % 2
        for tb in range(NQB):
            q0 = tb * QB
            bank_g = B.AV[tb % 2]
            bank_u = B.AV[2 + tb % 2]

            def peg(t, q0=q0, bg=bg, bank_g=bank_g):
                last = None
                for dk in range(DC):
                    last = t.matmul(
                        bank_g[:], B.WQS[bg][:, dk * 128:(dk + 1) * 128],
                        B.XQB[:, dk * S + q0: dk * S + q0 + QB],
                        start=(dk == 0), stop=(dk == DC - 1))
                return last
            pl.step("t", peg, waits=[wtok[bg], tok_xq] + bank_wait(B, bank_g),
                    bump=B.c_pe)
            tok_mmg = B.c_pe.tok()

            def peu(t, q0=q0, bu=bu, bank_u=bank_u):
                last = None
                for dk in range(DC):
                    last = t.matmul(
                        bank_u[:], B.WQS[bu][:, dk * 128:(dk + 1) * 128],
                        B.XQB[:, dk * S + q0: dk * S + q0 + QB],
                        start=(dk == 0), stop=(dk == DC - 1))
                return last
            pl.step("t", peu, waits=[wtok[bu]] + bank_wait(B, bank_u),
                    bump=B.c_pe)
            sb = sil_bufs[tb % 2]
            pl.step("s", lambda s, ic=ic, bank_g=bank_g, sb=sb: s.activation(
                sb[:], bank_g[:], AF.Silu, scale=B.SGB[:, ic:ic + 1]),
                waits=[tok_mmg, tok_sc, sil_guard[tb % 2]])
            bank_set(B, bank_g, B.c_a.tok())
            slot = B.H1S[ic * 4 + tb]
            pl.step("v", lambda v, ic=ic, bank_u=bank_u, sb=sb, slot=slot:
                    v.scalar_tensor_tensor(
                        slot, sb[:], B.SUB[:, ic:ic + 1], bank_u[:],
                        ALU.mult, ALU.mult),
                    waits=[B.c_a.tok(), B.c_pe.tok()], bump=B.c_v)
            sil_guard[tb % 2] = B.c_v.tok()
            bank_set(B, bank_u, B.c_v.tok())
            pl.step("g", lambda g, ic=ic, tb=tb, slot=slot: g.reduce_max(
                B.MXH[0:1, tb * 22 + ic: tb * 22 + ic + 1],
                slot, axis=AX.XYZWC, apply_absolute_value=True),
                waits=[B.c_v.tok()])
        wused[bg] = B.c_pe.tok()
        wused[bu] = B.c_pe.tok()
    pl.step("g", lambda g: g.tensor_reduce(
        B.MXH[0:1, 88:89], B.MXH[0:1, 0:88], axis=AX.XYZWC, op=ALU.max),
        bump=B.c_g)
    tok_hm = B.c_g.tok()
    pl.step("y", lambda y: y.dma_start(out=hmax_out, in_=B.MXH[0:1, 88:89]),
            waits=[tok_hm], bump=B.c_st, n=16)
    return [B.c_st.tok()]


def emit_ffn1b_res(pl, B, hmax8_dram, clip_d_dram, wdt_dram, sd_dram,
                   x1_dram, a1_dram, out_dram, guard=None):
    B.ffnb_guard = guard
    tok_sh = emit_sx(pl, B, hmax8_dram, clip_d_dram, B.RSHB[:], B.SHB[:],
                     gmax11=True, ld_guard=B.sx_ld_guard)
    for m in range(DC):
        pl.step("y", lambda y, m=m: y.dma_start(
            out=B.SDB[:, m:m + 1],
            in_=sd_dram[m:m + 1, :].rearrange("a b -> b a")),
            bump=B.c_ld, n=16)
    tok_s_l = B.c_ld.all()
    pl.step("v", lambda v: v.tensor_scalar(B.SDB[:], B.SDB[:], B.SHB[:],
                                           None, ALU.mult),
            waits=tok_s_l + [tok_sh], bump=B.c_v)
    tok_sd = B.c_v.tok()
    qtb = [None] * 4
    for tb in range(NQB):
        for ic in range(22):
            slot = B.H1S[ic * 4 + tb]
            pl.step("v", lambda v, slot=slot: v.tensor_scalar(
                B.H1I[:], slot, B.RSHB[:], None, ALU.mult), waits=[tok_sh])
            pl.step("v", lambda v, slot=slot: v.tensor_copy(slot, B.H1I[:]))
        qtb[tb] = B.c_v.tok()
    banks = B.AV + B.SC + [B.AUX, B.NRM]
    wtok = [None] * 4
    wused = [B.ffnb_guard] * 4

    def load_w(dg, ic):
        bi = ic % 4
        first = True
        for j in range(2):
            r = 2 * dg + j

            def ldw(g, r=r, j=j, ic=ic, bi=bi):
                return g.dma_start(
                    out=B.WQS[bi][:, j * 128:(j + 1) * 128],
                    in_=wdt_dram[r:r + 1, ic * 128:(ic + 1) * 128, :]
                    .rearrange("a p d -> p (a d)"))
            pl.step("g", ldw, waits=[wused[bi]] if first else [],
                    bump=B.c_wg, n=16)
            first = False
        wtok[bi] = B.c_wg.tok()

    res_st = {}

    def res_loads(dg, j, res_tok):
        m = 2 * dg + j
        for tb in range(NQB):
            b = tb % 4
            pl.step("y", lambda y, m=m, tb=tb, b=b: y.dma_start(
                out=B.XR4[b], in_=x1_dram[m * 128:(m + 1) * 128,
                                          tb * QB:(tb + 1) * QB]),
                waits=[res_st.get(("x", b)), B.ffnb_guard], bump=B.c_ld, n=16)
            res_tok[("x", j, tb)] = B.c_ld.tok()
            pl.step("y", lambda y, m=m, tb=tb, b=b: y.dma_start(
                out=B.AR4[b], in_=a1_dram[m * 128:(m + 1) * 128,
                                          tb * QB:(tb + 1) * QB]),
                waits=[res_st.get(("a", b)), B.ffnb_guard], bump=B.c_ld, n=16)
            res_tok[("a", j, tb)] = B.c_ld.tok()

    for dg in range(4):
        load_w(dg, 0)
        load_w(dg, 1)
        res_tok = {}
        res_loads(dg, 0, res_tok)
        for ic in range(22):
            if ic + 2 < 22:
                load_w(dg, ic + 2)
            bi = ic % 4

            def pe(t, ic=ic, bi=bi):
                last = None
                for tb in range(NQB):
                    for j in range(2):
                        last = t.matmul(
                            banks[tb * 2 + j][:],
                            B.WQS[bi][:, j * 128:(j + 1) * 128],
                            B.H1S[ic * 4 + tb],
                            start=(ic == 0), stop=(ic == 21))
                return last
            w = [wtok[bi]] + qtb
            if ic == 0:
                for k in range(8):
                    w += bank_wait(B, banks[k])
            pl.step("t", pe, waits=w, bump=B.c_pe)
            wused[bi] = B.c_pe.tok()
        tok_out = B.c_pe.tok()
        for j in range(2):
            m = 2 * dg + j
            for tb in range(NQB):
                b = tb % 4
                bank = banks[tb * 2 + j]
                pl.step("v", lambda v, m=m, b=b, bank=bank:
                        v.scalar_tensor_tensor(
                            B.XR4[b], bank[:], B.SDB[:, m:m + 1], B.XR4[b],
                            ALU.mult, ALU.add),
                        waits=[tok_out, tok_sd, res_tok[("x", j, tb)]],
                        bump=B.c_v)
                bank_set(B, bank, B.c_v.tok())
                pl.step("v", lambda v, b=b: v.tensor_tensor(
                    B.XR4[b], B.XR4[b], B.AR4[b], ALU.add),
                    waits=[res_tok[("a", j, tb)]], bump=B.c_v)
                tok_xp = B.c_v.tok()
                pl.step("y", lambda y, m=m, tb=tb, b=b: y.dma_start(
                    out=out_dram[m * 128:(m + 1) * 128, tb * QB:(tb + 1) * QB],
                    in_=B.XR4[b]), waits=[tok_xp], bump=B.c_st, n=16)
                res_st[("x", b)] = B.c_st.tok()
                res_st[("a", b)] = B.c_st.tok()
            if j == 0:
                res_loads(dg, 1, res_tok)
    return B.c_st.all()


def finish(P, pl, st_tok):
    wl = st_tok if isinstance(st_tok, list) else [st_tok]
    pl.step("y", lambda y: None, waits=wl)
    with P.nc.Block() as block:
        pl.emit_all(block)
    return P.fin()


def build_l0_attn(dbg=False):
    P = Prog()
    xn = P.din("xn", [D, S], BF16)
    wq_t = P.din("wq_t", [D, D], F32)
    wk_t = P.din("wk_t", [D, D], F32)
    wv_t = P.din("wv_t", [D, D], F32)
    wo_t = P.din("wo_t", [D, D], F32)
    bq2 = P.din("bq2", [128, DC], F32)
    bk2 = P.din("bk2", [128, DC], F32)
    bv_row = P.din("bv_row", [1, D], F32)
    bo2 = P.din("bo2", [128, DC], F32)
    a0 = P.dout("a0", [D, S], F32)

    B = alloc_common(P)
    alloc_l0_attn(P, B)
    pl = mk_plan(P, B)
    emit_setup(pl, B)
    t_ = load_xn(pl, B, xn)
    st_tok = emit_l0_attn(pl, B, wq_t, wk_t, wv_t, wo_t, bq2, bk2, bv_row, bo2,
                          a0, (t_, t_))
    if dbg:
        dumps = [("QF", B.QF, BF16), ("KF", B.KF, BF16), ("VT", B.VT, BF16),
                 ("EXPB", B.EXPB, BF16), ("CB", B.CBS, F32), ("MXR", B.MXR, F32),
                 ("ATT", B.ATT, BF16), ("BVB", B.BVB, BF16)]
        allw = [B.c_pe.tok(), B.c_v.tok(), B.c_a.tok(), B.c_g.tok()]
        for nm, buf, dt in dumps:
            od = P.dout("dbg_" + nm, list(buf.shape), dt)
            pl.step("y", lambda y, od=od, buf=buf: y.dma_start(out=od, in_=buf[:]),
                    waits=allw, bump=B.c_st, n=16)
        st_tok = B.c_st.all()
    return finish(P, pl, st_tok)


def build_l0_pre(dbg=False, no_ln=False):
    P = Prog()
    x = P.din("x", [D, S], F32)
    lg1 = P.din("lg1", [1, 256], F32)
    lg2 = P.din("lg2", [1, 256], F32)
    a1t = P.din("a1t", [256, 64, 1024], F32)
    b1 = P.din("b1", [256, 64, 4096], F32)
    a2t = P.din("a2t", [256, 64, 4096], F32)
    b2 = P.din("b2", [256, 64, 1024], F32)
    coreid = P.din("coreid", [1, 1], F32)
    xn0 = P.dout("xn0", [D, S], BF16)
    w1sl = P.dout("w1sl", [D, 512], BF16)
    w2sl = P.dout("w2sl", [512, D], BF16)

    B = alloc_common(P)
    B.XN = P.sb("XN", [128, DC * S], BF16)
    alloc_ln(P, B)
    alloc_compose(P, B)
    pl = mk_plan(P, B)
    emit_setup(pl, B)
    if not no_ln:
        tok_xn = emit_ln(pl, B, x)
        pl.step("y", lambda y: y.dma_start(
            out=xn0.rearrange("(c p) t -> p c t", p=128),
            in_=B.XN[:].rearrange("p (c t) -> p c t", c=DC)),
            waits=[tok_xn[0], tok_xn[1]], bump=B.c_st, n=16)
    else:
        pl.step("y", lambda y: y.dma_start(
            out=xn0.rearrange("(c p) t -> p c t", p=128),
            in_=B.XN[:].rearrange("p (c t) -> p c t", c=DC)),
            bump=B.c_st, n=16)
    emit_compose_setup(pl, B, coreid)
    st = emit_compose(pl, B, lg1, lg2, a1t, b1, a2t, b2, w1sl, w2sl)
    if dbg:
        allw0 = [B.c_pe.tok(), B.c_v.tok(), B.c_a.tok(), B.c_g.tok()] \
            + B.c_wg.all()
        pl.step("v", lambda v: v.tensor_copy(B.U2[:], B.IDXA[:]), waits=allw0)
        pl.step("v", lambda v: v.tensor_copy(B.IDXAf[:], B.IDX8[:]),
                bump=B.c_v)
        dumps = [("IDXA", B.U2, F32), ("IDX8", B.IDXAf, F32),
                 ("MS", B.MS, F32), ("W32", B.W32, F32), ("IVF", B.IVF, F32),
                 ("GA0", B.GA[0], BF16), ("GB0", B.GB[0], BF16),
                 ("GAS0", B.GAS[0], BF16), ("QM64", B.QM64, F32),
                 ("PEVB", B.PEVB, F32), ("SELB", B.SELB, F32)]
        allw = [B.c_pe.tok(), B.c_v.tok(), B.c_a.tok(), B.c_g.tok()]
        for nm, buf, dt in dumps:
            od = P.dout("dbg_" + nm, list(buf.shape), dt)
            pl.step("y", lambda y, od=od, buf=buf: y.dma_start(out=od, in_=buf[:]),
                    waits=allw, bump=B.c_st, n=16)
    return finish(P, pl, B.c_st.all())


def build_l0_ffn(dbg=False):
    P = Prog()
    xn = P.din("xn", [D, S], BF16)
    w1g = P.din("w1g", [8, D, 512], BF16)
    w2g = P.din("w2g", [4096, D], BF16)
    x = P.din("x", [D, S], F32)
    a0 = P.din("a0", [D, S], F32)
    x1 = P.dout("x1", [D, S], F32)
    xn1 = P.dout("xn1", [D, S], BF16)
    mx = P.dout("mx", [1, 1], F32)

    B = alloc_common(P)
    B.XN = P.sb("XN", [128, DC * S], BF16)
    B.AMC = P.sb("AMC", [128, DC + 1], F32)
    B.AMX = P.sb("AMX", [1, 16], F32)
    alloc_ffn0(P, B)
    alloc_ln(P, B)
    pl = mk_plan(P, B)
    emit_setup(pl, B)
    t = load_xn(pl, B, xn)
    st = emit_ffn0_apply(pl, B, w1g, w2g, x, a0, x1, (t, t),
                         no_gelu=(dbg == "nogelu"))
    tok_xn1 = emit_ln(pl, B, x1, wait=st)
    pl.step("y", lambda y: y.dma_start(
        out=xn1.rearrange("(c p) t -> p c t", p=128),
        in_=B.XN[:].rearrange("p (c t) -> p c t", c=DC)),
        waits=[tok_xn1[0], tok_xn1[1]], bump=B.c_st, n=16)
    tok_mx = emit_absmax_xn(pl, B, tok_xn1)
    pl.step("y", lambda y: y.dma_start(out=mx, in_=B.AMX[0:1, 0:1]),
            waits=[tok_mx], bump=B.c_st, n=16)
    st = B.c_st.all()
    if dbg:
        allw = [B.c_pe.tok(), B.c_v.tok(), B.c_a.tok(), B.c_g.tok(), st]
        od = P.dout("dbg_H", [128, 32 * 512], BF16)
        pl.step("y", lambda y, od=od: y.dma_start(out=od, in_=B.H[:]),
                waits=allw, bump=B.c_st, n=16)
        st = B.c_st.all()
    return finish(P, pl, st)


def build_l1_attn():
    P = Prog()
    xn = P.din("xn", [D, S], BF16)
    w_in = P.din("w_in", [D, D], F32)
    w_r = P.din("w_r", [D, D], F32)
    w_i = P.din("w_i", [D, D], F32)
    w_out = P.din("w_out", [D, D], F32)
    b_in2 = P.din("b_in2", [128, DC], F32)
    b_r2 = P.din("b_r2", [128, DC], F32)
    b_i2 = P.din("b_i2", [128, DC], F32)
    b_out2 = P.din("b_out2", [128, DC], F32)
    log_a2 = P.din("log_a2", [128, DC], F32)
    a1 = P.dout("a1", [D, S], F32)
    B = alloc_common(P)
    B.XN = P.sb("XN", [128, DC * S], BF16)
    alloc_l1_attn(P, B)
    pl = mk_plan(P, B)
    emit_setup(pl, B)
    t = load_xn(pl, B, xn)
    st = emit_l1_attn(pl, B, w_in, w_r, w_i, w_out, b_in2, b_r2, b_i2, b_out2,
                      log_a2, a1, (t, t))
    return finish(P, pl, st)


def build_l1_quant():
    P = Prog()
    wg_sl = P.din("wg_sl", [352, 1024], F32)
    wu_sl = P.din("wu_sl", [352, 1024], F32)
    wd_sl = P.din("wd_sl", [128, 2816], F32)
    wgt_sl = P.dout("wgt_sl", [D, 352], BF16)
    wut_sl = P.dout("wut_sl", [D, 352], BF16)
    wdt_sl = P.dout("wdt_sl", [2816, 128], BF16)
    sg_sl = P.dout("sg_sl", [1, 352], F32)
    su_sl = P.dout("su_sl", [1, 352], F32)
    sd_sl = P.dout("sd_sl", [1, 128], F32)
    B = alloc_common(P)
    alloc_l1_quant(P, B)
    pl = mk_plan(P, B)
    emit_setup(pl, B)
    st = emit_l1_quant(pl, B, wg_sl, wu_sl, wd_sl, wgt_sl, wut_sl, wdt_sl,
                       sg_sl, su_sl, sd_sl)
    return finish(P, pl, st)


def build_l1_ffn_a():
    P = Prog()
    xn = P.din("xn", [D, S], BF16)
    mx8 = P.din("mx8", [1, 8], F32)
    clip_x = P.din("clip_x", [1, 1], F32)
    wgt = P.din("wgt", [D, 2816], BF16)
    wut = P.din("wut", [D, 2816], BF16)
    sg = P.din("sg", [1, 2816], F32)
    su = P.din("su", [1, 2816], F32)
    h1 = P.dout("h1", [2816, S], BF16)
    hmax = P.dout("hmax", [1, 1], F32)
    B = alloc_common(P)
    B.XN = P.sb("XN", [128, DC * S], BF16)
    B.QF = P.sb("QF", [128, DC * S], BF16)
    B.KF = P.sb("KF", [128, DC * S], BF16)
    alloc_l1_ffn(P, B, "a")
    B.sx_ld_guard = None
    pl = mk_plan(P, B)
    emit_setup(pl, B)
    t = load_xn(pl, B, xn)
    st = emit_l1_ffn_a(pl, B, mx8, clip_x, wgt, wut, sg, su, h1, hmax, (t, t))
    return finish(P, pl, st)


def build_l1_ffn_b():
    P = Prog()
    hmax8 = P.din("hmax8", [1, 8], F32)
    clip_d = P.din("clip_d", [1, 1], F32)
    wdt = P.din("wdt", [2816, D], BF16)
    sd = P.din("sd", [1, D], F32)
    h1 = P.din("h1", [2816, S], BF16)
    x1 = P.din("x1", [D, S], F32)
    a1 = P.din("a1", [D, S], F32)
    out = P.dout("out", [D, S], F32)
    B = alloc_common(P)
    alloc_l1_ffn(P, B, "b")
    B.sx_ld_guard = None
    pl = mk_plan(P, B)
    emit_setup(pl, B)
    st = emit_l1_ffn_b(pl, B, hmax8, clip_d, wdt, sd, h1, x1, a1, out)
    return finish(P, pl, st)



def alloc_fused(P):
    B = alloc_common(P)
    B.XN = P.sb("XN", [128, DC * S], BF16)
    B.QF = P.sb("QF", [128, DC * S], BF16)
    B.KF = P.sb("KF", [128, DC * S], BF16)
    B.VT = P.sb("VT", [128, NT * D], BF16)
    B.STASH = P.sb("STASH", [128, 8 * 512], F16)
    B.ATT = B.STASH.bitcast(BF16)
    B.EXPB = P.sb("EXPB", [128, NT * 512], BF16)
    B.WARE = [P.sb(f"ware{i}", [128, DC * 256], BF16) for i in range(2)]
    B.RZ = P.sb("RZN", [128, 512], F32)
    B.RN = B.RZ
    B.FS = P.sb("FS", [128, 5 * 512], F32)
    FSs = [B.FS[:, i * 512:(i + 1) * 512] for i in range(5)]
    B.A0T = [FSs[0], FSs[1]]
    B.RSTB = FSs[2]
    B.MRB = FSs[3]
    B.GU = B.RZ[:]
    B.GS = FSs[4]
    B.XRT = [FSs[0], FSs[1]]
    B.ART = [FSs[2], FSs[3]]
    B.SIL = FSs[4]
    B.X2B = [B.EXPB[:, 0:512], B.EXPB[:, 512:1024]]
    B.ROWA = B.FS[0:1, 2048:2560]
    B.ROWB = B.RZ[0:1, 0:512]
    B.EPSB = P.sb("epsb", [1, 1], F32)
    B.eps_init = False
    B.MXR = P.sb("MXR", [1, 8], F32)
    B.NCR = P.sb("NCR", [1, 8], F32)
    B.CBS = P.sb("CBS", [128, 8], F32)
    B.BQ = P.sb("BQ", [128, DC], F32)
    B.BK = P.sb("BK", [128, DC], F32)
    B.BO = P.sb("BO", [128, DC], F32)
    B.BVR = B.EXPB[0:1, 2048:3072]
    B.BVB = B.EXPB[:, 1024:2048]
    B.LG = B.FS[0:1, 0:256]
    B.LGW = B.FS[32:33, 0:256]
    B.MV = P.sb("MV", [1, 32], F32)
    B.IV = P.sb("IV", [1, 32], U32)
    B.IVF = P.sb("IVF", [1, 32], F32)
    B.EW = P.sb("EW", [1, 32], F32)
    B.SWr = P.sb("SWr", [1, 2], F32)
    B.W32 = P.sb("W32", [1, 32], F32)
    B.SELB = P.sb("SELB", [128, 32], F32)
    B.W32B = P.sb("W32B", [128, 32], F32)
    B.PEVB = P.sb("PEVB", [128, 32], F32)
    B.EOT = P.sb("EOT", [128, 32], F32)
    B.UB = P.sb("UB", [128, 32], F32)
    B.U2 = P.sb("U2", [128, 16], F32)
    B.MS = P.sb("MS", [128, 16], F32)
    B.IDXAf = P.sb("IDXAf", [128, 16], F32)
    B.IDXA = P.sb("IDXA", [128, 16], U32)
    B.IDX8 = P.sb("IDX8", [128, 16], U32)
    B.QM64 = P.sb("QM64", [128, 1], F32)
    B.QI = P.sb("QI", [128, 1], I32)
    B.COREB = P.sb("COREB", [128, 1], F32)
    B.GA = [B.QF[:, 0:1024], B.QF[:, 1024:2048]]
    B.GAS = [B.QF[:, 2048:3072], B.QF[:, 3072:4096]]
    B.GB = [B.QF[:, 4096:5120], B.QF[:, 5120:6144]]
    B.STG = [B.QF[:, 6144:6656], B.QF[:, 6656:7168]]
    B.H = B.QF
    B.WST = [B.KF[:, 0:8192], B.KF[:, 8192:16384]]
    B.QWB = B.VT[:, 0:3072]
    B.QDB = B.VT[:, 3072:5888]
    B.QBF = B.VT[:, 5888:8960]
    B.TQB = B.VT[:, 8960:11776]
    B.TQD = B.VT[:, 11776:14592]
    B.IBUF = P.sb("IBUF", [128, 2816], I8)
    B.QI8 = B.IBUF[:, 0:2816]
    B.SCL = P.sb("SCL", [128, 4], F32)
    B.RCL = P.sb("RCL", [128, 4], F32)
    B.IP = B.QF
    B.RG = B.KF
    B.IG = B.VT
    B.BIN = P.sb("BIN", [128, DC], F32)
    B.BR = P.sb("BR", [128, DC], F32)
    B.BI = P.sb("BI", [128, DC], F32)
    B.BOUT = P.sb("BOUT", [128, DC], F32)
    B.ASG = P.sb("ASG", [128, DC], F32)
    B.XQB = B.KF
    B.XQI = B.IBUF[:, 0:2048]
    B.H1 = B.VT[:, 0:22 * 512]
    B.H1I = B.IBUF[:, 0:512]
    B.SGB = P.sb("SGB", [128, 22], F32)
    B.SUB = P.sb("SUB", [128, 22], F32)
    B.SDB = P.sb("SDB", [128, DC], F32)
    B.WQS = [B.QF[:, 0:1024], B.QF[:, 1024:2048],
             B.QF[:, 2048:3072], B.QF[:, 3072:4096]]
    B.MXH = P.sb("MXH", [1, 92], F32)
    B.AMC = P.sb("AMC", [128, DC + 1], F32)
    B.AMX = P.sb("AMX", [1, 16], F32)
    B.CLP = P.sb("CLP", [1, 4], F32)
    B.RSXB = P.sb("RSXB", [128, 1], F32)
    B.SXB = P.sb("SXB", [128, 1], F32)
    B.RSHB = P.sb("RSHB", [128, 1], F32)
    B.SHB = P.sb("SHB", [128, 1], F32)
    B.H1S = []
    for k in range(88):
        if k < 32:
            B.H1S.append(B.XN[:, k * 512:(k + 1) * 512])
        elif k < 64:
            B.H1S.append(B.VT[:, (k - 32) * 512:(k - 31) * 512])
        else:
            B.H1S.append(B.QF[:, (k - 56) * 512:(k - 55) * 512])
    B.SIL2 = B.RZ[:, 0:512]
    EXPF = B.EXPB.bitcast(F32)
    B.XR4 = [EXPF[:, i * 512:(i + 1) * 512] for i in range(4)]
    B.AR4 = [EXPF[:, (4 + i) * 512:(5 + i) * 512] for i in range(4)]
    return B


RG8 = [list(range(8))]


def build_fused():
    P = Prog()
    x = P.din("x", [D, S], F32)
    wq_t = P.din("wq_t", [D, D], F32)
    wk_t = P.din("wk_t", [D, D], F32)
    wv_t = P.din("wv_t", [D, D], F32)
    wo_t = P.din("wo_t", [D, D], F32)
    bq2 = P.din("bq2", [128, DC], F32)
    bk2 = P.din("bk2", [128, DC], F32)
    bv_row = P.din("bv_row", [1, D], F32)
    bo2 = P.din("bo2", [128, DC], F32)
    lg1 = P.din("lg1", [1, 256], F32)
    lg2 = P.din("lg2", [1, 256], F32)
    a1t = P.din("a1t", [256, 64, 1024], F32)
    b1 = P.din("b1", [256, 64, 4096], F32)
    a2t = P.din("a2t", [256, 64, 4096], F32)
    b2 = P.din("b2", [256, 64, 1024], F32)
    coreid = P.din("coreid", [1, 1], F32)
    w_in = P.din("w_in", [D, D], F32)
    w_r = P.din("w_r", [D, D], F32)
    w_i = P.din("w_i", [D, D], F32)
    w_out = P.din("w_out", [D, D], F32)
    b_in2 = P.din("b_in2", [128, DC], F32)
    b_r2 = P.din("b_r2", [128, DC], F32)
    b_i2 = P.din("b_i2", [128, DC], F32)
    b_out2 = P.din("b_out2", [128, DC], F32)
    log_a2 = P.din("log_a2", [128, DC], F32)
    wg_sl = P.din("wg_sl", [352, 1024], F32)
    wu_sl = P.din("wu_sl", [352, 1024], F32)
    wd_sl = P.din("wd_sl", [128, 2816], F32)
    clip_x = P.din("clip_x", [1, 1], F32)
    clip_d = P.din("clip_d", [1, 1], F32)
    out = P.dout("out", [D, S], F32)
    a0 = P.dtmp("a0", [D, S], F32)
    a1 = P.dtmp("a1", [D, S], F32)
    x1 = P.dtmp("x1", [D, S], F32)
    w1in = P.dtmp("w1in", [D, 512], BF16)
    w1out = P.dtmp("w1out", [8 * D, 512], BF16)
    w2in = P.dtmp("w2in", [512, D], BF16)
    w2out = P.dtmp("w2out", [4096, D], BF16)
    wgt_in = P.dtmp("wgt_in", [D, 352], BF16)
    wgt_out = P.dtmp("wgt_out", [8 * D, 352], BF16)
    wut_in = P.dtmp("wut_in", [D, 352], BF16)
    wut_out = P.dtmp("wut_out", [8 * D, 352], BF16)
    wdt_in = P.dtmp("wdt_in", [2816, 128], BF16)
    wdt_out = P.dtmp("wdt_out", [8 * 2816, 128], BF16)
    sg_in = P.dtmp("sg_in", [1, 352], F32)
    sg_out = P.dtmp("sg_out", [8, 352], F32)
    su_in = P.dtmp("su_in", [1, 352], F32)
    su_out = P.dtmp("su_out", [8, 352], F32)
    sd_in = P.dtmp("sd_in", [1, 128], F32)
    sd_out = P.dtmp("sd_out", [8, 128], F32)
    mx_in = P.dtmp("mx_in", [1, 16], F32)
    mx_out = P.dtmp("mx_out", [1, 16], F32)
    hm_in = P.dtmp("hm_in", [1, 16], F32)
    hm_out = P.dtmp("hm_out", [1, 16], F32)

    B = alloc_fused(P)
    B.sx_ld_guard = None
    pl = mk_plan(P, B)
    emit_setup(pl, B)
    emit_compose_setup(pl, B, coreid)

    tok_xn0 = emit_ln(pl, B, x)
    st_comp = emit_compose(pl, B, lg1, lg2, a1t, b1, a2t, b2, w1in, w2in)
    for cin, cout in ((w1in, w1out), (w2in, w2out)):
        pl.step("g", lambda g, cin=cin, cout=cout: g.collective_compute(
            "AllGather", ALU.bypass, replica_groups=RG8,
            ins=[cin.opt()], outs=[cout.opt()]),
            waits=[st_comp], bump=B.c_cc)
    tok_ag1 = B.c_cc.tok()
    bank_set(B, B.A0T[0], B.c_v.tok())
    st_attn = emit_l0_attn(pl, B, wq_t, wk_t, wv_t, wo_t, bq2, bk2, bv_row,
                           bo2, a0, tok_xn0, guard=[st_comp])
    attn_done = B.c_pe.tok()
    q_done = emit_l1_quant(pl, B, wg_sl, wu_sl, wd_sl, wgt_in, wut_in, wdt_in,
                           sg_in, su_in, sd_in, guard=attn_done)
    for cin, cout in ((wgt_in, wgt_out), (wut_in, wut_out), (wdt_in, wdt_out),
                      (sg_in, sg_out), (su_in, su_out), (sd_in, sd_out)):
        pl.step("g", lambda g, cin=cin, cout=cout: g.collective_compute(
            "AllGather", ALU.bypass, replica_groups=RG8,
            ins=[cin.opt()], outs=[cout.opt()]),
            waits=[q_done], bump=B.c_cc)
    tok_ag2 = B.c_cc.tok()
    st_ffn0 = emit_ffn0_apply(
        pl, B, w1out.rearrange("(r d) i -> r d i", r=8), w2out, x, a0, x1,
        tok_xn0, guard=[attn_done], wsrc_tok=tok_ag1)

    tok_xn1 = emit_ln(pl, B, x1, wait=st_ffn0)
    tok_mx = emit_absmax_xn(pl, B, tok_xn1)
    pl.step("y", lambda y: y.dma_start(out=mx_in[0:1, 0:1], in_=B.AMX[0:1, 0:1]),
            waits=[tok_mx], bump=B.c_st, n=16)
    tok_mxst = B.c_st.tok()
    pl.step("g", lambda g: g.collective_compute(
        "AllReduce", ALU.max, replica_groups=RG8,
        ins=[mx_in.opt()], outs=[mx_out.opt()]),
        waits=[tok_mxst], bump=B.c_cc)
    tok_ar3 = B.c_cc.tok()
    st_a1 = emit_l1_attn(pl, B, w_in, w_r, w_i, w_out, b_in2, b_r2, b_i2,
                         b_out2, log_a2, a1, tok_xn1, guard=attn_done,
                         evac_guard=q_done)
    l1_done_v = B.c_v.tok()
    B.sx_ld_guard = tok_ar3
    st_h1 = emit_ffn1a_res(
        pl, B, mx_out[0:1, 0:1], clip_x,
        wgt_out.rearrange("(r d) i -> r d i", r=8),
        wut_out.rearrange("(r d) i -> r d i", r=8),
        sg_out, su_out, hm_in[0:1, 0:1], tok_xn1,
        guard=[l1_done_v, tok_ag2, q_done])
    pl.step("g", lambda g: g.collective_compute(
        "AllReduce", ALU.max, replica_groups=RG8,
        ins=[hm_in.opt()], outs=[hm_out.opt()]),
        waits=[st_h1], bump=B.c_cc)
    tok_ar4 = B.c_cc.tok()
    ffnb_guard = [B.c_pe.tok(), B.c_v.tok(), B.c_g.tok()] + B.c_st.all()
    B.sx_ld_guard = tok_ar4
    st_out = emit_ffn1b_res(
        pl, B, hm_out[0:1, 0:1], clip_d,
        wdt_out.rearrange("(r i) d -> r i d", r=8), sd_out,
        x1, a1, out, guard=ffnb_guard)
    return finish(P, pl, st_out)


def lay_bias(b):
    return np.ascontiguousarray(b.reshape(DC, 128).T)


CORES = list(range(8))


def host_prep(g):
    x = np.asarray(g["L0_attn_batch"])
    shared = {
        "wq_t": np.ascontiguousarray(np.asarray(g["L0_attn_Wq"]).T),
        "wk_t": np.ascontiguousarray(np.asarray(g["L0_attn_Wk"]).T),
        "wv_t": np.ascontiguousarray(np.asarray(g["L0_attn_Wv"]).T),
        "wo_t": np.ascontiguousarray(np.asarray(g["L0_attn_Wo"]).T),
        "bq2": lay_bias(np.asarray(g["L0_attn_bq"])),
        "bk2": lay_bias(np.asarray(g["L0_attn_bk"])),
        "bo2": lay_bias(np.asarray(g["L0_attn_bo"])),
        "bv_row": np.ascontiguousarray(np.asarray(g["L0_attn_bv"]).reshape(1, D)),
        "lg1": np.ascontiguousarray(np.asarray(g["L0_ffn_fc1_logits"]).reshape(1, 256)),
        "lg2": np.ascontiguousarray(np.asarray(g["L0_ffn_fc2_logits"]).reshape(1, 256)),
        "a1t": np.ascontiguousarray(np.asarray(g["L0_ffn_fc1_A"]).transpose(0, 2, 1)),
        "b1": np.ascontiguousarray(g["L0_ffn_fc1_B"]),
        "a2t": np.ascontiguousarray(np.asarray(g["L0_ffn_fc2_A"]).transpose(0, 2, 1)),
        "b2": np.ascontiguousarray(g["L0_ffn_fc2_B"]),
        "w_in": np.ascontiguousarray(g["L1_attn_W_in"]),
        "w_r": np.ascontiguousarray(g["L1_attn_W_r"]),
        "w_i": np.ascontiguousarray(g["L1_attn_W_i"]),
        "w_out": np.ascontiguousarray(g["L1_attn_W_out"]),
        "b_in2": lay_bias(np.asarray(g["L1_attn_b_in"])),
        "b_r2": lay_bias(np.asarray(g["L1_attn_b_r"])),
        "b_i2": lay_bias(np.asarray(g["L1_attn_b_i"])),
        "b_out2": lay_bias(np.asarray(g["L1_attn_b_out"])),
        "log_a2": lay_bias(np.asarray(g["L1_attn_log_a"])),
        "clip_x": np.ascontiguousarray(np.asarray(g["L1_ffn_clip_x"]).reshape(1, 1)),
        "clip_d": np.ascontiguousarray(np.asarray(g["L1_ffn_clip_down"]).reshape(1, 1)),
    }
    wg = np.asarray(g["L1_ffn_w_gate"])
    wu = np.asarray(g["L1_ffn_w_up"])
    wd = np.asarray(g["L1_ffn_w_down"])
    in_maps = []
    for c in range(8):
        m = dict(shared)
        m["x"] = np.ascontiguousarray(x[c].T)
        m["coreid"] = np.array([[c]], dtype=np.float32)
        m["wg_sl"] = np.ascontiguousarray(wg[c * 352:(c + 1) * 352])
        m["wu_sl"] = np.ascontiguousarray(wu[c * 352:(c + 1) * 352])
        m["wd_sl"] = np.ascontiguousarray(wd[c * 128:(c + 1) * 128])
        in_maps.append(m)
    return in_maps


def kernel(**inputs):
    in_maps = host_prep(inputs)
    nc = build_fused()
    res = run_bass_kernel_spmd(nc, in_maps, core_ids=CORES)
    out = np.stack([np.asarray(res.results[c]["out"]).T for c in range(8)])
    return np.ascontiguousarray(out.astype(np.float32))


def kernel_unfused(**inputs):
    g = inputs
    x = np.asarray(g["L0_attn_batch"])
    xT = [np.ascontiguousarray(x[c].T) for c in range(8)]

    nc = build_l0_pre()
    shared = {
        "lg1": np.ascontiguousarray(g["L0_ffn_fc1_logits"].reshape(1, 256)),
        "lg2": np.ascontiguousarray(g["L0_ffn_fc2_logits"].reshape(1, 256)),
        "a1t": np.ascontiguousarray(np.asarray(g["L0_ffn_fc1_A"]).transpose(0, 2, 1)),
        "b1": np.ascontiguousarray(g["L0_ffn_fc1_B"]),
        "a2t": np.ascontiguousarray(np.asarray(g["L0_ffn_fc2_A"]).transpose(0, 2, 1)),
        "b2": np.ascontiguousarray(g["L0_ffn_fc2_B"]),
    }
    in_maps = []
    for c in range(8):
        m = dict(shared)
        m["x"] = xT[c]
        m["coreid"] = np.array([[c]], dtype=np.float32)
        in_maps.append(m)
    r1 = run_bass_kernel_spmd(nc, in_maps, core_ids=CORES)
    xn0 = [np.asarray(r1.results[c]["xn0"]) for c in range(8)]
    w1g = np.ascontiguousarray(np.stack(
        [np.asarray(r1.results[c]["w1sl"]) for c in range(8)]))
    w2g = np.ascontiguousarray(np.concatenate(
        [np.asarray(r1.results[c]["w2sl"]) for c in range(8)], axis=0))

    nc = build_l0_attn()
    shared = {
        "wq_t": np.ascontiguousarray(np.asarray(g["L0_attn_Wq"]).T),
        "wk_t": np.ascontiguousarray(np.asarray(g["L0_attn_Wk"]).T),
        "wv_t": np.ascontiguousarray(np.asarray(g["L0_attn_Wv"]).T),
        "wo_t": np.ascontiguousarray(np.asarray(g["L0_attn_Wo"]).T),
        "bq2": lay_bias(np.asarray(g["L0_attn_bq"])),
        "bk2": lay_bias(np.asarray(g["L0_attn_bk"])),
        "bo2": lay_bias(np.asarray(g["L0_attn_bo"])),
        "bv_row": np.ascontiguousarray(np.asarray(g["L0_attn_bv"]).reshape(1, D)),
    }
    in_maps = [dict(shared, xn=xn0[c]) for c in range(8)]
    r2 = run_bass_kernel_spmd(nc, in_maps, core_ids=CORES)
    a0 = [np.asarray(r2.results[c]["a0"]) for c in range(8)]

    nc = build_l0_ffn()
    in_maps = [{"xn": xn0[c], "w1g": w1g, "w2g": w2g, "x": xT[c], "a0": a0[c]}
               for c in range(8)]
    r3 = run_bass_kernel_spmd(nc, in_maps, core_ids=CORES)
    x1 = [np.asarray(r3.results[c]["x1"]) for c in range(8)]
    xn1 = [np.asarray(r3.results[c]["xn1"]) for c in range(8)]
    mx8 = np.ascontiguousarray(np.concatenate(
        [np.asarray(r3.results[c]["mx"]) for c in range(8)], axis=1))

    nc = build_l1_quant()
    wg = np.asarray(g["L1_ffn_w_gate"])
    wu = np.asarray(g["L1_ffn_w_up"])
    wd = np.asarray(g["L1_ffn_w_down"])
    in_maps = [{
        "wg_sl": np.ascontiguousarray(wg[c * 352:(c + 1) * 352]),
        "wu_sl": np.ascontiguousarray(wu[c * 352:(c + 1) * 352]),
        "wd_sl": np.ascontiguousarray(wd[c * 128:(c + 1) * 128]),
    } for c in range(8)]
    r4 = run_bass_kernel_spmd(nc, in_maps, core_ids=CORES)
    wgt = np.ascontiguousarray(np.concatenate(
        [np.asarray(r4.results[c]["wgt_sl"]) for c in range(8)], axis=1))
    wut = np.ascontiguousarray(np.concatenate(
        [np.asarray(r4.results[c]["wut_sl"]) for c in range(8)], axis=1))
    wdt = np.ascontiguousarray(np.concatenate(
        [np.asarray(r4.results[c]["wdt_sl"]) for c in range(8)], axis=1))
    sg = np.ascontiguousarray(np.concatenate(
        [np.asarray(r4.results[c]["sg_sl"]) for c in range(8)], axis=1))
    su = np.ascontiguousarray(np.concatenate(
        [np.asarray(r4.results[c]["su_sl"]) for c in range(8)], axis=1))
    sd = np.ascontiguousarray(np.concatenate(
        [np.asarray(r4.results[c]["sd_sl"]) for c in range(8)], axis=1))

    nc = build_l1_attn()
    shared = {
        "w_in": np.ascontiguousarray(g["L1_attn_W_in"]),
        "w_r": np.ascontiguousarray(g["L1_attn_W_r"]),
        "w_i": np.ascontiguousarray(g["L1_attn_W_i"]),
        "w_out": np.ascontiguousarray(g["L1_attn_W_out"]),
        "b_in2": lay_bias(np.asarray(g["L1_attn_b_in"])),
        "b_r2": lay_bias(np.asarray(g["L1_attn_b_r"])),
        "b_i2": lay_bias(np.asarray(g["L1_attn_b_i"])),
        "b_out2": lay_bias(np.asarray(g["L1_attn_b_out"])),
        "log_a2": lay_bias(np.asarray(g["L1_attn_log_a"])),
    }
    in_maps = [dict(shared, xn=xn1[c]) for c in range(8)]
    r5 = run_bass_kernel_spmd(nc, in_maps, core_ids=CORES)
    a1 = [np.asarray(r5.results[c]["a1"]) for c in range(8)]

    nc = build_l1_ffn_a()
    clip_x = np.ascontiguousarray(np.asarray(g["L1_ffn_clip_x"]).reshape(1, 1))
    clip_d = np.ascontiguousarray(np.asarray(g["L1_ffn_clip_down"]).reshape(1, 1))
    shared = {"mx8": mx8, "clip_x": clip_x, "wgt": wgt, "wut": wut,
              "sg": sg, "su": su}
    in_maps = [dict(shared, xn=xn1[c]) for c in range(8)]
    r6 = run_bass_kernel_spmd(nc, in_maps, core_ids=CORES)
    h1 = [np.asarray(r6.results[c]["h1"]) for c in range(8)]
    hm8 = np.ascontiguousarray(np.concatenate(
        [np.asarray(r6.results[c]["hmax"]) for c in range(8)], axis=1))

    nc = build_l1_ffn_b()
    shared = {"hmax8": hm8, "clip_d": clip_d, "wdt": wdt, "sd": sd}
    in_maps = [dict(shared, h1=h1[c], x1=x1[c], a1=a1[c]) for c in range(8)]
    r7 = run_bass_kernel_spmd(nc, in_maps, core_ids=CORES)
    out = np.stack([np.asarray(r7.results[c]["out"]).T for c in range(8)])
    return np.ascontiguousarray(out.astype(np.float32))


# version 09
# speedup vs baseline: 1.3852x; 1.0200x over previous
import numpy as np
from contextlib import ExitStack
import concourse.bass as bass
import concourse.mybir as mybir
from concourse.bass_core_rust import IndirectOffsetOnAxis
from concourse.bass_utils import run_bass_kernel_spmd

F32 = mybir.dt.float32
BF16 = mybir.dt.bfloat16
F16 = mybir.dt.float16
I8 = mybir.dt.int8
I32 = mybir.dt.int32
U32 = mybir.dt.uint32
AF = mybir.ActivationFunctionType
ALU = mybir.AluOpType
AX = mybir.AxisListType

S = 2048
D = 1024
DC = 8
NT = 16
QB = 512
NQB = S // QB
ATT_SCALE = 1.0 / 32.0


class Ctr:
    __slots__ = ("h", "v", "name")

    def __init__(self, h, name):
        self.h = h
        self.v = 0
        self.name = name

    def tok(self):
        return (self, self.v)


class DmaPool:

    def __init__(self, subs):
        self.subs = subs
        self.i = 0
        self.last = None

    def nxt(self):
        c = self.subs[self.i]
        self.i = (self.i + 1) % len(self.subs)
        return c

    def tok(self):
        return self.last

    def all(self):
        return [(c, c.v) for c in self.subs if c.v > 0]


class Prog:
    def __init__(self):
        self.nc = bass.Bass("TRN2", target_bir_lowering=False)
        self.es = ExitStack()

    def sb(self, name, shape, dt=F32):
        return self.es.enter_context(self.nc.sbuf_tensor(name, shape, dt))

    def psum(self, name, shape, dt=F32):
        return self.es.enter_context(self.nc.psum_tensor(name, shape, dt))

    def sem(self, name):
        return Ctr(self.es.enter_context(self.nc.semaphore(name)), name)

    def din(self, name, shape, dt=F32):
        return self.nc.dram_tensor(name, shape, dt, kind="ExternalInput").ap()

    def dout(self, name, shape, dt=F32):
        return self.nc.dram_tensor(name, shape, dt, kind="ExternalOutput").ap()

    def dtmp(self, name, shape, dt=F32):
        return self.nc.dram_tensor(name, shape, dt, kind="Internal").ap()

    def fin(self):
        self.es.close()
        return self.nc


ENGS = ("t", "v", "s", "g", "y")


class Plan:

    def __init__(self, P, own=None):
        self.P = P
        self.q = {e: [] for e in ENGS}
        self.last_wait = {e: {} for e in ENGS}
        self.own = own or {}

    def step(self, eng, emit, waits=(), bump=None, n=1):
        own = self.own.get(eng)
        waits = list(waits)
        if isinstance(bump, DmaPool):
            pool = bump
            bump = pool.nxt()
            waits.append(bump.tok())
            pool.last = (bump, bump.v + n)
        if own is not None:
            waits.append(own.tok())
            if bump is None:
                bump = own
                n = 1
        wl = []
        flat = []

        def _fl(ws):
            for w in ws:
                if isinstance(w, list):
                    _fl(w)
                else:
                    flat.append(w)
        _fl(waits)
        for w in flat:
            if w is None:
                continue
            c, v = w if isinstance(w, tuple) else (w, w.v)
            assert v <= c.v, f"wait on future value of {c.name}"
            if v <= 0:
                continue
            lw = self.last_wait[eng].get(c, 0)
            if v <= lw:
                continue
            self.last_wait[eng][c] = v
            wl.append((c, v))
        if bump is not None:
            bump.v += n
        self.q[eng].append((wl, emit, bump, n))

    def emit_all(self, block):
        def runner(lst):
            def run(e):
                for wl, emit, bmp, n in lst:
                    for c, v in wl:
                        e.wait_ge(c.h, v)
                    ins = emit(e)
                    if bmp is not None:
                        assert ins is not None
                        ins.then_inc(bmp.h, n)
            return run
        for key, meth in (("y", block.sync), ("g", block.gpsimd),
                          ("t", block.tensor), ("v", block.vector),
                          ("s", block.scalar)):
            if self.q[key]:
                meth(runner(self.q[key]))


class NS:
    def __init__(self, **kw):
        self.__dict__.update(kw)



def alloc_common(P):
    B = NS()
    B.c_pe = P.sem("c_pe")
    B.c_v = P.sem("c_v")
    B.c_a = P.sem("c_a")
    B.c_g = P.sem("c_g")
    B.c_ld = DmaPool([P.sem(f"c_ld{i}") for i in range(6)])
    B.c_wg = DmaPool([P.sem(f"c_wg{i}") for i in range(6)])
    B.c_st = DmaPool([P.sem(f"c_st{i}") for i in range(4)])
    B.c_sd = DmaPool([P.sem(f"c_sd{i}") for i in range(4)])
    B.c_cc = P.sem("c_cc")
    B.iotat = P.sb("iotat", [128, 128], I32)
    B.ident_f = P.sb("ident_f", [128, 128], F32)
    B.ones128 = P.sb("ones128", [128, 128], BF16)
    B.ones_c = P.sb("ones_c", [128, 1], BF16)
    B.ones_r = P.sb("ones_r", [1, 128], BF16)
    B.ones_rf = P.sb("ones_rf", [1, 128], F32)
    B.AV = [P.psum(f"av{i}", [128, 512], F32) for i in range(4)]
    B.SC = [P.psum(f"sc{i}", [128, 512], F32) for i in range(2)]
    B.AUX = P.psum("aux", [128, 512], F32)
    B.NRM = P.psum("nrm", [128, 512], F32)
    B.bank_tok = {}
    return B


def emit_setup(pl, B):
    pl.step("g", lambda g: (g.iota(B.iotat[:], [[1, 128]], base=0,
                            channel_multiplier=-1), g.engine_nop())[1])
    pl.step("g", lambda g: g.tensor_scalar(B.ident_f[:], B.iotat[:], 0, None, ALU.is_equal))
    pl.step("g", lambda g: g.memset(B.ones128[:], 1.0), bump=B.c_g)
    pl.step("v", lambda v: v.memset(B.ones_c[:], 1.0))
    pl.step("v", lambda v: v.memset(B.ones_r[:], 1.0))
    pl.step("v", lambda v: v.memset(B.ones_rf[:], 1.0), bump=B.c_v)
    B.tok_setup_g = B.c_g.tok()
    B.tok_setup_v = B.c_v.tok()


def mk_plan(P, B):
    return Plan(P, own={"v": B.c_v, "s": B.c_a, "g": B.c_g})


def bank_wait(B, ps):
    t = B.bank_tok.get(id(ps))
    return [t] if t is not None else []


def bank_set(B, ps, tok):
    B.bank_tok[id(ps)] = tok



def alloc_l0_attn(P, B):
    B.XN = P.sb("XN", [128, DC * S], BF16)
    B.QF = P.sb("QF", [128, DC * S], BF16)
    B.KF = P.sb("KF", [128, DC * S], BF16)
    B.VT = P.sb("VT", [128, NT * D], BF16)
    B.STASH = P.sb("STASH", [128, 8 * 512], F16)
    B.ATT = B.STASH.bitcast(BF16)
    B.EXPB = P.sb("EXPB", [128, NT * 512], BF16)
    B.WARE = [P.sb(f"ware{i}", [128, DC * 256], BF16) for i in range(2)]
    B.RZ = P.sb("RZN", [128, 512], F32)
    B.RN = B.RZ
    B.RZR = P.sb("RZR", [1, 512], F32)
    B.MXR = P.sb("MXR", [1, 8], F32)
    B.NCR = P.sb("NCR", [1, 8], F32)
    B.CBS = P.sb("CBS", [128, 8], F32)
    B.BQ = P.sb("BQ", [128, DC], F32)
    B.BK = P.sb("BK", [128, DC], F32)
    B.BO = P.sb("BO", [128, DC], F32)
    B.BVR = P.sb("BVR", [1, D], BF16)
    B.BVB = P.sb("BVB", [128, D], BF16)
    B.A0T = [P.sb(f"a0t{i}", [128, 512], F32) for i in range(2)]


def load_xn(pl, B, xn_dram):
    pl.step("y", lambda y: y.dma_start(
        out=B.XN[:].rearrange("p (c t) -> p c t", c=DC),
        in_=xn_dram.rearrange("(c p) t -> p c t", p=128)), bump=B.c_ld, n=16)
    return B.c_ld.tok()


def emit_l0_attn(pl, B, wq_t, wk_t, wv_t, wo_t, bq2, bk2, bv_row, bo2,
                 a0_dram, tok_xn, guard=None):
    for buf, src in ((B.BQ, bq2), (B.BK, bk2), (B.BO, bo2)):
        pl.step("y", lambda y, buf=buf, src=src: y.dma_start(out=buf[:], in_=src),
                bump=B.c_ld, n=16)
    tok_bias_l = B.c_ld.all()
    pl.step("g", lambda g: g.dma_start(out=B.BVR[:], in_=bv_row),
            bump=B.c_wg, n=16)
    tok_bvr = B.c_wg.tok()
    pl.step("v", lambda v: v.tensor_scalar(B.BQ[:], B.BQ[:], ATT_SCALE, None, ALU.mult),
            waits=tok_bias_l, bump=B.c_v)
    tok_bq = B.c_v.tok()
    for h in range(2):
        sl = slice(h * 512, (h + 1) * 512)
        pl.step("t", lambda t, sl=sl: t.matmul(
            B.AUX[:], B.ones_r[:], B.BVR[0:1, sl], start=True, stop=True),
            waits=[tok_bvr, B.tok_setup_v] + bank_wait(B, B.AUX), bump=B.c_pe)
        pl.step("s", lambda s, sl=sl: s.copy(B.BVB[:, sl], B.AUX[:]),
                waits=[B.c_pe.tok()], bump=B.c_a)
        bank_set(B, B.AUX, B.c_a.tok())
    tok_bvb = B.c_a.tok()

    def load_wq(wdram, h, buf, waits=()):
        def go(g):
            return g.dma_start(
                out=buf[:].rearrange("p (c i) -> p c i", c=DC),
                in_=wdram[:, h * 256:(h + 1) * 256].rearrange("(c p) i -> p c i", p=128))
        pl.step("g", go, waits=list(waits), bump=B.c_wg, n=16)
        return B.c_wg.tok()

    rr = [0]

    def next_bank():
        b = B.AV[rr[0]]
        rr[0] = (rr[0] + 1) % 4
        return b

    seqw = [("q", wq_t), ("k", wk_t), ("v", wv_t), ("o", wo_t)]
    order = [(nm, h) for nm, _ in seqw for h in range(4)]
    dram_of = dict(seqw)
    wtok = {order[0]: load_wq(wq_t, 0, B.WARE[0]),
            order[1]: load_wq(wq_t, 1, B.WARE[1])}

    def prefetch_after(idx):
        if idx + 2 < len(order):
            nm, nh = order[idx + 2]
            wtok[(nm, nh)] = load_wq(dram_of[nm], nh, B.WARE[idx % 2],
                                     waits=[B.c_pe.tok()])

    oi = 0
    for name, bias, dstb in (("q", B.BQ, B.QF), ("k", B.BK, B.KF)):
        for h in range(4):
            wt = wtok[(name, h)]
            for ic in range(2 * h, 2 * h + 2):
                icl = (ic % 2) * 128
                for qb in range(NQB):
                    bank = next_bank()

                    def pe(t, oi=oi, icl=icl, qb=qb, bank=bank):
                        last = None
                        for dk in range(DC):
                            last = t.matmul(
                                bank[:],
                                B.WARE[oi % 2][:, dk * 256 + icl: dk * 256 + icl + 128],
                                B.XN[:, dk * S + qb * QB: dk * S + (qb + 1) * QB],
                                start=(dk == 0), stop=(dk == DC - 1))
                        return last
                    pl.step("t", pe, waits=[wt, list(tok_xn)] + bank_wait(B, bank),
                            bump=B.c_pe)
                    ptok = B.c_pe.tok()
                    scl = ATT_SCALE if name == "q" else 1.0

                    def ev(s, bank=bank, ic=ic, qb=qb, bias=bias, dstb=dstb,
                           scl=scl):
                        return s.activation(
                            dstb[:, ic * S + qb * QB: ic * S + (qb + 1) * QB],
                            bank[:], AF.Identity, bias=bias[:, ic:ic + 1],
                            scale=scl)
                    pl.step("s", ev, waits=[ptok, tok_bq, guard] + tok_bias_l,
                            bump=B.c_a)
                    bank_set(B, bank, B.c_a.tok())
            prefetch_after(oi)
            oi += 1
    for h in range(4):
        wt = wtok[("v", h)]
        wb = oi % 2
        for tc in range(NT):
            bank = next_bank()

            def pe(t, wb=wb, tc=tc, bank=bank):
                last = None
                for dk in range(DC):
                    last = t.matmul(
                        bank[0:128, 0:256],
                        B.XN[:, dk * S + tc * 128: dk * S + tc * 128 + 128],
                        B.WARE[wb][:, dk * 256:(dk + 1) * 256],
                        start=(dk == 0), stop=(dk == DC - 1))
                return last
            pl.step("t", pe, waits=[wt, list(tok_xn)] + bank_wait(B, bank), bump=B.c_pe)
            ptok = B.c_pe.tok()

            def ev(v, h=h, tc=tc, bank=bank):
                return v.tensor_tensor(
                    B.VT[:, tc * D + h * 256: tc * D + (h + 1) * 256],
                    bank[0:128, 0:256], B.BVB[:, h * 256:(h + 1) * 256], ALU.add)
            pl.step("v", ev, waits=[ptok, tok_bvb], bump=B.c_v)
            bank_set(B, bank, B.c_v.tok())
        prefetch_after(oi)
        oi += 1
    tok_qf = B.c_a.tok()
    tok_vt = B.c_v.tok()

    expb_guard = tok_vt
    att_guard = None
    rzr_g = [None]
    slot_tok = [None] * 8
    for qb in range(NQB):
        q0 = qb * QB
        whtok = [None] * NT
        stok = [None] * NT
        ntok = [None] * NT
        etok = [None] * NT
        rtok = [None] * NT

        def emit_score(t_, q0=None):
            bank = B.SC[t_ % 2]

            def pe(t, t_=t_, q0=q0, bank=bank):
                last = None
                for dk in range(DC):
                    last = t.matmul(
                        bank[:],
                        B.KF[:, dk * S + t_ * 128: dk * S + t_ * 128 + 128],
                        B.QF[:, dk * S + q0: dk * S + q0 + QB],
                        start=(dk == 0), stop=(dk == DC - 1))
                return last
            pl.step("t", pe, waits=[tok_qf] + bank_wait(B, bank), bump=B.c_pe)
            ptok = B.c_pe.tok()
            tt = t_ % 8
            st_sl = slice(tt * 512, (tt + 1) * 512)
            pl.step("s", lambda s, bank=bank, st_sl=st_sl:
                    s.copy(B.STASH[:, st_sl], bank[:]),
                    waits=[ptok, slot_tok[tt]], bump=B.c_a)
            stok[t_] = B.c_a.tok()
            bank_set(B, bank, B.c_a.tok())

        def emit_max(t_):
            tt = t_ % 8
            st_sl = slice(tt * 512, (tt + 1) * 512)
            pl.step("g", lambda g, tt=tt, st_sl=st_sl: g.reduce_max(
                B.MXR[0:1, tt:tt + 1], B.STASH[:, st_sl], axis=AX.XYZWC),
                waits=[stok[t_]])
            pl.step("g", lambda g, tt=tt: g.tensor_scalar(
                B.NCR[0:1, tt:tt + 1], B.MXR[0:1, tt:tt + 1], -1.0, None,
                ALU.mult), bump=B.c_g)
            ntok[t_] = B.c_g.tok()
            slot_tok[tt] = B.c_g.tok()

        def emit_exp(t_):
            tt = t_ % 8
            st_sl = slice(tt * 512, (tt + 1) * 512)
            e_sl = slice(t_ * 512, (t_ + 1) * 512)
            pl.step("t", lambda t, tt=tt: t.matmul(
                B.AV[0][:, 0:1], B.ones_rf[:], B.NCR[0:1, tt:tt + 1],
                start=True, stop=True),
                waits=[ntok[t_], B.tok_setup_v] + bank_wait(B, B.AV[0]),
                bump=B.c_pe)
            cbtok = B.c_pe.tok()
            pl.step("s", lambda s, tt=tt: s.copy(B.CBS[:, tt:tt + 1],
                                                 B.AV[0][:, 0:1]),
                    waits=[cbtok], bump=B.c_a)
            bank_set(B, B.AV[0], B.c_a.tok())
            pl.step("s", lambda s, tt=tt, e_sl=e_sl, st_sl=st_sl: s.activation(
                B.EXPB[:, e_sl], B.STASH[:, st_sl], AF.Exp,
                bias=B.CBS[:, tt:tt + 1], scale=1.0),
                waits=[expb_guard], bump=B.c_a)
            etok[t_] = B.c_a.tok()

        def emit_z(t_):
            e_sl = slice(t_ * 512, (t_ + 1) * 512)
            pl.step("t", lambda t, e_sl=e_sl: t.matmul(
                B.AUX[0:1, 0:QB], B.ones_c[:], B.EXPB[:, e_sl],
                start=True, stop=True),
                waits=[etok[t_], B.tok_setup_v] + bank_wait(B, B.AUX),
                bump=B.c_pe)
            ztok = B.c_pe.tok()
            pl.step("v", lambda v: v.reciprocal(B.RZR[:], B.AUX[0:1, 0:QB]),
                    waits=[ztok, rzr_g[0]], bump=B.c_v)
            rtok[t_] = B.c_v.tok()
            bank_set(B, B.AUX, B.c_v.tok())

        def emit_bcast(t_):
            e_sl = slice(t_ * 512, (t_ + 1) * 512)
            pl.step("t", lambda t: t.matmul(
                B.NRM[:], B.ones_rf[:], B.RZR[:], start=True, stop=True),
                waits=[rtok[t_]] + bank_wait(B, B.NRM), bump=B.c_pe)
            btok = B.c_pe.tok()
            rzr_g[0] = btok
            pl.step("v", lambda v, e_sl=e_sl: v.tensor_tensor(
                B.EXPB[:, e_sl], B.EXPB[:, e_sl], B.NRM[:], ALU.mult),
                waits=[btok], bump=B.c_v)
            whtok[t_] = B.c_v.tok()
            bank_set(B, B.NRM, B.c_v.tok())

        for t_ in range(NT):
            emit_score(t_, q0)
            if t_ >= 1:
                emit_max(t_ - 1)
            if t_ >= 2:
                emit_exp(t_ - 2)
            if t_ >= 4:
                emit_bcast(t_ - 4)
            if t_ >= 3:
                emit_z(t_ - 3)
        emit_max(NT - 1)
        emit_exp(NT - 2)
        emit_exp(NT - 1)
        emit_bcast(NT - 4)
        emit_z(NT - 3)
        emit_bcast(NT - 3)
        emit_z(NT - 2)
        emit_bcast(NT - 2)
        emit_z(NT - 1)
        emit_bcast(NT - 1)
        tok_rn = None
        for p in range(2):
            for t_ in range(NT):
                def pe(t, p=p, t_=t_):
                    last = None
                    for md in range(4):
                        dch = p * 4 + md
                        last = t.matmul(
                            B.AV[md][:],
                            B.VT[:, t_ * D + dch * 128: t_ * D + dch * 128 + 128],
                            B.EXPB[:, t_ * 512:(t_ + 1) * 512],
                            start=(t_ == 0), stop=(t_ == NT - 1))
                    if p == 0:
                        last = t.matmul(B.NRM[:], B.ones128[:],
                                        B.EXPB[:, t_ * 512:(t_ + 1) * 512],
                                        start=(t_ == 0), stop=(t_ == NT - 1))
                    return last
                w = [whtok[t_], tok_vt]
                if t_ == 0:
                    for md in range(4):
                        w += bank_wait(B, B.AV[md])
                    if p == 0:
                        w += bank_wait(B, B.NRM)
                pl.step("t", pe, waits=w, bump=B.c_pe)
            tok_av = B.c_pe.tok()
            if p == 0:
                pl.step("v", lambda v: v.tensor_scalar(B.RN[:], B.NRM[:],
                                                       1e-6, None, ALU.add),
                        waits=[tok_av], bump=B.c_v)
                pl.step("v", lambda v: v.reciprocal(B.RN[:], B.RN[:]),
                        waits=[B.c_v.tok()], bump=B.c_v)
                bank_set(B, B.NRM, B.c_v.tok())
                tok_rn = B.c_v.tok()
            for md in range(4):
                dch = p * 4 + md
                pl.step("v", lambda v, md=md, dch=dch: v.tensor_tensor(
                    B.ATT[:, dch * 512:(dch + 1) * 512], B.AV[md][:], B.RN[:],
                    ALU.mult),
                    waits=[tok_av, tok_rn, att_guard], bump=B.c_v)
                bank_set(B, B.AV[md], B.c_v.tok())
        tok_att = B.c_v.tok()
        expb_guard = B.c_pe.tok()
        for qh in range(4):
            wt = wtok[("o", qh)]
            wb = qh % 2
            for ic in range(2 * qh, 2 * qh + 2):
                bank = B.SC[ic % 2]
                icl = (ic % 2) * 128

                def pe(t, wb=wb, icl=icl, bank=bank):
                    last = None
                    for dk in range(DC):
                        last = t.matmul(
                            bank[:],
                            B.WARE[wb][:, dk * 256 + icl: dk * 256 + icl + 128],
                            B.ATT[:, dk * 512:(dk + 1) * 512],
                            start=(dk == 0), stop=(dk == DC - 1))
                    return last
                pl.step("t", pe, waits=[wt, tok_att] + bank_wait(B, bank),
                        bump=B.c_pe)
                ptok = B.c_pe.tok()
                stg = B.A0T[ic % 2]
                pl.step("s", lambda s, bank=bank, ic=ic, stg=stg: s.activation(
                    stg[:], bank[:], AF.Identity, bias=B.BO[:, ic:ic + 1], scale=1.0),
                    waits=[ptok] + bank_wait(B, stg), bump=B.c_a)
                bank_set(B, bank, B.c_a.tok())
                tok_stg = B.c_a.tok()
                pl.step("y", lambda y, ic=ic, q0=q0, stg=stg: y.dma_start(
                    out=a0_dram[ic * 128:(ic + 1) * 128, q0:q0 + QB], in_=stg[:]),
                    waits=[tok_stg], bump=B.c_st, n=16)
                bank_set(B, stg, B.c_st.tok())
            nq = (qh + 2) % 4
            if not (qb == NQB - 1 and qh >= 2):
                wtok[("o", nq)] = load_wq(wo_t, nq, B.WARE[wb],
                                          waits=[B.c_pe.tok()])
        att_guard = B.c_pe.tok()
    return B.c_st.all()



def alloc_ln(P, B):
    B.X2B = [P.sb(f"x2b{i}", [128, 512], BF16) for i in range(2)]
    B.ROWA = P.sb("rowa", [1, 512], F32)
    B.ROWB = P.sb("rowb", [1, 512], F32)
    B.RSTB = P.sb("rstb", [128, 512], F32)
    B.MRB = P.sb("mrb", [128, 512], F32)
    B.EPSB = P.sb("epsb", [1, 1], F32)
    B.eps_init = False


def emit_ln(pl, B, x_dram, wait=None):
    B.ln_xn_guard = ()
    if not B.eps_init:
        B.eps_init = True
        pl.step("v", lambda v: v.memset(B.EPSB[:], 1e-5), bump=B.c_v)
        B.tok_eps = B.c_v.tok()
    wl = wait if isinstance(wait, list) else [wait]
    pl.step("g", lambda g: g.dma_start(
        out=B.XN[:].rearrange("p (c t) -> p c t", c=DC),
        in_=x_dram.rearrange("(c p) t -> p c t", p=128)),
        waits=wl, bump=B.c_wg, n=16)
    tok_x = B.c_wg.tok()
    x2tok = [None, None]
    for qb in range(NQB):
        q0 = qb * QB
        for c in range(DC):
            xb_sl = slice(c * S + q0, c * S + q0 + QB)
            pl.step("s", lambda s, xb_sl=xb_sl, c=c: s.activation(
                B.X2B[c % 2][:], B.XN[:, xb_sl], AF.Square, scale=1.0),
                waits=[tok_x, x2tok[c % 2]], bump=B.c_a)
            tok_sq = B.c_a.tok()

            def pe(t, xb_sl=xb_sl, c=c):
                t.matmul(B.AUX[0:1, :], B.ones_c[:], B.XN[:, xb_sl],
                         start=(c == 0), stop=(c == DC - 1))
                return t.matmul(B.NRM[0:1, :], B.ones_c[:], B.X2B[c % 2][:],
                                start=(c == 0), stop=(c == DC - 1))
            w = [tok_sq, B.tok_setup_v]
            if c == 0:
                w += bank_wait(B, B.AUX) + bank_wait(B, B.NRM)
            pl.step("t", pe, waits=w, bump=B.c_pe)
            x2tok[c % 2] = B.c_pe.tok()
        tok_sums = B.c_pe.tok()
        pl.step("v", lambda v: v.tensor_scalar(B.ROWA[:], B.AUX[0:1, :],
                                               1.0 / D, None, ALU.mult),
                waits=[tok_sums] + bank_wait(B, B.ROWA), bump=B.c_v)
        bank_set(B, B.AUX, B.c_v.tok())
        pl.step("v", lambda v: v.tensor_scalar(B.ROWB[:], B.NRM[0:1, :],
                                               1.0 / D, None, ALU.mult),
                waits=[tok_sums] + bank_wait(B, B.ROWB), bump=B.c_v)
        bank_set(B, B.NRM, B.c_v.tok())
        pl.step("s", lambda s: s.activation(B.AUX[0:1, :], B.ROWA[:],
                                            AF.Square, scale=1.0),
                waits=[B.c_v.tok()] + bank_wait(B, B.AUX), bump=B.c_a)
        pl.step("v", lambda v: v.tensor_tensor(B.ROWB[:], B.ROWB[:],
                                               B.AUX[0:1, :], ALU.subtract),
                waits=[B.c_a.tok()], bump=B.c_v)
        bank_set(B, B.AUX, B.c_v.tok())
        tok_var = B.c_v.tok()
        pl.step("s", lambda s: s.activation(B.ROWB[:], B.ROWB[:], AF.Sqrt,
                                            bias=B.EPSB[0:1, 0:1], scale=1.0),
                waits=[tok_var, B.tok_eps], bump=B.c_a)
        pl.step("v", lambda v: v.reciprocal(B.ROWB[:], B.ROWB[:]),
                waits=[B.c_a.tok()], bump=B.c_v)
        pl.step("v", lambda v: v.tensor_tensor(B.ROWA[:], B.ROWA[:], B.ROWB[:],
                                               ALU.mult), bump=B.c_v)
        tok_rows = B.c_v.tok()
        for row, dstb in ((B.ROWB, B.RSTB), (B.ROWA, B.MRB)):
            pl.step("t", lambda t, row=row: t.matmul(
                B.AUX[:], B.ones_rf[:], row[0:1, :], start=True, stop=True),
                waits=[tok_rows] + bank_wait(B, B.AUX), bump=B.c_pe)
            pl.step("s", lambda s, dstb=dstb: s.copy(dstb[:], B.AUX[:]),
                    waits=[B.c_pe.tok()] + list(B.ln_xn_guard)
                    + bank_wait(B, dstb), bump=B.c_a)
            bank_set(B, B.AUX, B.c_a.tok())
            bank_set(B, dstb, B.c_a.tok())
        tok_b = B.c_a.tok()
        for c in range(DC):
            eng = "v" if c % 2 == 0 else "g"
            xb_sl = slice(c * S + q0, c * S + q0 + QB)
            pl.step(eng, lambda e, xb_sl=xb_sl: e.tensor_tensor(
                B.XN[:, xb_sl], B.XN[:, xb_sl], B.RSTB[:], ALU.mult),
                waits=[tok_b, tok_sums])
            pl.step(eng, lambda e, xb_sl=xb_sl: e.tensor_tensor(
                B.XN[:, xb_sl], B.XN[:, xb_sl], B.MRB[:], ALU.subtract),
                bump=B.c_v if eng == "v" else B.c_g)
        B.ln_xn_guard = (B.c_v.tok(), B.c_g.tok())
    return (B.c_v.tok(), B.c_g.tok())



def alloc_compose(P, B):
    B.LG = P.sb("LG", [1, 256], F32)
    B.LGW = P.sb("LGW", [1, 256], F32)
    B.MV = P.sb("MV", [1, 32], F32)
    B.IV = P.sb("IV", [1, 32], U32)
    B.IVF = P.sb("IVF", [1, 32], F32)
    B.EW = P.sb("EW", [1, 32], F32)
    B.SWr = P.sb("SWr", [1, 2], F32)
    B.W32 = P.sb("W32", [1, 32], F32)
    B.SELB = P.sb("SELB", [128, 32], F32)
    B.W32B = P.sb("W32B", [128, 32], F32)
    B.PEVB = P.sb("PEVB", [128, 32], F32)
    B.EOT = P.sb("EOT", [128, 32], F32)
    B.UB = P.sb("UB", [128, 32], F32)
    B.U2 = P.sb("U2", [128, 16], F32)
    B.MS = P.sb("MS", [128, 16], F32)
    B.IDXAf = P.sb("IDXAf", [128, 16], F32)
    B.IDXA = P.sb("IDXA", [128, 16], U32)
    B.IDX8 = P.sb("IDX8", [128, 16], U32)
    B.QM64 = P.sb("QM64", [128, 1], F32)
    B.QI = P.sb("QI", [128, 1], I32)
    B.COREB = P.sb("COREB", [128, 1], F32)
    B.GA = [P.sb(f"ga{i}", [128, 1024], BF16) for i in range(2)]
    B.GAS = [P.sb(f"gas{i}", [128, 1024], BF16) for i in range(2)]
    B.GB = [P.sb(f"gb{i}", [128, 1024], BF16) for i in range(2)]
    B.STG = [P.sb(f"stg{i}", [128, 512], BF16) for i in range(2)]


def emit_compose_setup(pl, B, coreid):
    pl.step("g", lambda g: (g.iota(B.QI[:], [[0, 1]], base=0,
                            channel_multiplier=1), g.engine_nop())[1])
    pl.step("g", lambda g: (g.iota(
        B.iotat[:, 0:32].rearrange("p (a b) -> p a b", b=2),
        [[0, 16], [1, 2]], base=0, channel_multiplier=0), g.engine_nop())[1])
    tok_io = B.c_g.tok()
    pl.step("v", lambda v: v.tensor_scalar(B.QM64[:], B.QI[:], 64, None,
                                           ALU.is_ge), waits=[tok_io])
    pl.step("v", lambda v: v.scalar_tensor_tensor(B.QM64[:], B.QM64[:], -64.0,
                                                  B.QI[:], ALU.mult, ALU.add))
    pl.step("v", lambda v: v.tensor_scalar(B.EOT[:], B.iotat[:, 0:32], 0, None,
                                           ALU.is_equal), bump=B.c_v)
    pl.step("v", lambda v: v.tensor_copy(B.PEVB[0:64, :], B.EOT[0:64, :]))
    pl.step("v", lambda v: v.tensor_scalar(B.PEVB[64:128, :], B.EOT[64:128, :],
                                           1.0, None, ALU.subtract))
    pl.step("v", lambda v: v.tensor_scalar(B.PEVB[64:128, :], B.PEVB[64:128, :],
                                           -1.0, None, ALU.mult), bump=B.c_v)
    pl.step("g", lambda g: g.dma_start(out=B.SWr[0:1, 0:1], in_=coreid),
            bump=B.c_wg, n=16)
    pl.step("t", lambda t: t.matmul(B.AUX[:, 0:1], B.ones_rf[:], B.SWr[0:1, 0:1],
                                    start=True, stop=True),
            waits=[B.c_wg.tok(), B.tok_setup_v] + bank_wait(B, B.AUX),
            bump=B.c_pe)
    pl.step("s", lambda s: s.copy(B.COREB[:], B.AUX[:, 0:1]),
            waits=[B.c_pe.tok()], bump=B.c_a)
    bank_set(B, B.AUX, B.c_a.tok())
    B.tok_cmp_setup = (B.c_v.tok(), B.c_g.tok(), B.c_a.tok(), B.c_pe.tok())


def emit_topk_idx(pl, B, lg_row, guard=None):
    pl.step("y", lambda y: y.dma_start(out=B.LG[:], in_=lg_row), bump=B.c_ld, n=16)
    tok_lg = B.c_ld.tok()
    pl.step("v", lambda v: v.tensor_copy(B.LGW[:], B.LG[:]),
            waits=[tok_lg, B.tok_cmp_setup[0]] + (guard or []), bump=B.c_v)
    for r in range(4):
        sl = slice(r * 8, (r + 1) * 8)
        pl.step("v", lambda v, sl=sl: v.max(out=B.MV[:, sl], in_=B.LGW[:]),
                waits=[B.c_v.tok()], bump=B.c_v)
        pl.step("v", lambda v, sl=sl: v.max_index(out=B.IV[:, sl],
                                                  in_max=B.MV[:, sl],
                                                  in_values=B.LGW[:]),
                waits=[B.c_v.tok()], bump=B.c_v)
        pl.step("v", lambda v, sl=sl: v.match_replace(
            out=B.LGW[:], in_to_replace=B.MV[:, sl], in_values=B.LGW[:],
            imm_value=-1e30), waits=[B.c_v.tok()], bump=B.c_v)
    pl.step("v", lambda v: v.tensor_scalar(B.SWr[0:1, 0:1], B.MV[0:1, 0:1],
                                           -1.0, None, ALU.mult),
            waits=[B.c_v.tok(), B.tok_cmp_setup[3]], bump=B.c_v)
    pl.step("s", lambda s: s.activation(B.EW[:], B.MV[:], AF.Exp,
                                        bias=B.SWr[0:1, 0:1], scale=1.0),
            waits=[B.c_v.tok()], bump=B.c_a)
    pl.step("v", lambda v: v.reduce_sum(B.SWr[0:1, 1:2], B.EW[:], axis=AX.X),
            waits=[B.c_a.tok()], bump=B.c_v)
    pl.step("v", lambda v: v.reciprocal(B.SWr[0:1, 1:2], B.SWr[0:1, 1:2]),
            waits=[B.c_v.tok()], bump=B.c_v)
    pl.step("v", lambda v: v.tensor_scalar(B.W32[:], B.EW[:], B.SWr[0:1, 1:2],
                                           None, ALU.mult),
            waits=[B.c_v.tok()], bump=B.c_v)
    pl.step("v", lambda v: v.tensor_copy(B.IVF[:], B.IV[:]), bump=B.c_v)
    tok_rows = B.c_v.tok()
    for row, dstb in ((B.IVF, B.SELB), (B.W32, B.W32B)):
        pl.step("t", lambda t, row=row: t.matmul(
            B.AUX[:, 0:32], B.ones_rf[:], row[0:1, :], start=True, stop=True),
            waits=[tok_rows] + bank_wait(B, B.AUX), bump=B.c_pe)
        pl.step("s", lambda s, dstb=dstb: s.copy(dstb[:], B.AUX[:, 0:32]),
                waits=[B.c_pe.tok()], bump=B.c_a)
        bank_set(B, B.AUX, B.c_a.tok())
    tok_bc = B.c_a.tok()
    pl.step("v", lambda v: v.tensor_tensor(B.UB[:], B.SELB[:], B.PEVB[:],
                                           ALU.mult), waits=[tok_bc], bump=B.c_v)
    pl.step("v", lambda v: v.tensor_tensor(B.U2[:], B.UB[:, 0::2], B.UB[:, 1::2],
                                           ALU.add), bump=B.c_v)
    pl.step("v", lambda v: v.scalar_tensor_tensor(
        B.IDXAf[:], B.U2[:], 64.0, B.QM64[:].to_broadcast([128, 16]),
        ALU.mult, ALU.add), waits=[B.tok_cmp_setup[1]], bump=B.c_v)
    pl.step("v", lambda v: v.tensor_copy(B.IDXA[:], B.IDXAf[:]), bump=B.c_v)
    pl.step("v", lambda v: v.scalar_tensor_tensor(
        B.IDXAf[:], B.IDXAf[:], 8.0, B.COREB[:].to_broadcast([128, 16]),
        ALU.mult, ALU.add), waits=[B.tok_cmp_setup[2]], bump=B.c_v)
    pl.step("v", lambda v: v.tensor_copy(B.IDX8[:], B.IDXAf[:]), bump=B.c_v)
    pl.step("v", lambda v: v.tensor_tensor(B.UB[:], B.W32B[:], B.PEVB[:],
                                           ALU.mult), bump=B.c_v)
    pl.step("v", lambda v: v.tensor_tensor(B.MS[:], B.UB[:, 0::2], B.UB[:, 1::2],
                                           ALU.add), bump=B.c_v)
    return B.c_v.tok()


def emit_compose_one(pl, B, tok_idx, a_rows, a_width, b_rows, b_width,
                     a_idx, b_idx, n_m, n_n, out_emit):
    banks = B.AV + B.SC + [B.AUX, B.NRM]
    ga_tok = [None, None]
    gas_used = [None, None]
    for j in range(16):
        bi = j % 2

        def ld_a(g, j=j, bi=bi, a_rows=a_rows, a_width=a_width):
            return g.indirect_dma_start(
                out=B.GA[bi][:, 0:a_width], out_offset=None,
                in_=a_rows, in_offset=IndirectOffsetOnAxis(ap=a_idx(j), axis=0))
        pl.step("g", ld_a, waits=[tok_idx, ga_tok[bi]], bump=B.c_wg, n=16)
        tok_ga = B.c_wg.tok()

        def ld_b(g, j=j, bi=bi, b_rows=b_rows, b_width=b_width):
            return g.indirect_dma_start(
                out=B.GB[bi][:, 0:b_width], out_offset=None,
                in_=b_rows, in_offset=IndirectOffsetOnAxis(ap=b_idx(j), axis=0))
        pl.step("g", ld_b, waits=[], bump=B.c_wg, n=16)
        tok_gb = B.c_wg.tok()
        pl.step("v", lambda v, bi=bi, j=j, a_width=a_width: v.tensor_scalar(
            B.GAS[bi][:, 0:a_width], B.GA[bi][:, 0:a_width],
            B.MS[:, j:j + 1], None, ALU.mult),
            waits=[tok_ga, gas_used[bi]], bump=B.c_v)
        tok_gas = B.c_v.tok()

        def pe(t, j=j, bi=bi, n_m=n_m, n_n=n_n, b_width=b_width):
            last = None
            for m in range(n_m):
                for n in range(n_n):
                    last = t.matmul(
                        banks[m * n_n + n][:],
                        B.GAS[bi][:, m * 128:(m + 1) * 128],
                        B.GB[bi][:, n * 512:(n + 1) * 512],
                        start=(j == 0), stop=(j == 15))
            return last
        w = [tok_gas, tok_gb]
        if j == 0:
            for k in range(n_m * n_n):
                w += bank_wait(B, banks[k])
        pl.step("t", pe, waits=w, bump=B.c_pe)
        ga_tok[bi] = B.c_pe.tok()
        gas_used[bi] = B.c_pe.tok()
    tok_mm = B.c_pe.tok()
    for m in range(n_m):
        for n in range(n_n):
            bank = banks[m * n_n + n]
            out_emit(m, n, bank, tok_mm)


def emit_compose(pl, B, lg1_row, lg2_row, a1t, b1, a2t, b2,
                 w1sl_dram, w2sl_dram):
    a1_rows = a1t.rearrange("p r d -> (p r) d")
    b1_rows = b1.rearrange("p r (a b) -> (p r a) b", b=512)
    a2_rows = a2t.rearrange("p r (a b) -> (p r a) b", b=512)
    b2_rows = b2.rearrange("p r d -> (p r) d")

    def store(dram, m, n, bank, ptok):
        stg = B.STG[(m * 4 + n) % 2]
        pl.step("s", lambda s, stg=stg, bank=bank: s.copy(stg[:], bank[:]),
                waits=[ptok] + bank_wait(B, stg), bump=B.c_a)
        bank_set(B, bank, B.c_a.tok())
        tok_stg = B.c_a.tok()
        pl.step("y", lambda y, dram=dram, m=m, n=n, stg=stg: y.dma_start(
            out=dram[m * 128:(m + 1) * 128, n * 512:(n + 1) * 512], in_=stg[:]),
            waits=[tok_stg], bump=B.c_st, n=16)
        bank_set(B, stg, B.c_st.tok())

    tok_idx = emit_topk_idx(pl, B, lg1_row)
    emit_compose_one(pl, B, tok_idx, a1_rows, 1024, b1_rows, 512,
                     lambda j: B.IDXA[:, j:j + 1], lambda j: B.IDX8[:, j:j + 1],
                     8, 1, lambda m, n, bank, ptok: store(w1sl_dram, m, n, bank, ptok))
    tok_idx = emit_topk_idx(pl, B, lg2_row, guard=B.c_wg.all())
    emit_compose_one(pl, B, tok_idx, a2_rows, 512, b2_rows, 1024,
                     lambda j: B.IDX8[:, j:j + 1], lambda j: B.IDXA[:, j:j + 1],
                     4, 2, lambda m, n, bank, ptok: store(w2sl_dram, m, n, bank, ptok))
    return B.c_st.all()



def alloc_ffn0(P, B):
    B.H = P.sb("H", [128, 32 * 512], BF16)
    B.WST = [P.sb(f"wst{i}", [128, 8 * 1024], BF16) for i in range(2)]
    B.GU = P.sb("GU", [128, 512], F32)
    B.GZ = P.sb("GZ", [128, 512], F32)
    B.GS = P.sb("GS", [128, 512], F32)
    B.XRT = [P.sb(f"xrt{i}", [128, 512], F32) for i in range(2)]
    B.ART = [P.sb(f"art{i}", [128, 512], F32) for i in range(2)]
    B.XPT = [P.sb(f"xpt{i}", [128, 512], F32) for i in range(2)]


GELU_A = float(np.sqrt(0.044715))
GELU_K1 = 2.0 * 0.7978845608028654


def emit_ffn0_apply(pl, B, w1g, w2g, x_dram, a0_dram, x1_dram, tok_xn,
                    no_gelu=False, guard=None, wsrc_tok=None):
    h_guard = guard
    B.gu_guard = None
    B.gs_guard = None
    wst_guard = guard
    for tb in range(NQB):
        q0 = tb * QB
        wtok = [None, None]
        wused = [wst_guard, wst_guard]
        for fs in range(8):
            bi = fs % 2

            def ldw(y, fs=fs, bi=bi):
                return y.dma_start(
                    out=B.WST[bi][:, 0:8 * 512].rearrange("p (c i) -> p c i", c=DC),
                    in_=w1g[fs].rearrange("(c p) i -> p c i", p=128))
            qk, pool = (("g", B.c_wg) if (tb == 0 and fs < 2)
                        else ("y", B.c_ld))
            pl.step(qk, ldw, waits=[wused[bi], wsrc_tok], bump=pool, n=16)
            wtok[bi] = pool.tok()
            for fm in range(4):
                bank = B.AV[(fs * 4 + fm) % 4]

                def pe(t, bi=bi, fm=fm, q0=q0, bank=bank):
                    last = None
                    for dk in range(DC):
                        last = t.matmul(
                            bank[:],
                            B.WST[bi][:, dk * 512 + fm * 128: dk * 512 + fm * 128 + 128],
                            B.XN[:, dk * S + q0: dk * S + q0 + QB],
                            start=(dk == 0), stop=(dk == DC - 1))
                    return last
                pl.step("t", pe, waits=[wtok[bi], tok_xn[0], tok_xn[1]]
                        + bank_wait(B, bank), bump=B.c_pe)
                ptok = B.c_pe.tok()
                f = fs * 4 + fm
                h_sl = slice(f * 512, (f + 1) * 512)
                if no_gelu:
                    pl.step("s", lambda s, bank=bank, h_sl=h_sl: s.copy(
                        B.H[:, h_sl], bank[:]), waits=[ptok, h_guard])
                    bank_set(B, bank, B.c_a.tok())
                    continue
                pl.step("s", lambda s, bank=bank: s.activation(
                    B.GU[:], bank[:], AF.Square, scale=GELU_A),
                    waits=[ptok, B.gu_guard], bump=B.c_a)
                pl.step("v", lambda v, bank=bank: v.scalar_tensor_tensor(
                    B.GU[:], B.GU[:], 1.0, bank[:], ALU.add, ALU.mult),
                    waits=[B.c_a.tok()], bump=B.c_v)
                pl.step("s", lambda s: s.activation(
                    B.GS[:], B.GU[:], AF.Sigmoid, scale=GELU_K1),
                    waits=[B.c_v.tok(), B.gs_guard], bump=B.c_a)
                B.gu_guard = B.c_a.tok()
                pl.step("v", lambda v, bank=bank, h_sl=h_sl: v.tensor_tensor(
                    B.H[:, h_sl], B.GS[:], bank[:], ALU.mult),
                    waits=[B.c_a.tok(), h_guard], bump=B.c_v)
                B.gs_guard = B.c_v.tok()
                bank_set(B, bank, B.c_v.tok())
            wused[bi] = B.c_pe.tok()
        tok_h = B.c_v.tok()
        tok_h2 = B.c_a.tok()
        wst_guard = B.c_pe.tok()
        banks = B.AV + B.SC + [B.AUX, B.NRM]
        wtok = [None, None]
        wused = [wst_guard, wst_guard]
        for fs in range(8):
            bi = fs % 2

            def ldw2(y, fs=fs, bi=bi):
                return y.dma_start(
                    out=B.WST[bi][:, 0:4 * 1024].rearrange("p (c d) -> p c d", c=4),
                    in_=w2g[fs * 512:(fs + 1) * 512, :].rearrange(
                        "(c p) d -> p c d", p=128))
            qk, pool = (("g", B.c_wg) if (tb == 0 and fs < 2)
                        else ("y", B.c_ld))
            pl.step(qk, ldw2, waits=[wused[bi], wsrc_tok], bump=pool, n=16)
            wtok[bi] = pool.tok()

            def pe2(t, fs=fs, bi=bi):
                last = None
                for fl in range(4):
                    f = fs * 4 + fl
                    for m in range(8):
                        last = t.matmul(
                            banks[m][:],
                            B.WST[bi][:, fl * 1024 + m * 128: fl * 1024 + m * 128 + 128],
                            B.H[:, f * 512:(f + 1) * 512],
                            start=(f == 0), stop=(f == 31))
                return last
            w = [wtok[bi], tok_h, tok_h2]
            if fs == 0:
                for k in range(8):
                    w += bank_wait(B, banks[k])
            pl.step("t", pe2, waits=w, bump=B.c_pe)
            wused[bi] = B.c_pe.tok()
        tok_out = B.c_pe.tok()
        h_guard = tok_out
        wst_guard = tok_out
        ld_tok = [None, None]
        for m in range(8):
            bi = m % 2

            pl.step("y", lambda y, m=m, q0=q0, bi=bi: y.dma_start(
                out=B.XRT[bi][:], in_=x_dram[m * 128:(m + 1) * 128, q0:q0 + QB]),
                waits=[ld_tok[bi]] + bank_wait(B, B.XRT[bi]), bump=B.c_ld, n=16)
            tok_xr = B.c_ld.tok()
            pl.step("y", lambda y, m=m, q0=q0, bi=bi: y.dma_start(
                out=B.ART[bi][:], in_=a0_dram[m * 128:(m + 1) * 128, q0:q0 + QB]),
                waits=[ld_tok[bi]] + bank_wait(B, B.ART[bi]), bump=B.c_ld, n=16)
            tok_res = B.c_ld.tok()
            pl.step("v", lambda v, m=m, bi=bi: v.scalar_tensor_tensor(
                B.XRT[bi][:], banks[m][:], 1.0, B.XRT[bi][:], ALU.mult, ALU.add),
                waits=[tok_out, tok_xr], bump=B.c_v)
            bank_set(B, banks[m], B.c_v.tok())
            pl.step("v", lambda v, m=m, bi=bi: v.tensor_tensor(
                B.XRT[bi][:], B.XRT[bi][:], B.ART[bi][:], ALU.add),
                waits=[tok_res], bump=B.c_v)
            tok_xp = B.c_v.tok()
            pl.step("y", lambda y, m=m, q0=q0, bi=bi: y.dma_start(
                out=x1_dram[m * 128:(m + 1) * 128, q0:q0 + QB], in_=B.XRT[bi][:]),
                waits=[tok_xp], bump=B.c_st, n=16)
            ld_tok[bi] = B.c_st.tok()
            bank_set(B, B.XRT[bi], B.c_st.tok())
            bank_set(B, B.ART[bi], B.c_st.tok())
    return B.c_st.all()



def alloc_l1_attn(P, B):
    if not hasattr(B, "QF"):
        B.QF = P.sb("QF", [128, DC * S], BF16)
        B.KF = P.sb("KF", [128, DC * S], BF16)
        B.VT = P.sb("VT", [128, NT * D], BF16)
        B.WARE = [P.sb(f"ware{i}", [128, DC * 256], BF16) for i in range(2)]
        B.A0T = [P.sb(f"a0t{i}", [128, 512], F32) for i in range(2)]
    B.IP = B.QF
    B.RG = B.KF
    B.IG = B.VT
    B.BIN = P.sb("BIN", [128, DC], F32)
    B.BR = P.sb("BR", [128, DC], F32)
    B.BI = P.sb("BI", [128, DC], F32)
    B.BOUT = P.sb("BOUT", [128, DC], F32)
    B.ASG = P.sb("ASG", [128, DC], F32)


def emit_l1_attn(pl, B, w_in, w_r, w_i, w_out, b_in2, b_r2, b_i2, b_out2,
                 log_a2, a1_dram, tok_xn, guard=None, evac_guard=None):
    for buf, src in ((B.BIN, b_in2), (B.BR, b_r2), (B.BI, b_i2),
                     (B.BOUT, b_out2), (B.ASG, log_a2)):
        pl.step("y", lambda y, buf=buf, src=src: y.dma_start(out=buf[:], in_=src),
                bump=B.c_ld, n=16)
    tok_bias_l = B.c_ld.all()
    pl.step("s", lambda s: s.activation(B.ASG[:], B.ASG[:], AF.Sigmoid),
            waits=tok_bias_l)
    tok_asg = B.c_a.tok()

    def load_wq(wdram, h, buf, waits=()):
        def go(g):
            return g.dma_start(
                out=buf[:].rearrange("p (c i) -> p c i", c=DC),
                in_=wdram[:, h * 256:(h + 1) * 256].rearrange("(c p) i -> p c i", p=128))
        pl.step("g", go, waits=list(waits), bump=B.c_wg, n=16)
        return B.c_wg.tok()

    rr = [0]
    seqw = [("in", w_in), ("r", w_r), ("i", w_i), ("o", w_out)]
    order = [(nm, h) for nm, _ in seqw for h in range(4)]
    dram_of = dict(seqw)
    wtok = {order[0]: load_wq(w_in, 0, B.WARE[0], waits=[guard]),
            order[1]: load_wq(w_in, 1, B.WARE[1], waits=[guard])}

    def prefetch_after(idx):
        if idx + 2 < len(order):
            nm, nh = order[idx + 2]
            wtok[(nm, nh)] = load_wq(dram_of[nm], nh, B.WARE[idx % 2],
                                     waits=[B.c_pe.tok()])

    projs = (("in", B.BIN, B.IP, AF.Identity), ("r", B.BR, B.RG, AF.Sigmoid),
             ("i", B.BI, B.IG, AF.Sigmoid))
    oi = 0
    for name, bias, dstb, fn in projs:
        for h in range(4):
            wt = wtok[(name, h)]
            for ic in range(2 * h, 2 * h + 2):
                icl = (ic % 2) * 128
                for qb in range(NQB):
                    bank = B.AV[rr[0]]
                    rr[0] = (rr[0] + 1) % 4

                    def pe(t, oi=oi, icl=icl, qb=qb, bank=bank):
                        last = None
                        for dk in range(DC):
                            last = t.matmul(
                                bank[:],
                                B.WARE[oi % 2][:, dk * 256 + icl: dk * 256 + icl + 128],
                                B.XN[:, dk * S + qb * QB: dk * S + (qb + 1) * QB],
                                start=(dk == 0), stop=(dk == DC - 1))
                        return last
                    pl.step("t", pe, waits=[wt, tok_xn[0], tok_xn[1]]
                            + bank_wait(B, bank), bump=B.c_pe)
                    ptok = B.c_pe.tok()

                    def ev(s, bank=bank, ic=ic, qb=qb, bias=bias, dstb=dstb, fn=fn):
                        return s.activation(
                            dstb[:, ic * S + qb * QB: ic * S + (qb + 1) * QB],
                            bank[:], fn, bias=bias[:, ic:ic + 1], scale=1.0)
                    pl.step("s", ev, waits=[ptok, evac_guard] + tok_bias_l)
                    bank_set(B, bank, B.c_a.tok())
            prefetch_after(oi)
            oi += 1
    tok_proj = B.c_a.tok()
    for c in range(DC):
        eng = "v" if c % 2 == 0 else "g"
        for qb in range(NQB):
            sl = slice(c * S + qb * QB, c * S + (qb + 1) * QB)
            pl.step(eng, lambda e, sl=sl, c=c: e.tensor_scalar(
                B.RG[:, sl], B.RG[:, sl], B.ASG[:, c:c + 1], None, ALU.mult),
                waits=[tok_proj, tok_asg])
            pl.step(eng, lambda e, sl=sl: e.tensor_tensor(
                B.IP[:, sl], B.IG[:, sl], B.IP[:, sl], ALU.mult))
            pl.step(eng, lambda e, sl=sl: e.tensor_scalar(
                B.IG[:, sl], B.RG[:, sl], -1.0, 1.0, ALU.mult, ALU.add))
            pl.step(eng, lambda e, sl=sl: e.tensor_tensor(
                B.IP[:, sl], B.IP[:, sl], B.IG[:, sl], ALU.mult))
    tok_bt = (B.c_v.tok(), B.c_g.tok())
    for c in range(DC):
        sl = slice(c * S, (c + 1) * S)
        pl.step("v", lambda v, sl=sl: v.tensor_tensor_scan(
            B.IG[:, sl], B.RG[:, sl], B.IP[:, sl], 0.0, ALU.mult, ALU.add),
            waits=[tok_bt[0], tok_bt[1]])
    tok_h = B.c_v.tok()
    for h in range(4):
        wt = wtok[("o", h)]
        for ic in range(2 * h, 2 * h + 2):
            icl = (ic % 2) * 128
            for qb in range(NQB):
                bank = B.SC[(ic * NQB + qb) % 2]

                def pe(t, oi=oi, icl=icl, qb=qb, bank=bank):
                    last = None
                    for dk in range(DC):
                        last = t.matmul(
                            bank[:],
                            B.WARE[oi % 2][:, dk * 256 + icl: dk * 256 + icl + 128],
                            B.IG[:, dk * S + qb * QB: dk * S + (qb + 1) * QB],
                            start=(dk == 0), stop=(dk == DC - 1))
                    return last
                pl.step("t", pe, waits=[wt, tok_h] + bank_wait(B, bank),
                        bump=B.c_pe)
                ptok = B.c_pe.tok()
                stg = B.A0T[(ic * NQB + qb) % 2]
                pl.step("s", lambda s, bank=bank, ic=ic, stg=stg: s.activation(
                    stg[:], bank[:], AF.Identity, bias=B.BOUT[:, ic:ic + 1],
                    scale=1.0), waits=[ptok] + bank_wait(B, stg))
                bank_set(B, bank, B.c_a.tok())
                tok_stg = B.c_a.tok()
                pl.step("y", lambda y, ic=ic, qb=qb, stg=stg: y.dma_start(
                    out=a1_dram[ic * 128:(ic + 1) * 128,
                                qb * QB:(qb + 1) * QB], in_=stg[:]),
                    waits=[tok_stg], bump=B.c_st, n=16)
                bank_set(B, stg, B.c_st.tok())
        prefetch_after(oi)
        oi += 1
    return B.c_st.all()



def alloc_l1_quant(P, B):
    B.QWB = P.sb("QWB", [128, 3 * 1024], F32)
    B.QDB = P.sb("QDB", [128, 2816], F32)
    B.SCL = P.sb("SCL", [128, 4], F32)
    B.RCL = P.sb("RCL", [128, 4], F32)
    B.QI8 = P.sb("QI8", [128, 2816], I8)
    B.QBF = P.sb("QBF", [128, 3 * 1024], BF16)
    B.TQB = P.sb("TQB", [128, DC * 352], BF16)
    B.TQD = P.sb("TQD", [128, 22 * 128], BF16)


def emit_l1_quant(pl, B, wg_sl, wu_sl, wd_sl, wgt_out, wut_out, wdt_out,
                  sg_out, su_out, sd_out, guard=None):
    rows = [128, 128, 96]
    tr_guard = None
    scl_guard = []
    qwb_guard = guard
    qbf_guard = []
    for w_sl, wt_out, s_out in ((wg_sl, wgt_out, sg_out),
                                (wu_sl, wut_out, su_out)):
        ldt = []
        for c in range(3):
            r0 = c * 128
            nr = rows[c]
            pl.step("g", lambda g, c=c, r0=r0, nr=nr, w_sl=w_sl: g.dma_start(
                out=B.QWB[0:nr, c * 1024:(c + 1) * 1024],
                in_=w_sl[r0:r0 + nr, :]), waits=[qwb_guard], bump=B.c_wg, n=16)
            ldt.append(B.c_wg.tok())
        for c in range(3):
            nr = rows[c]
            csl = slice(c * 1024, (c + 1) * 1024)
            pl.step("v", lambda v, c=c, nr=nr, csl=csl: v.reduce_max(
                B.SCL[0:nr, c:c + 1], B.QWB[0:nr, csl], axis=AX.X,
                apply_absolute_value=True), waits=[ldt[c]] + scl_guard + qbf_guard)
            pl.step("v", lambda v, c=c, nr=nr: v.tensor_scalar(
                B.SCL[0:nr, c:c + 1], B.SCL[0:nr, c:c + 1], 1.0 / 127.0,
                None, ALU.mult))
            pl.step("v", lambda v, c=c, nr=nr: v.reciprocal(
                B.RCL[0:nr, c:c + 1], B.SCL[0:nr, c:c + 1]), bump=B.c_v)
            tok_rcl = B.c_v.tok()
            pl.step("v", lambda v, c=c, nr=nr, csl=csl: v.tensor_scalar(
                B.QI8[0:nr, 0:1024], B.QWB[0:nr, csl], B.RCL[0:nr, c:c + 1],
                None, ALU.mult), waits=[tok_rcl])
            pl.step("v", lambda v, c=c, nr=nr, csl=csl: v.tensor_copy(
                B.QBF[0:nr, csl], B.QI8[0:nr, 0:1024]), bump=B.c_v)
            tok_q = B.c_v.tok()
            for dc in range(DC):
                pl.step("y", lambda y, c=c, nr=nr, dc=dc: y.dma_start_transpose(
                    B.TQB[:, dc * 352 + c * 128: dc * 352 + c * 128 + nr],
                    B.QBF[0:nr, c * 1024 + dc * 128: c * 1024 + (dc + 1) * 128]),
                    waits=[tok_q, tr_guard], bump=B.c_sd, n=16)
        tok_tr_l = B.c_sd.all()
        pl.step("y", lambda y, wt_out=wt_out: y.dma_start(
            out=wt_out.rearrange("(c p) r -> p c r", p=128),
            in_=B.TQB[:].rearrange("p (c r) -> p c r", c=DC)),
            waits=tok_tr_l, bump=B.c_sd, n=16)
        tr_guard = B.c_sd.tok()
        scl_guard = []
        for c in range(3):
            nr = rows[c]
            pl.step("y", lambda y, c=c, nr=nr, s_out=s_out: y.dma_start(
                out=s_out[0:1, c * 128:c * 128 + nr].rearrange("a b -> b a"),
                in_=B.SCL[0:nr, c:c + 1]), bump=B.c_sd, n=16)
            scl_guard.append(B.c_sd.tok())
        qwb_guard = B.c_v.tok()
        qbf_guard = B.c_sd.all()
    pl.step("g", lambda g: g.dma_start(out=B.QDB[:], in_=wd_sl),
            waits=[qwb_guard], bump=B.c_wg, n=16)
    tok_ld = B.c_wg.tok()
    pl.step("v", lambda v: v.reduce_max(B.SCL[:, 3:4], B.QDB[:], axis=AX.X,
                                        apply_absolute_value=True),
            waits=[tok_ld] + scl_guard)
    pl.step("v", lambda v: v.tensor_scalar(B.SCL[:, 3:4], B.SCL[:, 3:4],
                                           1.0 / 127.0, None, ALU.mult))
    pl.step("v", lambda v: v.reciprocal(B.RCL[:, 3:4], B.SCL[:, 3:4]),
            bump=B.c_v)
    tok_rcl = B.c_v.tok()
    pl.step("v", lambda v: v.tensor_scalar(B.QI8[:], B.QDB[:], B.RCL[:, 3:4],
                                           None, ALU.mult),
            waits=[tok_rcl])
    pl.step("v", lambda v: v.tensor_copy(B.QBF[:, 0:2816], B.QI8[:]),
            waits=qbf_guard, bump=B.c_v)
    tok_q = B.c_v.tok()
    for ic in range(22):
        pl.step("y", lambda y, ic=ic: y.dma_start_transpose(
            B.TQD[:, ic * 128:(ic + 1) * 128],
            B.QBF[:, ic * 128:(ic + 1) * 128]),
            waits=[tok_q, tr_guard], bump=B.c_sd, n=16)
    tok_tr_l = B.c_sd.all()
    pl.step("y", lambda y: y.dma_start(
        out=wdt_out.rearrange("(c p) r -> p c r", p=128),
        in_=B.TQD[:].rearrange("p (c r) -> p c r", c=22)),
        waits=tok_tr_l, bump=B.c_sd, n=16)
    pl.step("y", lambda y: y.dma_start(
        out=sd_out.rearrange("a b -> b a"), in_=B.SCL[:, 3:4]),
        bump=B.c_sd, n=16)
    return B.c_sd.all()



def bcast_scalar(pl, B, src11, dst_col, tok):
    pl.step("t", lambda t: t.matmul(B.AUX[:, 0:1], B.ones_rf[:], src11,
                                    start=True, stop=True),
            waits=[tok, B.tok_setup_v] + bank_wait(B, B.AUX), bump=B.c_pe)
    pl.step("s", lambda s: s.copy(dst_col, B.AUX[:, 0:1]),
            waits=[B.c_pe.tok()])
    bank_set(B, B.AUX, B.c_a.tok())
    return B.c_a.tok()


def emit_absmax_xn(pl, B, tok_xn):
    for c in range(DC):
        pl.step("v", lambda v, c=c: v.reduce_max(
            B.AMC[:, c:c + 1], B.XN[:, c * S:(c + 1) * S], axis=AX.X,
            apply_absolute_value=True), waits=[tok_xn[0], tok_xn[1]])
    pl.step("v", lambda v: v.reduce_max(B.AMC[:, DC:DC + 1], B.AMC[:, 0:DC],
                                        axis=AX.X), bump=B.c_v)
    tok = B.c_v.tok()
    pl.step("g", lambda g: g.tensor_reduce(
        B.AMX[0:1, 0:1], B.AMC[:, DC:DC + 1], axis=AX.XYZWC, op=ALU.max),
        waits=[tok], bump=B.c_g)
    return B.c_g.tok()



def alloc_l1_ffn(P, B, phase):
    if phase == "a":
        B.XQB = B.KF
        B.XQI = P.sb("XQI", [128, S], I8)
        B.H1 = P.sb("H1", [128, 22 * 512], BF16)
        B.SIL = P.sb("SIL", [128, 512], F32)
        B.SGB = P.sb("SGB", [128, 22], F32)
        B.SUB = P.sb("SUB", [128, 22], F32)
        B.WQS = [P.sb(f"wqs{i}", [128, DC * 128], BF16) for i in range(4)]
        B.MXH = P.sb("MXH", [1, 92], F32)
        B.AMC = P.sb("AMC", [128, DC + 1], F32)
        B.AMX = P.sb("AMX", [1, 16], F32)
        B.CLP = P.sb("CLP", [1, 4], F32)
        B.RSXB = P.sb("RSXB", [128, 1], F32)
        B.SXB = P.sb("SXB", [128, 1], F32)
    else:
        B.H1 = P.sb("H1", [128, 22 * 512], BF16)
        B.H1I = P.sb("H1I", [128, 512], I8)
        B.WQS = [P.sb(f"wqs{i}", [128, DC * 128], BF16) for i in range(2)]
        B.SDB = P.sb("SDB", [128, DC], F32)
        B.AMX = P.sb("AMX", [1, 16], F32)
        B.CLP = P.sb("CLP", [1, 4], F32)
        B.RSHB = P.sb("RSHB", [128, 1], F32)
        B.SHB = P.sb("SHB", [128, 1], F32)
        B.XRT = [P.sb(f"xrt{i}", [128, 512], F32) for i in range(2)]
        B.ART = [P.sb(f"art{i}", [128, 512], F32) for i in range(2)]
        B.XPT = [P.sb(f"xpt{i}", [128, 512], F32) for i in range(2)]


def emit_sx(pl, B, mx8_dram, clip_dram, rsxb_col, shb_col=None, gmax11=False,
            ld_guard=None):
    pl.step("y", lambda y: y.dma_start(out=B.CLP[0:1, 0:1], in_=clip_dram),
            bump=B.c_ld, n=16)
    if gmax11:
        pl.step("y", lambda y: y.dma_start(out=B.AMX[0:1, 8:9], in_=mx8_dram),
                waits=[ld_guard], bump=B.c_ld, n=16)
    else:
        pl.step("y", lambda y: y.dma_start(out=B.AMX[0:1, 0:8], in_=mx8_dram),
                bump=B.c_ld, n=16)
    tok_ld_l = B.c_ld.all()
    pl.step("s", lambda s: s.activation(B.CLP[0:1, 1:2], B.CLP[0:1, 0:1],
                                        AF.Sigmoid), waits=tok_ld_l)
    tok_sig = B.c_a.tok()
    if not gmax11:
        pl.step("v", lambda v: v.reduce_max(B.AMX[0:1, 8:9], B.AMX[0:1, 0:8],
                                            axis=AX.X), waits=tok_ld_l)
    pl.step("v", lambda v: v.tensor_scalar(
        B.AMX[0:1, 9:10], B.AMX[0:1, 8:9], B.CLP[0:1, 1:2], 1.0 / 127.0,
        ALU.mult, ALU.mult), waits=[tok_sig] + tok_ld_l)
    pl.step("v", lambda v: v.reciprocal(B.AMX[0:1, 10:11], B.AMX[0:1, 9:10]),
            bump=B.c_v)
    tok = B.c_v.tok()
    t1 = bcast_scalar(pl, B, B.AMX[0:1, 10:11], rsxb_col, tok)
    if shb_col is not None:
        t1 = bcast_scalar(pl, B, B.AMX[0:1, 9:10], shb_col, tok)
    return t1


def emit_l1_ffn_a(pl, B, mx8_dram, clip_x_dram, wgt_dram, wut_dram,
                  sg_dram, su_dram, h1_dram, hmax_out, tok_xn, guard=None,
                  gmax11=False, wg_pieces=None):
    B.ffna_guard = guard
    tok_sx = emit_sx(pl, B, mx8_dram, clip_x_dram, B.RSXB[:], B.SXB[:],
                     gmax11=gmax11, ld_guard=B.sx_ld_guard)
    for c in range(DC):
        sl = slice(c * S, (c + 1) * S)
        pl.step("v", lambda v, sl=sl: v.tensor_scalar(
            B.XQI[:], B.XN[:, sl], B.RSXB[:], None, ALU.mult),
            waits=[tok_sx, tok_xn[0], tok_xn[1]])
        pl.step("v", lambda v, sl=sl: v.tensor_copy(B.XQB[:, sl], B.XQI[:]))
    tok_xq = B.c_v.tok()
    def pieces(i0, ln):
        out = []
        while ln > 0:
            r = i0 // 352
            lo = i0 - r * 352
            n = min(352 - lo, ln)
            out.append((r, lo, n))
            i0 += n
            ln -= n
        return out

    for ic in range(22):
        for dst, sdram in ((B.SGB, sg_dram), (B.SUB, su_dram)):
            if wg_pieces:
                off = 0
                for r, lo, n in pieces(ic * 128, 128):
                    pl.step("y", lambda y, ic=ic, dst=dst, sdram=sdram, r=r,
                            lo=lo, n=n, off=off: y.dma_start(
                        out=dst[off:off + n, ic:ic + 1],
                        in_=sdram[r:r + 1, lo:lo + n].rearrange("a b -> b a")),
                        bump=B.c_ld, n=16)
                    off += n
            else:
                pl.step("y", lambda y, ic=ic, dst=dst, sdram=sdram: y.dma_start(
                    out=dst[:, ic:ic + 1],
                    in_=sdram[0:1, ic * 128:(ic + 1) * 128].rearrange("a b -> b a")),
                    bump=B.c_ld, n=16)
    tok_s_l = B.c_ld.all()
    pl.step("v", lambda v: v.tensor_scalar(B.SGB[:], B.SGB[:], B.SXB[:],
                                           None, ALU.mult), waits=tok_s_l + [tok_sx])
    pl.step("v", lambda v: v.tensor_scalar(B.SUB[:], B.SUB[:], B.SXB[:],
                                           None, ALU.mult), bump=B.c_v)
    tok_sc = B.c_v.tok()
    wtok = [None] * 4
    wused = [B.ffna_guard] * 4
    hst_tok = B.ffna_guard
    B.sil_guard = None
    for tb in range(NQB):
        q0 = tb * QB
        for ic in range(22):
            bg = ic % 2
            bu = 2 + ic % 2
            for which, wdram, bi in (("g", wgt_dram, bg), ("u", wut_dram, bu)):
                if wg_pieces:
                    off = 0
                    first = True
                    for r, lo, n in pieces(ic * 128, 128):
                        def ldw(g, wdram=wdram, r=r, lo=lo, n=n, off=off, bi=bi):
                            return g.dma_start(
                                out=B.WQS[bi][:].rearrange(
                                    "p (c i) -> p c i", c=DC)[:, :, off:off + n],
                                in_=wdram[r].rearrange(
                                    "(c p) i -> p c i", p=128)[:, :, lo:lo + n])
                        pl.step("g", ldw, waits=[wused[bi]] if first else [],
                                bump=B.c_wg, n=16)
                        first = False
                        off += n
                else:
                    def ldw(g, wdram=wdram, ic=ic, bi=bi):
                        return g.dma_start(
                            out=B.WQS[bi][:].rearrange("p (c i) -> p c i", c=DC),
                            in_=wdram[:, ic * 128:(ic + 1) * 128].rearrange(
                                "(c p) i -> p c i", p=128))
                    pl.step("g", ldw, waits=[wused[bi]], bump=B.c_wg, n=16)
                wtok[bi] = B.c_wg.tok()
            bank_g = B.AV[ic % 2]
            bank_u = B.AV[2 + ic % 2]

            def peg(t, ic=ic, q0=q0, bg=bg, bank_g=bank_g):
                last = None
                for dk in range(DC):
                    last = t.matmul(
                        bank_g[:], B.WQS[bg][:, dk * 128:(dk + 1) * 128],
                        B.XQB[:, dk * S + q0: dk * S + q0 + QB],
                        start=(dk == 0), stop=(dk == DC - 1))
                return last
            pl.step("t", peg, waits=[wtok[bg], tok_xq] + bank_wait(B, bank_g),
                    bump=B.c_pe)
            wused[bg] = B.c_pe.tok()

            def peu(t, ic=ic, q0=q0, bu=bu, bank_u=bank_u):
                last = None
                for dk in range(DC):
                    last = t.matmul(
                        bank_u[:], B.WQS[bu][:, dk * 128:(dk + 1) * 128],
                        B.XQB[:, dk * S + q0: dk * S + q0 + QB],
                        start=(dk == 0), stop=(dk == DC - 1))
                return last
            pl.step("t", peu, waits=[wtok[bu]] + bank_wait(B, bank_u),
                    bump=B.c_pe)
            wused[bu] = B.c_pe.tok()
            tok_mm = B.c_pe.tok()
            pl.step("s", lambda s, ic=ic, bank_g=bank_g: s.activation(
                B.SIL[:], bank_g[:], AF.Silu, scale=B.SGB[:, ic:ic + 1]),
                waits=[tok_mm, tok_sc, B.sil_guard])
            bank_set(B, bank_g, B.c_a.tok())
            pl.step("v", lambda v, ic=ic, bank_u=bank_u: v.scalar_tensor_tensor(
                B.H1[:, ic * 512:(ic + 1) * 512], B.SIL[:], B.SUB[:, ic:ic + 1],
                bank_u[:], ALU.mult, ALU.mult),
                waits=[B.c_a.tok(), hst_tok])
            B.sil_guard = B.c_v.tok()
            bank_set(B, bank_u, B.c_v.tok())
            pl.step("g", lambda g, tb=tb, ic=ic: g.reduce_max(
                B.MXH[0:1, tb * 22 + ic: tb * 22 + ic + 1],
                B.H1[:, ic * 512:(ic + 1) * 512], axis=AX.XYZWC,
                apply_absolute_value=True), waits=[B.c_v.tok()])
        tok_hready = (B.c_v.tok(), B.c_g.tok())
        pl.step("y", lambda y, q0=q0: y.dma_start(
            out=h1_dram.rearrange("(c p) t -> p c t", p=128)[:, :, q0:q0 + QB],
            in_=B.H1[:].rearrange("p (c t) -> p c t", c=22)),
            waits=[tok_hready[0], tok_hready[1]], bump=B.c_st, n=16)
        hst_tok = B.c_st.tok()
    pl.step("g", lambda g: g.tensor_reduce(
        B.MXH[0:1, 88:89], B.MXH[0:1, 0:88], axis=AX.XYZWC, op=ALU.max),
        bump=B.c_g)
    tok_hm = B.c_g.tok()
    pl.step("y", lambda y: y.dma_start(out=hmax_out, in_=B.MXH[0:1, 88:89]),
            waits=[tok_hm], bump=B.c_st, n=16)
    return B.c_st.all()


def emit_l1_ffn_b(pl, B, hmax8_dram, clip_d_dram, wdt_dram, sd_dram,
                  h1_dram, x1_dram, a1_dram, out_dram, guard=None,
                  gmax11=False, wd_pieces=False):
    B.ffnb_guard = guard
    tok_sh = emit_sx(pl, B, hmax8_dram, clip_d_dram, B.RSHB[:], B.SHB[:],
                     gmax11=gmax11, ld_guard=B.sx_ld_guard)
    for m in range(DC):
        if wd_pieces:
            pl.step("y", lambda y, m=m: y.dma_start(
                out=B.SDB[:, m:m + 1],
                in_=sd_dram[m:m + 1, :].rearrange("a b -> b a")),
                bump=B.c_ld, n=16)
        else:
            pl.step("y", lambda y, m=m: y.dma_start(
                out=B.SDB[:, m:m + 1],
                in_=sd_dram[0:1, m * 128:(m + 1) * 128].rearrange("a b -> b a")),
                bump=B.c_ld, n=16)
    tok_s_l = B.c_ld.all()
    pl.step("v", lambda v: v.tensor_scalar(B.SDB[:], B.SDB[:], B.SHB[:],
                                           None, ALU.mult),
            waits=tok_s_l + [tok_sh], bump=B.c_v)
    tok_sd = B.c_v.tok()
    banks = B.AV + B.SC + [B.AUX, B.NRM]
    wused = [B.ffnb_guard, B.ffnb_guard]
    h1_guard = B.ffnb_guard
    for tb in range(NQB):
        q0 = tb * QB
        pl.step("y", lambda y, q0=q0: y.dma_start(
            out=B.H1[:].rearrange("p (c t) -> p c t", c=22),
            in_=h1_dram.rearrange("(c p) t -> p c t", p=128)[:, :, q0:q0 + QB]),
            waits=[h1_guard], bump=B.c_ld, n=16)
        tok_h1 = B.c_ld.tok()
        for ic in range(22):
            sl = slice(ic * 512, (ic + 1) * 512)
            pl.step("v", lambda v, sl=sl: v.tensor_scalar(
                B.H1I[:], B.H1[:, sl], B.RSHB[:], None, ALU.mult),
                waits=[tok_h1, tok_sh])
            pl.step("v", lambda v, sl=sl: v.tensor_copy(B.H1[:, sl], B.H1I[:]))
        tok_hq = B.c_v.tok()
        wtok = [None, None]
        for ic in range(22):
            bi = ic % 2
            if wd_pieces:
                first = True
                for r in range(DC):
                    def ldw(g, ic=ic, bi=bi, r=r):
                        return g.dma_start(
                            out=B.WQS[bi][:].rearrange(
                                "p (c d) -> p c d", c=DC)[:, r:r + 1, :],
                            in_=wdt_dram[r:r + 1, ic * 128:(ic + 1) * 128, :]
                            .rearrange("a p d -> p a d"))
                    pl.step("g", ldw, waits=[wused[bi]] if first else [],
                            bump=B.c_wg, n=16)
                    first = False
            else:
                def ldw(g, ic=ic, bi=bi):
                    return g.dma_start(
                        out=B.WQS[bi][:].rearrange("p (c d) -> p c d", c=DC),
                        in_=wdt_dram[ic * 128:(ic + 1) * 128, :].rearrange(
                            "p (c d) -> p c d", c=DC))
                pl.step("g", ldw, waits=[wused[bi]], bump=B.c_wg, n=16)
            wtok[bi] = B.c_wg.tok()

            def pe(t, ic=ic, bi=bi):
                last = None
                for m in range(DC):
                    last = t.matmul(
                        banks[m][:], B.WQS[bi][:, m * 128:(m + 1) * 128],
                        B.H1[:, ic * 512:(ic + 1) * 512],
                        start=(ic == 0), stop=(ic == 21))
                return last
            w = [wtok[bi], tok_hq]
            if ic == 0:
                for k in range(DC):
                    w += bank_wait(B, banks[k])
            pl.step("t", pe, waits=w, bump=B.c_pe)
            wused[bi] = B.c_pe.tok()
        tok_out = B.c_pe.tok()
        h1_guard = tok_out
        ld_tok = [None, None]
        for m in range(DC):
            bi = m % 2
            pl.step("y", lambda y, m=m, q0=q0, bi=bi: y.dma_start(
                out=B.XRT[bi][:], in_=x1_dram[m * 128:(m + 1) * 128, q0:q0 + QB]),
                waits=[ld_tok[bi]] + bank_wait(B, B.XRT[bi]), bump=B.c_ld, n=16)
            tok_xr = B.c_ld.tok()
            pl.step("y", lambda y, m=m, q0=q0, bi=bi: y.dma_start(
                out=B.ART[bi][:], in_=a1_dram[m * 128:(m + 1) * 128, q0:q0 + QB]),
                waits=[ld_tok[bi]] + bank_wait(B, B.ART[bi]), bump=B.c_ld, n=16)
            tok_res = B.c_ld.tok()
            pl.step("v", lambda v, m=m, bi=bi: v.scalar_tensor_tensor(
                B.XRT[bi][:], banks[m][:], B.SDB[:, m:m + 1], B.XRT[bi][:],
                ALU.mult, ALU.add), waits=[tok_out, tok_sd, tok_xr], bump=B.c_v)
            bank_set(B, banks[m], B.c_v.tok())
            pl.step("v", lambda v, m=m, bi=bi: v.tensor_tensor(
                B.XRT[bi][:], B.XRT[bi][:], B.ART[bi][:], ALU.add),
                waits=[tok_res], bump=B.c_v)
            tok_xp = B.c_v.tok()
            pl.step("y", lambda y, m=m, q0=q0, bi=bi: y.dma_start(
                out=out_dram[m * 128:(m + 1) * 128, q0:q0 + QB],
                in_=B.XRT[bi][:]), waits=[tok_xp], bump=B.c_st, n=16)
            ld_tok[bi] = B.c_st.tok()
            bank_set(B, B.XRT[bi], B.c_st.tok())
            bank_set(B, B.ART[bi], B.c_st.tok())
    return B.c_st.all()


def ag_pieces(i0, ln):
    out = []
    while ln > 0:
        r = i0 // 352
        lo = i0 - r * 352
        n = min(352 - lo, ln)
        out.append((r, lo, n))
        i0 += n
        ln -= n
    return out


def emit_ffn1a_res(pl, B, mx8_dram, clip_x_dram, wgt_dram, wut_dram,
                   sg_dram, su_dram, hmax_out, tok_xn, guard=None):
    B.ffna_guard = guard
    tok_sx = emit_sx(pl, B, mx8_dram, clip_x_dram, B.RSXB[:], B.SXB[:],
                     gmax11=True, ld_guard=B.sx_ld_guard)
    for c in range(DC):
        sl = slice(c * S, (c + 1) * S)
        pl.step("v", lambda v, sl=sl: v.tensor_scalar(
            B.XQI[:], B.XN[:, sl], B.RSXB[:], None, ALU.mult),
            waits=[tok_sx, tok_xn[0], tok_xn[1]])
        pl.step("v", lambda v, sl=sl: v.tensor_copy(B.XQB[:, sl], B.XQI[:]))
    tok_xq = B.c_v.tok()
    for ic in range(22):
        for dst, sdram in ((B.SGB, sg_dram), (B.SUB, su_dram)):
            off = 0
            for r, lo, n in ag_pieces(ic * 128, 128):
                pl.step("y", lambda y, ic=ic, dst=dst, sdram=sdram, r=r,
                        lo=lo, n=n, off=off: y.dma_start(
                    out=dst[off:off + n, ic:ic + 1],
                    in_=sdram[r:r + 1, lo:lo + n].rearrange("a b -> b a")),
                    bump=B.c_ld, n=16)
                off += n
    tok_s_l = B.c_ld.all()
    pl.step("v", lambda v: v.tensor_scalar(B.SGB[:], B.SGB[:], B.SXB[:],
                                           None, ALU.mult), waits=tok_s_l + [tok_sx])
    pl.step("v", lambda v: v.tensor_scalar(B.SUB[:], B.SUB[:], B.SXB[:],
                                           None, ALU.mult), bump=B.c_v)
    tok_sc = B.c_v.tok()
    wtok = [None] * 4
    wused = [B.ffna_guard] * 4

    def load_pair(ic):
        for wdram, bi in ((wgt_dram, ic % 2), (wut_dram, 2 + ic % 2)):
            off = 0
            first = True
            for r, lo, n in ag_pieces(ic * 128, 128):
                def ldw(g, wdram=wdram, r=r, lo=lo, n=n, off=off, bi=bi):
                    return g.dma_start(
                        out=B.WQS[bi][:].rearrange(
                            "p (c i) -> p c i", c=DC)[:, :, off:off + n],
                        in_=wdram[r].rearrange(
                            "(c p) i -> p c i", p=128)[:, :, lo:lo + n])
                pl.step("g", ldw, waits=[wused[bi]] if first else [],
                        bump=B.c_wg, n=16)
                first = False
                off += n
            wtok[bi] = B.c_wg.tok()

    load_pair(0)
    sil_bufs = [B.SIL, B.SIL2]
    sil_guard = [None, None]
    for ic in range(22):
        if ic + 1 < 22:
            load_pair(ic + 1)
        bg, bu = ic % 2, 2 + ic % 2
        for tb in range(NQB):
            q0 = tb * QB
            bank_g = B.AV[tb % 2]
            bank_u = B.AV[2 + tb % 2]

            def peg(t, q0=q0, bg=bg, bank_g=bank_g):
                last = None
                for dk in range(DC):
                    last = t.matmul(
                        bank_g[:], B.WQS[bg][:, dk * 128:(dk + 1) * 128],
                        B.XQB[:, dk * S + q0: dk * S + q0 + QB],
                        start=(dk == 0), stop=(dk == DC - 1))
                return last
            pl.step("t", peg, waits=[wtok[bg], tok_xq] + bank_wait(B, bank_g),
                    bump=B.c_pe)
            tok_mmg = B.c_pe.tok()

            def peu(t, q0=q0, bu=bu, bank_u=bank_u):
                last = None
                for dk in range(DC):
                    last = t.matmul(
                        bank_u[:], B.WQS[bu][:, dk * 128:(dk + 1) * 128],
                        B.XQB[:, dk * S + q0: dk * S + q0 + QB],
                        start=(dk == 0), stop=(dk == DC - 1))
                return last
            pl.step("t", peu, waits=[wtok[bu]] + bank_wait(B, bank_u),
                    bump=B.c_pe)
            sb = sil_bufs[tb % 2]
            pl.step("s", lambda s, ic=ic, bank_g=bank_g, sb=sb: s.activation(
                sb[:], bank_g[:], AF.Silu, scale=B.SGB[:, ic:ic + 1]),
                waits=[tok_mmg, tok_sc, sil_guard[tb % 2]])
            bank_set(B, bank_g, B.c_a.tok())
            slot = B.H1S[ic * 4 + tb]
            pl.step("v", lambda v, ic=ic, bank_u=bank_u, sb=sb, slot=slot:
                    v.scalar_tensor_tensor(
                        slot, sb[:], B.SUB[:, ic:ic + 1], bank_u[:],
                        ALU.mult, ALU.mult),
                    waits=[B.c_a.tok(), B.c_pe.tok()], bump=B.c_v)
            sil_guard[tb % 2] = B.c_v.tok()
            bank_set(B, bank_u, B.c_v.tok())
            pl.step("g", lambda g, ic=ic, tb=tb, slot=slot: g.reduce_max(
                B.MXH[0:1, tb * 22 + ic: tb * 22 + ic + 1],
                slot, axis=AX.XYZWC, apply_absolute_value=True),
                waits=[B.c_v.tok()])
        wused[bg] = B.c_pe.tok()
        wused[bu] = B.c_pe.tok()
    pl.step("g", lambda g: g.tensor_reduce(
        B.MXH[0:1, 88:89], B.MXH[0:1, 0:88], axis=AX.XYZWC, op=ALU.max),
        bump=B.c_g)
    tok_hm = B.c_g.tok()
    pl.step("y", lambda y: y.dma_start(out=hmax_out, in_=B.MXH[0:1, 88:89]),
            waits=[tok_hm], bump=B.c_st, n=16)
    return [B.c_st.tok()]


def emit_ffn1b_res(pl, B, hmax8_dram, clip_d_dram, wdt_dram, sd_dram,
                   x1_dram, a1_dram, out_dram, guard=None):
    B.ffnb_guard = guard
    tok_sh = emit_sx(pl, B, hmax8_dram, clip_d_dram, B.RSHB[:], B.SHB[:],
                     gmax11=True, ld_guard=B.sx_ld_guard)
    for m in range(DC):
        pl.step("y", lambda y, m=m: y.dma_start(
            out=B.SDB[:, m:m + 1],
            in_=sd_dram[m:m + 1, :].rearrange("a b -> b a")),
            bump=B.c_ld, n=16)
    tok_s_l = B.c_ld.all()
    pl.step("v", lambda v: v.tensor_scalar(B.SDB[:], B.SDB[:], B.SHB[:],
                                           None, ALU.mult),
            waits=tok_s_l + [tok_sh], bump=B.c_v)
    tok_sd = B.c_v.tok()
    qtb = [None] * 4
    for tb in range(NQB):
        for ic in range(22):
            slot = B.H1S[ic * 4 + tb]
            pl.step("v", lambda v, slot=slot: v.tensor_scalar(
                B.H1I[:], slot, B.RSHB[:], None, ALU.mult), waits=[tok_sh])
            pl.step("v", lambda v, slot=slot: v.tensor_copy(slot, B.H1I[:]))
        qtb[tb] = B.c_v.tok()
    banks = B.AV + B.SC + [B.AUX, B.NRM]
    wtok = [None] * 4
    wused = [B.ffnb_guard] * 4

    def load_w(dg, ic):
        bi = ic % 4
        first = True
        for j in range(2):
            r = 2 * dg + j

            def ldw(g, r=r, j=j, ic=ic, bi=bi):
                return g.dma_start(
                    out=B.WQS[bi][:, j * 128:(j + 1) * 128],
                    in_=wdt_dram[r:r + 1, ic * 128:(ic + 1) * 128, :]
                    .rearrange("a p d -> p (a d)"))
            pl.step("g", ldw, waits=[wused[bi]] if first else [],
                    bump=B.c_wg, n=16)
            first = False
        wtok[bi] = B.c_wg.tok()

    res_st = {}

    def res_loads(dg, j, res_tok):
        m = 2 * dg + j
        for tb in range(NQB):
            b = tb % 4
            pl.step("y", lambda y, m=m, tb=tb, b=b: y.dma_start(
                out=B.XR4[b], in_=x1_dram[m * 128:(m + 1) * 128,
                                          tb * QB:(tb + 1) * QB]),
                waits=[res_st.get(("x", b)), B.ffnb_guard], bump=B.c_ld, n=16)
            res_tok[("x", j, tb)] = B.c_ld.tok()
            pl.step("y", lambda y, m=m, tb=tb, b=b: y.dma_start(
                out=B.AR4[b], in_=a1_dram[m * 128:(m + 1) * 128,
                                          tb * QB:(tb + 1) * QB]),
                waits=[res_st.get(("a", b)), B.ffnb_guard], bump=B.c_ld, n=16)
            res_tok[("a", j, tb)] = B.c_ld.tok()

    for dg in range(4):
        load_w(dg, 0)
        load_w(dg, 1)
        res_tok = {}
        res_loads(dg, 0, res_tok)
        for ic in range(22):
            if ic + 2 < 22:
                load_w(dg, ic + 2)
            bi = ic % 4

            def pe(t, ic=ic, bi=bi):
                last = None
                for tb in range(NQB):
                    for j in range(2):
                        last = t.matmul(
                            banks[tb * 2 + j][:],
                            B.WQS[bi][:, j * 128:(j + 1) * 128],
                            B.H1S[ic * 4 + tb],
                            start=(ic == 0), stop=(ic == 21))
                return last
            w = [wtok[bi]] + qtb
            if ic == 0:
                for k in range(8):
                    w += bank_wait(B, banks[k])
            pl.step("t", pe, waits=w, bump=B.c_pe)
            wused[bi] = B.c_pe.tok()
        tok_out = B.c_pe.tok()
        for j in range(2):
            m = 2 * dg + j
            for tb in range(NQB):
                b = tb % 4
                bank = banks[tb * 2 + j]
                pl.step("v", lambda v, m=m, b=b, bank=bank:
                        v.scalar_tensor_tensor(
                            B.XR4[b], bank[:], B.SDB[:, m:m + 1], B.XR4[b],
                            ALU.mult, ALU.add),
                        waits=[tok_out, tok_sd, res_tok[("x", j, tb)]],
                        bump=B.c_v)
                bank_set(B, bank, B.c_v.tok())
                pl.step("v", lambda v, b=b: v.tensor_tensor(
                    B.XR4[b], B.XR4[b], B.AR4[b], ALU.add),
                    waits=[res_tok[("a", j, tb)]], bump=B.c_v)
                tok_xp = B.c_v.tok()
                pl.step("y", lambda y, m=m, tb=tb, b=b: y.dma_start(
                    out=out_dram[m * 128:(m + 1) * 128, tb * QB:(tb + 1) * QB],
                    in_=B.XR4[b]), waits=[tok_xp], bump=B.c_st, n=16)
                res_st[("x", b)] = B.c_st.tok()
                res_st[("a", b)] = B.c_st.tok()
            if j == 0:
                res_loads(dg, 1, res_tok)
    return B.c_st.all()


def finish(P, pl, st_tok):
    wl = st_tok if isinstance(st_tok, list) else [st_tok]
    pl.step("y", lambda y: None, waits=wl)
    with P.nc.Block() as block:
        pl.emit_all(block)
    return P.fin()


def build_l0_attn(dbg=False):
    P = Prog()
    xn = P.din("xn", [D, S], BF16)
    wq_t = P.din("wq_t", [D, D], F32)
    wk_t = P.din("wk_t", [D, D], F32)
    wv_t = P.din("wv_t", [D, D], F32)
    wo_t = P.din("wo_t", [D, D], F32)
    bq2 = P.din("bq2", [128, DC], F32)
    bk2 = P.din("bk2", [128, DC], F32)
    bv_row = P.din("bv_row", [1, D], F32)
    bo2 = P.din("bo2", [128, DC], F32)
    a0 = P.dout("a0", [D, S], F32)

    B = alloc_common(P)
    alloc_l0_attn(P, B)
    pl = mk_plan(P, B)
    emit_setup(pl, B)
    t_ = load_xn(pl, B, xn)
    st_tok = emit_l0_attn(pl, B, wq_t, wk_t, wv_t, wo_t, bq2, bk2, bv_row, bo2,
                          a0, (t_, t_))
    if dbg:
        dumps = [("QF", B.QF, BF16), ("KF", B.KF, BF16), ("VT", B.VT, BF16),
                 ("EXPB", B.EXPB, BF16), ("CB", B.CBS, F32), ("MXR", B.MXR, F32),
                 ("ATT", B.ATT, BF16), ("BVB", B.BVB, BF16)]
        allw = [B.c_pe.tok(), B.c_v.tok(), B.c_a.tok(), B.c_g.tok()]
        for nm, buf, dt in dumps:
            od = P.dout("dbg_" + nm, list(buf.shape), dt)
            pl.step("y", lambda y, od=od, buf=buf: y.dma_start(out=od, in_=buf[:]),
                    waits=allw, bump=B.c_st, n=16)
        st_tok = B.c_st.all()
    return finish(P, pl, st_tok)


def build_l0_pre(dbg=False, no_ln=False):
    P = Prog()
    x = P.din("x", [D, S], F32)
    lg1 = P.din("lg1", [1, 256], F32)
    lg2 = P.din("lg2", [1, 256], F32)
    a1t = P.din("a1t", [256, 64, 1024], F32)
    b1 = P.din("b1", [256, 64, 4096], F32)
    a2t = P.din("a2t", [256, 64, 4096], F32)
    b2 = P.din("b2", [256, 64, 1024], F32)
    coreid = P.din("coreid", [1, 1], F32)
    xn0 = P.dout("xn0", [D, S], BF16)
    w1sl = P.dout("w1sl", [D, 512], BF16)
    w2sl = P.dout("w2sl", [512, D], BF16)

    B = alloc_common(P)
    B.XN = P.sb("XN", [128, DC * S], BF16)
    alloc_ln(P, B)
    alloc_compose(P, B)
    pl = mk_plan(P, B)
    emit_setup(pl, B)
    if not no_ln:
        tok_xn = emit_ln(pl, B, x)
        pl.step("y", lambda y: y.dma_start(
            out=xn0.rearrange("(c p) t -> p c t", p=128),
            in_=B.XN[:].rearrange("p (c t) -> p c t", c=DC)),
            waits=[tok_xn[0], tok_xn[1]], bump=B.c_st, n=16)
    else:
        pl.step("y", lambda y: y.dma_start(
            out=xn0.rearrange("(c p) t -> p c t", p=128),
            in_=B.XN[:].rearrange("p (c t) -> p c t", c=DC)),
            bump=B.c_st, n=16)
    emit_compose_setup(pl, B, coreid)
    st = emit_compose(pl, B, lg1, lg2, a1t, b1, a2t, b2, w1sl, w2sl)
    if dbg:
        allw0 = [B.c_pe.tok(), B.c_v.tok(), B.c_a.tok(), B.c_g.tok()] \
            + B.c_wg.all()
        pl.step("v", lambda v: v.tensor_copy(B.U2[:], B.IDXA[:]), waits=allw0)
        pl.step("v", lambda v: v.tensor_copy(B.IDXAf[:], B.IDX8[:]),
                bump=B.c_v)
        dumps = [("IDXA", B.U2, F32), ("IDX8", B.IDXAf, F32),
                 ("MS", B.MS, F32), ("W32", B.W32, F32), ("IVF", B.IVF, F32),
                 ("GA0", B.GA[0], BF16), ("GB0", B.GB[0], BF16),
                 ("GAS0", B.GAS[0], BF16), ("QM64", B.QM64, F32),
                 ("PEVB", B.PEVB, F32), ("SELB", B.SELB, F32)]
        allw = [B.c_pe.tok(), B.c_v.tok(), B.c_a.tok(), B.c_g.tok()]
        for nm, buf, dt in dumps:
            od = P.dout("dbg_" + nm, list(buf.shape), dt)
            pl.step("y", lambda y, od=od, buf=buf: y.dma_start(out=od, in_=buf[:]),
                    waits=allw, bump=B.c_st, n=16)
    return finish(P, pl, B.c_st.all())


def build_l0_ffn(dbg=False):
    P = Prog()
    xn = P.din("xn", [D, S], BF16)
    w1g = P.din("w1g", [8, D, 512], BF16)
    w2g = P.din("w2g", [4096, D], BF16)
    x = P.din("x", [D, S], F32)
    a0 = P.din("a0", [D, S], F32)
    x1 = P.dout("x1", [D, S], F32)
    xn1 = P.dout("xn1", [D, S], BF16)
    mx = P.dout("mx", [1, 1], F32)

    B = alloc_common(P)
    B.XN = P.sb("XN", [128, DC * S], BF16)
    B.AMC = P.sb("AMC", [128, DC + 1], F32)
    B.AMX = P.sb("AMX", [1, 16], F32)
    alloc_ffn0(P, B)
    alloc_ln(P, B)
    pl = mk_plan(P, B)
    emit_setup(pl, B)
    t = load_xn(pl, B, xn)
    st = emit_ffn0_apply(pl, B, w1g, w2g, x, a0, x1, (t, t),
                         no_gelu=(dbg == "nogelu"))
    tok_xn1 = emit_ln(pl, B, x1, wait=st)
    pl.step("y", lambda y: y.dma_start(
        out=xn1.rearrange("(c p) t -> p c t", p=128),
        in_=B.XN[:].rearrange("p (c t) -> p c t", c=DC)),
        waits=[tok_xn1[0], tok_xn1[1]], bump=B.c_st, n=16)
    tok_mx = emit_absmax_xn(pl, B, tok_xn1)
    pl.step("y", lambda y: y.dma_start(out=mx, in_=B.AMX[0:1, 0:1]),
            waits=[tok_mx], bump=B.c_st, n=16)
    st = B.c_st.all()
    if dbg:
        allw = [B.c_pe.tok(), B.c_v.tok(), B.c_a.tok(), B.c_g.tok(), st]
        od = P.dout("dbg_H", [128, 32 * 512], BF16)
        pl.step("y", lambda y, od=od: y.dma_start(out=od, in_=B.H[:]),
                waits=allw, bump=B.c_st, n=16)
        st = B.c_st.all()
    return finish(P, pl, st)


def build_l1_attn():
    P = Prog()
    xn = P.din("xn", [D, S], BF16)
    w_in = P.din("w_in", [D, D], F32)
    w_r = P.din("w_r", [D, D], F32)
    w_i = P.din("w_i", [D, D], F32)
    w_out = P.din("w_out", [D, D], F32)
    b_in2 = P.din("b_in2", [128, DC], F32)
    b_r2 = P.din("b_r2", [128, DC], F32)
    b_i2 = P.din("b_i2", [128, DC], F32)
    b_out2 = P.din("b_out2", [128, DC], F32)
    log_a2 = P.din("log_a2", [128, DC], F32)
    a1 = P.dout("a1", [D, S], F32)
    B = alloc_common(P)
    B.XN = P.sb("XN", [128, DC * S], BF16)
    alloc_l1_attn(P, B)
    pl = mk_plan(P, B)
    emit_setup(pl, B)
    t = load_xn(pl, B, xn)
    st = emit_l1_attn(pl, B, w_in, w_r, w_i, w_out, b_in2, b_r2, b_i2, b_out2,
                      log_a2, a1, (t, t))
    return finish(P, pl, st)


def build_l1_quant():
    P = Prog()
    wg_sl = P.din("wg_sl", [352, 1024], F32)
    wu_sl = P.din("wu_sl", [352, 1024], F32)
    wd_sl = P.din("wd_sl", [128, 2816], F32)
    wgt_sl = P.dout("wgt_sl", [D, 352], BF16)
    wut_sl = P.dout("wut_sl", [D, 352], BF16)
    wdt_sl = P.dout("wdt_sl", [2816, 128], BF16)
    sg_sl = P.dout("sg_sl", [1, 352], F32)
    su_sl = P.dout("su_sl", [1, 352], F32)
    sd_sl = P.dout("sd_sl", [1, 128], F32)
    B = alloc_common(P)
    alloc_l1_quant(P, B)
    pl = mk_plan(P, B)
    emit_setup(pl, B)
    st = emit_l1_quant(pl, B, wg_sl, wu_sl, wd_sl, wgt_sl, wut_sl, wdt_sl,
                       sg_sl, su_sl, sd_sl)
    return finish(P, pl, st)


def build_l1_ffn_a():
    P = Prog()
    xn = P.din("xn", [D, S], BF16)
    mx8 = P.din("mx8", [1, 8], F32)
    clip_x = P.din("clip_x", [1, 1], F32)
    wgt = P.din("wgt", [D, 2816], BF16)
    wut = P.din("wut", [D, 2816], BF16)
    sg = P.din("sg", [1, 2816], F32)
    su = P.din("su", [1, 2816], F32)
    h1 = P.dout("h1", [2816, S], BF16)
    hmax = P.dout("hmax", [1, 1], F32)
    B = alloc_common(P)
    B.XN = P.sb("XN", [128, DC * S], BF16)
    B.QF = P.sb("QF", [128, DC * S], BF16)
    B.KF = P.sb("KF", [128, DC * S], BF16)
    alloc_l1_ffn(P, B, "a")
    B.sx_ld_guard = None
    pl = mk_plan(P, B)
    emit_setup(pl, B)
    t = load_xn(pl, B, xn)
    st = emit_l1_ffn_a(pl, B, mx8, clip_x, wgt, wut, sg, su, h1, hmax, (t, t))
    return finish(P, pl, st)


def build_l1_ffn_b():
    P = Prog()
    hmax8 = P.din("hmax8", [1, 8], F32)
    clip_d = P.din("clip_d", [1, 1], F32)
    wdt = P.din("wdt", [2816, D], BF16)
    sd = P.din("sd", [1, D], F32)
    h1 = P.din("h1", [2816, S], BF16)
    x1 = P.din("x1", [D, S], F32)
    a1 = P.din("a1", [D, S], F32)
    out = P.dout("out", [D, S], F32)
    B = alloc_common(P)
    alloc_l1_ffn(P, B, "b")
    B.sx_ld_guard = None
    pl = mk_plan(P, B)
    emit_setup(pl, B)
    st = emit_l1_ffn_b(pl, B, hmax8, clip_d, wdt, sd, h1, x1, a1, out)
    return finish(P, pl, st)



def alloc_fused(P):
    B = alloc_common(P)
    B.XN = P.sb("XN", [128, DC * S], BF16)
    B.QF = P.sb("QF", [128, DC * S], BF16)
    B.KF = P.sb("KF", [128, DC * S], BF16)
    B.VT = P.sb("VT", [128, NT * D], BF16)
    B.STASH = P.sb("STASH", [128, 8 * 512], F16)
    B.ATT = B.STASH.bitcast(BF16)
    B.EXPB = P.sb("EXPB", [128, NT * 512], BF16)
    B.WARE = [P.sb(f"ware{i}", [128, DC * 256], BF16) for i in range(2)]
    B.RZ = P.sb("RZN", [128, 512], F32)
    B.RN = B.RZ
    B.FS = P.sb("FS", [128, 5 * 512], F32)
    FSs = [B.FS[:, i * 512:(i + 1) * 512] for i in range(5)]
    B.RZR = B.FS[0:1, 2048:2560]
    B.A0T = [FSs[0], FSs[1]]
    B.RSTB = FSs[2]
    B.MRB = FSs[3]
    B.GU = B.RZ[:]
    B.GS = FSs[4]
    B.XRT = [FSs[0], FSs[1]]
    B.ART = [FSs[2], FSs[3]]
    B.SIL = FSs[4]
    B.X2B = [B.EXPB[:, 0:512], B.EXPB[:, 512:1024]]
    B.ROWA = B.FS[0:1, 2048:2560]
    B.ROWB = B.RZ[0:1, 0:512]
    B.EPSB = P.sb("epsb", [1, 1], F32)
    B.eps_init = False
    B.MXR = P.sb("MXR", [1, 8], F32)
    B.NCR = P.sb("NCR", [1, 8], F32)
    B.CBS = P.sb("CBS", [128, 8], F32)
    B.BQ = P.sb("BQ", [128, DC], F32)
    B.BK = P.sb("BK", [128, DC], F32)
    B.BO = P.sb("BO", [128, DC], F32)
    B.BVR = B.EXPB[0:1, 2048:3072]
    B.BVB = B.EXPB[:, 1024:2048]
    B.LG = B.FS[0:1, 0:256]
    B.LGW = B.FS[32:33, 0:256]
    B.MV = P.sb("MV", [1, 32], F32)
    B.IV = P.sb("IV", [1, 32], U32)
    B.IVF = P.sb("IVF", [1, 32], F32)
    B.EW = P.sb("EW", [1, 32], F32)
    B.SWr = P.sb("SWr", [1, 2], F32)
    B.W32 = P.sb("W32", [1, 32], F32)
    B.SELB = P.sb("SELB", [128, 32], F32)
    B.W32B = P.sb("W32B", [128, 32], F32)
    B.PEVB = P.sb("PEVB", [128, 32], F32)
    B.EOT = P.sb("EOT", [128, 32], F32)
    B.UB = P.sb("UB", [128, 32], F32)
    B.U2 = P.sb("U2", [128, 16], F32)
    B.MS = P.sb("MS", [128, 16], F32)
    B.IDXAf = P.sb("IDXAf", [128, 16], F32)
    B.IDXA = P.sb("IDXA", [128, 16], U32)
    B.IDX8 = P.sb("IDX8", [128, 16], U32)
    B.QM64 = P.sb("QM64", [128, 1], F32)
    B.QI = P.sb("QI", [128, 1], I32)
    B.COREB = P.sb("COREB", [128, 1], F32)
    B.GA = [B.QF[:, 0:1024], B.QF[:, 1024:2048]]
    B.GAS = [B.QF[:, 2048:3072], B.QF[:, 3072:4096]]
    B.GB = [B.QF[:, 4096:5120], B.QF[:, 5120:6144]]
    B.STG = [B.QF[:, 6144:6656], B.QF[:, 6656:7168]]
    B.H = B.QF
    B.WST = [B.KF[:, 0:8192], B.KF[:, 8192:16384]]
    B.QWB = B.VT[:, 0:3072]
    B.QDB = B.VT[:, 3072:5888]
    B.QBF = B.VT[:, 5888:8960]
    B.TQB = B.VT[:, 8960:11776]
    B.TQD = B.VT[:, 11776:14592]
    B.IBUF = P.sb("IBUF", [128, 2816], I8)
    B.QI8 = B.IBUF[:, 0:2816]
    B.SCL = P.sb("SCL", [128, 4], F32)
    B.RCL = P.sb("RCL", [128, 4], F32)
    B.IP = B.QF
    B.RG = B.KF
    B.IG = B.VT
    B.BIN = P.sb("BIN", [128, DC], F32)
    B.BR = P.sb("BR", [128, DC], F32)
    B.BI = P.sb("BI", [128, DC], F32)
    B.BOUT = P.sb("BOUT", [128, DC], F32)
    B.ASG = P.sb("ASG", [128, DC], F32)
    B.XQB = B.KF
    B.XQI = B.IBUF[:, 0:2048]
    B.H1 = B.VT[:, 0:22 * 512]
    B.H1I = B.IBUF[:, 0:512]
    B.SGB = P.sb("SGB", [128, 22], F32)
    B.SUB = P.sb("SUB", [128, 22], F32)
    B.SDB = P.sb("SDB", [128, DC], F32)
    B.WQS = [B.QF[:, 0:1024], B.QF[:, 1024:2048],
             B.QF[:, 2048:3072], B.QF[:, 3072:4096]]
    B.MXH = P.sb("MXH", [1, 92], F32)
    B.AMC = P.sb("AMC", [128, DC + 1], F32)
    B.AMX = P.sb("AMX", [1, 16], F32)
    B.CLP = P.sb("CLP", [1, 4], F32)
    B.RSXB = P.sb("RSXB", [128, 1], F32)
    B.SXB = P.sb("SXB", [128, 1], F32)
    B.RSHB = P.sb("RSHB", [128, 1], F32)
    B.SHB = P.sb("SHB", [128, 1], F32)
    B.H1S = []
    for k in range(88):
        if k < 32:
            B.H1S.append(B.XN[:, k * 512:(k + 1) * 512])
        elif k < 64:
            B.H1S.append(B.VT[:, (k - 32) * 512:(k - 31) * 512])
        else:
            B.H1S.append(B.QF[:, (k - 56) * 512:(k - 55) * 512])
    B.SIL2 = B.RZ[:, 0:512]
    EXPF = B.EXPB.bitcast(F32)
    B.XR4 = [EXPF[:, i * 512:(i + 1) * 512] for i in range(4)]
    B.AR4 = [EXPF[:, (4 + i) * 512:(5 + i) * 512] for i in range(4)]
    return B


RG8 = [list(range(8))]


def build_fused():
    P = Prog()
    x = P.din("x", [D, S], F32)
    wq_t = P.din("wq_t", [D, D], F32)
    wk_t = P.din("wk_t", [D, D], F32)
    wv_t = P.din("wv_t", [D, D], F32)
    wo_t = P.din("wo_t", [D, D], F32)
    bq2 = P.din("bq2", [128, DC], F32)
    bk2 = P.din("bk2", [128, DC], F32)
    bv_row = P.din("bv_row", [1, D], F32)
    bo2 = P.din("bo2", [128, DC], F32)
    lg1 = P.din("lg1", [1, 256], F32)
    lg2 = P.din("lg2", [1, 256], F32)
    a1t = P.din("a1t", [256, 64, 1024], F32)
    b1 = P.din("b1", [256, 64, 4096], F32)
    a2t = P.din("a2t", [256, 64, 4096], F32)
    b2 = P.din("b2", [256, 64, 1024], F32)
    coreid = P.din("coreid", [1, 1], F32)
    w_in = P.din("w_in", [D, D], F32)
    w_r = P.din("w_r", [D, D], F32)
    w_i = P.din("w_i", [D, D], F32)
    w_out = P.din("w_out", [D, D], F32)
    b_in2 = P.din("b_in2", [128, DC], F32)
    b_r2 = P.din("b_r2", [128, DC], F32)
    b_i2 = P.din("b_i2", [128, DC], F32)
    b_out2 = P.din("b_out2", [128, DC], F32)
    log_a2 = P.din("log_a2", [128, DC], F32)
    wg_sl = P.din("wg_sl", [352, 1024], F32)
    wu_sl = P.din("wu_sl", [352, 1024], F32)
    wd_sl = P.din("wd_sl", [128, 2816], F32)
    clip_x = P.din("clip_x", [1, 1], F32)
    clip_d = P.din("clip_d", [1, 1], F32)
    out = P.dout("out", [D, S], F32)
    a0 = P.dtmp("a0", [D, S], F32)
    a1 = P.dtmp("a1", [D, S], F32)
    x1 = P.dtmp("x1", [D, S], F32)
    w1in = P.dtmp("w1in", [D, 512], BF16)
    w1out = P.dtmp("w1out", [8 * D, 512], BF16)
    w2in = P.dtmp("w2in", [512, D], BF16)
    w2out = P.dtmp("w2out", [4096, D], BF16)
    wgt_in = P.dtmp("wgt_in", [D, 352], BF16)
    wgt_out = P.dtmp("wgt_out", [8 * D, 352], BF16)
    wut_in = P.dtmp("wut_in", [D, 352], BF16)
    wut_out = P.dtmp("wut_out", [8 * D, 352], BF16)
    wdt_in = P.dtmp("wdt_in", [2816, 128], BF16)
    wdt_out = P.dtmp("wdt_out", [8 * 2816, 128], BF16)
    sg_in = P.dtmp("sg_in", [1, 352], F32)
    sg_out = P.dtmp("sg_out", [8, 352], F32)
    su_in = P.dtmp("su_in", [1, 352], F32)
    su_out = P.dtmp("su_out", [8, 352], F32)
    sd_in = P.dtmp("sd_in", [1, 128], F32)
    sd_out = P.dtmp("sd_out", [8, 128], F32)
    mx_in = P.dtmp("mx_in", [1, 16], F32)
    mx_out = P.dtmp("mx_out", [1, 16], F32)
    hm_in = P.dtmp("hm_in", [1, 16], F32)
    hm_out = P.dtmp("hm_out", [1, 16], F32)

    B = alloc_fused(P)
    B.sx_ld_guard = None
    pl = mk_plan(P, B)
    emit_setup(pl, B)
    emit_compose_setup(pl, B, coreid)

    tok_xn0 = emit_ln(pl, B, x)
    st_comp = emit_compose(pl, B, lg1, lg2, a1t, b1, a2t, b2, w1in, w2in)
    for cin, cout in ((w1in, w1out), (w2in, w2out)):
        pl.step("g", lambda g, cin=cin, cout=cout: g.collective_compute(
            "AllGather", ALU.bypass, replica_groups=RG8,
            ins=[cin.opt()], outs=[cout.opt()]),
            waits=[st_comp], bump=B.c_cc)
    tok_ag1 = B.c_cc.tok()
    bank_set(B, B.A0T[0], B.c_v.tok())
    st_attn = emit_l0_attn(pl, B, wq_t, wk_t, wv_t, wo_t, bq2, bk2, bv_row,
                           bo2, a0, tok_xn0, guard=[st_comp])
    attn_done = B.c_pe.tok()
    q_done = emit_l1_quant(pl, B, wg_sl, wu_sl, wd_sl, wgt_in, wut_in, wdt_in,
                           sg_in, su_in, sd_in, guard=attn_done)
    st_ffn0 = emit_ffn0_apply(
        pl, B, w1out.rearrange("(r d) i -> r d i", r=8), w2out, x, a0, x1,
        tok_xn0, guard=[attn_done], wsrc_tok=tok_ag1)
    for cin, cout in ((wgt_in, wgt_out), (wut_in, wut_out), (wdt_in, wdt_out),
                      (sg_in, sg_out), (su_in, su_out), (sd_in, sd_out)):
        pl.step("g", lambda g, cin=cin, cout=cout: g.collective_compute(
            "AllGather", ALU.bypass, replica_groups=RG8,
            ins=[cin.opt()], outs=[cout.opt()]),
            waits=[q_done], bump=B.c_cc)
    tok_ag2 = B.c_cc.tok()

    tok_xn1 = emit_ln(pl, B, x1, wait=st_ffn0)
    tok_mx = emit_absmax_xn(pl, B, tok_xn1)
    pl.step("y", lambda y: y.dma_start(out=mx_in[0:1, 0:1], in_=B.AMX[0:1, 0:1]),
            waits=[tok_mx], bump=B.c_st, n=16)
    tok_mxst = B.c_st.tok()
    pl.step("g", lambda g: g.collective_compute(
        "AllReduce", ALU.max, replica_groups=RG8,
        ins=[mx_in.opt()], outs=[mx_out.opt()]),
        waits=[tok_mxst], bump=B.c_cc)
    tok_ar3 = B.c_cc.tok()
    st_a1 = emit_l1_attn(pl, B, w_in, w_r, w_i, w_out, b_in2, b_r2, b_i2,
                         b_out2, log_a2, a1, tok_xn1, guard=attn_done,
                         evac_guard=q_done)
    l1_done_v = B.c_v.tok()
    B.sx_ld_guard = tok_ar3
    st_h1 = emit_ffn1a_res(
        pl, B, mx_out[0:1, 0:1], clip_x,
        wgt_out.rearrange("(r d) i -> r d i", r=8),
        wut_out.rearrange("(r d) i -> r d i", r=8),
        sg_out, su_out, hm_in[0:1, 0:1], tok_xn1,
        guard=[l1_done_v, tok_ag2, q_done])
    pl.step("g", lambda g: g.collective_compute(
        "AllReduce", ALU.max, replica_groups=RG8,
        ins=[hm_in.opt()], outs=[hm_out.opt()]),
        waits=[st_h1], bump=B.c_cc)
    tok_ar4 = B.c_cc.tok()
    ffnb_guard = [B.c_pe.tok(), B.c_v.tok(), B.c_g.tok()] + B.c_st.all()
    B.sx_ld_guard = tok_ar4
    st_out = emit_ffn1b_res(
        pl, B, hm_out[0:1, 0:1], clip_d,
        wdt_out.rearrange("(r i) d -> r i d", r=8), sd_out,
        x1, a1, out, guard=ffnb_guard)
    return finish(P, pl, st_out)


def lay_bias(b):
    return np.ascontiguousarray(b.reshape(DC, 128).T)


CORES = list(range(8))


def host_prep(g):
    x = np.asarray(g["L0_attn_batch"])
    shared = {
        "wq_t": np.ascontiguousarray(np.asarray(g["L0_attn_Wq"]).T),
        "wk_t": np.ascontiguousarray(np.asarray(g["L0_attn_Wk"]).T),
        "wv_t": np.ascontiguousarray(np.asarray(g["L0_attn_Wv"]).T),
        "wo_t": np.ascontiguousarray(np.asarray(g["L0_attn_Wo"]).T),
        "bq2": lay_bias(np.asarray(g["L0_attn_bq"])),
        "bk2": lay_bias(np.asarray(g["L0_attn_bk"])),
        "bo2": lay_bias(np.asarray(g["L0_attn_bo"])),
        "bv_row": np.ascontiguousarray(np.asarray(g["L0_attn_bv"]).reshape(1, D)),
        "lg1": np.ascontiguousarray(np.asarray(g["L0_ffn_fc1_logits"]).reshape(1, 256)),
        "lg2": np.ascontiguousarray(np.asarray(g["L0_ffn_fc2_logits"]).reshape(1, 256)),
        "a1t": np.ascontiguousarray(np.asarray(g["L0_ffn_fc1_A"]).transpose(0, 2, 1)),
        "b1": np.ascontiguousarray(g["L0_ffn_fc1_B"]),
        "a2t": np.ascontiguousarray(np.asarray(g["L0_ffn_fc2_A"]).transpose(0, 2, 1)),
        "b2": np.ascontiguousarray(g["L0_ffn_fc2_B"]),
        "w_in": np.ascontiguousarray(g["L1_attn_W_in"]),
        "w_r": np.ascontiguousarray(g["L1_attn_W_r"]),
        "w_i": np.ascontiguousarray(g["L1_attn_W_i"]),
        "w_out": np.ascontiguousarray(g["L1_attn_W_out"]),
        "b_in2": lay_bias(np.asarray(g["L1_attn_b_in"])),
        "b_r2": lay_bias(np.asarray(g["L1_attn_b_r"])),
        "b_i2": lay_bias(np.asarray(g["L1_attn_b_i"])),
        "b_out2": lay_bias(np.asarray(g["L1_attn_b_out"])),
        "log_a2": lay_bias(np.asarray(g["L1_attn_log_a"])),
        "clip_x": np.ascontiguousarray(np.asarray(g["L1_ffn_clip_x"]).reshape(1, 1)),
        "clip_d": np.ascontiguousarray(np.asarray(g["L1_ffn_clip_down"]).reshape(1, 1)),
    }
    wg = np.asarray(g["L1_ffn_w_gate"])
    wu = np.asarray(g["L1_ffn_w_up"])
    wd = np.asarray(g["L1_ffn_w_down"])
    in_maps = []
    for c in range(8):
        m = dict(shared)
        m["x"] = np.ascontiguousarray(x[c].T)
        m["coreid"] = np.array([[c]], dtype=np.float32)
        m["wg_sl"] = np.ascontiguousarray(wg[c * 352:(c + 1) * 352])
        m["wu_sl"] = np.ascontiguousarray(wu[c * 352:(c + 1) * 352])
        m["wd_sl"] = np.ascontiguousarray(wd[c * 128:(c + 1) * 128])
        in_maps.append(m)
    return in_maps


def kernel(**inputs):
    in_maps = host_prep(inputs)
    nc = build_fused()
    res = run_bass_kernel_spmd(nc, in_maps, core_ids=CORES)
    out = np.stack([np.asarray(res.results[c]["out"]).T for c in range(8)])
    return np.ascontiguousarray(out.astype(np.float32))


def kernel_unfused(**inputs):
    g = inputs
    x = np.asarray(g["L0_attn_batch"])
    xT = [np.ascontiguousarray(x[c].T) for c in range(8)]

    nc = build_l0_pre()
    shared = {
        "lg1": np.ascontiguousarray(g["L0_ffn_fc1_logits"].reshape(1, 256)),
        "lg2": np.ascontiguousarray(g["L0_ffn_fc2_logits"].reshape(1, 256)),
        "a1t": np.ascontiguousarray(np.asarray(g["L0_ffn_fc1_A"]).transpose(0, 2, 1)),
        "b1": np.ascontiguousarray(g["L0_ffn_fc1_B"]),
        "a2t": np.ascontiguousarray(np.asarray(g["L0_ffn_fc2_A"]).transpose(0, 2, 1)),
        "b2": np.ascontiguousarray(g["L0_ffn_fc2_B"]),
    }
    in_maps = []
    for c in range(8):
        m = dict(shared)
        m["x"] = xT[c]
        m["coreid"] = np.array([[c]], dtype=np.float32)
        in_maps.append(m)
    r1 = run_bass_kernel_spmd(nc, in_maps, core_ids=CORES)
    xn0 = [np.asarray(r1.results[c]["xn0"]) for c in range(8)]
    w1g = np.ascontiguousarray(np.stack(
        [np.asarray(r1.results[c]["w1sl"]) for c in range(8)]))
    w2g = np.ascontiguousarray(np.concatenate(
        [np.asarray(r1.results[c]["w2sl"]) for c in range(8)], axis=0))

    nc = build_l0_attn()
    shared = {
        "wq_t": np.ascontiguousarray(np.asarray(g["L0_attn_Wq"]).T),
        "wk_t": np.ascontiguousarray(np.asarray(g["L0_attn_Wk"]).T),
        "wv_t": np.ascontiguousarray(np.asarray(g["L0_attn_Wv"]).T),
        "wo_t": np.ascontiguousarray(np.asarray(g["L0_attn_Wo"]).T),
        "bq2": lay_bias(np.asarray(g["L0_attn_bq"])),
        "bk2": lay_bias(np.asarray(g["L0_attn_bk"])),
        "bo2": lay_bias(np.asarray(g["L0_attn_bo"])),
        "bv_row": np.ascontiguousarray(np.asarray(g["L0_attn_bv"]).reshape(1, D)),
    }
    in_maps = [dict(shared, xn=xn0[c]) for c in range(8)]
    r2 = run_bass_kernel_spmd(nc, in_maps, core_ids=CORES)
    a0 = [np.asarray(r2.results[c]["a0"]) for c in range(8)]

    nc = build_l0_ffn()
    in_maps = [{"xn": xn0[c], "w1g": w1g, "w2g": w2g, "x": xT[c], "a0": a0[c]}
               for c in range(8)]
    r3 = run_bass_kernel_spmd(nc, in_maps, core_ids=CORES)
    x1 = [np.asarray(r3.results[c]["x1"]) for c in range(8)]
    xn1 = [np.asarray(r3.results[c]["xn1"]) for c in range(8)]
    mx8 = np.ascontiguousarray(np.concatenate(
        [np.asarray(r3.results[c]["mx"]) for c in range(8)], axis=1))

    nc = build_l1_quant()
    wg = np.asarray(g["L1_ffn_w_gate"])
    wu = np.asarray(g["L1_ffn_w_up"])
    wd = np.asarray(g["L1_ffn_w_down"])
    in_maps = [{
        "wg_sl": np.ascontiguousarray(wg[c * 352:(c + 1) * 352]),
        "wu_sl": np.ascontiguousarray(wu[c * 352:(c + 1) * 352]),
        "wd_sl": np.ascontiguousarray(wd[c * 128:(c + 1) * 128]),
    } for c in range(8)]
    r4 = run_bass_kernel_spmd(nc, in_maps, core_ids=CORES)
    wgt = np.ascontiguousarray(np.concatenate(
        [np.asarray(r4.results[c]["wgt_sl"]) for c in range(8)], axis=1))
    wut = np.ascontiguousarray(np.concatenate(
        [np.asarray(r4.results[c]["wut_sl"]) for c in range(8)], axis=1))
    wdt = np.ascontiguousarray(np.concatenate(
        [np.asarray(r4.results[c]["wdt_sl"]) for c in range(8)], axis=1))
    sg = np.ascontiguousarray(np.concatenate(
        [np.asarray(r4.results[c]["sg_sl"]) for c in range(8)], axis=1))
    su = np.ascontiguousarray(np.concatenate(
        [np.asarray(r4.results[c]["su_sl"]) for c in range(8)], axis=1))
    sd = np.ascontiguousarray(np.concatenate(
        [np.asarray(r4.results[c]["sd_sl"]) for c in range(8)], axis=1))

    nc = build_l1_attn()
    shared = {
        "w_in": np.ascontiguousarray(g["L1_attn_W_in"]),
        "w_r": np.ascontiguousarray(g["L1_attn_W_r"]),
        "w_i": np.ascontiguousarray(g["L1_attn_W_i"]),
        "w_out": np.ascontiguousarray(g["L1_attn_W_out"]),
        "b_in2": lay_bias(np.asarray(g["L1_attn_b_in"])),
        "b_r2": lay_bias(np.asarray(g["L1_attn_b_r"])),
        "b_i2": lay_bias(np.asarray(g["L1_attn_b_i"])),
        "b_out2": lay_bias(np.asarray(g["L1_attn_b_out"])),
        "log_a2": lay_bias(np.asarray(g["L1_attn_log_a"])),
    }
    in_maps = [dict(shared, xn=xn1[c]) for c in range(8)]
    r5 = run_bass_kernel_spmd(nc, in_maps, core_ids=CORES)
    a1 = [np.asarray(r5.results[c]["a1"]) for c in range(8)]

    nc = build_l1_ffn_a()
    clip_x = np.ascontiguousarray(np.asarray(g["L1_ffn_clip_x"]).reshape(1, 1))
    clip_d = np.ascontiguousarray(np.asarray(g["L1_ffn_clip_down"]).reshape(1, 1))
    shared = {"mx8": mx8, "clip_x": clip_x, "wgt": wgt, "wut": wut,
              "sg": sg, "su": su}
    in_maps = [dict(shared, xn=xn1[c]) for c in range(8)]
    r6 = run_bass_kernel_spmd(nc, in_maps, core_ids=CORES)
    h1 = [np.asarray(r6.results[c]["h1"]) for c in range(8)]
    hm8 = np.ascontiguousarray(np.concatenate(
        [np.asarray(r6.results[c]["hmax"]) for c in range(8)], axis=1))

    nc = build_l1_ffn_b()
    shared = {"hmax8": hm8, "clip_d": clip_d, "wdt": wdt, "sd": sd}
    in_maps = [dict(shared, h1=h1[c], x1=x1[c], a1=a1[c]) for c in range(8)]
    r7 = run_bass_kernel_spmd(nc, in_maps, core_ids=CORES)
    out = np.stack([np.asarray(r7.results[c]["out"]).T for c in range(8)])
    return np.ascontiguousarray(out.astype(np.float32))


# version 10
# speedup vs baseline: 1.4048x; 1.0142x over previous
import numpy as np
from contextlib import ExitStack
import concourse.bass as bass
import concourse.mybir as mybir
from concourse.bass_core_rust import IndirectOffsetOnAxis
from concourse.bass_utils import run_bass_kernel_spmd

F32 = mybir.dt.float32
BF16 = mybir.dt.bfloat16
F16 = mybir.dt.float16
I8 = mybir.dt.int8
I32 = mybir.dt.int32
U32 = mybir.dt.uint32
AF = mybir.ActivationFunctionType
ALU = mybir.AluOpType
AX = mybir.AxisListType

S = 2048
D = 1024
DC = 8
NT = 16
QB = 512
NQB = S // QB
ATT_SCALE = 1.0 / 32.0


class Ctr:
    __slots__ = ("h", "v", "name")

    def __init__(self, h, name):
        self.h = h
        self.v = 0
        self.name = name

    def tok(self):
        return (self, self.v)


class DmaPool:

    def __init__(self, subs):
        self.subs = subs
        self.i = 0
        self.last = None

    def nxt(self):
        c = self.subs[self.i]
        self.i = (self.i + 1) % len(self.subs)
        return c

    def tok(self):
        return self.last

    def all(self):
        return [(c, c.v) for c in self.subs if c.v > 0]


class Prog:
    def __init__(self):
        self.nc = bass.Bass("TRN2", target_bir_lowering=False)
        self.es = ExitStack()

    def sb(self, name, shape, dt=F32):
        return self.es.enter_context(self.nc.sbuf_tensor(name, shape, dt))

    def psum(self, name, shape, dt=F32):
        return self.es.enter_context(self.nc.psum_tensor(name, shape, dt))

    def sem(self, name):
        return Ctr(self.es.enter_context(self.nc.semaphore(name)), name)

    def din(self, name, shape, dt=F32):
        return self.nc.dram_tensor(name, shape, dt, kind="ExternalInput").ap()

    def dout(self, name, shape, dt=F32):
        return self.nc.dram_tensor(name, shape, dt, kind="ExternalOutput").ap()

    def dtmp(self, name, shape, dt=F32):
        return self.nc.dram_tensor(name, shape, dt, kind="Internal").ap()

    def fin(self):
        self.es.close()
        return self.nc


ENGS = ("t", "v", "s", "g", "y")


class Plan:

    def __init__(self, P, own=None):
        self.P = P
        self.q = {e: [] for e in ENGS}
        self.last_wait = {e: {} for e in ENGS}
        self.own = own or {}

    def step(self, eng, emit, waits=(), bump=None, n=1):
        own = self.own.get(eng)
        waits = list(waits)
        if isinstance(bump, DmaPool):
            pool = bump
            bump = pool.nxt()
            waits.append(bump.tok())
            pool.last = (bump, bump.v + n)
        if own is not None:
            waits.append(own.tok())
            if bump is None:
                bump = own
                n = 1
        wl = []
        flat = []

        def _fl(ws):
            for w in ws:
                if isinstance(w, list):
                    _fl(w)
                else:
                    flat.append(w)
        _fl(waits)
        for w in flat:
            if w is None:
                continue
            c, v = w if isinstance(w, tuple) else (w, w.v)
            assert v <= c.v, f"wait on future value of {c.name}"
            if v <= 0:
                continue
            lw = self.last_wait[eng].get(c, 0)
            if v <= lw:
                continue
            self.last_wait[eng][c] = v
            wl.append((c, v))
        if bump is not None:
            bump.v += n
        self.q[eng].append((wl, emit, bump, n))

    def emit_all(self, block):
        def runner(lst):
            def run(e):
                for wl, emit, bmp, n in lst:
                    for c, v in wl:
                        e.wait_ge(c.h, v)
                    ins = emit(e)
                    if bmp is not None:
                        assert ins is not None
                        ins.then_inc(bmp.h, n)
            return run
        for key, meth in (("y", block.sync), ("g", block.gpsimd),
                          ("t", block.tensor), ("v", block.vector),
                          ("s", block.scalar)):
            if self.q[key]:
                meth(runner(self.q[key]))


class NS:
    def __init__(self, **kw):
        self.__dict__.update(kw)



def alloc_common(P):
    B = NS()
    B.c_pe = P.sem("c_pe")
    B.c_v = P.sem("c_v")
    B.c_a = P.sem("c_a")
    B.c_g = P.sem("c_g")
    B.c_ld = DmaPool([P.sem(f"c_ld{i}") for i in range(6)])
    B.c_wg = DmaPool([P.sem(f"c_wg{i}") for i in range(6)])
    B.c_st = DmaPool([P.sem(f"c_st{i}") for i in range(4)])
    B.c_sd = DmaPool([P.sem(f"c_sd{i}") for i in range(4)])
    B.c_cc = P.sem("c_cc")
    B.iotat = P.sb("iotat", [128, 128], I32)
    B.ident_f = P.sb("ident_f", [128, 128], F32)
    B.ones128 = P.sb("ones128", [128, 128], BF16)
    B.ones_c = P.sb("ones_c", [128, 1], BF16)
    B.ones_r = P.sb("ones_r", [1, 128], BF16)
    B.ones_rf = P.sb("ones_rf", [1, 128], F32)
    B.AV = [P.psum(f"av{i}", [128, 512], F32) for i in range(4)]
    B.SC = [P.psum(f"sc{i}", [128, 512], F32) for i in range(2)]
    B.AUX = P.psum("aux", [128, 512], F32)
    B.NRM = P.psum("nrm", [128, 512], F32)
    B.bank_tok = {}
    return B


def emit_setup(pl, B):
    pl.step("g", lambda g: (g.iota(B.iotat[:], [[1, 128]], base=0,
                            channel_multiplier=-1), g.engine_nop())[1])
    pl.step("g", lambda g: g.tensor_scalar(B.ident_f[:], B.iotat[:], 0, None, ALU.is_equal))
    pl.step("g", lambda g: g.memset(B.ones128[:], 1.0), bump=B.c_g)
    pl.step("v", lambda v: v.memset(B.ones_c[:], 1.0))
    pl.step("v", lambda v: v.memset(B.ones_r[:], 1.0))
    pl.step("v", lambda v: v.memset(B.ones_rf[:], 1.0), bump=B.c_v)
    B.tok_setup_g = B.c_g.tok()
    B.tok_setup_v = B.c_v.tok()


def mk_plan(P, B):
    return Plan(P, own={"v": B.c_v, "s": B.c_a, "g": B.c_g})


def bank_wait(B, ps):
    t = B.bank_tok.get(id(ps))
    return [t] if t is not None else []


def bank_set(B, ps, tok):
    B.bank_tok[id(ps)] = tok



def alloc_l0_attn(P, B):
    B.XN = P.sb("XN", [128, DC * S], BF16)
    B.QF = P.sb("QF", [128, DC * S], BF16)
    B.KF = P.sb("KF", [128, DC * S], BF16)
    B.VT = P.sb("VT", [128, NT * D], BF16)
    B.STASH = P.sb("STASH", [128, 8 * 512], F16)
    B.ATT = B.STASH.bitcast(BF16)
    B.EXPB = P.sb("EXPB", [128, NT * 512], BF16)
    B.WARE = [P.sb(f"ware{i}", [128, DC * 256], BF16) for i in range(2)]
    B.RZ = P.sb("RZN", [128, 512], F32)
    B.RN = B.RZ
    B.RZR = P.sb("RZR", [1, 512], F32)
    B.MXR = P.sb("MXR", [1, 8], F32)
    B.NCR = P.sb("NCR", [1, 8], F32)
    B.CBS = P.sb("CBS", [128, 8], F32)
    B.BQ = P.sb("BQ", [128, DC], F32)
    B.BK = P.sb("BK", [128, DC], F32)
    B.BO = P.sb("BO", [128, DC], F32)
    B.BVR = P.sb("BVR", [1, D], BF16)
    B.BVB = P.sb("BVB", [128, D], BF16)
    B.A0T = [P.sb(f"a0t{i}", [128, 512], F32) for i in range(2)]


def load_xn(pl, B, xn_dram):
    pl.step("y", lambda y: y.dma_start(
        out=B.XN[:].rearrange("p (c t) -> p c t", c=DC),
        in_=xn_dram.rearrange("(c p) t -> p c t", p=128)), bump=B.c_ld, n=16)
    return B.c_ld.tok()


def emit_l0_attn(pl, B, wq_t, wk_t, wv_t, wo_t, bq2, bk2, bv_row, bo2,
                 a0_dram, tok_xn, guard=None):
    for buf, src in ((B.BQ, bq2), (B.BK, bk2), (B.BO, bo2)):
        pl.step("y", lambda y, buf=buf, src=src: y.dma_start(out=buf[:], in_=src),
                bump=B.c_ld, n=16)
    tok_bias_l = B.c_ld.all()
    pl.step("g", lambda g: g.dma_start(out=B.BVR[:], in_=bv_row),
            bump=B.c_wg, n=16)
    tok_bvr = B.c_wg.tok()
    pl.step("v", lambda v: v.tensor_scalar(B.BQ[:], B.BQ[:], ATT_SCALE, None, ALU.mult),
            waits=tok_bias_l, bump=B.c_v)
    tok_bq = B.c_v.tok()
    for h in range(2):
        sl = slice(h * 512, (h + 1) * 512)
        pl.step("t", lambda t, sl=sl: t.matmul(
            B.AUX[:], B.ones_r[:], B.BVR[0:1, sl], start=True, stop=True),
            waits=[tok_bvr, B.tok_setup_v] + bank_wait(B, B.AUX), bump=B.c_pe)
        pl.step("s", lambda s, sl=sl: s.copy(B.BVB[:, sl], B.AUX[:]),
                waits=[B.c_pe.tok()], bump=B.c_a)
        bank_set(B, B.AUX, B.c_a.tok())
    tok_bvb = B.c_a.tok()

    def load_wq(wdram, h, buf, waits=()):
        def go(g):
            return g.dma_start(
                out=buf[:].rearrange("p (c i) -> p c i", c=DC),
                in_=wdram[:, h * 256:(h + 1) * 256].rearrange("(c p) i -> p c i", p=128))
        pl.step("g", go, waits=list(waits), bump=B.c_wg, n=16)
        return B.c_wg.tok()

    rr = [0]

    def next_bank():
        b = B.AV[rr[0]]
        rr[0] = (rr[0] + 1) % 4
        return b

    seqw = [("q", wq_t), ("k", wk_t), ("v", wv_t), ("o", wo_t)]
    order = [(nm, h) for nm, _ in seqw for h in range(4)]
    dram_of = dict(seqw)
    wtok = {order[0]: load_wq(wq_t, 0, B.WARE[0]),
            order[1]: load_wq(wq_t, 1, B.WARE[1])}

    def prefetch_after(idx):
        if idx + 2 < len(order):
            nm, nh = order[idx + 2]
            wtok[(nm, nh)] = load_wq(dram_of[nm], nh, B.WARE[idx % 2],
                                     waits=[B.c_pe.tok()])

    oi = 0
    for name, bias, dstb in (("q", B.BQ, B.QF), ("k", B.BK, B.KF)):
        for h in range(4):
            wt = wtok[(name, h)]
            for ic in range(2 * h, 2 * h + 2):
                icl = (ic % 2) * 128
                for qb in range(NQB):
                    bank = next_bank()

                    def pe(t, oi=oi, icl=icl, qb=qb, bank=bank):
                        last = None
                        for dk in range(DC):
                            last = t.matmul(
                                bank[:],
                                B.WARE[oi % 2][:, dk * 256 + icl: dk * 256 + icl + 128],
                                B.XN[:, dk * S + qb * QB: dk * S + (qb + 1) * QB],
                                start=(dk == 0), stop=(dk == DC - 1))
                        return last
                    pl.step("t", pe, waits=[wt, list(tok_xn)] + bank_wait(B, bank),
                            bump=B.c_pe)
                    ptok = B.c_pe.tok()
                    scl = ATT_SCALE if name == "q" else 1.0

                    def ev(s, bank=bank, ic=ic, qb=qb, bias=bias, dstb=dstb,
                           scl=scl):
                        return s.activation(
                            dstb[:, ic * S + qb * QB: ic * S + (qb + 1) * QB],
                            bank[:], AF.Identity, bias=bias[:, ic:ic + 1],
                            scale=scl)
                    pl.step("s", ev, waits=[ptok, tok_bq, guard] + tok_bias_l,
                            bump=B.c_a)
                    bank_set(B, bank, B.c_a.tok())
            prefetch_after(oi)
            oi += 1
    for h in range(4):
        wt = wtok[("v", h)]
        wb = oi % 2
        for tc in range(NT):
            bank = next_bank()

            def pe(t, wb=wb, tc=tc, bank=bank):
                last = None
                for dk in range(DC):
                    last = t.matmul(
                        bank[0:128, 0:256],
                        B.XN[:, dk * S + tc * 128: dk * S + tc * 128 + 128],
                        B.WARE[wb][:, dk * 256:(dk + 1) * 256],
                        start=(dk == 0), stop=(dk == DC - 1))
                return last
            pl.step("t", pe, waits=[wt, list(tok_xn)] + bank_wait(B, bank), bump=B.c_pe)
            ptok = B.c_pe.tok()

            def ev(v, h=h, tc=tc, bank=bank):
                return v.tensor_tensor(
                    B.VT[:, tc * D + h * 256: tc * D + (h + 1) * 256],
                    bank[0:128, 0:256], B.BVB[:, h * 256:(h + 1) * 256], ALU.add)
            pl.step("v", ev, waits=[ptok, tok_bvb], bump=B.c_v)
            bank_set(B, bank, B.c_v.tok())
        prefetch_after(oi)
        oi += 1
    tok_qf = B.c_a.tok()
    tok_vt = B.c_v.tok()

    expb_guard = tok_vt
    att_guard = None
    rzr_g = [None]
    slot_tok = [None] * 8
    for qb in range(NQB):
        q0 = qb * QB
        whtok = [None] * NT
        stok = [None] * NT
        ntok = [None] * NT
        etok = [None] * NT
        rtok = [None] * NT

        def emit_score(t_, q0=None):
            bank = B.SC[t_ % 2]

            def pe(t, t_=t_, q0=q0, bank=bank):
                last = None
                for dk in range(DC):
                    last = t.matmul(
                        bank[:],
                        B.KF[:, dk * S + t_ * 128: dk * S + t_ * 128 + 128],
                        B.QF[:, dk * S + q0: dk * S + q0 + QB],
                        start=(dk == 0), stop=(dk == DC - 1))
                return last
            pl.step("t", pe, waits=[tok_qf] + bank_wait(B, bank), bump=B.c_pe)
            ptok = B.c_pe.tok()
            tt = t_ % 8
            st_sl = slice(tt * 512, (tt + 1) * 512)
            pl.step("s", lambda s, bank=bank, st_sl=st_sl:
                    s.copy(B.STASH[:, st_sl], bank[:]),
                    waits=[ptok, slot_tok[tt]], bump=B.c_a)
            stok[t_] = B.c_a.tok()
            bank_set(B, bank, B.c_a.tok())

        def emit_max(t_):
            tt = t_ % 8
            st_sl = slice(tt * 512, (tt + 1) * 512)
            pl.step("g", lambda g, tt=tt, st_sl=st_sl: g.reduce_max(
                B.MXR[0:1, tt:tt + 1], B.STASH[:, st_sl], axis=AX.XYZWC),
                waits=[stok[t_]])
            pl.step("g", lambda g, tt=tt: g.tensor_scalar(
                B.NCR[0:1, tt:tt + 1], B.MXR[0:1, tt:tt + 1], -1.0, None,
                ALU.mult), bump=B.c_g)
            ntok[t_] = B.c_g.tok()
            slot_tok[tt] = B.c_g.tok()

        def emit_exp(t_):
            tt = t_ % 8
            st_sl = slice(tt * 512, (tt + 1) * 512)
            e_sl = slice(t_ * 512, (t_ + 1) * 512)
            pl.step("t", lambda t, tt=tt: t.matmul(
                B.AV[0][:, 0:1], B.ones_rf[:], B.NCR[0:1, tt:tt + 1],
                start=True, stop=True),
                waits=[ntok[t_], B.tok_setup_v] + bank_wait(B, B.AV[0]),
                bump=B.c_pe)
            cbtok = B.c_pe.tok()
            pl.step("s", lambda s, tt=tt: s.copy(B.CBS[:, tt:tt + 1],
                                                 B.AV[0][:, 0:1]),
                    waits=[cbtok], bump=B.c_a)
            bank_set(B, B.AV[0], B.c_a.tok())
            pl.step("s", lambda s, tt=tt, e_sl=e_sl, st_sl=st_sl: s.activation(
                B.EXPB[:, e_sl], B.STASH[:, st_sl], AF.Exp,
                bias=B.CBS[:, tt:tt + 1], scale=1.0),
                waits=[expb_guard], bump=B.c_a)
            etok[t_] = B.c_a.tok()

        def emit_z(t_):
            e_sl = slice(t_ * 512, (t_ + 1) * 512)
            pl.step("t", lambda t, e_sl=e_sl: t.matmul(
                B.AUX[0:1, 0:QB], B.ones_c[:], B.EXPB[:, e_sl],
                start=True, stop=True),
                waits=[etok[t_], B.tok_setup_v] + bank_wait(B, B.AUX),
                bump=B.c_pe)
            ztok = B.c_pe.tok()
            pl.step("v", lambda v: v.reciprocal(B.RZR[:], B.AUX[0:1, 0:QB]),
                    waits=[ztok, rzr_g[0]], bump=B.c_v)
            rtok[t_] = B.c_v.tok()
            bank_set(B, B.AUX, B.c_v.tok())

        def emit_bcast(t_):
            e_sl = slice(t_ * 512, (t_ + 1) * 512)
            pl.step("t", lambda t: t.matmul(
                B.NRM[:], B.ones_rf[:], B.RZR[:], start=True, stop=True),
                waits=[rtok[t_]] + bank_wait(B, B.NRM), bump=B.c_pe)
            btok = B.c_pe.tok()
            rzr_g[0] = btok
            pl.step("v", lambda v, e_sl=e_sl: v.tensor_tensor(
                B.EXPB[:, e_sl], B.EXPB[:, e_sl], B.NRM[:], ALU.mult),
                waits=[btok], bump=B.c_v)
            whtok[t_] = B.c_v.tok()
            bank_set(B, B.NRM, B.c_v.tok())

        for t_ in range(NT):
            emit_score(t_, q0)
            if t_ >= 1:
                emit_max(t_ - 1)
            if t_ >= 2:
                emit_exp(t_ - 2)
            if t_ >= 4:
                emit_bcast(t_ - 4)
            if t_ >= 3:
                emit_z(t_ - 3)
        emit_max(NT - 1)
        emit_exp(NT - 2)
        emit_exp(NT - 1)
        emit_bcast(NT - 4)
        emit_z(NT - 3)
        emit_bcast(NT - 3)
        emit_z(NT - 2)
        emit_bcast(NT - 2)
        emit_z(NT - 1)
        emit_bcast(NT - 1)
        tok_rn = None
        for p in range(2):
            for t_ in range(NT):
                def pe(t, p=p, t_=t_):
                    last = None
                    for md in range(4):
                        dch = p * 4 + md
                        last = t.matmul(
                            B.AV[md][:],
                            B.VT[:, t_ * D + dch * 128: t_ * D + dch * 128 + 128],
                            B.EXPB[:, t_ * 512:(t_ + 1) * 512],
                            start=(t_ == 0), stop=(t_ == NT - 1))
                    if p == 0:
                        last = t.matmul(B.NRM[:], B.ones128[:],
                                        B.EXPB[:, t_ * 512:(t_ + 1) * 512],
                                        start=(t_ == 0), stop=(t_ == NT - 1))
                    return last
                w = [whtok[t_], tok_vt]
                if t_ == 0:
                    for md in range(4):
                        w += bank_wait(B, B.AV[md])
                    if p == 0:
                        w += bank_wait(B, B.NRM)
                pl.step("t", pe, waits=w, bump=B.c_pe)
            tok_av = B.c_pe.tok()
            if p == 0:
                pl.step("v", lambda v: v.tensor_scalar(B.RN[:], B.NRM[:],
                                                       1e-6, None, ALU.add),
                        waits=[tok_av], bump=B.c_v)
                pl.step("v", lambda v: v.reciprocal(B.RN[:], B.RN[:]),
                        waits=[B.c_v.tok()], bump=B.c_v)
                bank_set(B, B.NRM, B.c_v.tok())
                tok_rn = B.c_v.tok()
            for md in range(4):
                dch = p * 4 + md
                pl.step("v", lambda v, md=md, dch=dch: v.tensor_tensor(
                    B.ATT[:, dch * 512:(dch + 1) * 512], B.AV[md][:], B.RN[:],
                    ALU.mult),
                    waits=[tok_av, tok_rn, att_guard], bump=B.c_v)
                bank_set(B, B.AV[md], B.c_v.tok())
        tok_att = B.c_v.tok()
        expb_guard = B.c_pe.tok()
        for qh in range(4):
            wt = wtok[("o", qh)]
            wb = qh % 2
            for ic in range(2 * qh, 2 * qh + 2):
                bank = B.SC[ic % 2]
                icl = (ic % 2) * 128

                def pe(t, wb=wb, icl=icl, bank=bank):
                    last = None
                    for dk in range(DC):
                        last = t.matmul(
                            bank[:],
                            B.WARE[wb][:, dk * 256 + icl: dk * 256 + icl + 128],
                            B.ATT[:, dk * 512:(dk + 1) * 512],
                            start=(dk == 0), stop=(dk == DC - 1))
                    return last
                pl.step("t", pe, waits=[wt, tok_att] + bank_wait(B, bank),
                        bump=B.c_pe)
                ptok = B.c_pe.tok()
                stg = B.A0T[ic % 2]
                pl.step("s", lambda s, bank=bank, ic=ic, stg=stg: s.activation(
                    stg[:], bank[:], AF.Identity, bias=B.BO[:, ic:ic + 1], scale=1.0),
                    waits=[ptok] + bank_wait(B, stg), bump=B.c_a)
                bank_set(B, bank, B.c_a.tok())
                tok_stg = B.c_a.tok()
                pl.step("y", lambda y, ic=ic, q0=q0, stg=stg: y.dma_start(
                    out=a0_dram[ic * 128:(ic + 1) * 128, q0:q0 + QB], in_=stg[:]),
                    waits=[tok_stg], bump=B.c_st, n=16)
                bank_set(B, stg, B.c_st.tok())
            nq = (qh + 2) % 4
            if not (qb == NQB - 1 and qh >= 2):
                wtok[("o", nq)] = load_wq(wo_t, nq, B.WARE[wb],
                                          waits=[B.c_pe.tok()])
        att_guard = B.c_pe.tok()
    return B.c_st.all()



def alloc_ln(P, B):
    B.X2B = [P.sb(f"x2b{i}", [128, 512], BF16) for i in range(2)]
    B.ROWA = P.sb("rowa", [1, 512], F32)
    B.ROWB = P.sb("rowb", [1, 512], F32)
    B.RSTB = P.sb("rstb", [128, 512], F32)
    B.MRB = P.sb("mrb", [128, 512], F32)
    B.EPSB = P.sb("epsb", [1, 1], F32)
    B.eps_init = False


def emit_ln(pl, B, x_dram, wait=None):
    B.ln_xn_guard = ()
    if not B.eps_init:
        B.eps_init = True
        pl.step("v", lambda v: v.memset(B.EPSB[:], 1e-5), bump=B.c_v)
        B.tok_eps = B.c_v.tok()
    wl = wait if isinstance(wait, list) else [wait]
    pl.step("g", lambda g: g.dma_start(
        out=B.XN[:].rearrange("p (c t) -> p c t", c=DC),
        in_=x_dram.rearrange("(c p) t -> p c t", p=128)),
        waits=wl, bump=B.c_wg, n=16)
    tok_x = B.c_wg.tok()
    x2tok = [None, None]
    for qb in range(NQB):
        q0 = qb * QB
        for c in range(DC):
            xb_sl = slice(c * S + q0, c * S + q0 + QB)
            pl.step("s", lambda s, xb_sl=xb_sl, c=c: s.activation(
                B.X2B[c % 2][:], B.XN[:, xb_sl], AF.Square, scale=1.0),
                waits=[tok_x, x2tok[c % 2]], bump=B.c_a)
            tok_sq = B.c_a.tok()

            def pe(t, xb_sl=xb_sl, c=c):
                t.matmul(B.AUX[0:1, :], B.ones_c[:], B.XN[:, xb_sl],
                         start=(c == 0), stop=(c == DC - 1))
                return t.matmul(B.NRM[0:1, :], B.ones_c[:], B.X2B[c % 2][:],
                                start=(c == 0), stop=(c == DC - 1))
            w = [tok_sq, B.tok_setup_v]
            if c == 0:
                w += bank_wait(B, B.AUX) + bank_wait(B, B.NRM)
            pl.step("t", pe, waits=w, bump=B.c_pe)
            x2tok[c % 2] = B.c_pe.tok()
        tok_sums = B.c_pe.tok()
        pl.step("v", lambda v: v.tensor_scalar(B.ROWA[:], B.AUX[0:1, :],
                                               1.0 / D, None, ALU.mult),
                waits=[tok_sums] + bank_wait(B, B.ROWA), bump=B.c_v)
        bank_set(B, B.AUX, B.c_v.tok())
        pl.step("v", lambda v: v.tensor_scalar(B.ROWB[:], B.NRM[0:1, :],
                                               1.0 / D, None, ALU.mult),
                waits=[tok_sums] + bank_wait(B, B.ROWB), bump=B.c_v)
        bank_set(B, B.NRM, B.c_v.tok())
        pl.step("s", lambda s: s.activation(B.AUX[0:1, :], B.ROWA[:],
                                            AF.Square, scale=1.0),
                waits=[B.c_v.tok()] + bank_wait(B, B.AUX), bump=B.c_a)
        pl.step("v", lambda v: v.tensor_tensor(B.ROWB[:], B.ROWB[:],
                                               B.AUX[0:1, :], ALU.subtract),
                waits=[B.c_a.tok()], bump=B.c_v)
        bank_set(B, B.AUX, B.c_v.tok())
        tok_var = B.c_v.tok()
        pl.step("s", lambda s: s.activation(B.ROWB[:], B.ROWB[:], AF.Sqrt,
                                            bias=B.EPSB[0:1, 0:1], scale=1.0),
                waits=[tok_var, B.tok_eps], bump=B.c_a)
        pl.step("v", lambda v: v.reciprocal(B.ROWB[:], B.ROWB[:]),
                waits=[B.c_a.tok()], bump=B.c_v)
        pl.step("v", lambda v: v.tensor_tensor(B.ROWA[:], B.ROWA[:], B.ROWB[:],
                                               ALU.mult), bump=B.c_v)
        tok_rows = B.c_v.tok()
        for row, dstb in ((B.ROWB, B.RSTB), (B.ROWA, B.MRB)):
            pl.step("t", lambda t, row=row: t.matmul(
                B.AUX[:], B.ones_rf[:], row[0:1, :], start=True, stop=True),
                waits=[tok_rows] + bank_wait(B, B.AUX), bump=B.c_pe)
            pl.step("s", lambda s, dstb=dstb: s.copy(dstb[:], B.AUX[:]),
                    waits=[B.c_pe.tok()] + list(B.ln_xn_guard)
                    + bank_wait(B, dstb), bump=B.c_a)
            bank_set(B, B.AUX, B.c_a.tok())
            bank_set(B, dstb, B.c_a.tok())
        tok_b = B.c_a.tok()
        for c in range(DC):
            eng = "v" if c % 2 == 0 else "g"
            xb_sl = slice(c * S + q0, c * S + q0 + QB)
            pl.step(eng, lambda e, xb_sl=xb_sl: e.tensor_tensor(
                B.XN[:, xb_sl], B.XN[:, xb_sl], B.RSTB[:], ALU.mult),
                waits=[tok_b, tok_sums])
            pl.step(eng, lambda e, xb_sl=xb_sl: e.tensor_tensor(
                B.XN[:, xb_sl], B.XN[:, xb_sl], B.MRB[:], ALU.subtract),
                bump=B.c_v if eng == "v" else B.c_g)
        B.ln_xn_guard = (B.c_v.tok(), B.c_g.tok())
    return (B.c_v.tok(), B.c_g.tok())



def alloc_compose(P, B):
    B.LG = P.sb("LG", [1, 256], F32)
    B.LGW = P.sb("LGW", [1, 256], F32)
    B.MV = P.sb("MV", [1, 32], F32)
    B.IV = P.sb("IV", [1, 32], U32)
    B.IVF = P.sb("IVF", [1, 32], F32)
    B.EW = P.sb("EW", [1, 32], F32)
    B.SWr = P.sb("SWr", [1, 2], F32)
    B.W32 = P.sb("W32", [1, 32], F32)
    B.SELB = P.sb("SELB", [128, 32], F32)
    B.W32B = P.sb("W32B", [128, 32], F32)
    B.PEVB = P.sb("PEVB", [128, 32], F32)
    B.EOT = P.sb("EOT", [128, 32], F32)
    B.UB = P.sb("UB", [128, 32], F32)
    B.U2 = P.sb("U2", [128, 16], F32)
    B.MS = P.sb("MS", [128, 16], F32)
    B.IDXAf = P.sb("IDXAf", [128, 16], F32)
    B.IDXA = P.sb("IDXA", [128, 16], U32)
    B.IDX8 = P.sb("IDX8", [128, 16], U32)
    B.QM64 = P.sb("QM64", [128, 1], F32)
    B.QI = P.sb("QI", [128, 1], I32)
    B.COREB = P.sb("COREB", [128, 1], F32)
    B.GA = [P.sb(f"ga{i}", [128, 1024], BF16) for i in range(2)]
    B.GAS = [P.sb(f"gas{i}", [128, 1024], BF16) for i in range(2)]
    B.GB = [P.sb(f"gb{i}", [128, 1024], BF16) for i in range(2)]
    B.STG = [P.sb(f"stg{i}", [128, 512], BF16) for i in range(2)]


def emit_compose_setup(pl, B, coreid):
    pl.step("g", lambda g: (g.iota(B.QI[:], [[0, 1]], base=0,
                            channel_multiplier=1), g.engine_nop())[1])
    pl.step("g", lambda g: (g.iota(
        B.iotat[:, 0:32].rearrange("p (a b) -> p a b", b=2),
        [[0, 16], [1, 2]], base=0, channel_multiplier=0), g.engine_nop())[1])
    tok_io = B.c_g.tok()
    pl.step("v", lambda v: v.tensor_scalar(B.QM64[:], B.QI[:], 64, None,
                                           ALU.is_ge), waits=[tok_io])
    pl.step("v", lambda v: v.scalar_tensor_tensor(B.QM64[:], B.QM64[:], -64.0,
                                                  B.QI[:], ALU.mult, ALU.add))
    pl.step("v", lambda v: v.tensor_scalar(B.EOT[:], B.iotat[:, 0:32], 0, None,
                                           ALU.is_equal), bump=B.c_v)
    pl.step("v", lambda v: v.tensor_copy(B.PEVB[0:64, :], B.EOT[0:64, :]))
    pl.step("v", lambda v: v.tensor_scalar(B.PEVB[64:128, :], B.EOT[64:128, :],
                                           1.0, None, ALU.subtract))
    pl.step("v", lambda v: v.tensor_scalar(B.PEVB[64:128, :], B.PEVB[64:128, :],
                                           -1.0, None, ALU.mult), bump=B.c_v)
    pl.step("g", lambda g: g.dma_start(out=B.SWr[0:1, 0:1], in_=coreid),
            bump=B.c_wg, n=16)
    pl.step("t", lambda t: t.matmul(B.AUX[:, 0:1], B.ones_rf[:], B.SWr[0:1, 0:1],
                                    start=True, stop=True),
            waits=[B.c_wg.tok(), B.tok_setup_v] + bank_wait(B, B.AUX),
            bump=B.c_pe)
    pl.step("s", lambda s: s.copy(B.COREB[:], B.AUX[:, 0:1]),
            waits=[B.c_pe.tok()], bump=B.c_a)
    bank_set(B, B.AUX, B.c_a.tok())
    B.tok_cmp_setup = (B.c_v.tok(), B.c_g.tok(), B.c_a.tok(), B.c_pe.tok())


def emit_topk_idx(pl, B, lg_row, guard=None):
    pl.step("y", lambda y: y.dma_start(out=B.LG[:], in_=lg_row), bump=B.c_ld, n=16)
    tok_lg = B.c_ld.tok()
    pl.step("v", lambda v: v.tensor_copy(B.LGW[:], B.LG[:]),
            waits=[tok_lg, B.tok_cmp_setup[0]] + (guard or []), bump=B.c_v)
    for r in range(4):
        sl = slice(r * 8, (r + 1) * 8)
        pl.step("v", lambda v, sl=sl: v.max(out=B.MV[:, sl], in_=B.LGW[:]),
                waits=[B.c_v.tok()], bump=B.c_v)
        pl.step("v", lambda v, sl=sl: v.max_index(out=B.IV[:, sl],
                                                  in_max=B.MV[:, sl],
                                                  in_values=B.LGW[:]),
                waits=[B.c_v.tok()], bump=B.c_v)
        pl.step("v", lambda v, sl=sl: v.match_replace(
            out=B.LGW[:], in_to_replace=B.MV[:, sl], in_values=B.LGW[:],
            imm_value=-1e30), waits=[B.c_v.tok()], bump=B.c_v)
    pl.step("v", lambda v: v.tensor_scalar(B.SWr[0:1, 0:1], B.MV[0:1, 0:1],
                                           -1.0, None, ALU.mult),
            waits=[B.c_v.tok(), B.tok_cmp_setup[3]], bump=B.c_v)
    pl.step("s", lambda s: s.activation(B.EW[:], B.MV[:], AF.Exp,
                                        bias=B.SWr[0:1, 0:1], scale=1.0),
            waits=[B.c_v.tok()], bump=B.c_a)
    pl.step("v", lambda v: v.reduce_sum(B.SWr[0:1, 1:2], B.EW[:], axis=AX.X),
            waits=[B.c_a.tok()], bump=B.c_v)
    pl.step("v", lambda v: v.reciprocal(B.SWr[0:1, 1:2], B.SWr[0:1, 1:2]),
            waits=[B.c_v.tok()], bump=B.c_v)
    pl.step("v", lambda v: v.tensor_scalar(B.W32[:], B.EW[:], B.SWr[0:1, 1:2],
                                           None, ALU.mult),
            waits=[B.c_v.tok()], bump=B.c_v)
    pl.step("v", lambda v: v.tensor_copy(B.IVF[:], B.IV[:]), bump=B.c_v)
    tok_rows = B.c_v.tok()
    for row, dstb in ((B.IVF, B.SELB), (B.W32, B.W32B)):
        pl.step("t", lambda t, row=row: t.matmul(
            B.AUX[:, 0:32], B.ones_rf[:], row[0:1, :], start=True, stop=True),
            waits=[tok_rows] + bank_wait(B, B.AUX), bump=B.c_pe)
        pl.step("s", lambda s, dstb=dstb: s.copy(dstb[:], B.AUX[:, 0:32]),
                waits=[B.c_pe.tok()], bump=B.c_a)
        bank_set(B, B.AUX, B.c_a.tok())
    tok_bc = B.c_a.tok()
    pl.step("v", lambda v: v.tensor_tensor(B.UB[:], B.SELB[:], B.PEVB[:],
                                           ALU.mult), waits=[tok_bc], bump=B.c_v)
    pl.step("v", lambda v: v.tensor_tensor(B.U2[:], B.UB[:, 0::2], B.UB[:, 1::2],
                                           ALU.add), bump=B.c_v)
    pl.step("v", lambda v: v.scalar_tensor_tensor(
        B.IDXAf[:], B.U2[:], 64.0, B.QM64[:].to_broadcast([128, 16]),
        ALU.mult, ALU.add), waits=[B.tok_cmp_setup[1]], bump=B.c_v)
    pl.step("v", lambda v: v.tensor_copy(B.IDXA[:], B.IDXAf[:]), bump=B.c_v)
    pl.step("v", lambda v: v.scalar_tensor_tensor(
        B.IDXAf[:], B.IDXAf[:], 8.0, B.COREB[:].to_broadcast([128, 16]),
        ALU.mult, ALU.add), waits=[B.tok_cmp_setup[2]], bump=B.c_v)
    pl.step("v", lambda v: v.tensor_copy(B.IDX8[:], B.IDXAf[:]), bump=B.c_v)
    pl.step("v", lambda v: v.tensor_tensor(B.UB[:], B.W32B[:], B.PEVB[:],
                                           ALU.mult), bump=B.c_v)
    pl.step("v", lambda v: v.tensor_tensor(B.MS[:], B.UB[:, 0::2], B.UB[:, 1::2],
                                           ALU.add), bump=B.c_v)
    return B.c_v.tok()


def emit_compose_one(pl, B, tok_idx, a_rows, a_width, b_rows, b_width,
                     a_idx, b_idx, n_m, n_n, out_emit):
    banks = B.AV + B.SC + [B.AUX, B.NRM]
    ga_tok = [None, None]
    gas_used = [None, None]
    for j in range(16):
        bi = j % 2

        def ld_a(g, j=j, bi=bi, a_rows=a_rows, a_width=a_width):
            return g.indirect_dma_start(
                out=B.GA[bi][:, 0:a_width], out_offset=None,
                in_=a_rows, in_offset=IndirectOffsetOnAxis(ap=a_idx(j), axis=0))
        pl.step("g", ld_a, waits=[tok_idx, ga_tok[bi]], bump=B.c_wg, n=16)
        tok_ga = B.c_wg.tok()

        def ld_b(g, j=j, bi=bi, b_rows=b_rows, b_width=b_width):
            return g.indirect_dma_start(
                out=B.GB[bi][:, 0:b_width], out_offset=None,
                in_=b_rows, in_offset=IndirectOffsetOnAxis(ap=b_idx(j), axis=0))
        pl.step("g", ld_b, waits=[], bump=B.c_wg, n=16)
        tok_gb = B.c_wg.tok()
        pl.step("v", lambda v, bi=bi, j=j, a_width=a_width: v.tensor_scalar(
            B.GAS[bi][:, 0:a_width], B.GA[bi][:, 0:a_width],
            B.MS[:, j:j + 1], None, ALU.mult),
            waits=[tok_ga, gas_used[bi]], bump=B.c_v)
        tok_gas = B.c_v.tok()

        def pe(t, j=j, bi=bi, n_m=n_m, n_n=n_n, b_width=b_width):
            last = None
            for m in range(n_m):
                for n in range(n_n):
                    last = t.matmul(
                        banks[m * n_n + n][:],
                        B.GAS[bi][:, m * 128:(m + 1) * 128],
                        B.GB[bi][:, n * 512:(n + 1) * 512],
                        start=(j == 0), stop=(j == 15))
            return last
        w = [tok_gas, tok_gb]
        if j == 0:
            for k in range(n_m * n_n):
                w += bank_wait(B, banks[k])
        pl.step("t", pe, waits=w, bump=B.c_pe)
        ga_tok[bi] = B.c_pe.tok()
        gas_used[bi] = B.c_pe.tok()
    tok_mm = B.c_pe.tok()
    for m in range(n_m):
        for n in range(n_n):
            bank = banks[m * n_n + n]
            out_emit(m, n, bank, tok_mm)


def emit_compose(pl, B, lg1_row, lg2_row, a1t, b1, a2t, b2,
                 w1sl_dram, w2sl_dram):
    a1_rows = a1t.rearrange("p r d -> (p r) d")
    b1_rows = b1.rearrange("p r (a b) -> (p r a) b", b=512)
    a2_rows = a2t.rearrange("p r (a b) -> (p r a) b", b=512)
    b2_rows = b2.rearrange("p r d -> (p r) d")

    def store(dram, m, n, bank, ptok):
        stg = B.STG[(m * 4 + n) % 2]
        pl.step("s", lambda s, stg=stg, bank=bank: s.copy(stg[:], bank[:]),
                waits=[ptok] + bank_wait(B, stg), bump=B.c_a)
        bank_set(B, bank, B.c_a.tok())
        tok_stg = B.c_a.tok()
        pl.step("y", lambda y, dram=dram, m=m, n=n, stg=stg: y.dma_start(
            out=dram[m * 128:(m + 1) * 128, n * 512:(n + 1) * 512], in_=stg[:]),
            waits=[tok_stg], bump=B.c_st, n=16)
        bank_set(B, stg, B.c_st.tok())

    tok_idx = emit_topk_idx(pl, B, lg1_row)
    emit_compose_one(pl, B, tok_idx, a1_rows, 1024, b1_rows, 512,
                     lambda j: B.IDXA[:, j:j + 1], lambda j: B.IDX8[:, j:j + 1],
                     8, 1, lambda m, n, bank, ptok: store(w1sl_dram, m, n, bank, ptok))
    tok_idx = emit_topk_idx(pl, B, lg2_row, guard=B.c_wg.all())
    emit_compose_one(pl, B, tok_idx, a2_rows, 512, b2_rows, 1024,
                     lambda j: B.IDX8[:, j:j + 1], lambda j: B.IDXA[:, j:j + 1],
                     4, 2, lambda m, n, bank, ptok: store(w2sl_dram, m, n, bank, ptok))
    return B.c_st.all()



def alloc_ffn0(P, B):
    B.H = P.sb("H", [128, 32 * 512], BF16)
    B.WST = [P.sb(f"wst{i}", [128, 8 * 1024], BF16) for i in range(2)]
    B.GU = P.sb("GU", [128, 512], F32)
    B.GZ = P.sb("GZ", [128, 512], F32)
    B.GS = P.sb("GS", [128, 512], F32)
    B.XRT = [P.sb(f"xrt{i}", [128, 512], F32) for i in range(2)]
    B.ART = [P.sb(f"art{i}", [128, 512], F32) for i in range(2)]
    B.XPT = [P.sb(f"xpt{i}", [128, 512], F32) for i in range(2)]


GELU_A = float(np.sqrt(0.044715))
GELU_K1 = 2.0 * 0.7978845608028654


def emit_ffn0_apply(pl, B, w1g, w2g, x_dram, a0_dram, x1_dram, tok_xn,
                    no_gelu=False, guard=None, wsrc_tok=None, res_guard=None):
    h_guard = guard
    B.gu_guard = None
    B.gs_guard = None
    wst_guard = guard
    res_st = {}
    rguard = list(res_guard or [])
    for tb in range(NQB):
        q0 = tb * QB
        wtok = [None, None]
        wused = [wst_guard, wst_guard]
        for fs in range(8):
            bi = fs % 2

            def ldw(y, fs=fs, bi=bi):
                return y.dma_start(
                    out=B.WST[bi][:, 0:8 * 512].rearrange("p (c i) -> p c i", c=DC),
                    in_=w1g[fs].rearrange("(c p) i -> p c i", p=128))
            qk, pool = (("g", B.c_wg) if (tb == 0 and fs < 2)
                        else ("y", B.c_ld))
            pl.step(qk, ldw, waits=[wused[bi], wsrc_tok], bump=pool, n=16)
            wtok[bi] = pool.tok()
            for fm in range(4):
                bank = B.AV[(fs * 4 + fm) % 4]

                def pe(t, bi=bi, fm=fm, q0=q0, bank=bank):
                    last = None
                    for dk in range(DC):
                        last = t.matmul(
                            bank[:],
                            B.WST[bi][:, dk * 512 + fm * 128: dk * 512 + fm * 128 + 128],
                            B.XN[:, dk * S + q0: dk * S + q0 + QB],
                            start=(dk == 0), stop=(dk == DC - 1))
                    return last
                pl.step("t", pe, waits=[wtok[bi], tok_xn[0], tok_xn[1]]
                        + bank_wait(B, bank), bump=B.c_pe)
                ptok = B.c_pe.tok()
                f = fs * 4 + fm
                h_sl = slice(f * 512, (f + 1) * 512)
                if no_gelu:
                    pl.step("s", lambda s, bank=bank, h_sl=h_sl: s.copy(
                        B.H[:, h_sl], bank[:]), waits=[ptok, h_guard])
                    bank_set(B, bank, B.c_a.tok())
                    continue
                pl.step("s", lambda s, bank=bank: s.activation(
                    B.GU[:], bank[:], AF.Square, scale=GELU_A),
                    waits=[ptok, B.gu_guard], bump=B.c_a)
                pl.step("v", lambda v, bank=bank: v.scalar_tensor_tensor(
                    B.GU[:], B.GU[:], 1.0, bank[:], ALU.add, ALU.mult),
                    waits=[B.c_a.tok()], bump=B.c_v)
                pl.step("s", lambda s: s.activation(
                    B.GS[:], B.GU[:], AF.Sigmoid, scale=GELU_K1),
                    waits=[B.c_v.tok(), B.gs_guard], bump=B.c_a)
                B.gu_guard = B.c_a.tok()
                pl.step("v", lambda v, bank=bank, h_sl=h_sl: v.tensor_tensor(
                    B.H[:, h_sl], B.GS[:], bank[:], ALU.mult),
                    waits=[B.c_a.tok(), h_guard], bump=B.c_v)
                B.gs_guard = B.c_v.tok()
                bank_set(B, bank, B.c_v.tok())
            wused[bi] = B.c_pe.tok()
        tok_h = B.c_v.tok()
        tok_h2 = B.c_a.tok()
        wst_guard = B.c_pe.tok()
        banks = B.AV + B.SC + [B.AUX, B.NRM]
        res_tok = {}

        def res_loads(ms, q0=q0, res_tok=res_tok):
            for m in ms:
                b = m % 4
                pl.step("g", lambda g, m=m, b=b, q0=q0: g.dma_start(
                    out=B.XR4[b], in_=x_dram[m * 128:(m + 1) * 128, q0:q0 + QB]),
                    waits=[res_st.get(("x", b))] + rguard, bump=B.c_wg, n=16)
                res_tok[("x", m)] = B.c_wg.tok()
                pl.step("g", lambda g, m=m, b=b, q0=q0: g.dma_start(
                    out=B.AR4[b], in_=a0_dram[m * 128:(m + 1) * 128, q0:q0 + QB]),
                    waits=[res_st.get(("a", b))] + rguard, bump=B.c_wg, n=16)
                res_tok[("a", m)] = B.c_wg.tok()

        res_loads(range(0, 4))
        rguard = []
        wtok = [None, None]
        wused = [wst_guard, wst_guard]
        for fs in range(8):
            bi = fs % 2

            def ldw2(y, fs=fs, bi=bi):
                return y.dma_start(
                    out=B.WST[bi][:, 0:4 * 1024].rearrange("p (c d) -> p c d", c=4),
                    in_=w2g[fs * 512:(fs + 1) * 512, :].rearrange(
                        "(c p) d -> p c d", p=128))
            qk, pool = (("g", B.c_wg) if (tb == 0 and fs < 2)
                        else ("y", B.c_ld))
            pl.step(qk, ldw2, waits=[wused[bi], wsrc_tok], bump=pool, n=16)
            wtok[bi] = pool.tok()

            def pe2(t, fs=fs, bi=bi):
                last = None
                for fl in range(4):
                    f = fs * 4 + fl
                    for m in range(8):
                        last = t.matmul(
                            banks[m][:],
                            B.WST[bi][:, fl * 1024 + m * 128: fl * 1024 + m * 128 + 128],
                            B.H[:, f * 512:(f + 1) * 512],
                            start=(f == 0), stop=(f == 31))
                return last
            w = [wtok[bi], tok_h, tok_h2]
            if fs == 0:
                for k in range(8):
                    w += bank_wait(B, banks[k])
            pl.step("t", pe2, waits=w, bump=B.c_pe)
            wused[bi] = B.c_pe.tok()
        tok_out = B.c_pe.tok()
        h_guard = tok_out
        wst_guard = tok_out
        for m in range(8):
            b = m % 4
            pl.step("v", lambda v, m=m, b=b: v.scalar_tensor_tensor(
                B.XR4[b], banks[m][:], 1.0, B.XR4[b], ALU.mult, ALU.add),
                waits=[tok_out, res_tok[("x", m)]], bump=B.c_v)
            bank_set(B, banks[m], B.c_v.tok())
            pl.step("v", lambda v, b=b: v.tensor_tensor(
                B.XR4[b], B.XR4[b], B.AR4[b], ALU.add),
                waits=[res_tok[("a", m)]], bump=B.c_v)
            tok_xp = B.c_v.tok()
            pl.step("y", lambda y, m=m, q0=q0, b=b: y.dma_start(
                out=x1_dram[m * 128:(m + 1) * 128, q0:q0 + QB], in_=B.XR4[b]),
                waits=[tok_xp], bump=B.c_st, n=16)
            res_st[("x", b)] = B.c_st.tok()
            res_st[("a", b)] = B.c_st.tok()
            if m == 3:
                res_loads(range(4, 8))
    return B.c_st.all()



def alloc_l1_attn(P, B):
    if not hasattr(B, "QF"):
        B.QF = P.sb("QF", [128, DC * S], BF16)
        B.KF = P.sb("KF", [128, DC * S], BF16)
        B.VT = P.sb("VT", [128, NT * D], BF16)
        B.WARE = [P.sb(f"ware{i}", [128, DC * 256], BF16) for i in range(2)]
        B.A0T = [P.sb(f"a0t{i}", [128, 512], F32) for i in range(2)]
    B.IP = B.QF
    B.RG = B.KF
    B.IG = B.VT
    B.BIN = P.sb("BIN", [128, DC], F32)
    B.BR = P.sb("BR", [128, DC], F32)
    B.BI = P.sb("BI", [128, DC], F32)
    B.BOUT = P.sb("BOUT", [128, DC], F32)
    B.ASG = P.sb("ASG", [128, DC], F32)


def emit_l1_attn(pl, B, w_in, w_r, w_i, w_out, b_in2, b_r2, b_i2, b_out2,
                 log_a2, a1_dram, tok_xn, guard=None, evac_guard=None):
    for buf, src in ((B.BIN, b_in2), (B.BR, b_r2), (B.BI, b_i2),
                     (B.BOUT, b_out2), (B.ASG, log_a2)):
        pl.step("y", lambda y, buf=buf, src=src: y.dma_start(out=buf[:], in_=src),
                bump=B.c_ld, n=16)
    tok_bias_l = B.c_ld.all()
    pl.step("s", lambda s: s.activation(B.ASG[:], B.ASG[:], AF.Sigmoid),
            waits=tok_bias_l)
    tok_asg = B.c_a.tok()

    def load_wq(wdram, h, buf, waits=()):
        def go(g):
            return g.dma_start(
                out=buf[:].rearrange("p (c i) -> p c i", c=DC),
                in_=wdram[:, h * 256:(h + 1) * 256].rearrange("(c p) i -> p c i", p=128))
        pl.step("g", go, waits=list(waits), bump=B.c_wg, n=16)
        return B.c_wg.tok()

    rr = [0]
    seqw = [("in", w_in), ("r", w_r), ("i", w_i), ("o", w_out)]
    order = [(nm, h) for nm, _ in seqw for h in range(4)]
    dram_of = dict(seqw)
    wtok = {order[0]: load_wq(w_in, 0, B.WARE[0], waits=[guard]),
            order[1]: load_wq(w_in, 1, B.WARE[1], waits=[guard])}

    def prefetch_after(idx):
        if idx + 2 < len(order):
            nm, nh = order[idx + 2]
            wtok[(nm, nh)] = load_wq(dram_of[nm], nh, B.WARE[idx % 2],
                                     waits=[B.c_pe.tok()])

    projs = (("in", B.BIN, B.IP, AF.Identity), ("r", B.BR, B.RG, AF.Sigmoid),
             ("i", B.BI, B.IG, AF.Sigmoid))
    oi = 0
    evt_i = [None] * DC
    for name, bias, dstb, fn in projs:
        for h in range(4):
            wt = wtok[(name, h)]
            for ic in range(2 * h, 2 * h + 2):
                icl = (ic % 2) * 128
                for qb in range(NQB):
                    bank = B.AV[rr[0]]
                    rr[0] = (rr[0] + 1) % 4

                    def pe(t, oi=oi, icl=icl, qb=qb, bank=bank):
                        last = None
                        for dk in range(DC):
                            last = t.matmul(
                                bank[:],
                                B.WARE[oi % 2][:, dk * 256 + icl: dk * 256 + icl + 128],
                                B.XN[:, dk * S + qb * QB: dk * S + (qb + 1) * QB],
                                start=(dk == 0), stop=(dk == DC - 1))
                        return last
                    pl.step("t", pe, waits=[wt, tok_xn[0], tok_xn[1]]
                            + bank_wait(B, bank), bump=B.c_pe)
                    ptok = B.c_pe.tok()

                    def ev(s, bank=bank, ic=ic, qb=qb, bias=bias, dstb=dstb, fn=fn):
                        return s.activation(
                            dstb[:, ic * S + qb * QB: ic * S + (qb + 1) * QB],
                            bank[:], fn, bias=bias[:, ic:ic + 1], scale=1.0)
                    pl.step("s", ev, waits=[ptok, evac_guard] + tok_bias_l)
                    bank_set(B, bank, B.c_a.tok())
                if name == "i":
                    evt_i[ic] = B.c_a.tok()
            prefetch_after(oi)
            oi += 1
    for c in range(DC):
        eng = "v" if c % 2 == 0 else "g"
        ctr = B.c_v if eng == "v" else B.c_g
        for qb in range(NQB):
            sl = slice(c * S + qb * QB, c * S + (qb + 1) * QB)
            pl.step(eng, lambda e, sl=sl, c=c: e.tensor_scalar(
                B.RG[:, sl], B.RG[:, sl], B.ASG[:, c:c + 1], None, ALU.mult),
                waits=[evt_i[c], tok_asg])
            pl.step(eng, lambda e, sl=sl: e.tensor_tensor(
                B.IP[:, sl], B.IG[:, sl], B.IP[:, sl], ALU.mult))
            pl.step(eng, lambda e, sl=sl: e.tensor_scalar(
                B.IG[:, sl], B.RG[:, sl], -1.0, 1.0, ALU.mult, ALU.add))
            pl.step(eng, lambda e, sl=sl: e.tensor_tensor(
                B.IP[:, sl], B.IP[:, sl], B.IG[:, sl], ALU.mult))
        tok_prep = ctr.tok()
        csl = slice(c * S, (c + 1) * S)
        pl.step("v", lambda v, csl=csl: v.tensor_tensor_scan(
            B.IG[:, csl], B.RG[:, csl], B.IP[:, csl], 0.0, ALU.mult, ALU.add),
            waits=[tok_prep])
    tok_h = B.c_v.tok()
    for h in range(4):
        wt = wtok[("o", h)]
        for ic in range(2 * h, 2 * h + 2):
            icl = (ic % 2) * 128
            for qb in range(NQB):
                bank = B.SC[(ic * NQB + qb) % 2]

                def pe(t, oi=oi, icl=icl, qb=qb, bank=bank):
                    last = None
                    for dk in range(DC):
                        last = t.matmul(
                            bank[:],
                            B.WARE[oi % 2][:, dk * 256 + icl: dk * 256 + icl + 128],
                            B.IG[:, dk * S + qb * QB: dk * S + (qb + 1) * QB],
                            start=(dk == 0), stop=(dk == DC - 1))
                    return last
                pl.step("t", pe, waits=[wt, tok_h] + bank_wait(B, bank),
                        bump=B.c_pe)
                ptok = B.c_pe.tok()
                stg = B.A0T[(ic * NQB + qb) % 2]
                pl.step("s", lambda s, bank=bank, ic=ic, stg=stg: s.activation(
                    stg[:], bank[:], AF.Identity, bias=B.BOUT[:, ic:ic + 1],
                    scale=1.0), waits=[ptok] + bank_wait(B, stg))
                bank_set(B, bank, B.c_a.tok())
                tok_stg = B.c_a.tok()
                pl.step("y", lambda y, ic=ic, qb=qb, stg=stg: y.dma_start(
                    out=a1_dram[ic * 128:(ic + 1) * 128,
                                qb * QB:(qb + 1) * QB], in_=stg[:]),
                    waits=[tok_stg], bump=B.c_st, n=16)
                bank_set(B, stg, B.c_st.tok())
        prefetch_after(oi)
        oi += 1
    return B.c_st.all()



def alloc_l1_quant(P, B):
    B.QWB = P.sb("QWB", [128, 3 * 1024], F32)
    B.QDB = P.sb("QDB", [128, 2816], F32)
    B.SCL = P.sb("SCL", [128, 4], F32)
    B.RCL = P.sb("RCL", [128, 4], F32)
    B.QI8 = P.sb("QI8", [128, 2816], I8)
    B.QBF = P.sb("QBF", [128, 3 * 1024], BF16)
    B.TQB = P.sb("TQB", [128, DC * 352], BF16)
    B.TQD = P.sb("TQD", [128, 22 * 128], BF16)


def emit_l1_quant(pl, B, wg_sl, wu_sl, wd_sl, wgt_out, wut_out, wdt_out,
                  sg_out, su_out, sd_out, guard=None):
    rows = [128, 128, 96]
    tr_guard = None
    scl_guard = []
    qwb_guard = guard
    qbf_guard = []
    for w_sl, wt_out, s_out in ((wg_sl, wgt_out, sg_out),
                                (wu_sl, wut_out, su_out)):
        ldt = []
        for c in range(3):
            r0 = c * 128
            nr = rows[c]
            pl.step("g", lambda g, c=c, r0=r0, nr=nr, w_sl=w_sl: g.dma_start(
                out=B.QWB[0:nr, c * 1024:(c + 1) * 1024],
                in_=w_sl[r0:r0 + nr, :]), waits=[qwb_guard], bump=B.c_wg, n=16)
            ldt.append(B.c_wg.tok())
        for c in range(3):
            nr = rows[c]
            csl = slice(c * 1024, (c + 1) * 1024)
            pl.step("v", lambda v, c=c, nr=nr, csl=csl: v.reduce_max(
                B.SCL[0:nr, c:c + 1], B.QWB[0:nr, csl], axis=AX.X,
                apply_absolute_value=True), waits=[ldt[c]] + scl_guard + qbf_guard)
            pl.step("v", lambda v, c=c, nr=nr: v.tensor_scalar(
                B.SCL[0:nr, c:c + 1], B.SCL[0:nr, c:c + 1], 1.0 / 127.0,
                None, ALU.mult))
            pl.step("v", lambda v, c=c, nr=nr: v.reciprocal(
                B.RCL[0:nr, c:c + 1], B.SCL[0:nr, c:c + 1]), bump=B.c_v)
            tok_rcl = B.c_v.tok()
            pl.step("v", lambda v, c=c, nr=nr, csl=csl: v.tensor_scalar(
                B.QI8[0:nr, 0:1024], B.QWB[0:nr, csl], B.RCL[0:nr, c:c + 1],
                None, ALU.mult), waits=[tok_rcl])
            pl.step("v", lambda v, c=c, nr=nr, csl=csl: v.tensor_copy(
                B.QBF[0:nr, csl], B.QI8[0:nr, 0:1024]), bump=B.c_v)
            tok_q = B.c_v.tok()
            for dc in range(DC):
                pl.step("y", lambda y, c=c, nr=nr, dc=dc: y.dma_start_transpose(
                    B.TQB[:, dc * 352 + c * 128: dc * 352 + c * 128 + nr],
                    B.QBF[0:nr, c * 1024 + dc * 128: c * 1024 + (dc + 1) * 128]),
                    waits=[tok_q, tr_guard], bump=B.c_sd, n=16)
        tok_tr_l = B.c_sd.all()
        pl.step("y", lambda y, wt_out=wt_out: y.dma_start(
            out=wt_out.rearrange("(c p) r -> p c r", p=128),
            in_=B.TQB[:].rearrange("p (c r) -> p c r", c=DC)),
            waits=tok_tr_l, bump=B.c_sd, n=16)
        tr_guard = B.c_sd.tok()
        scl_guard = []
        for c in range(3):
            nr = rows[c]
            pl.step("y", lambda y, c=c, nr=nr, s_out=s_out: y.dma_start(
                out=s_out[0:1, c * 128:c * 128 + nr].rearrange("a b -> b a"),
                in_=B.SCL[0:nr, c:c + 1]), bump=B.c_sd, n=16)
            scl_guard.append(B.c_sd.tok())
        qwb_guard = B.c_v.tok()
        qbf_guard = B.c_sd.all()
    pl.step("g", lambda g: g.dma_start(out=B.QDB[:], in_=wd_sl),
            waits=[qwb_guard], bump=B.c_wg, n=16)
    tok_ld = B.c_wg.tok()
    pl.step("v", lambda v: v.reduce_max(B.SCL[:, 3:4], B.QDB[:], axis=AX.X,
                                        apply_absolute_value=True),
            waits=[tok_ld] + scl_guard)
    pl.step("v", lambda v: v.tensor_scalar(B.SCL[:, 3:4], B.SCL[:, 3:4],
                                           1.0 / 127.0, None, ALU.mult))
    pl.step("v", lambda v: v.reciprocal(B.RCL[:, 3:4], B.SCL[:, 3:4]),
            bump=B.c_v)
    tok_rcl = B.c_v.tok()
    pl.step("v", lambda v: v.tensor_scalar(B.QI8[:], B.QDB[:], B.RCL[:, 3:4],
                                           None, ALU.mult),
            waits=[tok_rcl])
    pl.step("v", lambda v: v.tensor_copy(B.QBF[:, 0:2816], B.QI8[:]),
            waits=qbf_guard, bump=B.c_v)
    tok_q = B.c_v.tok()
    for ic in range(22):
        pl.step("y", lambda y, ic=ic: y.dma_start_transpose(
            B.TQD[:, ic * 128:(ic + 1) * 128],
            B.QBF[:, ic * 128:(ic + 1) * 128]),
            waits=[tok_q, tr_guard], bump=B.c_sd, n=16)
    tok_tr_l = B.c_sd.all()
    pl.step("y", lambda y: y.dma_start(
        out=wdt_out.rearrange("(c p) r -> p c r", p=128),
        in_=B.TQD[:].rearrange("p (c r) -> p c r", c=22)),
        waits=tok_tr_l, bump=B.c_sd, n=16)
    pl.step("y", lambda y: y.dma_start(
        out=sd_out.rearrange("a b -> b a"), in_=B.SCL[:, 3:4]),
        bump=B.c_sd, n=16)
    return B.c_sd.all()



def bcast_scalar(pl, B, src11, dst_col, tok):
    pl.step("t", lambda t: t.matmul(B.AUX[:, 0:1], B.ones_rf[:], src11,
                                    start=True, stop=True),
            waits=[tok, B.tok_setup_v] + bank_wait(B, B.AUX), bump=B.c_pe)
    pl.step("s", lambda s: s.copy(dst_col, B.AUX[:, 0:1]),
            waits=[B.c_pe.tok()])
    bank_set(B, B.AUX, B.c_a.tok())
    return B.c_a.tok()


def emit_absmax_xn(pl, B, tok_xn):
    for c in range(DC):
        pl.step("v", lambda v, c=c: v.reduce_max(
            B.AMC[:, c:c + 1], B.XN[:, c * S:(c + 1) * S], axis=AX.X,
            apply_absolute_value=True), waits=[tok_xn[0], tok_xn[1]])
    pl.step("v", lambda v: v.reduce_max(B.AMC[:, DC:DC + 1], B.AMC[:, 0:DC],
                                        axis=AX.X), bump=B.c_v)
    tok = B.c_v.tok()
    pl.step("g", lambda g: g.tensor_reduce(
        B.AMX[0:1, 0:1], B.AMC[:, DC:DC + 1], axis=AX.XYZWC, op=ALU.max),
        waits=[tok], bump=B.c_g)
    return B.c_g.tok()



def alloc_l1_ffn(P, B, phase):
    if phase == "a":
        B.XQB = B.KF
        B.XQI = P.sb("XQI", [128, S], I8)
        B.H1 = P.sb("H1", [128, 22 * 512], BF16)
        B.SIL = P.sb("SIL", [128, 512], F32)
        B.SGB = P.sb("SGB", [128, 22], F32)
        B.SUB = P.sb("SUB", [128, 22], F32)
        B.WQS = [P.sb(f"wqs{i}", [128, DC * 128], BF16) for i in range(4)]
        B.MXH = P.sb("MXH", [1, 92], F32)
        B.AMC = P.sb("AMC", [128, DC + 1], F32)
        B.AMX = P.sb("AMX", [1, 16], F32)
        B.CLP = P.sb("CLP", [1, 4], F32)
        B.RSXB = P.sb("RSXB", [128, 1], F32)
        B.SXB = P.sb("SXB", [128, 1], F32)
    else:
        B.H1 = P.sb("H1", [128, 22 * 512], BF16)
        B.H1I = P.sb("H1I", [128, 512], I8)
        B.WQS = [P.sb(f"wqs{i}", [128, DC * 128], BF16) for i in range(2)]
        B.SDB = P.sb("SDB", [128, DC], F32)
        B.AMX = P.sb("AMX", [1, 16], F32)
        B.CLP = P.sb("CLP", [1, 4], F32)
        B.RSHB = P.sb("RSHB", [128, 1], F32)
        B.SHB = P.sb("SHB", [128, 1], F32)
        B.XRT = [P.sb(f"xrt{i}", [128, 512], F32) for i in range(2)]
        B.ART = [P.sb(f"art{i}", [128, 512], F32) for i in range(2)]
        B.XPT = [P.sb(f"xpt{i}", [128, 512], F32) for i in range(2)]


def emit_sx(pl, B, mx8_dram, clip_dram, rsxb_col, shb_col=None, gmax11=False,
            ld_guard=None):
    pl.step("y", lambda y: y.dma_start(out=B.CLP[0:1, 0:1], in_=clip_dram),
            bump=B.c_ld, n=16)
    if gmax11:
        pl.step("y", lambda y: y.dma_start(out=B.AMX[0:1, 8:9], in_=mx8_dram),
                waits=[ld_guard], bump=B.c_ld, n=16)
    else:
        pl.step("y", lambda y: y.dma_start(out=B.AMX[0:1, 0:8], in_=mx8_dram),
                bump=B.c_ld, n=16)
    tok_ld_l = B.c_ld.all()
    pl.step("s", lambda s: s.activation(B.CLP[0:1, 1:2], B.CLP[0:1, 0:1],
                                        AF.Sigmoid), waits=tok_ld_l)
    tok_sig = B.c_a.tok()
    if not gmax11:
        pl.step("v", lambda v: v.reduce_max(B.AMX[0:1, 8:9], B.AMX[0:1, 0:8],
                                            axis=AX.X), waits=tok_ld_l)
    pl.step("v", lambda v: v.tensor_scalar(
        B.AMX[0:1, 9:10], B.AMX[0:1, 8:9], B.CLP[0:1, 1:2], 1.0 / 127.0,
        ALU.mult, ALU.mult), waits=[tok_sig] + tok_ld_l)
    pl.step("v", lambda v: v.reciprocal(B.AMX[0:1, 10:11], B.AMX[0:1, 9:10]),
            bump=B.c_v)
    tok = B.c_v.tok()
    t1 = bcast_scalar(pl, B, B.AMX[0:1, 10:11], rsxb_col, tok)
    if shb_col is not None:
        t1 = bcast_scalar(pl, B, B.AMX[0:1, 9:10], shb_col, tok)
    return t1


def emit_l1_ffn_a(pl, B, mx8_dram, clip_x_dram, wgt_dram, wut_dram,
                  sg_dram, su_dram, h1_dram, hmax_out, tok_xn, guard=None,
                  gmax11=False, wg_pieces=None):
    B.ffna_guard = guard
    tok_sx = emit_sx(pl, B, mx8_dram, clip_x_dram, B.RSXB[:], B.SXB[:],
                     gmax11=gmax11, ld_guard=B.sx_ld_guard)
    for c in range(DC):
        sl = slice(c * S, (c + 1) * S)
        pl.step("v", lambda v, sl=sl: v.tensor_scalar(
            B.XQI[:], B.XN[:, sl], B.RSXB[:], None, ALU.mult),
            waits=[tok_sx, tok_xn[0], tok_xn[1]])
        pl.step("v", lambda v, sl=sl: v.tensor_copy(B.XQB[:, sl], B.XQI[:]))
    tok_xq = B.c_v.tok()
    def pieces(i0, ln):
        out = []
        while ln > 0:
            r = i0 // 352
            lo = i0 - r * 352
            n = min(352 - lo, ln)
            out.append((r, lo, n))
            i0 += n
            ln -= n
        return out

    for ic in range(22):
        for dst, sdram in ((B.SGB, sg_dram), (B.SUB, su_dram)):
            if wg_pieces:
                off = 0
                for r, lo, n in pieces(ic * 128, 128):
                    pl.step("y", lambda y, ic=ic, dst=dst, sdram=sdram, r=r,
                            lo=lo, n=n, off=off: y.dma_start(
                        out=dst[off:off + n, ic:ic + 1],
                        in_=sdram[r:r + 1, lo:lo + n].rearrange("a b -> b a")),
                        bump=B.c_ld, n=16)
                    off += n
            else:
                pl.step("y", lambda y, ic=ic, dst=dst, sdram=sdram: y.dma_start(
                    out=dst[:, ic:ic + 1],
                    in_=sdram[0:1, ic * 128:(ic + 1) * 128].rearrange("a b -> b a")),
                    bump=B.c_ld, n=16)
    tok_s_l = B.c_ld.all()
    pl.step("v", lambda v: v.tensor_scalar(B.SGB[:], B.SGB[:], B.SXB[:],
                                           None, ALU.mult), waits=tok_s_l + [tok_sx])
    pl.step("v", lambda v: v.tensor_scalar(B.SUB[:], B.SUB[:], B.SXB[:],
                                           None, ALU.mult), bump=B.c_v)
    tok_sc = B.c_v.tok()
    wtok = [None] * 4
    wused = [B.ffna_guard] * 4
    hst_tok = B.ffna_guard
    B.sil_guard = None
    for tb in range(NQB):
        q0 = tb * QB
        for ic in range(22):
            bg = ic % 2
            bu = 2 + ic % 2
            for which, wdram, bi in (("g", wgt_dram, bg), ("u", wut_dram, bu)):
                if wg_pieces:
                    off = 0
                    first = True
                    for r, lo, n in pieces(ic * 128, 128):
                        def ldw(g, wdram=wdram, r=r, lo=lo, n=n, off=off, bi=bi):
                            return g.dma_start(
                                out=B.WQS[bi][:].rearrange(
                                    "p (c i) -> p c i", c=DC)[:, :, off:off + n],
                                in_=wdram[r].rearrange(
                                    "(c p) i -> p c i", p=128)[:, :, lo:lo + n])
                        pl.step("g", ldw, waits=[wused[bi]] if first else [],
                                bump=B.c_wg, n=16)
                        first = False
                        off += n
                else:
                    def ldw(g, wdram=wdram, ic=ic, bi=bi):
                        return g.dma_start(
                            out=B.WQS[bi][:].rearrange("p (c i) -> p c i", c=DC),
                            in_=wdram[:, ic * 128:(ic + 1) * 128].rearrange(
                                "(c p) i -> p c i", p=128))
                    pl.step("g", ldw, waits=[wused[bi]], bump=B.c_wg, n=16)
                wtok[bi] = B.c_wg.tok()
            bank_g = B.AV[ic % 2]
            bank_u = B.AV[2 + ic % 2]

            def peg(t, ic=ic, q0=q0, bg=bg, bank_g=bank_g):
                last = None
                for dk in range(DC):
                    last = t.matmul(
                        bank_g[:], B.WQS[bg][:, dk * 128:(dk + 1) * 128],
                        B.XQB[:, dk * S + q0: dk * S + q0 + QB],
                        start=(dk == 0), stop=(dk == DC - 1))
                return last
            pl.step("t", peg, waits=[wtok[bg], tok_xq] + bank_wait(B, bank_g),
                    bump=B.c_pe)
            wused[bg] = B.c_pe.tok()

            def peu(t, ic=ic, q0=q0, bu=bu, bank_u=bank_u):
                last = None
                for dk in range(DC):
                    last = t.matmul(
                        bank_u[:], B.WQS[bu][:, dk * 128:(dk + 1) * 128],
                        B.XQB[:, dk * S + q0: dk * S + q0 + QB],
                        start=(dk == 0), stop=(dk == DC - 1))
                return last
            pl.step("t", peu, waits=[wtok[bu]] + bank_wait(B, bank_u),
                    bump=B.c_pe)
            wused[bu] = B.c_pe.tok()
            tok_mm = B.c_pe.tok()
            pl.step("s", lambda s, ic=ic, bank_g=bank_g: s.activation(
                B.SIL[:], bank_g[:], AF.Silu, scale=B.SGB[:, ic:ic + 1]),
                waits=[tok_mm, tok_sc, B.sil_guard])
            bank_set(B, bank_g, B.c_a.tok())
            pl.step("v", lambda v, ic=ic, bank_u=bank_u: v.scalar_tensor_tensor(
                B.H1[:, ic * 512:(ic + 1) * 512], B.SIL[:], B.SUB[:, ic:ic + 1],
                bank_u[:], ALU.mult, ALU.mult),
                waits=[B.c_a.tok(), hst_tok])
            B.sil_guard = B.c_v.tok()
            bank_set(B, bank_u, B.c_v.tok())
            pl.step("g", lambda g, tb=tb, ic=ic: g.reduce_max(
                B.MXH[0:1, tb * 22 + ic: tb * 22 + ic + 1],
                B.H1[:, ic * 512:(ic + 1) * 512], axis=AX.XYZWC,
                apply_absolute_value=True), waits=[B.c_v.tok()])
        tok_hready = (B.c_v.tok(), B.c_g.tok())
        pl.step("y", lambda y, q0=q0: y.dma_start(
            out=h1_dram.rearrange("(c p) t -> p c t", p=128)[:, :, q0:q0 + QB],
            in_=B.H1[:].rearrange("p (c t) -> p c t", c=22)),
            waits=[tok_hready[0], tok_hready[1]], bump=B.c_st, n=16)
        hst_tok = B.c_st.tok()
    pl.step("g", lambda g: g.tensor_reduce(
        B.MXH[0:1, 88:89], B.MXH[0:1, 0:88], axis=AX.XYZWC, op=ALU.max),
        bump=B.c_g)
    tok_hm = B.c_g.tok()
    pl.step("y", lambda y: y.dma_start(out=hmax_out, in_=B.MXH[0:1, 88:89]),
            waits=[tok_hm], bump=B.c_st, n=16)
    return B.c_st.all()


def emit_l1_ffn_b(pl, B, hmax8_dram, clip_d_dram, wdt_dram, sd_dram,
                  h1_dram, x1_dram, a1_dram, out_dram, guard=None,
                  gmax11=False, wd_pieces=False):
    B.ffnb_guard = guard
    tok_sh = emit_sx(pl, B, hmax8_dram, clip_d_dram, B.RSHB[:], B.SHB[:],
                     gmax11=gmax11, ld_guard=B.sx_ld_guard)
    for m in range(DC):
        if wd_pieces:
            pl.step("y", lambda y, m=m: y.dma_start(
                out=B.SDB[:, m:m + 1],
                in_=sd_dram[m:m + 1, :].rearrange("a b -> b a")),
                bump=B.c_ld, n=16)
        else:
            pl.step("y", lambda y, m=m: y.dma_start(
                out=B.SDB[:, m:m + 1],
                in_=sd_dram[0:1, m * 128:(m + 1) * 128].rearrange("a b -> b a")),
                bump=B.c_ld, n=16)
    tok_s_l = B.c_ld.all()
    pl.step("v", lambda v: v.tensor_scalar(B.SDB[:], B.SDB[:], B.SHB[:],
                                           None, ALU.mult),
            waits=tok_s_l + [tok_sh], bump=B.c_v)
    tok_sd = B.c_v.tok()
    banks = B.AV + B.SC + [B.AUX, B.NRM]
    wused = [B.ffnb_guard, B.ffnb_guard]
    h1_guard = B.ffnb_guard
    for tb in range(NQB):
        q0 = tb * QB
        pl.step("y", lambda y, q0=q0: y.dma_start(
            out=B.H1[:].rearrange("p (c t) -> p c t", c=22),
            in_=h1_dram.rearrange("(c p) t -> p c t", p=128)[:, :, q0:q0 + QB]),
            waits=[h1_guard], bump=B.c_ld, n=16)
        tok_h1 = B.c_ld.tok()
        for ic in range(22):
            sl = slice(ic * 512, (ic + 1) * 512)
            pl.step("v", lambda v, sl=sl: v.tensor_scalar(
                B.H1I[:], B.H1[:, sl], B.RSHB[:], None, ALU.mult),
                waits=[tok_h1, tok_sh])
            pl.step("v", lambda v, sl=sl: v.tensor_copy(B.H1[:, sl], B.H1I[:]))
        tok_hq = B.c_v.tok()
        wtok = [None, None]
        for ic in range(22):
            bi = ic % 2
            if wd_pieces:
                first = True
                for r in range(DC):
                    def ldw(g, ic=ic, bi=bi, r=r):
                        return g.dma_start(
                            out=B.WQS[bi][:].rearrange(
                                "p (c d) -> p c d", c=DC)[:, r:r + 1, :],
                            in_=wdt_dram[r:r + 1, ic * 128:(ic + 1) * 128, :]
                            .rearrange("a p d -> p a d"))
                    pl.step("g", ldw, waits=[wused[bi]] if first else [],
                            bump=B.c_wg, n=16)
                    first = False
            else:
                def ldw(g, ic=ic, bi=bi):
                    return g.dma_start(
                        out=B.WQS[bi][:].rearrange("p (c d) -> p c d", c=DC),
                        in_=wdt_dram[ic * 128:(ic + 1) * 128, :].rearrange(
                            "p (c d) -> p c d", c=DC))
                pl.step("g", ldw, waits=[wused[bi]], bump=B.c_wg, n=16)
            wtok[bi] = B.c_wg.tok()

            def pe(t, ic=ic, bi=bi):
                last = None
                for m in range(DC):
                    last = t.matmul(
                        banks[m][:], B.WQS[bi][:, m * 128:(m + 1) * 128],
                        B.H1[:, ic * 512:(ic + 1) * 512],
                        start=(ic == 0), stop=(ic == 21))
                return last
            w = [wtok[bi], tok_hq]
            if ic == 0:
                for k in range(DC):
                    w += bank_wait(B, banks[k])
            pl.step("t", pe, waits=w, bump=B.c_pe)
            wused[bi] = B.c_pe.tok()
        tok_out = B.c_pe.tok()
        h1_guard = tok_out
        ld_tok = [None, None]
        for m in range(DC):
            bi = m % 2
            pl.step("y", lambda y, m=m, q0=q0, bi=bi: y.dma_start(
                out=B.XRT[bi][:], in_=x1_dram[m * 128:(m + 1) * 128, q0:q0 + QB]),
                waits=[ld_tok[bi]] + bank_wait(B, B.XRT[bi]), bump=B.c_ld, n=16)
            tok_xr = B.c_ld.tok()
            pl.step("y", lambda y, m=m, q0=q0, bi=bi: y.dma_start(
                out=B.ART[bi][:], in_=a1_dram[m * 128:(m + 1) * 128, q0:q0 + QB]),
                waits=[ld_tok[bi]] + bank_wait(B, B.ART[bi]), bump=B.c_ld, n=16)
            tok_res = B.c_ld.tok()
            pl.step("v", lambda v, m=m, bi=bi: v.scalar_tensor_tensor(
                B.XRT[bi][:], banks[m][:], B.SDB[:, m:m + 1], B.XRT[bi][:],
                ALU.mult, ALU.add), waits=[tok_out, tok_sd, tok_xr], bump=B.c_v)
            bank_set(B, banks[m], B.c_v.tok())
            pl.step("v", lambda v, m=m, bi=bi: v.tensor_tensor(
                B.XRT[bi][:], B.XRT[bi][:], B.ART[bi][:], ALU.add),
                waits=[tok_res], bump=B.c_v)
            tok_xp = B.c_v.tok()
            pl.step("y", lambda y, m=m, q0=q0, bi=bi: y.dma_start(
                out=out_dram[m * 128:(m + 1) * 128, q0:q0 + QB],
                in_=B.XRT[bi][:]), waits=[tok_xp], bump=B.c_st, n=16)
            ld_tok[bi] = B.c_st.tok()
            bank_set(B, B.XRT[bi], B.c_st.tok())
            bank_set(B, B.ART[bi], B.c_st.tok())
    return B.c_st.all()


def ag_pieces(i0, ln):
    out = []
    while ln > 0:
        r = i0 // 352
        lo = i0 - r * 352
        n = min(352 - lo, ln)
        out.append((r, lo, n))
        i0 += n
        ln -= n
    return out


def emit_ffn1a_res(pl, B, mx8_dram, clip_x_dram, wgt_dram, wut_dram,
                   sg_dram, su_dram, hmax_out, tok_xn, guard=None):
    B.ffna_guard = guard
    tok_sx = emit_sx(pl, B, mx8_dram, clip_x_dram, B.RSXB[:], B.SXB[:],
                     gmax11=True, ld_guard=B.sx_ld_guard)
    for c in range(DC):
        sl = slice(c * S, (c + 1) * S)
        pl.step("v", lambda v, sl=sl: v.tensor_scalar(
            B.XQI[:], B.XN[:, sl], B.RSXB[:], None, ALU.mult),
            waits=[tok_sx, tok_xn[0], tok_xn[1]])
        pl.step("v", lambda v, sl=sl: v.tensor_copy(B.XQB[:, sl], B.XQI[:]))
    tok_xq = B.c_v.tok()
    for ic in range(22):
        for dst, sdram in ((B.SGB, sg_dram), (B.SUB, su_dram)):
            off = 0
            for r, lo, n in ag_pieces(ic * 128, 128):
                pl.step("y", lambda y, ic=ic, dst=dst, sdram=sdram, r=r,
                        lo=lo, n=n, off=off: y.dma_start(
                    out=dst[off:off + n, ic:ic + 1],
                    in_=sdram[r:r + 1, lo:lo + n].rearrange("a b -> b a")),
                    bump=B.c_ld, n=16)
                off += n
    tok_s_l = B.c_ld.all()
    pl.step("v", lambda v: v.tensor_scalar(B.SGB[:], B.SGB[:], B.SXB[:],
                                           None, ALU.mult), waits=tok_s_l + [tok_sx])
    pl.step("v", lambda v: v.tensor_scalar(B.SUB[:], B.SUB[:], B.SXB[:],
                                           None, ALU.mult), bump=B.c_v)
    tok_sc = B.c_v.tok()
    wtok = [None] * 4
    wused = [B.ffna_guard] * 4

    def load_pair(ic):
        for wdram, bi in ((wgt_dram, ic % 2), (wut_dram, 2 + ic % 2)):
            off = 0
            first = True
            for r, lo, n in ag_pieces(ic * 128, 128):
                def ldw(g, wdram=wdram, r=r, lo=lo, n=n, off=off, bi=bi):
                    return g.dma_start(
                        out=B.WQS[bi][:].rearrange(
                            "p (c i) -> p c i", c=DC)[:, :, off:off + n],
                        in_=wdram[r].rearrange(
                            "(c p) i -> p c i", p=128)[:, :, lo:lo + n])
                pl.step("g", ldw, waits=[wused[bi]] if first else [],
                        bump=B.c_wg, n=16)
                first = False
                off += n
            wtok[bi] = B.c_wg.tok()

    load_pair(0)
    sil_bufs = [B.SIL, B.SIL2]
    sil_guard = [None, None]
    for ic in range(22):
        if ic + 1 < 22:
            load_pair(ic + 1)
        bg, bu = ic % 2, 2 + ic % 2
        for tb in range(NQB):
            q0 = tb * QB
            bank_g = B.AV[tb % 2]
            bank_u = B.AV[2 + tb % 2]

            def peg(t, q0=q0, bg=bg, bank_g=bank_g):
                last = None
                for dk in range(DC):
                    last = t.matmul(
                        bank_g[:], B.WQS[bg][:, dk * 128:(dk + 1) * 128],
                        B.XQB[:, dk * S + q0: dk * S + q0 + QB],
                        start=(dk == 0), stop=(dk == DC - 1))
                return last
            pl.step("t", peg, waits=[wtok[bg], tok_xq] + bank_wait(B, bank_g),
                    bump=B.c_pe)
            tok_mmg = B.c_pe.tok()

            def peu(t, q0=q0, bu=bu, bank_u=bank_u):
                last = None
                for dk in range(DC):
                    last = t.matmul(
                        bank_u[:], B.WQS[bu][:, dk * 128:(dk + 1) * 128],
                        B.XQB[:, dk * S + q0: dk * S + q0 + QB],
                        start=(dk == 0), stop=(dk == DC - 1))
                return last
            pl.step("t", peu, waits=[wtok[bu]] + bank_wait(B, bank_u),
                    bump=B.c_pe)
            sb = sil_bufs[tb % 2]
            pl.step("s", lambda s, ic=ic, bank_g=bank_g, sb=sb: s.activation(
                sb[:], bank_g[:], AF.Silu, scale=B.SGB[:, ic:ic + 1]),
                waits=[tok_mmg, tok_sc, sil_guard[tb % 2]])
            bank_set(B, bank_g, B.c_a.tok())
            slot = B.H1S[ic * 4 + tb]
            pl.step("v", lambda v, ic=ic, bank_u=bank_u, sb=sb, slot=slot:
                    v.scalar_tensor_tensor(
                        slot, sb[:], B.SUB[:, ic:ic + 1], bank_u[:],
                        ALU.mult, ALU.mult),
                    waits=[B.c_a.tok(), B.c_pe.tok()], bump=B.c_v)
            sil_guard[tb % 2] = B.c_v.tok()
            bank_set(B, bank_u, B.c_v.tok())
            pl.step("g", lambda g, ic=ic, tb=tb, slot=slot: g.reduce_max(
                B.MXH[0:1, tb * 22 + ic: tb * 22 + ic + 1],
                slot, axis=AX.XYZWC, apply_absolute_value=True),
                waits=[B.c_v.tok()])
        wused[bg] = B.c_pe.tok()
        wused[bu] = B.c_pe.tok()
    pl.step("g", lambda g: g.tensor_reduce(
        B.MXH[0:1, 88:89], B.MXH[0:1, 0:88], axis=AX.XYZWC, op=ALU.max),
        bump=B.c_g)
    tok_hm = B.c_g.tok()
    pl.step("y", lambda y: y.dma_start(out=hmax_out, in_=B.MXH[0:1, 88:89]),
            waits=[tok_hm], bump=B.c_st, n=16)
    return [B.c_st.tok()]


def emit_ffn1b_res(pl, B, hmax8_dram, clip_d_dram, wdt_dram, sd_dram,
                   x1_dram, a1_dram, out_dram, guard=None):
    B.ffnb_guard = guard
    tok_sh = emit_sx(pl, B, hmax8_dram, clip_d_dram, B.RSHB[:], B.SHB[:],
                     gmax11=True, ld_guard=B.sx_ld_guard)
    for m in range(DC):
        pl.step("y", lambda y, m=m: y.dma_start(
            out=B.SDB[:, m:m + 1],
            in_=sd_dram[m:m + 1, :].rearrange("a b -> b a")),
            bump=B.c_ld, n=16)
    tok_s_l = B.c_ld.all()
    pl.step("v", lambda v: v.tensor_scalar(B.SDB[:], B.SDB[:], B.SHB[:],
                                           None, ALU.mult),
            waits=tok_s_l + [tok_sh], bump=B.c_v)
    tok_sd = B.c_v.tok()
    qtb = [None] * 4
    for tb in range(NQB):
        for ic in range(22):
            slot = B.H1S[ic * 4 + tb]
            pl.step("v", lambda v, slot=slot: v.tensor_scalar(
                B.H1I[:], slot, B.RSHB[:], None, ALU.mult), waits=[tok_sh])
            pl.step("v", lambda v, slot=slot: v.tensor_copy(slot, B.H1I[:]))
        qtb[tb] = B.c_v.tok()
    banks = B.AV + B.SC + [B.AUX, B.NRM]
    wtok = [None] * 4
    wused = [B.ffnb_guard] * 4

    def load_w(dg, ic):
        bi = ic % 4
        first = True
        for j in range(2):
            r = 2 * dg + j

            def ldw(g, r=r, j=j, ic=ic, bi=bi):
                return g.dma_start(
                    out=B.WQS[bi][:, j * 128:(j + 1) * 128],
                    in_=wdt_dram[r:r + 1, ic * 128:(ic + 1) * 128, :]
                    .rearrange("a p d -> p (a d)"))
            pl.step("g", ldw, waits=[wused[bi]] if first else [],
                    bump=B.c_wg, n=16)
            first = False
        wtok[bi] = B.c_wg.tok()

    res_st = {}

    def res_loads(dg, j, res_tok):
        m = 2 * dg + j
        for tb in range(NQB):
            b = tb % 4
            pl.step("y", lambda y, m=m, tb=tb, b=b: y.dma_start(
                out=B.XR4[b], in_=x1_dram[m * 128:(m + 1) * 128,
                                          tb * QB:(tb + 1) * QB]),
                waits=[res_st.get(("x", b)), B.ffnb_guard], bump=B.c_ld, n=16)
            res_tok[("x", j, tb)] = B.c_ld.tok()
            pl.step("y", lambda y, m=m, tb=tb, b=b: y.dma_start(
                out=B.AR4[b], in_=a1_dram[m * 128:(m + 1) * 128,
                                          tb * QB:(tb + 1) * QB]),
                waits=[res_st.get(("a", b)), B.ffnb_guard], bump=B.c_ld, n=16)
            res_tok[("a", j, tb)] = B.c_ld.tok()

    for dg in range(4):
        load_w(dg, 0)
        load_w(dg, 1)
        res_tok = {}
        res_loads(dg, 0, res_tok)
        for ic in range(22):
            if ic + 2 < 22:
                load_w(dg, ic + 2)
            bi = ic % 4

            def pe(t, ic=ic, bi=bi):
                last = None
                for tb in range(NQB):
                    for j in range(2):
                        last = t.matmul(
                            banks[tb * 2 + j][:],
                            B.WQS[bi][:, j * 128:(j + 1) * 128],
                            B.H1S[ic * 4 + tb],
                            start=(ic == 0), stop=(ic == 21))
                return last
            w = [wtok[bi]] + qtb
            if ic == 0:
                for k in range(8):
                    w += bank_wait(B, banks[k])
            pl.step("t", pe, waits=w, bump=B.c_pe)
            wused[bi] = B.c_pe.tok()
        tok_out = B.c_pe.tok()
        for j in range(2):
            m = 2 * dg + j
            for tb in range(NQB):
                b = tb % 4
                bank = banks[tb * 2 + j]
                pl.step("v", lambda v, m=m, b=b, bank=bank:
                        v.scalar_tensor_tensor(
                            B.XR4[b], bank[:], B.SDB[:, m:m + 1], B.XR4[b],
                            ALU.mult, ALU.add),
                        waits=[tok_out, tok_sd, res_tok[("x", j, tb)]],
                        bump=B.c_v)
                bank_set(B, bank, B.c_v.tok())
                pl.step("v", lambda v, b=b: v.tensor_tensor(
                    B.XR4[b], B.XR4[b], B.AR4[b], ALU.add),
                    waits=[res_tok[("a", j, tb)]], bump=B.c_v)
                tok_xp = B.c_v.tok()
                pl.step("y", lambda y, m=m, tb=tb, b=b: y.dma_start(
                    out=out_dram[m * 128:(m + 1) * 128, tb * QB:(tb + 1) * QB],
                    in_=B.XR4[b]), waits=[tok_xp], bump=B.c_st, n=16)
                res_st[("x", b)] = B.c_st.tok()
                res_st[("a", b)] = B.c_st.tok()
            if j == 0:
                res_loads(dg, 1, res_tok)
    return B.c_st.all()


def finish(P, pl, st_tok):
    wl = st_tok if isinstance(st_tok, list) else [st_tok]
    pl.step("y", lambda y: None, waits=wl)
    with P.nc.Block() as block:
        pl.emit_all(block)
    return P.fin()


def build_l0_attn(dbg=False):
    P = Prog()
    xn = P.din("xn", [D, S], BF16)
    wq_t = P.din("wq_t", [D, D], F32)
    wk_t = P.din("wk_t", [D, D], F32)
    wv_t = P.din("wv_t", [D, D], F32)
    wo_t = P.din("wo_t", [D, D], F32)
    bq2 = P.din("bq2", [128, DC], F32)
    bk2 = P.din("bk2", [128, DC], F32)
    bv_row = P.din("bv_row", [1, D], F32)
    bo2 = P.din("bo2", [128, DC], F32)
    a0 = P.dout("a0", [D, S], F32)

    B = alloc_common(P)
    alloc_l0_attn(P, B)
    pl = mk_plan(P, B)
    emit_setup(pl, B)
    t_ = load_xn(pl, B, xn)
    st_tok = emit_l0_attn(pl, B, wq_t, wk_t, wv_t, wo_t, bq2, bk2, bv_row, bo2,
                          a0, (t_, t_))
    if dbg:
        dumps = [("QF", B.QF, BF16), ("KF", B.KF, BF16), ("VT", B.VT, BF16),
                 ("EXPB", B.EXPB, BF16), ("CB", B.CBS, F32), ("MXR", B.MXR, F32),
                 ("ATT", B.ATT, BF16), ("BVB", B.BVB, BF16)]
        allw = [B.c_pe.tok(), B.c_v.tok(), B.c_a.tok(), B.c_g.tok()]
        for nm, buf, dt in dumps:
            od = P.dout("dbg_" + nm, list(buf.shape), dt)
            pl.step("y", lambda y, od=od, buf=buf: y.dma_start(out=od, in_=buf[:]),
                    waits=allw, bump=B.c_st, n=16)
        st_tok = B.c_st.all()
    return finish(P, pl, st_tok)


def build_l0_pre(dbg=False, no_ln=False):
    P = Prog()
    x = P.din("x", [D, S], F32)
    lg1 = P.din("lg1", [1, 256], F32)
    lg2 = P.din("lg2", [1, 256], F32)
    a1t = P.din("a1t", [256, 64, 1024], F32)
    b1 = P.din("b1", [256, 64, 4096], F32)
    a2t = P.din("a2t", [256, 64, 4096], F32)
    b2 = P.din("b2", [256, 64, 1024], F32)
    coreid = P.din("coreid", [1, 1], F32)
    xn0 = P.dout("xn0", [D, S], BF16)
    w1sl = P.dout("w1sl", [D, 512], BF16)
    w2sl = P.dout("w2sl", [512, D], BF16)

    B = alloc_common(P)
    B.XN = P.sb("XN", [128, DC * S], BF16)
    alloc_ln(P, B)
    alloc_compose(P, B)
    pl = mk_plan(P, B)
    emit_setup(pl, B)
    if not no_ln:
        tok_xn = emit_ln(pl, B, x)
        pl.step("y", lambda y: y.dma_start(
            out=xn0.rearrange("(c p) t -> p c t", p=128),
            in_=B.XN[:].rearrange("p (c t) -> p c t", c=DC)),
            waits=[tok_xn[0], tok_xn[1]], bump=B.c_st, n=16)
    else:
        pl.step("y", lambda y: y.dma_start(
            out=xn0.rearrange("(c p) t -> p c t", p=128),
            in_=B.XN[:].rearrange("p (c t) -> p c t", c=DC)),
            bump=B.c_st, n=16)
    emit_compose_setup(pl, B, coreid)
    st = emit_compose(pl, B, lg1, lg2, a1t, b1, a2t, b2, w1sl, w2sl)
    if dbg:
        allw0 = [B.c_pe.tok(), B.c_v.tok(), B.c_a.tok(), B.c_g.tok()] \
            + B.c_wg.all()
        pl.step("v", lambda v: v.tensor_copy(B.U2[:], B.IDXA[:]), waits=allw0)
        pl.step("v", lambda v: v.tensor_copy(B.IDXAf[:], B.IDX8[:]),
                bump=B.c_v)
        dumps = [("IDXA", B.U2, F32), ("IDX8", B.IDXAf, F32),
                 ("MS", B.MS, F32), ("W32", B.W32, F32), ("IVF", B.IVF, F32),
                 ("GA0", B.GA[0], BF16), ("GB0", B.GB[0], BF16),
                 ("GAS0", B.GAS[0], BF16), ("QM64", B.QM64, F32),
                 ("PEVB", B.PEVB, F32), ("SELB", B.SELB, F32)]
        allw = [B.c_pe.tok(), B.c_v.tok(), B.c_a.tok(), B.c_g.tok()]
        for nm, buf, dt in dumps:
            od = P.dout("dbg_" + nm, list(buf.shape), dt)
            pl.step("y", lambda y, od=od, buf=buf: y.dma_start(out=od, in_=buf[:]),
                    waits=allw, bump=B.c_st, n=16)
    return finish(P, pl, B.c_st.all())


def build_l0_ffn(dbg=False):
    P = Prog()
    xn = P.din("xn", [D, S], BF16)
    w1g = P.din("w1g", [8, D, 512], BF16)
    w2g = P.din("w2g", [4096, D], BF16)
    x = P.din("x", [D, S], F32)
    a0 = P.din("a0", [D, S], F32)
    x1 = P.dout("x1", [D, S], F32)
    xn1 = P.dout("xn1", [D, S], BF16)
    mx = P.dout("mx", [1, 1], F32)

    B = alloc_common(P)
    B.XN = P.sb("XN", [128, DC * S], BF16)
    B.AMC = P.sb("AMC", [128, DC + 1], F32)
    B.AMX = P.sb("AMX", [1, 16], F32)
    alloc_ffn0(P, B)
    alloc_ln(P, B)
    pl = mk_plan(P, B)
    emit_setup(pl, B)
    t = load_xn(pl, B, xn)
    st = emit_ffn0_apply(pl, B, w1g, w2g, x, a0, x1, (t, t),
                         no_gelu=(dbg == "nogelu"))
    tok_xn1 = emit_ln(pl, B, x1, wait=st)
    pl.step("y", lambda y: y.dma_start(
        out=xn1.rearrange("(c p) t -> p c t", p=128),
        in_=B.XN[:].rearrange("p (c t) -> p c t", c=DC)),
        waits=[tok_xn1[0], tok_xn1[1]], bump=B.c_st, n=16)
    tok_mx = emit_absmax_xn(pl, B, tok_xn1)
    pl.step("y", lambda y: y.dma_start(out=mx, in_=B.AMX[0:1, 0:1]),
            waits=[tok_mx], bump=B.c_st, n=16)
    st = B.c_st.all()
    if dbg:
        allw = [B.c_pe.tok(), B.c_v.tok(), B.c_a.tok(), B.c_g.tok(), st]
        od = P.dout("dbg_H", [128, 32 * 512], BF16)
        pl.step("y", lambda y, od=od: y.dma_start(out=od, in_=B.H[:]),
                waits=allw, bump=B.c_st, n=16)
        st = B.c_st.all()
    return finish(P, pl, st)


def build_l1_attn():
    P = Prog()
    xn = P.din("xn", [D, S], BF16)
    w_in = P.din("w_in", [D, D], F32)
    w_r = P.din("w_r", [D, D], F32)
    w_i = P.din("w_i", [D, D], F32)
    w_out = P.din("w_out", [D, D], F32)
    b_in2 = P.din("b_in2", [128, DC], F32)
    b_r2 = P.din("b_r2", [128, DC], F32)
    b_i2 = P.din("b_i2", [128, DC], F32)
    b_out2 = P.din("b_out2", [128, DC], F32)
    log_a2 = P.din("log_a2", [128, DC], F32)
    a1 = P.dout("a1", [D, S], F32)
    B = alloc_common(P)
    B.XN = P.sb("XN", [128, DC * S], BF16)
    alloc_l1_attn(P, B)
    pl = mk_plan(P, B)
    emit_setup(pl, B)
    t = load_xn(pl, B, xn)
    st = emit_l1_attn(pl, B, w_in, w_r, w_i, w_out, b_in2, b_r2, b_i2, b_out2,
                      log_a2, a1, (t, t))
    return finish(P, pl, st)


def build_l1_quant():
    P = Prog()
    wg_sl = P.din("wg_sl", [352, 1024], F32)
    wu_sl = P.din("wu_sl", [352, 1024], F32)
    wd_sl = P.din("wd_sl", [128, 2816], F32)
    wgt_sl = P.dout("wgt_sl", [D, 352], BF16)
    wut_sl = P.dout("wut_sl", [D, 352], BF16)
    wdt_sl = P.dout("wdt_sl", [2816, 128], BF16)
    sg_sl = P.dout("sg_sl", [1, 352], F32)
    su_sl = P.dout("su_sl", [1, 352], F32)
    sd_sl = P.dout("sd_sl", [1, 128], F32)
    B = alloc_common(P)
    alloc_l1_quant(P, B)
    pl = mk_plan(P, B)
    emit_setup(pl, B)
    st = emit_l1_quant(pl, B, wg_sl, wu_sl, wd_sl, wgt_sl, wut_sl, wdt_sl,
                       sg_sl, su_sl, sd_sl)
    return finish(P, pl, st)


def build_l1_ffn_a():
    P = Prog()
    xn = P.din("xn", [D, S], BF16)
    mx8 = P.din("mx8", [1, 8], F32)
    clip_x = P.din("clip_x", [1, 1], F32)
    wgt = P.din("wgt", [D, 2816], BF16)
    wut = P.din("wut", [D, 2816], BF16)
    sg = P.din("sg", [1, 2816], F32)
    su = P.din("su", [1, 2816], F32)
    h1 = P.dout("h1", [2816, S], BF16)
    hmax = P.dout("hmax", [1, 1], F32)
    B = alloc_common(P)
    B.XN = P.sb("XN", [128, DC * S], BF16)
    B.QF = P.sb("QF", [128, DC * S], BF16)
    B.KF = P.sb("KF", [128, DC * S], BF16)
    alloc_l1_ffn(P, B, "a")
    B.sx_ld_guard = None
    pl = mk_plan(P, B)
    emit_setup(pl, B)
    t = load_xn(pl, B, xn)
    st = emit_l1_ffn_a(pl, B, mx8, clip_x, wgt, wut, sg, su, h1, hmax, (t, t))
    return finish(P, pl, st)


def build_l1_ffn_b():
    P = Prog()
    hmax8 = P.din("hmax8", [1, 8], F32)
    clip_d = P.din("clip_d", [1, 1], F32)
    wdt = P.din("wdt", [2816, D], BF16)
    sd = P.din("sd", [1, D], F32)
    h1 = P.din("h1", [2816, S], BF16)
    x1 = P.din("x1", [D, S], F32)
    a1 = P.din("a1", [D, S], F32)
    out = P.dout("out", [D, S], F32)
    B = alloc_common(P)
    alloc_l1_ffn(P, B, "b")
    B.sx_ld_guard = None
    pl = mk_plan(P, B)
    emit_setup(pl, B)
    st = emit_l1_ffn_b(pl, B, hmax8, clip_d, wdt, sd, h1, x1, a1, out)
    return finish(P, pl, st)



def alloc_fused(P):
    B = alloc_common(P)
    B.XN = P.sb("XN", [128, DC * S], BF16)
    B.QF = P.sb("QF", [128, DC * S], BF16)
    B.KF = P.sb("KF", [128, DC * S], BF16)
    B.VT = P.sb("VT", [128, NT * D], BF16)
    B.STASH = P.sb("STASH", [128, 8 * 512], F16)
    B.ATT = B.STASH.bitcast(BF16)
    B.EXPB = P.sb("EXPB", [128, NT * 512], BF16)
    B.WARE = [P.sb(f"ware{i}", [128, DC * 256], BF16) for i in range(2)]
    B.RZ = P.sb("RZN", [128, 512], F32)
    B.RN = B.RZ
    B.FS = P.sb("FS", [128, 5 * 512], F32)
    FSs = [B.FS[:, i * 512:(i + 1) * 512] for i in range(5)]
    B.RZR = B.FS[0:1, 2048:2560]
    B.A0T = [FSs[0], FSs[1]]
    B.RSTB = FSs[2]
    B.MRB = FSs[3]
    B.GU = B.RZ[:]
    B.GS = FSs[4]
    B.XRT = [FSs[0], FSs[1]]
    B.ART = [FSs[2], FSs[3]]
    B.SIL = FSs[4]
    B.X2B = [B.EXPB[:, 0:512], B.EXPB[:, 512:1024]]
    B.ROWA = B.FS[0:1, 2048:2560]
    B.ROWB = B.RZ[0:1, 0:512]
    B.EPSB = P.sb("epsb", [1, 1], F32)
    B.eps_init = False
    B.MXR = P.sb("MXR", [1, 8], F32)
    B.NCR = P.sb("NCR", [1, 8], F32)
    B.CBS = P.sb("CBS", [128, 8], F32)
    B.BQ = P.sb("BQ", [128, DC], F32)
    B.BK = P.sb("BK", [128, DC], F32)
    B.BO = P.sb("BO", [128, DC], F32)
    B.BVR = B.EXPB[0:1, 2048:3072]
    B.BVB = B.EXPB[:, 1024:2048]
    B.LG = B.FS[0:1, 0:256]
    B.LGW = B.FS[32:33, 0:256]
    B.MV = P.sb("MV", [1, 32], F32)
    B.IV = P.sb("IV", [1, 32], U32)
    B.IVF = P.sb("IVF", [1, 32], F32)
    B.EW = P.sb("EW", [1, 32], F32)
    B.SWr = P.sb("SWr", [1, 2], F32)
    B.W32 = P.sb("W32", [1, 32], F32)
    B.SELB = P.sb("SELB", [128, 32], F32)
    B.W32B = P.sb("W32B", [128, 32], F32)
    B.PEVB = P.sb("PEVB", [128, 32], F32)
    B.EOT = P.sb("EOT", [128, 32], F32)
    B.UB = P.sb("UB", [128, 32], F32)
    B.U2 = P.sb("U2", [128, 16], F32)
    B.MS = P.sb("MS", [128, 16], F32)
    B.IDXAf = P.sb("IDXAf", [128, 16], F32)
    B.IDXA = P.sb("IDXA", [128, 16], U32)
    B.IDX8 = P.sb("IDX8", [128, 16], U32)
    B.QM64 = P.sb("QM64", [128, 1], F32)
    B.QI = P.sb("QI", [128, 1], I32)
    B.COREB = P.sb("COREB", [128, 1], F32)
    B.GA = [B.QF[:, 0:1024], B.QF[:, 1024:2048]]
    B.GAS = [B.QF[:, 2048:3072], B.QF[:, 3072:4096]]
    B.GB = [B.QF[:, 4096:5120], B.QF[:, 5120:6144]]
    B.STG = [B.QF[:, 6144:6656], B.QF[:, 6656:7168]]
    B.H = B.QF
    B.WST = [B.KF[:, 0:8192], B.KF[:, 8192:16384]]
    B.QWB = B.VT[:, 0:3072]
    B.QDB = B.VT[:, 3072:5888]
    B.QBF = B.VT[:, 5888:8960]
    B.TQB = B.VT[:, 8960:11776]
    B.TQD = B.VT[:, 11776:14592]
    B.IBUF = P.sb("IBUF", [128, 2816], I8)
    B.QI8 = B.IBUF[:, 0:2816]
    B.SCL = P.sb("SCL", [128, 4], F32)
    B.RCL = P.sb("RCL", [128, 4], F32)
    B.IP = B.QF
    B.RG = B.KF
    B.IG = B.VT
    B.BIN = P.sb("BIN", [128, DC], F32)
    B.BR = P.sb("BR", [128, DC], F32)
    B.BI = P.sb("BI", [128, DC], F32)
    B.BOUT = P.sb("BOUT", [128, DC], F32)
    B.ASG = P.sb("ASG", [128, DC], F32)
    B.XQB = B.KF
    B.XQI = B.IBUF[:, 0:2048]
    B.H1 = B.VT[:, 0:22 * 512]
    B.H1I = B.IBUF[:, 0:512]
    B.SGB = P.sb("SGB", [128, 22], F32)
    B.SUB = P.sb("SUB", [128, 22], F32)
    B.SDB = P.sb("SDB", [128, DC], F32)
    B.WQS = [B.QF[:, 0:1024], B.QF[:, 1024:2048],
             B.QF[:, 2048:3072], B.QF[:, 3072:4096]]
    B.MXH = P.sb("MXH", [1, 92], F32)
    B.AMC = P.sb("AMC", [128, DC + 1], F32)
    B.AMX = P.sb("AMX", [1, 16], F32)
    B.CLP = P.sb("CLP", [1, 4], F32)
    B.RSXB = P.sb("RSXB", [128, 1], F32)
    B.SXB = P.sb("SXB", [128, 1], F32)
    B.RSHB = P.sb("RSHB", [128, 1], F32)
    B.SHB = P.sb("SHB", [128, 1], F32)
    B.H1S = []
    for k in range(88):
        if k < 32:
            B.H1S.append(B.XN[:, k * 512:(k + 1) * 512])
        elif k < 64:
            B.H1S.append(B.VT[:, (k - 32) * 512:(k - 31) * 512])
        else:
            B.H1S.append(B.QF[:, (k - 56) * 512:(k - 55) * 512])
    B.SIL2 = B.RZ[:, 0:512]
    EXPF = B.EXPB.bitcast(F32)
    B.XR4 = [EXPF[:, i * 512:(i + 1) * 512] for i in range(4)]
    B.AR4 = [EXPF[:, (4 + i) * 512:(5 + i) * 512] for i in range(4)]
    return B


RG8 = [list(range(8))]


def build_fused():
    P = Prog()
    x = P.din("x", [D, S], F32)
    wq_t = P.din("wq_t", [D, D], F32)
    wk_t = P.din("wk_t", [D, D], F32)
    wv_t = P.din("wv_t", [D, D], F32)
    wo_t = P.din("wo_t", [D, D], F32)
    bq2 = P.din("bq2", [128, DC], F32)
    bk2 = P.din("bk2", [128, DC], F32)
    bv_row = P.din("bv_row", [1, D], F32)
    bo2 = P.din("bo2", [128, DC], F32)
    lg1 = P.din("lg1", [1, 256], F32)
    lg2 = P.din("lg2", [1, 256], F32)
    a1t = P.din("a1t", [256, 64, 1024], F32)
    b1 = P.din("b1", [256, 64, 4096], F32)
    a2t = P.din("a2t", [256, 64, 4096], F32)
    b2 = P.din("b2", [256, 64, 1024], F32)
    coreid = P.din("coreid", [1, 1], F32)
    w_in = P.din("w_in", [D, D], F32)
    w_r = P.din("w_r", [D, D], F32)
    w_i = P.din("w_i", [D, D], F32)
    w_out = P.din("w_out", [D, D], F32)
    b_in2 = P.din("b_in2", [128, DC], F32)
    b_r2 = P.din("b_r2", [128, DC], F32)
    b_i2 = P.din("b_i2", [128, DC], F32)
    b_out2 = P.din("b_out2", [128, DC], F32)
    log_a2 = P.din("log_a2", [128, DC], F32)
    wg_sl = P.din("wg_sl", [352, 1024], F32)
    wu_sl = P.din("wu_sl", [352, 1024], F32)
    wd_sl = P.din("wd_sl", [128, 2816], F32)
    clip_x = P.din("clip_x", [1, 1], F32)
    clip_d = P.din("clip_d", [1, 1], F32)
    out = P.dout("out", [D, S], F32)
    a0 = P.dtmp("a0", [D, S], F32)
    a1 = P.dtmp("a1", [D, S], F32)
    x1 = P.dtmp("x1", [D, S], F32)
    w1in = P.dtmp("w1in", [D, 512], BF16)
    w1out = P.dtmp("w1out", [8 * D, 512], BF16)
    w2in = P.dtmp("w2in", [512, D], BF16)
    w2out = P.dtmp("w2out", [4096, D], BF16)
    wgt_in = P.dtmp("wgt_in", [D, 352], BF16)
    wgt_out = P.dtmp("wgt_out", [8 * D, 352], BF16)
    wut_in = P.dtmp("wut_in", [D, 352], BF16)
    wut_out = P.dtmp("wut_out", [8 * D, 352], BF16)
    wdt_in = P.dtmp("wdt_in", [2816, 128], BF16)
    wdt_out = P.dtmp("wdt_out", [8 * 2816, 128], BF16)
    sg_in = P.dtmp("sg_in", [1, 352], F32)
    sg_out = P.dtmp("sg_out", [8, 352], F32)
    su_in = P.dtmp("su_in", [1, 352], F32)
    su_out = P.dtmp("su_out", [8, 352], F32)
    sd_in = P.dtmp("sd_in", [1, 128], F32)
    sd_out = P.dtmp("sd_out", [8, 128], F32)
    mx_in = P.dtmp("mx_in", [1, 16], F32)
    mx_out = P.dtmp("mx_out", [1, 16], F32)
    hm_in = P.dtmp("hm_in", [1, 16], F32)
    hm_out = P.dtmp("hm_out", [1, 16], F32)

    B = alloc_fused(P)
    B.sx_ld_guard = None
    pl = mk_plan(P, B)
    emit_setup(pl, B)
    emit_compose_setup(pl, B, coreid)

    tok_xn0 = emit_ln(pl, B, x)
    st_comp = emit_compose(pl, B, lg1, lg2, a1t, b1, a2t, b2, w1in, w2in)
    for cin, cout in ((w1in, w1out), (w2in, w2out)):
        pl.step("g", lambda g, cin=cin, cout=cout: g.collective_compute(
            "AllGather", ALU.bypass, replica_groups=RG8,
            ins=[cin.opt()], outs=[cout.opt()]),
            waits=[st_comp], bump=B.c_cc)
    tok_ag1 = B.c_cc.tok()
    bank_set(B, B.A0T[0], B.c_v.tok())
    st_attn = emit_l0_attn(pl, B, wq_t, wk_t, wv_t, wo_t, bq2, bk2, bv_row,
                           bo2, a0, tok_xn0, guard=[st_comp])
    attn_done = B.c_pe.tok()
    q_done = emit_l1_quant(pl, B, wg_sl, wu_sl, wd_sl, wgt_in, wut_in, wdt_in,
                           sg_in, su_in, sd_in, guard=attn_done)
    st_ffn0 = emit_ffn0_apply(
        pl, B, w1out.rearrange("(r d) i -> r d i", r=8), w2out, x, a0, x1,
        tok_xn0, guard=[attn_done], wsrc_tok=tok_ag1,
        res_guard=[attn_done, B.c_v.tok(), B.c_a.tok(), st_attn])
    for cin, cout in ((wgt_in, wgt_out), (wut_in, wut_out), (wdt_in, wdt_out),
                      (sg_in, sg_out), (su_in, su_out), (sd_in, sd_out)):
        pl.step("g", lambda g, cin=cin, cout=cout: g.collective_compute(
            "AllGather", ALU.bypass, replica_groups=RG8,
            ins=[cin.opt()], outs=[cout.opt()]),
            waits=[q_done], bump=B.c_cc)
    tok_ag2 = B.c_cc.tok()

    tok_xn1 = emit_ln(pl, B, x1, wait=st_ffn0)
    tok_mx = emit_absmax_xn(pl, B, tok_xn1)
    pl.step("y", lambda y: y.dma_start(out=mx_in[0:1, 0:1], in_=B.AMX[0:1, 0:1]),
            waits=[tok_mx], bump=B.c_st, n=16)
    tok_mxst = B.c_st.tok()
    pl.step("g", lambda g: g.collective_compute(
        "AllReduce", ALU.max, replica_groups=RG8,
        ins=[mx_in.opt()], outs=[mx_out.opt()]),
        waits=[tok_mxst], bump=B.c_cc)
    tok_ar3 = B.c_cc.tok()
    st_a1 = emit_l1_attn(pl, B, w_in, w_r, w_i, w_out, b_in2, b_r2, b_i2,
                         b_out2, log_a2, a1, tok_xn1, guard=attn_done,
                         evac_guard=q_done)
    l1_done_v = B.c_v.tok()
    B.sx_ld_guard = tok_ar3
    st_h1 = emit_ffn1a_res(
        pl, B, mx_out[0:1, 0:1], clip_x,
        wgt_out.rearrange("(r d) i -> r d i", r=8),
        wut_out.rearrange("(r d) i -> r d i", r=8),
        sg_out, su_out, hm_in[0:1, 0:1], tok_xn1,
        guard=[l1_done_v, tok_ag2, q_done])
    pl.step("g", lambda g: g.collective_compute(
        "AllReduce", ALU.max, replica_groups=RG8,
        ins=[hm_in.opt()], outs=[hm_out.opt()]),
        waits=[st_h1], bump=B.c_cc)
    tok_ar4 = B.c_cc.tok()
    ffnb_guard = [B.c_pe.tok(), B.c_v.tok(), B.c_g.tok()] + B.c_st.all()
    B.sx_ld_guard = tok_ar4
    st_out = emit_ffn1b_res(
        pl, B, hm_out[0:1, 0:1], clip_d,
        wdt_out.rearrange("(r i) d -> r i d", r=8), sd_out,
        x1, a1, out, guard=ffnb_guard)
    return finish(P, pl, st_out)


def lay_bias(b):
    return np.ascontiguousarray(b.reshape(DC, 128).T)


CORES = list(range(8))


def host_prep(g):
    x = np.asarray(g["L0_attn_batch"])
    shared = {
        "wq_t": np.ascontiguousarray(np.asarray(g["L0_attn_Wq"]).T),
        "wk_t": np.ascontiguousarray(np.asarray(g["L0_attn_Wk"]).T),
        "wv_t": np.ascontiguousarray(np.asarray(g["L0_attn_Wv"]).T),
        "wo_t": np.ascontiguousarray(np.asarray(g["L0_attn_Wo"]).T),
        "bq2": lay_bias(np.asarray(g["L0_attn_bq"])),
        "bk2": lay_bias(np.asarray(g["L0_attn_bk"])),
        "bo2": lay_bias(np.asarray(g["L0_attn_bo"])),
        "bv_row": np.ascontiguousarray(np.asarray(g["L0_attn_bv"]).reshape(1, D)),
        "lg1": np.ascontiguousarray(np.asarray(g["L0_ffn_fc1_logits"]).reshape(1, 256)),
        "lg2": np.ascontiguousarray(np.asarray(g["L0_ffn_fc2_logits"]).reshape(1, 256)),
        "a1t": np.ascontiguousarray(np.asarray(g["L0_ffn_fc1_A"]).transpose(0, 2, 1)),
        "b1": np.ascontiguousarray(g["L0_ffn_fc1_B"]),
        "a2t": np.ascontiguousarray(np.asarray(g["L0_ffn_fc2_A"]).transpose(0, 2, 1)),
        "b2": np.ascontiguousarray(g["L0_ffn_fc2_B"]),
        "w_in": np.ascontiguousarray(g["L1_attn_W_in"]),
        "w_r": np.ascontiguousarray(g["L1_attn_W_r"]),
        "w_i": np.ascontiguousarray(g["L1_attn_W_i"]),
        "w_out": np.ascontiguousarray(g["L1_attn_W_out"]),
        "b_in2": lay_bias(np.asarray(g["L1_attn_b_in"])),
        "b_r2": lay_bias(np.asarray(g["L1_attn_b_r"])),
        "b_i2": lay_bias(np.asarray(g["L1_attn_b_i"])),
        "b_out2": lay_bias(np.asarray(g["L1_attn_b_out"])),
        "log_a2": lay_bias(np.asarray(g["L1_attn_log_a"])),
        "clip_x": np.ascontiguousarray(np.asarray(g["L1_ffn_clip_x"]).reshape(1, 1)),
        "clip_d": np.ascontiguousarray(np.asarray(g["L1_ffn_clip_down"]).reshape(1, 1)),
    }
    wg = np.asarray(g["L1_ffn_w_gate"])
    wu = np.asarray(g["L1_ffn_w_up"])
    wd = np.asarray(g["L1_ffn_w_down"])
    in_maps = []
    for c in range(8):
        m = dict(shared)
        m["x"] = np.ascontiguousarray(x[c].T)
        m["coreid"] = np.array([[c]], dtype=np.float32)
        m["wg_sl"] = np.ascontiguousarray(wg[c * 352:(c + 1) * 352])
        m["wu_sl"] = np.ascontiguousarray(wu[c * 352:(c + 1) * 352])
        m["wd_sl"] = np.ascontiguousarray(wd[c * 128:(c + 1) * 128])
        in_maps.append(m)
    return in_maps


def kernel(**inputs):
    in_maps = host_prep(inputs)
    nc = build_fused()
    res = run_bass_kernel_spmd(nc, in_maps, core_ids=CORES)
    out = np.stack([np.asarray(res.results[c]["out"]).T for c in range(8)])
    return np.ascontiguousarray(out.astype(np.float32))


def kernel_unfused(**inputs):
    g = inputs
    x = np.asarray(g["L0_attn_batch"])
    xT = [np.ascontiguousarray(x[c].T) for c in range(8)]

    nc = build_l0_pre()
    shared = {
        "lg1": np.ascontiguousarray(g["L0_ffn_fc1_logits"].reshape(1, 256)),
        "lg2": np.ascontiguousarray(g["L0_ffn_fc2_logits"].reshape(1, 256)),
        "a1t": np.ascontiguousarray(np.asarray(g["L0_ffn_fc1_A"]).transpose(0, 2, 1)),
        "b1": np.ascontiguousarray(g["L0_ffn_fc1_B"]),
        "a2t": np.ascontiguousarray(np.asarray(g["L0_ffn_fc2_A"]).transpose(0, 2, 1)),
        "b2": np.ascontiguousarray(g["L0_ffn_fc2_B"]),
    }
    in_maps = []
    for c in range(8):
        m = dict(shared)
        m["x"] = xT[c]
        m["coreid"] = np.array([[c]], dtype=np.float32)
        in_maps.append(m)
    r1 = run_bass_kernel_spmd(nc, in_maps, core_ids=CORES)
    xn0 = [np.asarray(r1.results[c]["xn0"]) for c in range(8)]
    w1g = np.ascontiguousarray(np.stack(
        [np.asarray(r1.results[c]["w1sl"]) for c in range(8)]))
    w2g = np.ascontiguousarray(np.concatenate(
        [np.asarray(r1.results[c]["w2sl"]) for c in range(8)], axis=0))

    nc = build_l0_attn()
    shared = {
        "wq_t": np.ascontiguousarray(np.asarray(g["L0_attn_Wq"]).T),
        "wk_t": np.ascontiguousarray(np.asarray(g["L0_attn_Wk"]).T),
        "wv_t": np.ascontiguousarray(np.asarray(g["L0_attn_Wv"]).T),
        "wo_t": np.ascontiguousarray(np.asarray(g["L0_attn_Wo"]).T),
        "bq2": lay_bias(np.asarray(g["L0_attn_bq"])),
        "bk2": lay_bias(np.asarray(g["L0_attn_bk"])),
        "bo2": lay_bias(np.asarray(g["L0_attn_bo"])),
        "bv_row": np.ascontiguousarray(np.asarray(g["L0_attn_bv"]).reshape(1, D)),
    }
    in_maps = [dict(shared, xn=xn0[c]) for c in range(8)]
    r2 = run_bass_kernel_spmd(nc, in_maps, core_ids=CORES)
    a0 = [np.asarray(r2.results[c]["a0"]) for c in range(8)]

    nc = build_l0_ffn()
    in_maps = [{"xn": xn0[c], "w1g": w1g, "w2g": w2g, "x": xT[c], "a0": a0[c]}
               for c in range(8)]
    r3 = run_bass_kernel_spmd(nc, in_maps, core_ids=CORES)
    x1 = [np.asarray(r3.results[c]["x1"]) for c in range(8)]
    xn1 = [np.asarray(r3.results[c]["xn1"]) for c in range(8)]
    mx8 = np.ascontiguousarray(np.concatenate(
        [np.asarray(r3.results[c]["mx"]) for c in range(8)], axis=1))

    nc = build_l1_quant()
    wg = np.asarray(g["L1_ffn_w_gate"])
    wu = np.asarray(g["L1_ffn_w_up"])
    wd = np.asarray(g["L1_ffn_w_down"])
    in_maps = [{
        "wg_sl": np.ascontiguousarray(wg[c * 352:(c + 1) * 352]),
        "wu_sl": np.ascontiguousarray(wu[c * 352:(c + 1) * 352]),
        "wd_sl": np.ascontiguousarray(wd[c * 128:(c + 1) * 128]),
    } for c in range(8)]
    r4 = run_bass_kernel_spmd(nc, in_maps, core_ids=CORES)
    wgt = np.ascontiguousarray(np.concatenate(
        [np.asarray(r4.results[c]["wgt_sl"]) for c in range(8)], axis=1))
    wut = np.ascontiguousarray(np.concatenate(
        [np.asarray(r4.results[c]["wut_sl"]) for c in range(8)], axis=1))
    wdt = np.ascontiguousarray(np.concatenate(
        [np.asarray(r4.results[c]["wdt_sl"]) for c in range(8)], axis=1))
    sg = np.ascontiguousarray(np.concatenate(
        [np.asarray(r4.results[c]["sg_sl"]) for c in range(8)], axis=1))
    su = np.ascontiguousarray(np.concatenate(
        [np.asarray(r4.results[c]["su_sl"]) for c in range(8)], axis=1))
    sd = np.ascontiguousarray(np.concatenate(
        [np.asarray(r4.results[c]["sd_sl"]) for c in range(8)], axis=1))

    nc = build_l1_attn()
    shared = {
        "w_in": np.ascontiguousarray(g["L1_attn_W_in"]),
        "w_r": np.ascontiguousarray(g["L1_attn_W_r"]),
        "w_i": np.ascontiguousarray(g["L1_attn_W_i"]),
        "w_out": np.ascontiguousarray(g["L1_attn_W_out"]),
        "b_in2": lay_bias(np.asarray(g["L1_attn_b_in"])),
        "b_r2": lay_bias(np.asarray(g["L1_attn_b_r"])),
        "b_i2": lay_bias(np.asarray(g["L1_attn_b_i"])),
        "b_out2": lay_bias(np.asarray(g["L1_attn_b_out"])),
        "log_a2": lay_bias(np.asarray(g["L1_attn_log_a"])),
    }
    in_maps = [dict(shared, xn=xn1[c]) for c in range(8)]
    r5 = run_bass_kernel_spmd(nc, in_maps, core_ids=CORES)
    a1 = [np.asarray(r5.results[c]["a1"]) for c in range(8)]

    nc = build_l1_ffn_a()
    clip_x = np.ascontiguousarray(np.asarray(g["L1_ffn_clip_x"]).reshape(1, 1))
    clip_d = np.ascontiguousarray(np.asarray(g["L1_ffn_clip_down"]).reshape(1, 1))
    shared = {"mx8": mx8, "clip_x": clip_x, "wgt": wgt, "wut": wut,
              "sg": sg, "su": su}
    in_maps = [dict(shared, xn=xn1[c]) for c in range(8)]
    r6 = run_bass_kernel_spmd(nc, in_maps, core_ids=CORES)
    h1 = [np.asarray(r6.results[c]["h1"]) for c in range(8)]
    hm8 = np.ascontiguousarray(np.concatenate(
        [np.asarray(r6.results[c]["hmax"]) for c in range(8)], axis=1))

    nc = build_l1_ffn_b()
    shared = {"hmax8": hm8, "clip_d": clip_d, "wdt": wdt, "sd": sd}
    in_maps = [dict(shared, h1=h1[c], x1=x1[c], a1=a1[c]) for c in range(8)]
    r7 = run_bass_kernel_spmd(nc, in_maps, core_ids=CORES)
    out = np.stack([np.asarray(r7.results[c]["out"]).T for c in range(8)])
    return np.ascontiguousarray(out.astype(np.float32))
